# version 05
# baseline (speedup 1.0000x reference)
import numpy as np
import ml_dtypes
from contextlib import ExitStack

import concourse.bass as bass
import concourse.mybir as mybir
from concourse.bass_utils import run_bass_kernel_spmd

F32 = mybir.dt.float32
BF16 = mybir.dt.bfloat16
I8 = mybir.dt.int8
AF = mybir.ActivationFunctionType
ALU = mybir.AluOpType
AX = mybir.AxisListType

D = 1024
HID = 4096
DCH = D // 128
HCH = HID // 128
LN_EPS = 1e-5
L1_ATTN_EPS = 1e-6
GN_EPS = 1e-5


class Sem:
    def __init__(self, ctx, nc, name):
        self.h = ctx.enter_context(nc.semaphore(name))
        self.n = 0

    def inc(self, instr, k=1):
        instr.then_inc(self.h, k)
        self.n += k
        return self.n

    def wait(self, eng, n=None):
        eng.wait_ge(self.h, self.n if n is None else n)


class Env:
    def __init__(self, nc, ctx):
        self.nc = nc
        self.ctx = ctx
        self.pools = [ctx]
        self._semid = 0

    def sem(self, name=None):
        self._semid += 1
        return Sem(self.ctx, self.nc, name or f"sem{self._semid}")

    def sb(self, name, shape, dt):
        return self.pools[-1].enter_context(self.nc.sbuf_tensor(name, shape, dt))

    def ps(self, name, shape, dt=F32):
        return self.pools[-1].enter_context(self.nc.psum_tensor(name, shape, dt))

    def push_pool(self):
        self.pools.append(ExitStack())

    def pop_pool(self):
        self.pools.pop().close()


def emit_cast_weights(env, pairs, tag):
    nc = env.nc
    maxc = max(p[0].shape[1] for p in pairs)
    buf = [env.sb(f"cw_{tag}_{i}", [128, maxc], BF16) for i in range(2)]
    ld = [env.sem(f"cw_ld{i}_{tag}") for i in range(2)]
    st = [env.sem(f"cw_st{i}_{tag}") for i in range(2)]
    chunks = []
    for src, dst in pairs:
        r, c = src.shape
        assert r % 128 == 0 and dst.shape[0] == r and dst.shape[1] == c
        for i in range(r // 128):
            chunks.append((src, dst, i, c))
    with nc.Block() as blk:
        @blk.gpsimd
        def _(g):
            for k, (src, dst, i, c) in enumerate(chunks):
                p = k % 2
                if k >= 2:
                    g.wait_ge(st[p].h, 16 * (k // 2))
                ld[p].inc(g.dma_start(out=buf[p][:, 0:c],
                                      in_=src[i * 128:(i + 1) * 128, :]), 16)

        @blk.sync
        def _(s):
            for k, (src, dst, i, c) in enumerate(chunks):
                p = k % 2
                s.wait_ge(ld[p].h, 16 * (k // 2 + 1))
                st[p].inc(s.dma_start(out=dst[i * 128:(i + 1) * 128, :],
                                      in_=buf[p][:, 0:c]), 16)
            s.wait_ge(st[0].h, st[0].n)
            s.wait_ge(st[1].h, st[1].n)


def emit_s4(env, dd, T, consts):
    nc = env.nc
    NT = 512
    ntiles = T // NT
    identbf = consts["identbf"]

    w1b = nc.dram_tensor("s4_w1b", [D, HID], BF16).ap()
    w2b = nc.dram_tensor("s4_w2b", [D, HID], BF16).ap()
    h1b = nc.dram_tensor("s4_h1b", [HID, HID], BF16).ap()
    h2b = nc.dram_tensor("s4_h2b", [HID, HID], BF16).ap()
    w3b = nc.dram_tensor("s4_w3b", [HID, D], BF16).ap()
    emit_cast_weights(env, [(dd["w1T"], w1b), (dd["w2T"], w2b),
                            (dd["h1T"], h1b), (dd["h2T"], h2b),
                            (dd["w3T"], w3b)], "s4")

    s1_sb = env.sb("s4_s1", [128, HCH], F32)
    s2_sb = env.sb("s4_s2", [128, HCH], F32)
    h1s_sb = env.sb("s4_h1s", [128, HCH], F32)
    h2s_sb = env.sb("s4_h2s", [128, HCH], F32)
    w3s_sb = env.sb("s4_w3s", [128, DCH], F32)
    ld0 = env.sem("s4_ld0")
    with nc.Block() as blk:
        @blk.sync
        def _(s):
            for sb, key in ((s1_sb, "s1"), (s2_sb, "s2"), (h1s_sb, "h1s"),
                            (h2s_sb, "h2s"), (w3s_sb, "w3s")):
                ld0.inc(s.dma_start(out=sb[:, :], in_=dd[key][:, :]), 16)
            s.wait_ge(ld0.h, ld0.n)

    x3tm = env.sb("s4_x3tm", [128, 2 * 4 * 1024], BF16)
    x3fm = env.sb("s4_x3fm", [128, DCH * NT], BF16)
    ht = env.sb("s4_ht", [128, HCH * NT], BF16)
    h2sb = env.sb("s4_h2", [128, HCH * NT], BF16)
    t1 = env.sb("s4_t1", [128, NT], BF16)
    t2 = env.sb("s4_t2", [128, NT], BF16)
    wblkA = [env.sb(f"s4_wA{i}", [128, DCH * 128], BF16) for i in range(2)]
    wblkB = [env.sb(f"s4_wB{i}", [128, DCH * 128], BF16) for i in range(2)]
    wblkH = [env.sb(f"s4_wH{i}", [128, HCH * 128], BF16) for i in range(2)]
    y4fm = env.sb("s4_y4fm", [128, DCH * NT], BF16)
    v4 = env.sb("s4_v4", [128, 4096], F32)
    sq4 = env.sb("s4_sq4", [128, 4096], F32)
    o_sb = env.sb("s4_o", [128, 4 * 1024], F32)
    ssq4 = env.sb("s4_ssq4", [128, 4], F32)
    sm1_4 = env.sb("s4_sm14", [128, 4], F32)
    sm2_4 = env.sb("s4_sm24", [128, 4], F32)
    ssq2_4 = env.sb("s4_ssq24", [128, 4], F32)
    t4 = env.sb("s4_t4", [128, 4], F32)
    rstd4 = env.sb("s4_rstd4", [128, 4], F32)

    psZ = [env.ps(f"s4_psZ{i}", [128, 512], F32) for i in range(4)]
    ps_tp = [env.ps(f"s4_pstp{i}", [128, 512], BF16) for i in range(2)]
    ps_tm = [env.ps(f"s4_pstm{i}", [128, 1024], BF16) for i in range(2)]

    ldx = [env.sem("s4_ldx0"), env.sem("s4_ldx1")]
    tpx = env.sem("s4_tpx")
    evx = env.sem("s4_evx")
    ldw = [env.sem("s4_ldw0"), env.sem("s4_ldw1")]
    mmA = env.sem("s4_mmA")
    mmB = env.sem("s4_mmB")
    acA = env.sem("s4_acA")
    evB = env.sem("s4_evB")
    mulS = env.sem("s4_mul")
    tpy = env.sem("s4_tpy")
    vst = env.sem("s4_vst")
    sselfS = env.sem("s4_sselfS")
    fin = env.sem("s4_fin")
    sto = env.sem("s4_sto")

    with nc.Block() as blk:
        @blk.sync
        def _(s):
            for j in range(4):
                ldx[0].inc(s.dma_start(
                    out=x3tm[:, j * 1024:(j + 1) * 1024],
                    in_=dd["x3_tm"][j * 128:(j + 1) * 128, :]), 16)

    for t in range(ntiles):
        slot = t % 2
        xoff = slot * 4096
        nslot = (t + 1) % 2

        with nc.Block() as blk:
            ldx_t0 = ldx[slot].n
            evx0, tpx0 = evx.n, tpx.n

            @blk.sync
            def _(s):
                if t + 1 < ntiles:
                    for j in range(4):
                        ldx[nslot].inc(s.dma_start(
                            out=x3tm[:, nslot * 4096 + j * 1024: nslot * 4096 + (j + 1) * 1024],
                            in_=dd["x3_tm"][(t + 1) * NT + j * 128:(t + 1) * NT + (j + 1) * 128, :]), 16)

            @blk.tensor
            def _(te):
                te.wait_ge(ldx[slot].h, ldx_t0)
                for i in range(DCH):
                    if i >= 2:
                        te.wait_ge(evx.h, evx0 + i - 1)
                    for j in range(4):
                        ins = te.transpose(
                            ps_tp[i % 2][:, j * 128:(j + 1) * 128],
                            x3tm[:, xoff + j * 1024 + i * 128: xoff + j * 1024 + (i + 1) * 128],
                            identbf[:, :])
                    tpx.inc(ins)

            @blk.vector
            def _(v):
                for i in range(DCH):
                    v.wait_ge(tpx.h, tpx0 + i + 1)
                    evx.inc(v.tensor_copy(x3fm[:, i * NT:(i + 1) * NT], ps_tp[i % 2][:, :]))

        with nc.Block() as blk:
            mmA0, mmB0, acA0, evB0, mulS0 = mmA.n, mmB.n, acA.n, evB.n, mulS.n
            ldw0 = [ldw[0].n, ldw[1].n]

            @blk.gpsimd
            def _(g):
                for o in range(HCH):
                    p = o % 2
                    if o >= 2:
                        g.wait_ge(mmB.h, mmB0 + o - 1)
                    ldw[p].inc(g.dma_start(
                        out=wblkA[p][:, :],
                        in_=w1b.rearrange("(ic p) o -> p ic o", p=128)[:, :, o * 128:(o + 1) * 128]), 16)
                    ldw[p].inc(g.dma_start(
                        out=wblkB[p][:, :],
                        in_=w2b.rearrange("(ic p) o -> p ic o", p=128)[:, :, o * 128:(o + 1) * 128]), 16)

            @blk.tensor
            def _(te):
                for o in range(HCH):
                    te.wait_ge(ldw[o % 2].h, ldw0[o % 2] + 32 * (o // 2 + 1))
                    if o >= 2:
                        te.wait_ge(acA.h, acA0 + o - 1)
                        te.wait_ge(evB.h, evB0 + o - 1)
                    for i in range(DCH):
                        ins = te.matmul(psZ[2 * (o % 2)][:, :],
                                        wblkA[o % 2][:, i * 128:(i + 1) * 128],
                                        x3fm[:, i * NT:(i + 1) * NT],
                                        start=(i == 0), stop=(i == DCH - 1))
                    mmA.inc(ins)
                    for i in range(DCH):
                        ins = te.matmul(psZ[2 * (o % 2) + 1][:, :],
                                        wblkB[o % 2][:, i * 128:(i + 1) * 128],
                                        x3fm[:, i * NT:(i + 1) * NT],
                                        start=(i == 0), stop=(i == DCH - 1))
                    mmB.inc(ins)

            @blk.scalar
            def _(sc):
                for o in range(HCH):
                    sc.wait_ge(mmA.h, mmA0 + o + 1)
                    if o >= 1:
                        sc.wait_ge(mulS.h, mulS0 + o)
                    acA.inc(sc.activation(t1[:, :], psZ[2 * (o % 2)][:, :], AF.Silu,
                                          bias=0.0, scale=s1_sb[:, o:o + 1]))

            @blk.vector
            def _(v):
                for o in range(HCH):
                    v.wait_ge(mmB.h, mmB0 + o + 1)
                    evB.inc(v.tensor_scalar_mul(t2[:, :], psZ[2 * (o % 2) + 1][:, :],
                                                s2_sb[:, o:o + 1]))
                    v.wait_ge(acA.h, acA0 + o + 1)
                    mulS.inc(v.tensor_mul(ht[:, o * NT:(o + 1) * NT], t1[:, :], t2[:, :]))

        for wsrc, ssb, src, dst in ((h1b, h1s_sb, ht, h2sb), (h2b, h2s_sb, h2sb, ht)):
            with nc.Block() as blk:
                mm0, ac0 = mmA.n, acA.n
                ldw0 = [ldw[0].n, ldw[1].n]

                @blk.gpsimd
                def _(g):
                    for o in range(HCH):
                        if o >= 2:
                            g.wait_ge(mmA.h, mm0 + o - 1)
                        ldw[o % 2].inc(g.dma_start(
                            out=wblkH[o % 2][:, :],
                            in_=wsrc.rearrange("(ic p) o -> p ic o", p=128)[:, :, o * 128:(o + 1) * 128]), 16)

                @blk.tensor
                def _(te):
                    for o in range(HCH):
                        te.wait_ge(ldw[o % 2].h, ldw0[o % 2] + 16 * (o // 2 + 1))
                        if o >= 2:
                            te.wait_ge(acA.h, ac0 + o - 1)
                        ps = psZ[o % 2]
                        for i in range(HCH):
                            ins = te.matmul(ps[:, :], wblkH[o % 2][:, i * 128:(i + 1) * 128],
                                            src[:, i * NT:(i + 1) * NT],
                                            start=(i == 0), stop=(i == HCH - 1))
                        mmA.inc(ins)

                @blk.scalar
                def _(sc):
                    for o in range(HCH):
                        sc.wait_ge(mmA.h, mm0 + o + 1)
                        acA.inc(sc.activation(dst[:, o * NT:(o + 1) * NT], psZ[o % 2][:, :],
                                              AF.Gelu, bias=0.0, scale=ssb[:, o:o + 1]))

        with nc.Block() as blk:
            mm0, ev0 = mmA.n, evB.n
            ldw0 = [ldw[0].n, ldw[1].n]

            @blk.gpsimd
            def _(g):
                for o in range(DCH):
                    if o >= 2:
                        g.wait_ge(mmA.h, mm0 + o - 1)
                    ldw[o % 2].inc(g.dma_start(
                        out=wblkH[o % 2][:, :],
                        in_=w3b.rearrange("(ic p) o -> p ic o", p=128)[:, :, o * 128:(o + 1) * 128]), 16)

            @blk.tensor
            def _(te):
                for o in range(DCH):
                    te.wait_ge(ldw[o % 2].h, ldw0[o % 2] + 16 * (o // 2 + 1))
                    if o >= 2:
                        te.wait_ge(evB.h, ev0 + o - 1)
                    ps = psZ[o % 2]
                    for i in range(HCH):
                        ins = te.matmul(ps[:, :], wblkH[o % 2][:, i * 128:(i + 1) * 128],
                                        ht[:, i * NT:(i + 1) * NT],
                                        start=(i == 0), stop=(i == HCH - 1))
                    mmA.inc(ins)

            @blk.vector
            def _(v):
                for o in range(DCH):
                    v.wait_ge(mmA.h, mm0 + o + 1)
                    evB.inc(v.tensor_scalar_mul(y4fm[:, o * NT:(o + 1) * NT],
                                                psZ[o % 2][:, :], w3s_sb[:, o:o + 1]))

        with nc.Block() as blk:
            tp0, vst0 = tpy.n, vst.n

            @blk.tensor
            def _(te):
                for j in range(4):
                    if j >= 2:
                        te.wait_ge(vst.h, vst0 + j - 1)
                    for i in range(DCH):
                        ins = te.transpose(
                            ps_tm[j % 2][:, i * 128:(i + 1) * 128],
                            y4fm[:, i * NT + j * 128: i * NT + (j + 1) * 128],
                            identbf[:, :])
                    tpy.inc(ins)

            @blk.vector
            def _(v):
                for j in range(4):
                    v.wait_ge(tpy.h, tp0 + j + 1)
                    vst.inc(v.tensor_add(v4[:, j * 1024:(j + 1) * 1024],
                                         x3tm[:, xoff + j * 1024: xoff + (j + 1) * 1024],
                                         ps_tm[j % 2][:, :]))
                for j in range(4):
                    v.reduce_sum(ssq4[:, j:j + 1], v4[:, j * 1024:(j + 1) * 1024], axis=AX.X)

        with nc.Block() as blk:
            @blk.vector
            def _(v):
                v.tensor_scalar_mul(sm1_4[:, :], ssq4[:, :], -1.0 / D)
                v.tensor_scalar_mul(sm2_4[:, :], ssq4[:, :], 1.0 / D)

        with nc.Block() as blk:
            @blk.scalar
            def _(sc):
                for j in range(4):
                    sc.activation(sq4[:, j * 1024:(j + 1) * 1024],
                                  v4[:, j * 1024:(j + 1) * 1024], AF.Square,
                                  bias=sm1_4[:, j:j + 1], scale=1.0)

        with nc.Block() as blk:
            @blk.vector
            def _(v):
                for j in range(4):
                    v.reduce_sum(ssq2_4[:, j:j + 1], sq4[:, j * 1024:(j + 1) * 1024], axis=AX.X)

        with nc.Block() as blk:
            @blk.vector
            def _(v):
                v.tensor_scalar(t4[:, :], ssq2_4[:, :], 1.0 / D, LN_EPS, ALU.mult, ALU.add)

        with nc.Block() as blk:
            @blk.scalar
            def _(sc):
                sc.activation(rstd4[:, :], t4[:, :], AF.Ln).then_inc(sselfS.h, 1)
                sselfS.n += 1
                sc.wait_ge(sselfS.h, sselfS.n)
                sc.activation(rstd4[:, :], rstd4[:, :], AF.Exp, bias=0.0, scale=-0.5)

        with nc.Block() as blk:
            fin0 = fin.n

            @blk.vector
            def _(v):
                for j in range(4):
                    if t >= 1 and j == 0:
                        v.wait_ge(sto.h, 16 * 4 * t)
                    fin.inc(v.tensor_scalar(o_sb[:, j * 1024:(j + 1) * 1024],
                                            v4[:, j * 1024:(j + 1) * 1024],
                                            sm2_4[:, j:j + 1], rstd4[:, j:j + 1],
                                            ALU.subtract, ALU.mult))

            @blk.sync
            def _(s):
                for j in range(4):
                    s.wait_ge(fin.h, fin0 + j + 1)
                    sto.inc(s.dma_start(
                        out=dd["out"][t * NT + j * 128: t * NT + (j + 1) * 128, :],
                        in_=o_sb[:, j * 1024:(j + 1) * 1024]), 16)

    with nc.Block() as blk:
        @blk.sync
        def _(s):
            s.wait_ge(sto.h, sto.n)
    return {"x3fm": x3fm, "ht": ht, "h2sb": h2sb, "y4fm": y4fm, "o_sb": o_sb,
            "x3tm": x3tm}


def emit_ln_tail(env, consts, v4, o_sb, nsub, stats, cast_out_dt=BF16):
    nc = env.nc
    sq4 = stats["sq4"]
    with nc.Block() as blk:
        @blk.vector
        def _(v):
            for j in range(nsub):
                v.reduce_sum(stats["ssq4"][:, j:j + 1], v4[:, j * 1024:(j + 1) * 1024], axis=AX.X)
    with nc.Block() as blk:
        @blk.vector
        def _(v):
            v.tensor_scalar_mul(stats["sm1_4"][:, 0:nsub], stats["ssq4"][:, 0:nsub], -1.0 / D)
            v.tensor_scalar_mul(stats["sm2_4"][:, 0:nsub], stats["ssq4"][:, 0:nsub], 1.0 / D)
    with nc.Block() as blk:
        @blk.scalar
        def _(sc):
            for j in range(nsub):
                sc.activation(sq4[:, j * 1024:(j + 1) * 1024],
                              v4[:, j * 1024:(j + 1) * 1024], AF.Square,
                              bias=stats["sm1_4"][:, j:j + 1], scale=1.0)
    with nc.Block() as blk:
        @blk.vector
        def _(v):
            for j in range(nsub):
                v.reduce_sum(stats["ssq2_4"][:, j:j + 1], sq4[:, j * 1024:(j + 1) * 1024], axis=AX.X)
    with nc.Block() as blk:
        @blk.vector
        def _(v):
            v.tensor_scalar(stats["t4"][:, 0:nsub], stats["ssq2_4"][:, 0:nsub],
                            1.0 / D, LN_EPS, ALU.mult, ALU.add)
    with nc.Block() as blk:
        @blk.scalar
        def _(sc):
            ss = stats["selfsem"]
            sc.activation(stats["rstd4"][:, 0:nsub], stats["t4"][:, 0:nsub], AF.Ln).then_inc(ss.h, 1)
            ss.n += 1
            sc.wait_ge(ss.h, ss.n)
            sc.activation(stats["rstd4"][:, 0:nsub], stats["rstd4"][:, 0:nsub],
                          AF.Exp, bias=0.0, scale=-0.5)
    with nc.Block() as blk:
        @blk.vector
        def _(v):
            for j in range(nsub):
                v.tensor_scalar(o_sb[:, j * 1024:(j + 1) * 1024],
                                v4[:, j * 1024:(j + 1) * 1024],
                                stats["sm2_4"][:, j:j + 1], stats["rstd4"][:, j:j + 1],
                                ALU.subtract, ALU.mult)


def mk_ln_stats(env, tag, nsub=4):
    return {
        "sq4": env.sb(f"{tag}_sq4", [128, nsub * 1024], BF16),
        "ssq4": env.sb(f"{tag}_ssq4", [128, nsub], F32),
        "sm1_4": env.sb(f"{tag}_sm14", [128, nsub], F32),
        "sm2_4": env.sb(f"{tag}_sm24", [128, nsub], F32),
        "ssq2_4": env.sb(f"{tag}_ssq24", [128, nsub], F32),
        "t4": env.sb(f"{tag}_t4", [128, nsub], F32),
        "rstd4": env.sb(f"{tag}_rstd4", [128, nsub], F32),
        "selfsem": env.sem(f"{tag}_lnself"),
    }


def emit_global_mean_abs(env, tag, src_hbm, R, C, out_bc, consts):
    nc = env.nc
    nch = R // 128
    buf = [env.sb(f"{tag}_mb{i}", [128, C], F32) for i in range(2)]
    red = env.sb(f"{tag}_red", [128, max(nch, 2)], F32)
    acc1 = env.sb(f"{tag}_acc1", [128, 1], F32)
    accp = env.sb(f"{tag}_accp", [1, 128], F32)
    sc1 = env.sb(f"{tag}_sc1", [1, 1], F32)
    sc2 = env.sb(f"{tag}_sc2", [1, 1], F32)
    scr_hbm = nc.dram_tensor(f"{tag}_schbm", [1, 2], F32).ap()
    ps = env.ps(f"{tag}_ps", [1, 128], F32)
    ld = [env.sem(f"{tag}_ld{i}") for i in range(2)]
    rd = env.sem(f"{tag}_rd")
    with nc.Block() as blk:
        @blk.sync
        def _(s):
            for k in range(nch):
                p = k % 2
                if k >= 2:
                    s.wait_ge(rd.h, k - 1)
                ld[p].inc(s.dma_start(out=buf[p][:, :],
                                      in_=src_hbm[k * 128:(k + 1) * 128, :]), 16)

        @blk.vector
        def _(v):
            for k in range(nch):
                p = k % 2
                v.wait_ge(ld[p].h, 16 * (k // 2 + 1))
                rd.inc(v.reduce_sum(red[:, k:k + 1], buf[p][:, :], axis=AX.X,
                                    apply_absolute_value=True))
    with nc.Block() as blk:
        @blk.vector
        def _(v):
            v.reduce_sum(acc1[:, :], red[:, 0:nch], axis=AX.X)
    with nc.Block() as blk:
        @blk.tensor
        def _(te):
            te.transpose(ps[:, :], acc1[:, :], consts["identf"][:, :])
    with nc.Block() as blk:
        @blk.vector
        def _(v):
            v.tensor_copy(accp[:, :], ps[:, :])
    with nc.Block() as blk:
        @blk.vector
        def _(v):
            v.reduce_sum(sc1[:, :], accp[:, :], axis=AX.X)
    with nc.Block() as blk:
        @blk.vector
        def _(v):
            v.tensor_scalar(sc1[:, :], sc1[:, :], 1.0 / (R * C), 1e-5, ALU.mult, ALU.max)
    with nc.Block() as blk:
        @blk.vector
        def _(v):
            v.reciprocal(sc2[:, :], sc1[:, :])
    st = env.sem(f"{tag}_st")
    with nc.Block() as blk:
        @blk.sync
        def _(s):
            st.inc(s.dma_start(out=scr_hbm[:, 0:1], in_=sc1[:, :]), 16)
            st.inc(s.dma_start(out=scr_hbm[:, 1:2], in_=sc2[:, :]), 16)
            s.wait_ge(st.h, st.n)
            st.inc(s.dma_start(out=out_bc["mean"][:, :],
                               in_=scr_hbm[0:1, 0:1].broadcast_to((128, 1))), 16)
            st.inc(s.dma_start(out=out_bc["inv"][:, :],
                               in_=scr_hbm[0:1, 1:2].broadcast_to((128, 1))), 16)
            s.wait_ge(st.h, st.n)


def emit_ternarize(env, tag, src_hbm, dst_hbm, R, C, inv_bc):
    nc = env.nc
    buf = [env.sb(f"{tag}_tb{i}", [128, C], F32) for i in range(2)]
    tf = env.sb(f"{tag}_tf", [128, C], F32)
    ti = env.sb(f"{tag}_ti", [128, C], I8)
    tb = [env.sb(f"{tag}_tbf{i}", [128, C], BF16) for i in range(2)]
    ld = [env.sem(f"{tag}_tld{i}") for i in range(2)]
    cv = env.sem(f"{tag}_tcv")
    st = [env.sem(f"{tag}_tst{i}") for i in range(2)]
    nch = R // 128
    with nc.Block() as blk:
        @blk.gpsimd
        def _(g):
            for k in range(nch):
                p = k % 2
                if k >= 2:
                    g.wait_ge(cv.h, (k - 1) * 4)
                ld[p].inc(g.dma_start(out=buf[p][:, :],
                                      in_=src_hbm[k * 128:(k + 1) * 128, :]), 16)

        @blk.vector
        def _(v):
            for k in range(nch):
                p = k % 2
                v.wait_ge(ld[p].h, 16 * (k // 2 + 1))
                if k >= 2:
                    v.wait_ge(st[p].h, 16 * (k // 2))
                v.tensor_scalar_mul(tf[:, :], buf[p][:, :], inv_bc[:, :])
                v.tensor_copy(ti[:, :], tf[:, :])
                v.tensor_copy(tb[p][:, :], ti[:, :])
                cv.inc(v.tensor_scalar(tb[p][:, :], tb[p][:, :], 1.0, -1.0,
                                       ALU.min, ALU.max), 4)

        @blk.sync
        def _(s):
            for k in range(nch):
                p = k % 2
                s.wait_ge(cv.h, (k + 1) * 4)
                st[p].inc(s.dma_start(out=dst_hbm[k * 128:(k + 1) * 128, :],
                                      in_=tb[p][:, :]), 16)
            s.wait_ge(st[0].h, st[0].n)
            s.wait_ge(st[1].h, st[1].n)


def emit_bcast_rows(env, tag, src_row_hbm, nrows, C, dst_tiles, consts, scale=1.0):
    nc = env.nc
    row_f = env.sb(f"{tag}_rowf", [1, C], F32)
    row_b = env.sb(f"{tag}_rowb", [1, C], BF16)
    ps = [env.ps(f"{tag}_bps{i}", [128, 512], F32) for i in range(2)]
    ld = env.sem(f"{tag}_bld")
    mm = env.sem(f"{tag}_bmm")
    ev = env.sem(f"{tag}_bev")
    for r in range(nrows):
        with nc.Block() as blk:
            @blk.sync
            def _(s):
                ld.inc(s.dma_start(out=row_f[:, :], in_=src_row_hbm[r:r + 1, :]), 16)

            @blk.vector
            def _(v):
                v.wait_ge(ld.h, ld.n)
                if scale != 1.0:
                    v.tensor_scalar_mul(row_b[:, :], row_f[:, :], scale)
                else:
                    v.tensor_copy(row_b[:, :], row_f[:, :])
        with nc.Block() as blk:
            mm0, ev0 = mm.n, ev.n

            @blk.tensor
            def _(te):
                for c in range(C // 512):
                    if c >= 2:
                        te.wait_ge(ev.h, ev0 + c - 1)
                    mm.inc(te.matmul(ps[c % 2][:, :], consts["onesbf"][:, :],
                                     row_b[:, c * 512:(c + 1) * 512],
                                     start=True, stop=True))

            @blk.vector
            def _(v):
                for c in range(C // 512):
                    v.wait_ge(mm.h, mm0 + c + 1)
                    ev.inc(v.tensor_copy(dst_tiles[r][:, c * 512:(c + 1) * 512],
                                         ps[c % 2][:, :]))


def emit_s2(env, dd, T, consts, seqlen=4096):
    nc = env.nc
    NT = 512
    S = seqlen
    ntiles = T // NT
    identbf = consts["identbf"]

    w1m = {"mean": env.sb("s2_w1mean", [128, 1], F32), "inv": env.sb("s2_w1inv", [128, 1], F32)}
    w2m = {"mean": env.sb("s2_w2mean", [128, 1], F32), "inv": env.sb("s2_w2inv", [128, 1], F32)}
    env.push_pool()
    emit_global_mean_abs(env, "s2w1", dd["w1T"], D, HID, w1m, consts)
    emit_global_mean_abs(env, "s2w2", dd["w2T"], HID, D, w2m, consts)
    env.pop_pool()
    t1_hbm = nc.dram_tensor("s2_t1b", [D, HID], BF16).ap()
    t2_hbm = nc.dram_tensor("s2_t2b", [HID, D], BF16).ap()
    env.push_pool()
    emit_ternarize(env, "s2t1", dd["w1T"], t1_hbm, D, HID, w1m["inv"])
    emit_ternarize(env, "s2t2", dd["w2T"], t2_hbm, HID, D, w2m["inv"])
    env.pop_pool()
    wbc = [env.sb(f"s2_wbc{i}", [128, HID], BF16) for i in range(4)]
    env.push_pool()
    emit_bcast_rows(env, "s2cw", dd["convw"], 3, HID, wbc[0:3], consts)
    emit_bcast_rows(env, "s2cb", dd["convb"], 1, HID, wbc[3:4], consts)
    env.pop_pool()

    h_hbm = nc.dram_tensor("s2_h", [T, HID], BF16).ap()

    qs = env.sem("s2_qs")

    def selfsem(v, ins):
        qs.n += 1
        ins.then_inc(qs.h, 1)
        v.wait_ge(qs.h, qs.n)

    env.push_pool()
    t1_sb = env.sb("s2_t1sb", [128, DCH * HID], BF16)
    ldr = env.sem("s2_ldr")
    with nc.Block() as blk:
        @blk.sync
        def _(s):
            ldr.inc(s.dma_start(out=t1_sb[:, :],
                                in_=t1_hbm.rearrange("(ic p) o -> p ic o", p=128)), 16)
            s.wait_ge(ldr.h, ldr.n)

    x1tm = env.sb("s2_x1tm", [128, 2 * 4 * 1024], BF16)
    am4 = env.sb("s2_am4", [128, 2 * 4], F32)
    s4t = env.sb("s2_s4t", [128, 4], F32)
    u4t = env.sb("s2_u4t", [128, 4], F32)
    xqf = env.sb("s2_xqf", [128, 1024], F32)
    xqi = env.sb("s2_xqi", [128, 1024], I8)
    xqb = env.sb("s2_xqb", [128, 4 * 1024], BF16)
    xq_fm = env.sb("s2_xqfm", [128, DCH * NT], BF16)
    h_sb = env.sb("s2_hsb", [128, 2 * 4096], BF16)
    ps_tp = [env.ps(f"s2_pstp{i}", [128, 512], BF16) for i in range(2)]
    psM = [env.ps(f"s2_psM{i}", [128, 512], F32) for i in range(2)]

    ldx = [env.sem("s2_ldx0"), env.sem("s2_ldx1")]
    tpq = env.sem("s2_tpq")
    evq = env.sem("s2_evq")
    mmh = env.sem("s2_mmh")
    evh = env.sem("s2_evh")
    sth = env.sem("s2_sth")

    with nc.Block() as blk:
        @blk.sync
        def _(s):
            for j in range(4):
                ldx[0].inc(s.dma_start(out=x1tm[:, j * 1024:(j + 1) * 1024],
                                       in_=dd["x1_tm"][j * 128:(j + 1) * 128, :]), 16)
            ldx[0].inc(s.dma_start(out=am4[:, 0:4], in_=dd["amax1_c"][:, 0:4]), 16)

    for t in range(ntiles):
        slot = t % 2
        xoff = slot * 4096
        nslot = (t + 1) % 2

        with nc.Block() as blk:
            ld_t0 = ldx[slot].n

            @blk.sync
            def _(s):
                if t + 1 < ntiles:
                    for j in range(4):
                        ldx[nslot].inc(s.dma_start(
                            out=x1tm[:, nslot * 4096 + j * 1024: nslot * 4096 + (j + 1) * 1024],
                            in_=dd["x1_tm"][(t + 1) * NT + j * 128:(t + 1) * NT + (j + 1) * 128, :]), 16)
                    ldx[nslot].inc(s.dma_start(out=am4[:, nslot * 4:nslot * 4 + 4],
                                               in_=dd["amax1_c"][:, (t + 1) * 4:(t + 1) * 4 + 4]), 16)

            @blk.vector
            def _(v):
                v.wait_ge(ldx[slot].h, ld_t0)
                selfsem(v, v.tensor_scalar_max(s4t[:, :], am4[:, slot * 4:slot * 4 + 4], 1e-5))
                selfsem(v, v.tensor_scalar_mul(u4t[:, :], s4t[:, :], w1m["mean"][:, :]))
                v.tensor_scalar_mul(u4t[:, :], u4t[:, :], 1.0 / 127.0)
                selfsem(v, v.reciprocal(s4t[:, :], s4t[:, :]))
                selfsem(v, v.tensor_scalar_mul(s4t[:, :], s4t[:, :], 127.0))
                for j in range(4):
                    v.tensor_scalar_mul(xqf[:, :], x1tm[:, xoff + j * 1024: xoff + (j + 1) * 1024],
                                        s4t[:, j:j + 1])
                    v.tensor_copy(xqi[:, :], xqf[:, :])
                    v.tensor_copy(xqb[:, j * 1024:(j + 1) * 1024], xqi[:, :])

        with nc.Block() as blk:
            tp0, ev0 = tpq.n, evq.n

            @blk.tensor
            def _(te):
                for k in range(8):
                    j, g = k // 2, k % 2
                    if k >= 2:
                        te.wait_ge(evq.h, ev0 + k - 1)
                    for ii in range(4):
                        i = g * 4 + ii
                        ins = te.transpose(ps_tp[k % 2][:, ii * 128:(ii + 1) * 128],
                                           xqb[:, j * 1024 + i * 128: j * 1024 + (i + 1) * 128],
                                           identbf[:, :])
                    tpq.inc(ins)

            @blk.vector
            def _(v):
                for k in range(8):
                    j, g = k // 2, k % 2
                    v.wait_ge(tpq.h, tp0 + k + 1)
                    evq.inc(v.tensor_copy(
                        xq_fm.rearrange("p (i n) -> p i n", n=NT)[:, g * 4:(g + 1) * 4, j * 128:(j + 1) * 128],
                        ps_tp[k % 2].rearrange("p (i n) -> p i n", n=128)[:, :, :]))

        with nc.Block() as blk:
            mm0, eh0, st0 = mmh.n, evh.n, sth.n

            @blk.tensor
            def _(te):
                for j in range(4):
                    for o in range(8):
                        k = j * 8 + o
                        if k >= 2:
                            te.wait_ge(evh.h, eh0 + k - 1)
                        for i in range(DCH):
                            ins = te.matmul(psM[k % 2][:, :],
                                            xq_fm[:, i * NT + j * 128: i * NT + (j + 1) * 128],
                                            t1_sb[:, i * HID + o * 512: i * HID + (o + 1) * 512],
                                            start=(i == 0), stop=(i == DCH - 1))
                        mmh.inc(ins)

            @blk.vector
            def _(v):
                for j in range(4):
                    hoff = (j % 2) * 4096
                    if j >= 2:
                        v.wait_ge(sth.h, st0 + 16 * j)
                    for o in range(8):
                        k = j * 8 + o
                        v.wait_ge(mmh.h, mm0 + k + 1)
                        evh.inc(v.tensor_scalar_mul(h_sb[:, hoff + o * 512: hoff + (o + 1) * 512],
                                                    psM[k % 2][:, :], u4t[:, j:j + 1]))

            @blk.sync
            def _(s):
                for j in range(4):
                    hoff = (j % 2) * 4096
                    s.wait_ge(evh.h, eh0 + (j + 1) * 8)
                    sth.inc(s.dma_start(
                        out=h_hbm[t * NT + j * 128: t * NT + (j + 1) * 128, :],
                        in_=h_sb[:, hoff:hoff + 4096]), 16)

    with nc.Block() as blk:
        @blk.sync
        def _(s):
            s.wait_ge(sth.h, sth.n)
    env.pop_pool()

    env.push_pool()
    h0_sb = env.sb("s2_h0", [128, 4 * 4096], BF16)
    hm_sb = env.sb("s2_hm", [128, 4096], BF16)
    hp_sb = env.sb("s2_hp", [128, 4096], BF16)
    cv_sb = env.sb("s2_cv", [128, 4096], BF16)
    sil_sb = env.sb("s2_sil", [128, 4096], BF16)
    x1p = env.sb("s2_x1p", [128, 4 * 1024], BF16)
    a2t = env.sb("s2_a2t", [128, 4], F32)
    s2t = env.sb("s2_s2t", [128, 4], F32)
    u2t = env.sb("s2_u2t", [128, 4], F32)
    xq2f = env.sb("s2_xq2f", [128, 2048], F32)
    xq2i = env.sb("s2_xq2i", [128, 4096], I8)
    xq2b = env.sb("s2_xq2b", [128, 4096], BF16)
    xq2fm = env.sb("s2_xq2fm", [128, 4 * 4096], BF16)
    y2_sb = env.sb("s2_y2", [128, 4 * 1024], BF16)
    t2blk = [env.sb(f"s2_t2blk{i}", [128, 512], BF16) for i in range(2)]
    ps_tp2 = [env.ps(f"s2_pstp2{i}", [128, 512], BF16) for i in range(2)]
    ps4 = [env.ps(f"s2_ps4{i}", [128, 512], F32) for i in range(4)]
    v4 = env.sb("s2_v4", [128, 4 * 1024], F32)
    o2_sb = env.sb("s2_o2", [128, 4 * 1024], BF16)
    lnst = mk_ln_stats(env, "s2ln")

    ldh = env.sem("s2_ldh")
    shd = env.sem("s2_shd")
    cvs = env.sem("s2_cvs")
    cvg = env.sem("s2_cvg")
    sil = env.sem("s2_sil")
    tp2 = env.sem("s2_tp2")
    ev2 = env.sem("s2_ev2")
    ld2 = [env.sem("s2_ld2a"), env.sem("s2_ld2b")]
    mm2 = env.sem("s2_mm2")
    evy = env.sem("s2_evy")
    st2 = env.sem("s2_st2")

    for t in range(ntiles):
        with nc.Block() as blk:
            ld0 = ldh.n

            @blk.sync
            def _(s):
                for j in range(4):
                    r0 = t * NT + j * 128
                    ldh.inc(s.dma_start(out=h0_sb[:, j * 4096:(j + 1) * 4096],
                                        in_=h_hbm[r0:r0 + 128, :]), 16)
                    ldh.inc(s.dma_start(out=x1p[:, j * 1024:(j + 1) * 1024],
                                        in_=dd["x1_tm"][r0:r0 + 128, :]), 16)

        for j in range(4):
            r0 = t * NT + j * 128
            hj = j * 4096
            with nc.Block() as blk:
                sh0 = shd.n

                @blk.vector
                def _(v):
                    if r0 % S == 0:
                        v.memset(hm_sb[:, :], 0.0)
                    if (r0 + 128) % S == 0:
                        v.memset(hp_sb[:, :], 0.0)

            with nc.Block() as blk:
                @blk.sync
                def _(s):
                    if r0 % S == 0:
                        shd.inc(s.dma_start(out=hm_sb[1:128, :], in_=h_hbm[r0:r0 + 127, :]), 16)
                    else:
                        shd.inc(s.dma_start(out=hm_sb[:, :], in_=h_hbm[r0 - 1:r0 + 127, :]), 16)
                    if (r0 + 128) % S == 0:
                        shd.inc(s.dma_start(out=hp_sb[0:127, :], in_=h_hbm[r0 + 1:r0 + 128, :]), 16)
                    else:
                        shd.inc(s.dma_start(out=hp_sb[:, :], in_=h_hbm[r0 + 1:r0 + 129, :]), 16)
                    s.wait_ge(shd.h, shd.n)
                    s.wait_ge(ldh.h, ldh.n)

            with nc.Block() as blk:
                cv0, cg0 = cvs.n, cvg.n

                @blk.vector
                def _(v):
                    cvs.inc(v.tensor_mul(cv_sb[:, :], hm_sb[:, :], wbc[0][:, :]))
                    cvs.inc(v.tensor_add(cv_sb[:, :], cv_sb[:, :], wbc[3][:, :]))
                    v.wait_ge(cvg.h, cg0 + 1)
                    cvs.inc(v.tensor_add(cv_sb[:, :], cv_sb[:, :], hp_sb[:, :]))

                @blk.gpsimd
                def _(g):
                    g.wait_ge(cvs.h, cv0 + 1)
                    g.tensor_mul(hm_sb[:, :], hp_sb[:, :], wbc[2][:, :])
                    g.tensor_mul(hp_sb[:, :], h0_sb[:, hj:hj + 4096], wbc[1][:, :])
                    cvg.inc(g.tensor_add(hp_sb[:, :], hp_sb[:, :], hm_sb[:, :]))

            with nc.Block() as blk:
                @blk.scalar
                def _(sc):
                    sc.activation(sil_sb[:, :], cv_sb[:, :], AF.Silu)

            with nc.Block() as blk:
                @blk.vector
                def _(v):
                    v.reduce_max(a2t[:, j:j + 1], sil_sb[:, :], axis=AX.X,
                                 apply_absolute_value=True)

            with nc.Block() as blk:
                @blk.vector
                def _(v):
                    selfsem(v, v.tensor_scalar_max(s2t[:, j:j + 1], a2t[:, j:j + 1], 1e-5))
                    selfsem(v, v.tensor_scalar_mul(u2t[:, j:j + 1], s2t[:, j:j + 1], w2m["mean"][:, :]))
                    v.tensor_scalar_mul(u2t[:, j:j + 1], u2t[:, j:j + 1], 1.0 / 127.0)
                    selfsem(v, v.reciprocal(s2t[:, j:j + 1], s2t[:, j:j + 1]))
                    selfsem(v, v.tensor_scalar_mul(s2t[:, j:j + 1], s2t[:, j:j + 1], 127.0))
                    for hh in range(2):
                        v.tensor_scalar_mul(xq2f[:, :], sil_sb[:, hh * 2048:(hh + 1) * 2048],
                                            s2t[:, j:j + 1])
                        v.tensor_copy(xq2i[:, hh * 2048:(hh + 1) * 2048], xq2f[:, :])
                    v.tensor_copy(xq2b[:, :], xq2i[:, :])

            with nc.Block() as blk:
                tp0, ev0 = tp2.n, ev2.n

                @blk.tensor
                def _(te):
                    for g in range(8):
                        if g >= 2:
                            te.wait_ge(ev2.h, ev0 + g - 1)
                        for ii in range(4):
                            c = g * 4 + ii
                            ins = te.transpose(ps_tp2[g % 2][:, ii * 128:(ii + 1) * 128],
                                               xq2b[:, c * 128:(c + 1) * 128], identbf[:, :])
                        tp2.inc(ins)

                @blk.vector
                def _(v):
                    for g in range(8):
                        v.wait_ge(tp2.h, tp0 + g + 1)
                        ev2.inc(v.tensor_copy(xq2fm[:, hj + g * 512: hj + (g + 1) * 512],
                                              ps_tp2[g % 2][:, :]))

        for o in range(2):
            with nc.Block() as blk:
                mm0, ey0 = mm2.n, evy.n
                l20 = [ld2[0].n, ld2[1].n]

                @blk.gpsimd
                def _(g):
                    for c in range(HCH):
                        p = c % 2
                        if c >= 2:
                            g.wait_ge(mm2.h, mm0 + 4 * (c - 1))
                        ld2[p].inc(g.dma_start(out=t2blk[p][:, :],
                                               in_=t2_hbm[c * 128:(c + 1) * 128, o * 512:(o + 1) * 512]), 16)

                @blk.tensor
                def _(te):
                    for c in range(HCH):
                        p = c % 2
                        te.wait_ge(ld2[p].h, l20[p] + 16 * (c // 2 + 1))
                        for j in range(4):
                            mm2.inc(te.matmul(ps4[j][:, :],
                                              xq2fm[:, j * 4096 + c * 128: j * 4096 + (c + 1) * 128],
                                              t2blk[p][:, :],
                                              start=(c == 0), stop=(c == HCH - 1)))

                @blk.vector
                def _(v):
                    v.wait_ge(mm2.h, mm0 + 4 * HCH)
                    for j in range(4):
                        evy.inc(v.tensor_scalar_mul(
                            y2_sb[:, j * 1024 + o * 512: j * 1024 + (o + 1) * 512],
                            ps4[j][:, :], u2t[:, j:j + 1]))

        with nc.Block() as blk:
            @blk.vector
            def _(v):
                for j in range(4):
                    v.tensor_add(v4[:, j * 1024:(j + 1) * 1024],
                                 x1p[:, j * 1024:(j + 1) * 1024],
                                 y2_sb[:, j * 1024:(j + 1) * 1024])

        emit_ln_tail(env, consts, v4, o2_sb, 4, lnst)
        with nc.Block() as blk:
            @blk.sync
            def _(s):
                for j in range(4):
                    st2.inc(s.dma_start(
                        out=dd["x2_tm"][t * NT + j * 128: t * NT + (j + 1) * 128, :],
                        in_=o2_sb[:, j * 1024:(j + 1) * 1024]), 16)
                s.wait_ge(st2.h, st2.n)
    env.pop_pool()


def emit_s3(env, dd, T, consts, L=4096):
    nc = env.nc
    nb = T // L
    LC = L // 128
    LT = L // 512
    identbf = consts["identbf"]
    identf = consts["identf"]
    EPS = L1_ATTN_EPS
    NH = 16
    NP = NH // 2

    ctx_hbm = nc.dram_tensor("s3_ctx", [T, D], BF16).ap()
    scr_hbm = nc.dram_tensor("s3_scr", [1, 8], F32).ap()

    qs = env.sem("s3_qs")

    def selfsem(v, ins):
        qs.n += 1
        ins.then_inc(qs.h, 1)
        v.wait_ge(qs.h, qs.n)

    env.push_pool()
    wq_sb = env.sb("s3_wq", [128, DCH * D], BF16)
    wk_sb = env.sb("s3_wk", [128, DCH * D], BF16)
    wv_sb = env.sb("s3_wv", [128, DCH * D], BF16)
    w0_sb = env.sb("s3_w0", [128, 512], BF16)
    bq_sb = env.sb("s3_bq", [128, 8], F32)
    bk_sb = env.sb("s3_bk", [128, 8], F32)
    bv_sb = env.sb("s3_bv", [128, 8], F32)
    ldw0 = env.sem("s3_ldw0")
    with nc.Block() as blk:
        @blk.gpsimd
        def _(g):
            for sb, key in ((wq_sb, "wqT"), (wk_sb, "wkT"), (wv_sb, "wvT")):
                ldw0.inc(g.dma_start(out=sb[:, :],
                                     in_=dd[key].rearrange("(ic p) o -> p ic o", p=128)), 16)
            ldw0.inc(g.dma_start(out=w0_sb[:, :], in_=dd["w0l"][:, :]), 16)

        @blk.sync
        def _(s):
            for sb, key in ((bq_sb, "bq"), (bk_sb, "bk"), (bv_sb, "bv")):
                ldw0.inc(s.dma_start(out=sb[:, :], in_=dd[key][:, :]), 16)
            s.wait_ge(ldw0.h, 16 * 7)

    x2fm = env.sb("s3_x2fm", [128, DCH * L], BF16)
    x2t = env.sb("s3_x2t", [128, 1024], BF16)
    Qp = env.sb("s3_Qp", [128, L], BF16)
    Kp = env.sb("s3_Kp", [128, L], BF16)
    Vf = env.sb("s3_Vf", [128, L], BF16)
    Vaug = env.sb("s3_Vaug", [128, LC * 130], BF16)
    Ktm = env.sb("s3_Ktm", [128, LC * 128], BF16)
    ksq = env.sb("s3_ksq", [128, 128], F32)
    ssqp = env.sb("s3_ssqp", [128, 2 * LC], F32)
    ssqb = env.sb("s3_ssqb", [128, 2 * LC], F32)
    mxk = env.sb("s3_mxk", [128, 2 * LC], F32)
    mx2 = env.sb("s3_mx2", [128, 2], F32)
    mrow2 = env.sb("s3_mrow2", [2, 128], F32)
    mb21 = env.sb("s3_mb21", [2, 1], F32)
    mr12 = env.sb("s3_mr12", [1, 2], F32)
    mh2 = env.sb("s3_mh2", [128, 2], F32)
    e_pair = env.sb("s3_ep", [128, 2 * LC * 64], BF16)
    phih = env.sb("s3_phih", [64, L], BF16)
    nd_sb = env.sb("s3_nd", [65, 64], F32)
    ndT = env.sb("s3_ndT", [64, 66], BF16)
    qlogt = env.sb("s3_qlogt", [128, 2048], F32)
    mxq = env.sb("s3_mxq", [128, 32], F32)
    mxqn = env.sb("s3_mxqn", [128, 32], F32)
    phitm = env.sb("s3_phitm", [128, 2048], BF16)
    znd = env.sb("s3_znd", [65, 2], F32)
    ez_bc = env.sb("s3_ez", [128, 2], F32)
    ez1 = env.sb("s3_ez1", [128, 1], F32)
    num_sb = env.sb("s3_num", [128, LC * 66], F32)
    dent = env.sb("s3_dent", [128, LC], F32)
    slt = env.sb("s3_slt", [128, LC], F32)
    rlt = env.sb("s3_rlt", [128, LC], F32)
    strip = env.sb("s3_strip", [128, LC * 128], BF16)

    ps_mm = [env.ps(f"s3_psmm{i}", [128, 512], F32) for i in range(2)]
    ps_tp = [env.ps(f"s3_pstp{i}", [128, 512], BF16) for i in range(2)]
    ps_small = [env.ps(f"s3_pssm{i}", [128, 512], F32) for i in range(2)]
    ps_kl = [ps_small[0][:, 0:64], ps_small[1][:, 0:64]]
    ps_num = [ps_small[0][:, 64:130], ps_small[1][:, 64:130]]
    ps_nd = ps_small[0][0:65, 192:256]
    ps_ndT = ps_small[1][0:64, 192:257]
    ps_ph = [ps_mm[0][0:64, :], ps_mm[1][0:64, :]]

    ldx = env.sem("s3_ldx")
    tpx = env.sem("s3_tpx")
    evx = env.sem("s3_evx")
    mmp = env.sem("s3_mmp")
    evp = env.sem("s3_evp")
    tpv = env.sem("s3_tpv")
    evv = env.sem("s3_evv")
    tpk = env.sem("s3_tpk")
    evk = env.sem("s3_evk")
    ack = env.sem("s3_ack")
    rdk = env.sem("s3_rdk")
    mmk = env.sem("s3_mmk")
    ace = env.sem("s3_ace")
    mmn = env.sem("s3_mmn")
    evn = env.sem("s3_evn")
    tpn = env.sem("s3_tpn")
    mmf = env.sem("s3_mmf")
    acf = env.sem("s3_acf")
    mmu = env.sem("s3_mmu")
    evu = env.sem("s3_evu")
    stc = env.sem("s3_stc")
    zst = env.sem("s3_zst")

    for b in range(nb):
        for lt in range(LT):
            with nc.Block() as blk:
                lx0, tp0, ev0 = ldx.n, tpx.n, evx.n

                @blk.sync
                def _(s):
                    for j in range(4):
                        if j > 0:
                            s.wait_ge(tpx.h, tp0 + 8 * j)
                        ldx.inc(s.dma_start(out=x2t[:, :],
                                            in_=dd["x2_tm"][b * L + lt * 512 + j * 128:
                                                            b * L + lt * 512 + (j + 1) * 128, :]), 16)
                        s.wait_ge(ldx.h, ldx.n)

                @blk.tensor
                def _(te):
                    for j in range(4):
                        te.wait_ge(ldx.h, lx0 + 16 * (j + 1))
                        for i in range(DCH):
                            k = j * DCH + i
                            if k >= 2:
                                te.wait_ge(evx.h, ev0 + k - 1)
                            tpx.inc(te.transpose(ps_tp[k % 2][:, 0:128],
                                                 x2t[:, i * 128:(i + 1) * 128], identbf[:, :]))

                @blk.vector
                def _(v):
                    for j in range(4):
                        for i in range(DCH):
                            k = j * DCH + i
                            v.wait_ge(tpx.h, tp0 + k + 1)
                            evx.inc(v.tensor_copy(
                                x2fm[:, i * L + lt * 512 + j * 128: i * L + lt * 512 + (j + 1) * 128],
                                ps_tp[k % 2][:, 0:128]))

        for hp in range(NP):
            po = hp * 128

            with nc.Block() as blk:
                mm0, ev0 = mmp.n, evp.n

                @blk.tensor
                def _(te):
                    for dst, wsb, _bias in ((Qp, wq_sb, bq_sb), (Kp, wk_sb, bk_sb), (Vf, wv_sb, bv_sb)):
                        for lt in range(LT):
                            k = mmp.n - mm0
                            if k >= 2:
                                te.wait_ge(evp.h, ev0 + k - 1)
                            for i in range(DCH):
                                ins = te.matmul(ps_mm[k % 2][:, :],
                                                wsb[:, i * D + po: i * D + po + 128],
                                                x2fm[:, i * L + lt * 512: i * L + (lt + 1) * 512],
                                                start=(i == 0), stop=(i == DCH - 1))
                            mmp.inc(ins)

                @blk.vector
                def _(v):
                    for di, (dst, wsb, bias) in enumerate(
                            ((Qp, wq_sb, bq_sb), (Kp, wk_sb, bk_sb), (Vf, wv_sb, bv_sb))):
                        for lt in range(LT):
                            k = di * LT + lt
                            v.wait_ge(mmp.h, mm0 + k + 1)
                            evp.inc(v.tensor_scalar_add(dst[:, lt * 512:(lt + 1) * 512],
                                                        ps_mm[k % 2][:, :], bias[:, hp:hp + 1]))

            with nc.Block() as blk:
                tp0, ev0 = tpv.n, evv.n

                @blk.vector
                def _(v):
                    v.memset(Vaug[:, :], 1.0)

                @blk.tensor
                def _(te):
                    for lc in range(LC):
                        if lc >= 2:
                            te.wait_ge(evv.h, ev0 + lc - 1)
                        tpv.inc(te.transpose(ps_tp[lc % 2][:, 0:128],
                                             Vf[:, lc * 128:(lc + 1) * 128], identbf[:, :]))

                @blk.vector
                def _(v):
                    for lc in range(LC):
                        v.wait_ge(tpv.h, tp0 + lc + 1)
                        v.tensor_copy(Vaug[:, lc * 130: lc * 130 + 64],
                                      ps_tp[lc % 2][:, 0:64])
                        evv.inc(v.tensor_copy(Vaug[:, lc * 130 + 65: lc * 130 + 129],
                                              ps_tp[lc % 2][:, 64:128]))

            with nc.Block() as blk:
                tp0, ev0 = tpk.n, evk.n

                @blk.tensor
                def _(te):
                    for lc in range(LC):
                        if lc >= 2:
                            te.wait_ge(evk.h, ev0 + lc - 1)
                        tpk.inc(te.transpose(ps_tp[lc % 2][:, 0:128],
                                             Kp[:, lc * 128:(lc + 1) * 128], identbf[:, :]))

                @blk.vector
                def _(v):
                    for lc in range(LC):
                        v.wait_ge(tpk.h, tp0 + lc + 1)
                        evk.inc(v.tensor_copy(Ktm[:, lc * 128:(lc + 1) * 128],
                                              ps_tp[lc % 2][:, 0:128]))

            with nc.Block() as blk:
                ac0, rd0 = ack.n, rdk.n

                @blk.scalar
                def _(sc):
                    for lc in range(LC):
                        if lc >= 1:
                            sc.wait_ge(rdk.h, rd0 + lc)
                        ack.inc(sc.activation(ksq[:, :], Ktm[:, lc * 128:(lc + 1) * 128],
                                              AF.Square))

                @blk.vector
                def _(v):
                    for lc in range(LC):
                        v.wait_ge(ack.h, ac0 + lc + 1)
                        v.reduce_sum(ssqp[:, lc:lc + 1], ksq[:, 0:64], axis=AX.X)
                        rdk.inc(v.reduce_sum(ssqp[:, LC + lc:LC + lc + 1],
                                             ksq[:, 64:128], axis=AX.X))

            with nc.Block() as blk:
                @blk.vector
                def _(v):
                    v.tensor_scalar_mul(ssqp[:, :], ssqp[:, :], -0.5)

            with nc.Block() as blk:
                mm0, ae0 = mmk.n, ace.n

                @blk.tensor
                def _(te):
                    for h2 in range(2):
                        for lc in range(LC):
                            k = h2 * LC + lc
                            if k >= 2:
                                te.wait_ge(ace.h, ae0 + k - 1)
                            mmk.inc(te.matmul(ps_kl[k % 2],
                                              Kp[h2 * 64:(h2 + 1) * 64, lc * 128:(lc + 1) * 128],
                                              w0_sb[h2 * 64:(h2 + 1) * 64, hp * 64:(hp + 1) * 64],
                                              start=True, stop=True))

                @blk.vector
                def _(v):
                    for h2 in range(2):
                        for lc in range(LC):
                            k = h2 * LC + lc
                            v.wait_ge(mmk.h, mm0 + k + 1)
                            ace.inc(v.reduce_max(mxk[:, k:k + 1], ps_kl[k % 2], axis=AX.X))

            with nc.Block() as blk:
                @blk.vector
                def _(v):
                    v.tensor_add(mxk[:, :], mxk[:, :], ssqp[:, :])

            with nc.Block() as blk:
                @blk.vector
                def _(v):
                    v.reduce_max(mx2[:, 0:1], mxk[:, 0:LC], axis=AX.X)
                    v.reduce_max(mx2[:, 1:2], mxk[:, LC:2 * LC], axis=AX.X)

            with nc.Block() as blk:
                @blk.tensor
                def _(te):
                    te.transpose(ps_small[1][0:2, 256:384], mx2[:, :], identf[:, :])

            with nc.Block() as blk:
                @blk.vector
                def _(v):
                    v.tensor_copy(mrow2[0:2, :], ps_small[1][0:2, 256:384])

            with nc.Block() as blk:
                @blk.vector
                def _(v):
                    v.reduce_max(mb21[0:2, :], mrow2[0:2, :], axis=AX.X)

            with nc.Block() as blk:
                @blk.tensor
                def _(te):
                    te.transpose(ps_small[1][0:1, 384:386], mb21[0:2, :], identf[0:2, 0:2])

            with nc.Block() as blk:
                @blk.vector
                def _(v):
                    v.tensor_copy(mr12[:, :], ps_small[1][0:1, 384:386])

            with nc.Block() as blk:
                @blk.sync
                def _(s):
                    zst.inc(s.dma_start(out=scr_hbm[0:1, 2:4], in_=mr12[:, :]), 16)
                    s.wait_ge(zst.h, zst.n)
                    zst.inc(s.dma_start(out=mh2[:, 0:1],
                                        in_=scr_hbm[0:1, 2:3].broadcast_to((128, 1))), 16)
                    zst.inc(s.dma_start(out=mh2[:, 1:2],
                                        in_=scr_hbm[0:1, 3:4].broadcast_to((128, 1))), 16)
                    s.wait_ge(zst.h, zst.n)

            with nc.Block() as blk:
                @blk.vector
                def _(v):
                    for h2 in range(2):
                        v.tensor_single_scalar(ssqb[:, h2 * LC:(h2 + 1) * LC],
                                               ssqp[:, h2 * LC:(h2 + 1) * LC],
                                               mh2[:, h2:h2 + 1], ALU.subtract)

            with nc.Block() as blk:
                mm0, ae0 = mmk.n, ace.n

                @blk.tensor
                def _(te):
                    for h2 in range(2):
                        for lc in range(LC):
                            k = h2 * LC + lc
                            if k >= 2:
                                te.wait_ge(ace.h, ae0 + k - 1)
                            mmk.inc(te.matmul(ps_kl[k % 2],
                                              Kp[h2 * 64:(h2 + 1) * 64, lc * 128:(lc + 1) * 128],
                                              w0_sb[h2 * 64:(h2 + 1) * 64, hp * 64:(hp + 1) * 64],
                                              start=True, stop=True))

                @blk.scalar
                def _(sc):
                    for h2 in range(2):
                        for lc in range(LC):
                            k = h2 * LC + lc
                            sc.wait_ge(mmk.h, mm0 + k + 1)
                            ace.inc(sc.activation(
                                e_pair[:, h2 * LC * 64 + lc * 64: h2 * LC * 64 + (lc + 1) * 64],
                                ps_kl[k % 2], AF.Exp,
                                bias=ssqb[:, h2 * LC + lc: h2 * LC + lc + 1],
                                scale=1.0))

            for h2 in range(2):
                with nc.Block() as blk:
                    @blk.tensor
                    def _(te):
                        for lc in range(LC):
                            ins = te.matmul(ps_nd,
                                            Vaug[:, lc * 130 + h2 * 65: lc * 130 + (h2 + 1) * 65],
                                            e_pair[:, h2 * LC * 64 + lc * 64: h2 * LC * 64 + (lc + 1) * 64],
                                            start=(lc == 0), stop=(lc == LC - 1))
                        mmn.inc(ins)

                with nc.Block() as blk:
                    @blk.vector
                    def _(v):
                        v.tensor_copy(nd_sb[:, :], ps_nd)

                with nc.Block() as blk:
                    @blk.vector
                    def _(v):
                        v.reduce_sum(znd[64:65, h2:h2 + 1], nd_sb[64:65, :], axis=AX.X)

                with nc.Block() as blk:
                    @blk.tensor
                    def _(te):
                        te.transpose(ps_ndT, nd_sb[:, :], identf[0:65, 0:65])

                with nc.Block() as blk:
                    @blk.vector
                    def _(v):
                        v.memset(ndT[:, 65:66], 1.0)
                        v.tensor_copy(ndT[:, 0:65], ps_ndT)

                with nc.Block() as blk:
                    mm0, cq0 = mmf.n, acf.n

                    @blk.tensor
                    def _(te):
                        for lc in range(LC):
                            if lc >= 2:
                                te.wait_ge(acf.h, cq0 + lc - 1)
                            mmf.inc(te.matmul(ps_kl[lc % 2],
                                              Qp[h2 * 64:(h2 + 1) * 64, lc * 128:(lc + 1) * 128],
                                              w0_sb[h2 * 64:(h2 + 1) * 64, hp * 64:(hp + 1) * 64],
                                              start=True, stop=True))

                    @blk.vector
                    def _(v):
                        for lc in range(LC):
                            v.wait_ge(mmf.h, mm0 + lc + 1)
                            v.reduce_max(mxq[:, lc:lc + 1], ps_kl[lc % 2], axis=AX.X)
                            acf.inc(v.tensor_copy(qlogt[:, lc * 64:(lc + 1) * 64],
                                                  ps_kl[lc % 2]))

                with nc.Block() as blk:
                    @blk.vector
                    def _(v):
                        v.tensor_scalar_mul(mxqn[:, :], mxq[:, :], -1.0)

                with nc.Block() as blk:
                    @blk.scalar
                    def _(sc):
                        for lc in range(LC):
                            sc.activation(phitm[:, lc * 64:(lc + 1) * 64],
                                          qlogt[:, lc * 64:(lc + 1) * 64], AF.Exp,
                                          bias=mxqn[:, lc:lc + 1], scale=1.0)

                with nc.Block() as blk:
                    tp0, ev0 = tpn.n, mmn.n

                    @blk.tensor
                    def _(te):
                        for lc in range(LC):
                            if lc >= 2:
                                te.wait_ge(mmn.h, ev0 + lc - 1)
                            tpn.inc(te.transpose(ps_tp[lc % 2][0:64, 0:128],
                                                 phitm[:, lc * 64:(lc + 1) * 64],
                                                 identbf[:, :]))

                    @blk.vector
                    def _(v):
                        for lc in range(LC):
                            v.wait_ge(tpn.h, tp0 + lc + 1)
                            mmn.inc(v.tensor_copy(phih[:, lc * 128:(lc + 1) * 128],
                                                  ps_tp[lc % 2][0:64, 0:128]))

                with nc.Block() as blk:
                    mm0, ev0 = mmu.n, evu.n

                    @blk.tensor
                    def _(te):
                        for lc in range(LC):
                            if lc >= 2:
                                te.wait_ge(evu.h, ev0 + lc - 1)
                            mmu.inc(te.matmul(ps_num[lc % 2],
                                              phih[:, lc * 128:(lc + 1) * 128],
                                              ndT[:, :], start=True, stop=True))

                    @blk.vector
                    def _(v):
                        for lc in range(LC):
                            v.wait_ge(mmu.h, mm0 + lc + 1)
                            evu.inc(v.tensor_copy(num_sb[:, lc * 66:(lc + 1) * 66],
                                                  ps_num[lc % 2]))

                with nc.Block() as blk:
                    @blk.sync
                    def _(s):
                        zst.inc(s.dma_start(out=scr_hbm[0:1, h2:h2 + 1],
                                            in_=znd[64:65, h2:h2 + 1]), 16)
                        s.wait_ge(zst.h, zst.n)
                        zst.inc(s.dma_start(out=ez_bc[:, h2:h2 + 1],
                                            in_=scr_hbm[0:1, h2:h2 + 1].broadcast_to((128, 1))), 16)
                        s.wait_ge(zst.h, zst.n)

                with nc.Block() as blk:
                    @blk.vector
                    def _(v):
                        v.tensor_copy(dent.rearrange("p (c o) -> p c o", o=1)[:, :, :],
                                      num_sb.rearrange("p (c k) -> p c k", k=66)[:, :, 64:65])
                        v.tensor_copy(slt.rearrange("p (c o) -> p c o", o=1)[:, :, :],
                                      num_sb.rearrange("p (c k) -> p c k", k=66)[:, :, 65:66])

                with nc.Block() as blk:
                    @blk.vector
                    def _(v):
                        selfsem(v, v.tensor_scalar_mul(ez1[:, :], ez_bc[:, h2:h2 + 1], EPS))
                        selfsem(v, v.scalar_tensor_tensor(dent[:, :], slt[:, :], ez1[:, :],
                                                          dent[:, :], ALU.mult, ALU.add))
                        selfsem(v, v.reciprocal(rlt[:, :], dent[:, :]))
                        for lc in range(LC):
                            v.tensor_scalar_mul(
                                strip[:, lc * 128 + h2 * 64: lc * 128 + h2 * 64 + 64],
                                num_sb[:, lc * 66: lc * 66 + 64], rlt[:, lc:lc + 1])

            with nc.Block() as blk:
                @blk.sync
                def _(s):
                    stc.inc(s.dma_start(
                        out=ctx_hbm[b * L:(b + 1) * L, po:po + 128]
                            .rearrange("(c p) f -> p c f", p=128),
                        in_=strip.rearrange("p (c f) -> p c f", f=128)), 16)
                    s.wait_ge(stc.h, stc.n)
    env.pop_pool()

    env.push_pool()
    wo_sb = env.sb("s3_wo", [128, DCH * D], BF16)
    bo_bc = env.sb("s3_bobc", [128, D], BF16)
    ldo = env.sem("s3_ldo")
    with nc.Block() as blk:
        @blk.gpsimd
        def _(g):
            ldo.inc(g.dma_start(out=wo_sb[:, :],
                                in_=dd["woT"].rearrange("(ic p) o -> p ic o", p=128)), 16)

        @blk.sync
        def _(s):
            s.wait_ge(ldo.h, 16)
    emit_bcast_rows(env, "s3bo", dd["bo_row"], 1, D, [bo_bc], consts)

    ctxt = env.sb("s3_ctxt", [128, 4 * 1024], BF16)
    x2r = env.sb("s3_x2r", [128, 4 * 1024], BF16)
    cfm = env.sb("s3_cfm", [128, DCH * 128], BF16)
    y3 = env.sb("s3_y3", [128, 1024], BF16)
    v4 = env.sb("s3_v4", [128, 4 * 1024], F32)
    o3 = env.sb("s3_o3", [128, 4 * 1024], BF16)
    lnst3 = mk_ln_stats(env, "s3ln")
    ps_tp3 = [env.ps(f"s3_pstp3{i}", [128, 512], BF16) for i in range(2)]
    ps_mm3 = [env.ps(f"s3_psmm3{i}", [128, 512], F32) for i in range(2)]

    ldc = env.sem("s3_ldc")
    tpc = env.sem("s3_tpc")
    evc = env.sem("s3_evc")
    mmo = env.sem("s3_mmo")
    evo = env.sem("s3_evo")
    st3 = env.sem("s3_st3")

    for t in range(T // 512):
        with nc.Block() as blk:
            ld0 = ldc.n

            @blk.sync
            def _(s):
                for j in range(4):
                    r0 = t * 512 + j * 128
                    ldc.inc(s.dma_start(out=ctxt[:, j * 1024:(j + 1) * 1024],
                                        in_=ctx_hbm[r0:r0 + 128, :]), 16)
                    ldc.inc(s.dma_start(out=x2r[:, j * 1024:(j + 1) * 1024],
                                        in_=dd["x2_tm"][r0:r0 + 128, :]), 16)

        for j in range(4):
            with nc.Block() as blk:
                tp0, ev0 = tpc.n, evc.n

                @blk.tensor
                def _(te):
                    te.wait_ge(ldc.h, ldc.n)
                    for g in range(2):
                        if g >= 2:
                            te.wait_ge(evc.h, ev0 + g - 1)
                        for ii in range(4):
                            i = g * 4 + ii
                            ins = te.transpose(ps_tp3[g % 2][:, ii * 128:(ii + 1) * 128],
                                               ctxt[:, j * 1024 + i * 128: j * 1024 + (i + 1) * 128],
                                               identbf[:, :])
                        tpc.inc(ins)

                @blk.vector
                def _(v):
                    for g in range(2):
                        v.wait_ge(tpc.h, tp0 + g + 1)
                        evc.inc(v.tensor_copy(
                            cfm.rearrange("p (i n) -> p i n", n=128)[:, g * 4:(g + 1) * 4, :],
                            ps_tp3[g % 2].rearrange("p (i n) -> p i n", n=128)[:, :, :]))

            with nc.Block() as blk:
                mm0, eo0 = mmo.n, evo.n

                @blk.tensor
                def _(te):
                    for o in range(2):
                        for c in range(DCH):
                            ins = te.matmul(ps_mm3[o % 2][:, :],
                                            cfm[:, c * 128:(c + 1) * 128],
                                            wo_sb[:, c * D + o * 512: c * D + (o + 1) * 512],
                                            start=(c == 0), stop=(c == DCH - 1))
                        mmo.inc(ins)

                @blk.vector
                def _(v):
                    for o in range(2):
                        v.wait_ge(mmo.h, mm0 + o + 1)
                        evo.inc(v.scalar_tensor_tensor(y3[:, o * 512:(o + 1) * 512],
                                                       ps_mm3[o % 2][:, :], 1.0,
                                                       bo_bc[:, o * 512:(o + 1) * 512],
                                                       ALU.mult, ALU.add))
                    v.tensor_add(v4[:, j * 1024:(j + 1) * 1024],
                                 x2r[:, j * 1024:(j + 1) * 1024], y3[:, :])

        emit_ln_tail(env, consts, v4, o3, 4, lnst3)
        with nc.Block() as blk:
            @blk.sync
            def _(s):
                for j in range(4):
                    st3.inc(s.dma_start(
                        out=dd["x3_tm"][t * 512 + j * 128: t * 512 + (j + 1) * 128, :],
                        in_=o3[:, j * 1024:(j + 1) * 1024]), 16)
                s.wait_ge(st3.h, st3.n)
    env.pop_pool()


def emit_s1(env, dd, T, consts, S=4096):
    nc = env.nc
    nb = T // S
    RB = S // 16
    NSB = RB // 128
    NKT = S // 128
    HALF = S // 2
    NQC = HALF // 128
    identbf = consts["identbf"]
    identf = consts["identf"]

    y_hbm = nc.dram_tensor("s1_y", [T, D], BF16).ap()

    qs = env.sem("s1_qs")

    def selfsem(v, ins):
        qs.n += 1
        ins.then_inc(qs.h, 1)
        v.wait_ge(qs.h, qs.n)

    env.push_pool()
    wq_sb = env.sb("s1_wq", [128, DCH * D], BF16)
    wk_sb = env.sb("s1_wk", [128, DCH * D], BF16)
    wv_sb = env.sb("s1_wv", [128, DCH * D], BF16)
    ldw = env.sem("s1_ldw")
    with nc.Block() as blk:
        @blk.gpsimd
        def _(g):
            for sb, key in ((wq_sb, "wqT"), (wk_sb, "wkT"), (wv_sb, "wvT")):
                ldw.inc(g.dma_start(out=sb[:, :],
                                    in_=dd[key].rearrange("(ic p) o -> p ic o", p=128)), 16)

        @blk.sync
        def _(s):
            s.wait_ge(ldw.h, 48)

    xf_sb = env.sb("s1_xf", [128, DCH * RB], F32)
    xb_sb = env.sb("s1_xb", [128, DCH * RB], BF16)
    xtm_sb = env.sb("s1_xtm", [128, NSB * 1024], F32)
    Yq = env.sb("s1_Yq", [128, DCH * RB], BF16)
    Yk = env.sb("s1_Yk", [128, DCH * RB], BF16)
    Yqa = env.sb("s1_Yqa", [128, DCH * RB], BF16)
    Vaug = env.sb("s1_Vaug", [128, NKT * 65], BF16)
    tmsq = env.sb("s1_tmsq", [128, 128], F32)
    ssqall = env.sb("s1_ssqall", [128, 4 * DCH * NSB], F32)
    mx1 = env.sb("s1_mx1", [128, 2], F32)
    s2r = env.sb("s1_s2r", [1, 256], F32)
    m11 = env.sb("s1_m11", [1, 4], F32)
    cb1 = env.sb("s1_cb1", [1, 1], BF16)
    cbias = env.sb("s1_cbias", [128, 1], F32)
    e_sb = [env.sb(f"s1_e{i}", [128, 2048], BF16) for i in range(2)]
    e1_sb = env.sb("s1_ea", [128, 2048], F32)
    e2_sb = env.sb("s1_eb", [128, 2048], F32)
    pv_sb = env.sb("s1_pv", [65, 2048], F32)
    rl1 = env.sb("s1_rl1", [128, 1], F32)
    ytm = env.sb("s1_ytm", [128, NSB * 1024], BF16)

    ldx = env.sem("s1_ldx")
    mmp = env.sem("s1_mmp")
    evp = env.sem("s1_evp")
    alt = env.sem("s1_alt")
    tpb = env.sem("s1_tpb")
    acb = env.sem("s1_acb")
    rdb = env.sem("s1_rdb")
    tsc = env.sem("s1_tsc")
    aex = env.sem("s1_aex")
    psq = env.sem("s1_psq")
    vsq = env.sem("s1_vsq")
    tpv2 = env.sem("s1_tpv")
    tpo = env.sem("s1_tpo")
    evo = env.sem("s1_evo")
    sty = env.sem("s1_sty")

    for inst in range(nb * 16):
        b, g = inst // 16, inst % 16
        c0 = b * S + g * RB

        with nc.Block() as blk:
            lx0 = ldx.n

            @blk.sync
            def _(s):
                ldx.inc(s.dma_start(
                    out=xf_sb.rearrange("p (i r) -> p i r", r=RB),
                    in_=dd["x_fm"].rearrange("(i p) t -> p i t", p=128)[:, :, c0:c0 + RB]), 16)
                for jj in range(NSB):
                    ldx.inc(s.dma_start(out=xtm_sb[:, jj * 1024:(jj + 1) * 1024],
                                        in_=dd["x_tm"][c0 + jj * 128: c0 + (jj + 1) * 128, :]), 16)
                s.wait_ge(ldx.h, ldx.n)

            @blk.vector
            def _(v):
                v.wait_ge(ldx.h, lx0 + 16 * (NSB + 1))
                v.tensor_copy(xb_sb[:, :], xf_sb[:, :])

        env.push_pool()
        ps_a = [env.ps(f"s1_psa{i}_{inst}", [128, 512], F32) for i in range(2)]
        ps_tpb = [env.ps(f"s1_pstpb{i}_{inst}", [128, 128], BF16) for i in range(2)]
        with nc.Block() as blk:
            mm0, ev0 = mmp.n, evp.n

            @blk.tensor
            def _(te):
                k = 0
                for dst, wsb in ((Yq, wq_sb), (Yk, wk_sb)):
                    for o in range(DCH):
                        if k >= 2:
                            te.wait_ge(evp.h, ev0 + k - 1)
                        for i in range(DCH):
                            ins = te.matmul(ps_a[k % 2][:, 0:RB],
                                            wsb[:, i * D + o * 128: i * D + (o + 1) * 128],
                                            xb_sb[:, i * RB:(i + 1) * RB],
                                            start=(i == 0), stop=(i == DCH - 1))
                        mmp.inc(ins)
                        k += 1
                for jj in range(NSB):
                    for o in range(2):
                        if k >= 2:
                            te.wait_ge(evp.h, ev0 + k - 1)
                        for i in range(DCH):
                            ins = te.matmul(ps_a[k % 2][:, :],
                                            xb_sb[:, i * RB + jj * 128: i * RB + (jj + 1) * 128],
                                            wv_sb[:, i * D + o * 512: i * D + (o + 1) * 512],
                                            start=(i == 0), stop=(i == DCH - 1))
                        mmp.inc(ins)
                        k += 1

            @blk.vector
            def _(v):
                v.memset(Vaug[:, :], 1.0)
                k = 0
                for dst, wsb in ((Yq, wq_sb), (Yk, wk_sb)):
                    for o in range(DCH):
                        v.wait_ge(mmp.h, mm0 + k + 1)
                        if dst is Yq:
                            evp.inc(v.tensor_scalar_mul(dst[:, o * RB:(o + 1) * RB],
                                                        ps_a[k % 2][:, 0:RB], 0.25))
                        else:
                            evp.inc(v.tensor_copy(dst[:, o * RB:(o + 1) * RB], ps_a[k % 2][:, 0:RB]))
                        k += 1
                for jj in range(NSB):
                    for o in range(2):
                        v.wait_ge(mmp.h, mm0 + k + 1)
                        for uu in range(8):
                            ku = o * 8 + uu
                            kt = ku * NSB + jj
                            ins = v.tensor_copy(Vaug[:, kt * 65: kt * 65 + 64],
                                                ps_a[k % 2][:, uu * 64:(uu + 1) * 64])
                        evp.inc(ins)
                        k += 1

        with nc.Block() as blk:
            @blk.sync
            def _(s):
                alt.inc(s.dma_start(out=Yqa[0:64, :], in_=Yq[64:128, :]), 16)
                alt.inc(s.dma_start(out=Yqa[64:128, :], in_=Yq[0:64, :]), 16)
                s.wait_ge(alt.h, alt.n)

        ntp = DCH * NSB
        with nc.Block() as blk:
            tp0, ac0, rd0 = tpb.n, acb.n, rdb.n

            @blk.tensor
            def _(te):
                for src_i, src in enumerate((Yq, Yk)):
                    for jc in range(ntp):
                        k = src_i * ntp + jc
                        if k >= 2:
                            te.wait_ge(acb.h, ac0 + k - 1)
                        tpb.inc(te.transpose(ps_tpb[k % 2][:, :],
                                             src[:, jc * 128:(jc + 1) * 128], identbf[:, :]))

            @blk.scalar
            def _(sc):
                for k in range(2 * ntp):
                    sc.wait_ge(tpb.h, tp0 + k + 1)
                    if k >= 1:
                        sc.wait_ge(rdb.h, rd0 + k)
                    acb.inc(sc.activation(tmsq[:, :], ps_tpb[k % 2][:, :], AF.Square))

            @blk.vector
            def _(v):
                for k in range(2 * ntp):
                    v.wait_ge(acb.h, ac0 + k + 1)
                    rdb.inc(v.reduce_sum(
                        ssqall.rearrange("p (k two) -> p k two", two=2)[:, k:k + 1, :],
                        tmsq.rearrange("p (u f) -> p u f", f=64), axis=AX.X))

        with nc.Block() as blk:
            @blk.vector
            def _(v):
                v.reduce_max(mx1[:, 0:1], ssqall[:, 0:2 * ntp], axis=AX.X)
                v.reduce_max(mx1[:, 1:2], ssqall[:, 2 * ntp:4 * ntp], axis=AX.X)

        with nc.Block() as blk:
            @blk.tensor
            def _(te):
                te.transpose(ps_a[0][0:1, 0:128], mx1[:, 0:1], identf[:, :])
                te.transpose(ps_a[0][0:1, 128:256], mx1[:, 1:2], identf[:, :])

        with nc.Block() as blk:
            @blk.vector
            def _(v):
                v.tensor_copy(s2r[:, :], ps_a[0][0:1, 0:256])

        with nc.Block() as blk:
            @blk.vector
            def _(v):
                v.reduce_max(m11[:, 0:1], s2r[:, 0:128], axis=AX.X)
                v.reduce_max(m11[:, 1:2], s2r[:, 128:256], axis=AX.X)

        with nc.Block() as blk:
            @blk.vector
            def _(v):
                v.tensor_mul(m11[:, 2:3], m11[:, 0:1], m11[:, 1:2])

        with nc.Block() as blk:
            @blk.scalar
            def _(sc):
                sc.activation(m11[:, 3:4], m11[:, 2:3], AF.Sqrt)

        with nc.Block() as blk:
            @blk.vector
            def _(v):
                selfsem(v, v.tensor_scalar_mul(m11[:, 2:3], m11[:, 3:4], -1.0 / 16.0))
                v.tensor_copy(cb1[:, :], m11[:, 2:3])

        with nc.Block() as blk:
            @blk.tensor
            def _(te):
                te.matmul(ps_a[1][:, 0:1], consts["onesbf"][:, :], cb1[:, :],
                          start=True, stop=True)

        with nc.Block() as blk:
            @blk.vector
            def _(v):
                v.tensor_copy(cbias[:, :], ps_a[1][:, 0:1])
        env.pop_pool()

        env.push_pool()
        ps_sc = env.ps(f"s1_pssc_{inst}", [128, HALF], F32)
        ps_pv = env.ps(f"s1_pspv_{inst}", [65, HALF], F32)
        for qh in range(2):
            with nc.Block() as blk:
                ts0, ae0, tv0, ps0, vs0 = tsc.n, aex.n, tpv2.n, psq.n, vsq.n

                def sc_mms(te, kt):
                    ku = kt // NSB
                    r0 = (kt % NSB) * 128
                    kb = (ku % 2) * 64
                    lhsT = Yk[kb:kb + 64, (ku // 2) * RB + r0:(ku // 2) * RB + r0 + 128]
                    for qq in range(8):
                        qu = qh * 8 + qq
                        srcq = Yq if (qu % 2) == (ku % 2) else Yqa
                        rhs = srcq[kb:kb + 64, (qu // 2) * RB:(qu // 2 + 1) * RB]
                        ins = te.matmul(ps_sc[:, qq * RB:(qq + 1) * RB], lhsT, rhs,
                                        start=True, stop=True)
                    return ins

                def pv_mms(te, kt):
                    for qg in range(HALF // 512):
                        ins = te.matmul(ps_pv[:, qg * 512:(qg + 1) * 512],
                                        Vaug[:, kt * 65:(kt + 1) * 65],
                                        e_sb[kt % 2][:, qg * 512:(qg + 1) * 512],
                                        start=(kt == 0), stop=(kt == NKT - 1))
                    return ins

                @blk.tensor
                def _(te):
                    tsc.inc(sc_mms(te, 0))
                    for kt in range(1, NKT):
                        te.wait_ge(aex.h, ae0 + kt)
                        te.wait_ge(vsq.h, vs0 + kt)
                        tpv2.inc(pv_mms(te, kt - 1))
                        tsc.inc(sc_mms(te, kt))
                    te.wait_ge(vsq.h, vs0 + NKT)
                    tpv2.inc(pv_mms(te, NKT - 1))

                @blk.scalar
                def _(sc):
                    for kt in range(NKT):
                        sc.wait_ge(tsc.h, ts0 + kt + 1)
                        if kt >= 1:
                            sc.wait_ge(psq.h, ps0 + kt)
                        aex.inc(sc.activation(e1_sb[:, 0:HALF], ps_sc[:, :], AF.Exp,
                                              bias=cbias[:, :], scale=1.0))

                @blk.gpsimd
                def _(g):
                    for kt in range(NKT):
                        g.wait_ge(aex.h, ae0 + kt + 1)
                        if kt >= 1:
                            g.wait_ge(vsq.h, vs0 + kt)
                        psq.inc(g.tensor_mul(e2_sb[:, 0:HALF], e1_sb[:, 0:HALF], e1_sb[:, 0:HALF]))

                @blk.vector
                def _(v):
                    for kt in range(NKT):
                        v.wait_ge(psq.h, ps0 + kt + 1)
                        if kt >= 2:
                            v.wait_ge(tpv2.h, tv0 + kt - 1)
                        vsq.inc(v.tensor_mul(e_sb[kt % 2][:, 0:HALF], e2_sb[:, 0:HALF],
                                             e2_sb[:, 0:HALF]))

            with nc.Block() as blk:
                @blk.vector
                def _(v):
                    v.tensor_copy(pv_sb[:, 0:HALF], ps_pv[:, :])

            with nc.Block() as blk:
                to0, eo0 = tpo.n, evo.n

                @blk.tensor
                def _(te):
                    for qc in range(NQC):
                        if qc >= 2:
                            te.wait_ge(evo.h, eo0 + qc - 1)
                        tpo.inc(te.transpose(ps_sc[:, (qc % 2) * 512:(qc % 2) * 512 + 65],
                                             pv_sb[:, qc * 128:(qc + 1) * 128],
                                             identf[0:65, 0:65]))

                @blk.vector
                def _(v):
                    for qc in range(NQC):
                        qu = qh * 8 + qc // NSB
                        sb0 = qc % NSB
                        v.wait_ge(tpo.h, to0 + qc + 1)
                        selfsem(v, v.reciprocal(rl1[:, :],
                                                ps_sc[:, (qc % 2) * 512 + 64:(qc % 2) * 512 + 65]))
                        evo.inc(v.scalar_tensor_tensor(
                            ytm[:, sb0 * 1024 + qu * 64: sb0 * 1024 + (qu + 1) * 64],
                            ps_sc[:, (qc % 2) * 512:(qc % 2) * 512 + 64],
                            rl1[:, :],
                            xtm_sb[:, sb0 * 1024 + qu * 64: sb0 * 1024 + (qu + 1) * 64],
                            ALU.mult, ALU.add))
        env.pop_pool()

        with nc.Block() as blk:
            @blk.sync
            def _(s):
                for jj in range(NSB):
                    sty.inc(s.dma_start(out=y_hbm[c0 + jj * 128: c0 + (jj + 1) * 128, :],
                                        in_=ytm[:, jj * 1024:(jj + 1) * 1024]), 16)
                s.wait_ge(sty.h, sty.n)
    env.pop_pool()
    return y_hbm


def emit_s1b(env, dd, y_hbm, T, consts, S=4096):
    nc = env.nc
    nb = T // S
    NBT = S // 128
    ntiles = T // 128
    GN = S * 1024
    identf = consts["identf"]

    env.push_pool()
    ybuf = env.sb("s1b_y", [128, 2 * 1024], BF16)
    ysq = env.sb("s1b_ysq", [128, 1024], BF16)
    ys = env.sb("s1b_ys", [128, ntiles], F32)
    sqs = env.sb("s1b_sqs", [128, ntiles], F32)
    bsum = env.sb("s1b_bsum", [128, 2], F32)
    brow2 = env.sb("s1b_brow2", [2, 128], F32)
    b21 = env.sb("s1b_b21", [2, 1], F32)
    bsc = env.sb("s1b_bsc", [1, 8], F32)
    mub = env.sb("s1b_mub", [128, 2], F32)
    rsb = env.sb("s1b_rsb", [128, 2], F32)
    scr_hbm = nc.dram_tensor("s1b_scr", [1, 4], F32).ap()
    ps_b = env.ps("s1b_ps", [128, 512], F32)

    qs = env.sem("s1b_qs")

    def selfsem(v, ins):
        qs.n += 1
        ins.then_inc(qs.h, 1)
        v.wait_ge(qs.h, qs.n)

    ldy = [env.sem("s1b_ldy0"), env.sem("s1b_ldy1")]
    rds = env.sem("s1b_rds")
    acq = env.sem("s1b_acq")
    rdq = env.sem("s1b_rdq")
    zst = env.sem("s1b_zst")

    with nc.Block() as blk:
        r0, a0, q0 = rds.n, acq.n, rdq.n

        @blk.sync
        def _(s):
            for t in range(ntiles):
                p = t % 2
                if t >= 2:
                    s.wait_ge(rdq.h, q0 + t - 1)
                ldy[p].inc(s.dma_start(out=ybuf[:, p * 1024:(p + 1) * 1024],
                                       in_=y_hbm[t * 128:(t + 1) * 128, :]), 16)

        @blk.vector
        def _(v):
            for t in range(ntiles):
                p = t % 2
                v.wait_ge(ldy[p].h, 16 * (t // 2 + 1))
                rds.inc(v.reduce_sum(ys[:, t:t + 1], ybuf[:, p * 1024:(p + 1) * 1024], axis=AX.X))
                v.wait_ge(acq.h, a0 + t + 1)
                rdq.inc(v.reduce_sum(sqs[:, t:t + 1], ysq[:, :], axis=AX.X))

        @blk.scalar
        def _(sc):
            for t in range(ntiles):
                p = t % 2
                sc.wait_ge(ldy[p].h, 16 * (t // 2 + 1))
                if t >= 1:
                    sc.wait_ge(rdq.h, q0 + t)
                acq.inc(sc.activation(ysq[:, :], ybuf[:, p * 1024:(p + 1) * 1024], AF.Square))

    for b in range(nb):
        with nc.Block() as blk:
            @blk.vector
            def _(v):
                v.reduce_sum(bsum[:, 0:1], ys[:, b * NBT:(b + 1) * NBT], axis=AX.X)
                v.reduce_sum(bsum[:, 1:2], sqs[:, b * NBT:(b + 1) * NBT], axis=AX.X)

        with nc.Block() as blk:
            @blk.tensor
            def _(te):
                te.transpose(ps_b[0:2, 0:128], bsum[:, :], identf[:, :])

        with nc.Block() as blk:
            @blk.vector
            def _(v):
                v.tensor_copy(brow2[0:2, :], ps_b[0:2, 0:128])

        with nc.Block() as blk:
            @blk.vector
            def _(v):
                v.reduce_sum(b21[0:2, :], brow2[0:2, :], axis=AX.X)

        with nc.Block() as blk:
            @blk.tensor
            def _(te):
                te.transpose(ps_b[0:1, 0:2], b21[0:2, :], identf[0:2, 0:2])

        with nc.Block() as blk:
            @blk.vector
            def _(v):
                v.tensor_copy(bsc[:, 0:2], ps_b[0:1, 0:2])

        with nc.Block() as blk:
            @blk.vector
            def _(v):
                selfsem(v, v.tensor_scalar_mul(bsc[:, 2:3], bsc[:, 0:1], 1.0 / GN))
                selfsem(v, v.tensor_scalar_mul(bsc[:, 3:4], bsc[:, 1:2], 1.0 / GN))
                selfsem(v, v.tensor_mul(bsc[:, 4:5], bsc[:, 2:3], bsc[:, 2:3]))
                selfsem(v, v.tensor_sub(bsc[:, 5:6], bsc[:, 3:4], bsc[:, 4:5]))

        with nc.Block() as blk:
            @blk.scalar
            def _(sc):
                sc.activation(bsc[:, 6:7], bsc[:, 5:6], AF.Ln,
                              bias=consts["eps_gn"][0:1, :]).then_inc(qs.h, 1)
                qs.n += 1
                sc.wait_ge(qs.h, qs.n)
                sc.activation(bsc[:, 7:8], bsc[:, 6:7], AF.Exp, bias=0.0, scale=-0.5)

        with nc.Block() as blk:
            @blk.sync
            def _(s):
                zst.inc(s.dma_start(out=scr_hbm[0:1, 0:1], in_=bsc[:, 2:3]), 16)
                zst.inc(s.dma_start(out=scr_hbm[0:1, 1:2], in_=bsc[:, 7:8]), 16)
                s.wait_ge(zst.h, zst.n)
                zst.inc(s.dma_start(out=mub[:, b:b + 1],
                                    in_=scr_hbm[0:1, 0:1].broadcast_to((128, 1))), 16)
                zst.inc(s.dma_start(out=rsb[:, b:b + 1],
                                    in_=scr_hbm[0:1, 1:2].broadcast_to((128, 1))), 16)
                s.wait_ge(zst.h, zst.n)

    x1v = env.sb("s1b_x1v", [128, 1024], F32)
    xttile = env.sb("s1b_xt", [128, 4 * 1024], F32)
    v4 = env.sb("s1b_v4", [128, 4 * 1024], F32)
    o1 = env.sb("s1b_o1", [128, 4 * 1024], BF16)
    amall = env.sb("s1b_am", [128, ntiles], F32)
    lnst = mk_ln_stats(env, "s1bln")
    lda = env.sem("s1b_lda")
    sta = env.sem("s1b_sta")

    vcons = env.sem("s1b_vcons")
    for tt in range(T // 512):
        b = (tt * 512) // S
        with nc.Block() as blk:
            la0, vc0 = lda.n, vcons.n

            @blk.sync
            def _(s):
                for j in range(4):
                    r0 = tt * 512 + j * 128
                    if j >= 2:
                        s.wait_ge(vcons.h, vc0 + j - 1)
                    lda.inc(s.dma_start(out=ybuf[:, (j % 2) * 1024:(j % 2) * 1024 + 1024],
                                        in_=y_hbm[r0:r0 + 128, :]), 16)
                    s.wait_ge(lda.h, lda.n)
                    lda.inc(s.dma_start(out=xttile[:, j * 1024:(j + 1) * 1024],
                                        in_=dd["x_tm"][r0:r0 + 128, :]), 16)
                    s.wait_ge(lda.h, lda.n)

            @blk.vector
            def _(v):
                for j in range(4):
                    v.wait_ge(lda.h, la0 + 32 * (j + 1))
                    vcons.inc(v.tensor_scalar(x1v[:, :],
                                              ybuf[:, (j % 2) * 1024:(j % 2) * 1024 + 1024],
                                              mub[:, b:b + 1], rsb[:, b:b + 1],
                                              ALU.subtract, ALU.mult))
                    v.tensor_add(v4[:, j * 1024:(j + 1) * 1024],
                                 xttile[:, j * 1024:(j + 1) * 1024], x1v[:, :])

        emit_ln_tail(env, consts, v4, o1, 4, lnst)
        with nc.Block() as blk:
            @blk.vector
            def _(v):
                for j in range(4):
                    v.reduce_max(amall[:, tt * 4 + j: tt * 4 + j + 1],
                                 o1[:, j * 1024:(j + 1) * 1024], axis=AX.X,
                                 apply_absolute_value=True)

            @blk.sync
            def _(s):
                for j in range(4):
                    sta.inc(s.dma_start(
                        out=dd["x1_tm"][tt * 512 + j * 128: tt * 512 + (j + 1) * 128, :],
                        in_=o1[:, j * 1024:(j + 1) * 1024]), 16)
                s.wait_ge(sta.h, sta.n)

    with nc.Block() as blk:
        @blk.sync
        def _(s):
            sta.inc(s.dma_start(out=dd["amax1_c"][:, :], in_=amall[:, :]), 16)
            s.wait_ge(sta.h, sta.n)
    env.pop_pool()


def load_consts(env, dd):
    nc = env.nc
    identbf = env.sb("c_identbf", [128, 128], BF16)
    identf = env.sb("c_identf", [128, 128], F32)
    ones_bf = env.sb("c_onesbf", [1, 128], BF16)
    eps_ln = env.sb("c_epsln", [128, 1], F32)
    eps_gn = env.sb("c_epsgn", [128, 1], F32)
    ld = env.sem("c_ld")
    with nc.Block() as blk:
        @blk.sync
        def _(s):
            ld.inc(s.dma_start(out=identbf[:, :], in_=dd["identbf"][:, :]), 16)
            ld.inc(s.dma_start(out=identf[:, :], in_=dd["identf"][:, :]), 16)
            ld.inc(s.dma_start(out=ones_bf[:, :], in_=dd["onesbf"][:, :]), 16)
            s.wait_ge(ld.h, ld.n)

        @blk.vector
        def _(v):
            v.memset(eps_ln[:, :], LN_EPS)
            v.memset(eps_gn[:, :], GN_EPS)
    return {"identbf": identbf, "identf": identf, "onesbf": ones_bf,
            "eps_ln": eps_ln, "eps_gn": eps_gn}


def host_consts():
    return {
        "identbf": np.eye(128, dtype=ml_dtypes.bfloat16),
        "identf": np.eye(128, dtype=np.float32),
        "onesbf": np.ones((1, 128), dtype=ml_dtypes.bfloat16),
    }


def vec_layout(v, p=128):
    return np.ascontiguousarray(v.reshape(-1, p).T)


def build_s4(T, debug=False):
    nc = bass.Bass("TRN2", target_bir_lowering=False)
    dd = {}
    dd["x3_tm"] = nc.dram_tensor("x3_tm", [T, D], BF16, kind="ExternalInput").ap()
    dd["w1T"] = nc.dram_tensor("w1T", [D, HID], F32, kind="ExternalInput").ap()
    dd["w2T"] = nc.dram_tensor("w2T", [D, HID], F32, kind="ExternalInput").ap()
    dd["h1T"] = nc.dram_tensor("h1T", [HID, HID], F32, kind="ExternalInput").ap()
    dd["h2T"] = nc.dram_tensor("h2T", [HID, HID], F32, kind="ExternalInput").ap()
    dd["w3T"] = nc.dram_tensor("w3T", [HID, D], F32, kind="ExternalInput").ap()
    for k, w in (("s1", HCH), ("s2", HCH), ("h1s", HCH), ("h2s", HCH), ("w3s", DCH)):
        dd[k] = nc.dram_tensor(k, [128, w], F32, kind="ExternalInput").ap()
    for k in ("identbf", "identf", "onesbf"):
        shape = [1, 128] if k == "onesbf" else [128, 128]
        dt = F32 if k == "identf" else BF16
        dd[k] = nc.dram_tensor(k, shape, dt, kind="ExternalInput").ap()
    dd["out"] = nc.dram_tensor("out", [T, D], F32, kind="ExternalOutput").ap()
    with ExitStack() as ctx:
        env = Env(nc, ctx)
        consts = load_consts(env, dd)
        tens = emit_s4(env, dd, T, consts)
        if debug:
            dsem = env.sem("dbg_st")
            douts = {}
            for k, sb in tens.items():
                p, f = sb.shape
                douts[k] = nc.dram_tensor("dbg_" + k, [p, f],
                                          mybir.dt(sb.dtype) if not isinstance(sb.dtype, mybir.dt) else sb.dtype,
                                          kind="ExternalOutput").ap()
            with nc.Block() as blk:
                @blk.sync
                def _(s):
                    for k, sb in tens.items():
                        dsem.inc(s.dma_start(out=douts[k][:, :], in_=sb[:, :]), 16)
                    s.wait_ge(dsem.h, dsem.n)
    return nc


def build_s2(T, seqlen=4096):
    nc = bass.Bass("TRN2", target_bir_lowering=False)
    dd = {}
    dd["x1_tm"] = nc.dram_tensor("x1_tm", [T, D], BF16, kind="ExternalInput").ap()
    dd["amax1_c"] = nc.dram_tensor("amax1_c", [128, T // 128], F32, kind="ExternalInput").ap()
    dd["w1T"] = nc.dram_tensor("w1T", [D, HID], F32, kind="ExternalInput").ap()
    dd["convw"] = nc.dram_tensor("convw", [3, HID], F32, kind="ExternalInput").ap()
    dd["convb"] = nc.dram_tensor("convb", [1, HID], F32, kind="ExternalInput").ap()
    dd["w2T"] = nc.dram_tensor("w2T", [HID, D], F32, kind="ExternalInput").ap()
    for k in ("identbf", "identf", "onesbf"):
        shape = [1, 128] if k == "onesbf" else [128, 128]
        dt = F32 if k == "identf" else BF16
        dd[k] = nc.dram_tensor(k, shape, dt, kind="ExternalInput").ap()
    dd["x2_tm"] = nc.dram_tensor("x2_tm", [T, D], BF16, kind="ExternalOutput").ap()
    with ExitStack() as ctx:
        env = Env(nc, ctx)
        consts = load_consts(env, dd)
        emit_s2(env, dd, T, consts, seqlen=seqlen)
    return nc


def build_s3(T, L=4096):
    nc = bass.Bass("TRN2", target_bir_lowering=False)
    dd = {}
    dd["x2_tm"] = nc.dram_tensor("x2_tm", [T, D], BF16, kind="ExternalInput").ap()
    for k in ("wqT", "wkT", "wvT", "woT"):
        dd[k] = nc.dram_tensor(k, [D, D], F32, kind="ExternalInput").ap()
    for k in ("bq", "bk", "bv"):
        dd[k] = nc.dram_tensor(k, [128, 8], F32, kind="ExternalInput").ap()
    dd["bo_row"] = nc.dram_tensor("bo_row", [1, D], F32, kind="ExternalInput").ap()
    dd["w0l"] = nc.dram_tensor("w0l", [128, 512], F32, kind="ExternalInput").ap()
    for k in ("identbf", "identf", "onesbf"):
        shape = [1, 128] if k == "onesbf" else [128, 128]
        dt = F32 if k == "identf" else BF16
        dd[k] = nc.dram_tensor(k, shape, dt, kind="ExternalInput").ap()
    dd["x3_tm"] = nc.dram_tensor("x3_tm", [T, D], BF16, kind="ExternalOutput").ap()
    with ExitStack() as ctx:
        env = Env(nc, ctx)
        consts = load_consts(env, dd)
        emit_s3(env, dd, T, consts, L=L)
    return nc


def build_s1(T, S=4096):
    nc = bass.Bass("TRN2", target_bir_lowering=False)
    dd = {}
    dd["x_fm"] = nc.dram_tensor("x_fm", [D, T], F32, kind="ExternalInput").ap()
    dd["x_tm"] = nc.dram_tensor("x_tm", [T, D], F32, kind="ExternalInput").ap()
    for k in ("wqT", "wkT", "wvT"):
        dd[k] = nc.dram_tensor(k, [D, D], F32, kind="ExternalInput").ap()
    for k in ("identbf", "identf", "onesbf"):
        shape = [1, 128] if k == "onesbf" else [128, 128]
        dt = F32 if k == "identf" else BF16
        dd[k] = nc.dram_tensor(k, shape, dt, kind="ExternalInput").ap()
    dd["x1_tm"] = nc.dram_tensor("x1_tm", [T, D], BF16, kind="ExternalOutput").ap()
    dd["amax1_c"] = nc.dram_tensor("amax1_c", [128, T // 128], F32, kind="ExternalOutput").ap()
    with ExitStack() as ctx:
        env = Env(nc, ctx)
        consts = load_consts(env, dd)
        y_hbm = emit_s1(env, dd, T, consts, S=S)
        emit_s1b(env, dd, y_hbm, T, consts, S=S)
    return nc


T_CORE = 8192
_PROGS = {}


def _get_progs():
    if not _PROGS:
        _PROGS["s1"] = build_s1(T_CORE)
        _PROGS["s2"] = build_s2(T_CORE)
        _PROGS["s3"] = build_s3(T_CORE)
        _PROGS["s4"] = build_s4(T_CORE)
    return _PROGS


def kernel(**inputs):
    ins = {k: np.asarray(v) for k, v in inputs.items()}
    x = ins["L0_attn_x"]
    ncore = 8
    consts = host_consts()

    w_s1 = {"wqT": np.ascontiguousarray(ins["L0_attn_Wq"].T),
            "wkT": np.ascontiguousarray(ins["L0_attn_Wk"].T),
            "wvT": np.ascontiguousarray(ins["L0_attn_Wv"].T)}
    w_s2 = {"w1T": np.ascontiguousarray(ins["L0_ffn_w1"].T),
            "convw": np.ascontiguousarray(ins["L0_ffn_conv_w"].reshape(4096, 3).T),
            "convb": np.ascontiguousarray(ins["L0_ffn_conv_b"].reshape(1, 4096)),
            "w2T": np.ascontiguousarray(ins["L0_ffn_w2"].T)}
    w0l = np.zeros((128, 512), np.float32)
    wrf = ins["L1_attn_Wrf"][0]
    for h in range(16):
        w0l[(h % 2) * 64:(h % 2) * 64 + 64, (h // 2) * 64:(h // 2) * 64 + 64] = wrf[h]
    w_s3 = {"wqT": np.ascontiguousarray(ins["L1_attn_Wq"].T),
            "wkT": np.ascontiguousarray(ins["L1_attn_Wk"].T),
            "wvT": np.ascontiguousarray(ins["L1_attn_Wv"].T),
            "woT": np.ascontiguousarray(ins["L1_attn_Wo"].T),
            "bq": vec_layout(ins["L1_attn_bq"]), "bk": vec_layout(ins["L1_attn_bk"]),
            "bv": vec_layout(ins["L1_attn_bv"]),
            "bo_row": np.ascontiguousarray(ins["L1_attn_bo"].reshape(1, 1024)),
            "w0l": w0l}
    w_s4 = {"w1T": np.ascontiguousarray(ins["L1_ffn_w1_w"].T),
            "w2T": np.ascontiguousarray(ins["L1_ffn_w2_w"].T),
            "h1T": np.ascontiguousarray(ins["L1_ffn_h1_w"].T),
            "h2T": np.ascontiguousarray(ins["L1_ffn_h2_w"].T),
            "w3T": np.ascontiguousarray(ins["L1_ffn_w3_w"].T),
            "s1": vec_layout(ins["L1_ffn_w1_s"]), "s2": vec_layout(ins["L1_ffn_w2_s"]),
            "h1s": vec_layout(ins["L1_ffn_h1_s"]), "h2s": vec_layout(ins["L1_ffn_h2_s"]),
            "w3s": vec_layout(ins["L1_ffn_w3_s"])}

    progs = _get_progs()
    core_ids = list(range(ncore))

    maps = []
    for i in range(ncore):
        xc = np.ascontiguousarray(x[2 * i:2 * i + 2].reshape(T_CORE, 1024))
        m = {"x_fm": np.ascontiguousarray(xc.T), "x_tm": xc}
        m.update(w_s1)
        m.update(consts)
        maps.append(m)
    r1 = run_bass_kernel_spmd(progs["s1"], maps, core_ids=core_ids)

    maps = []
    for i in range(ncore):
        m = {"x1_tm": np.asarray(r1.results[i]["x1_tm"]),
             "amax1_c": np.asarray(r1.results[i]["amax1_c"])}
        m.update(w_s2)
        m.update(consts)
        maps.append(m)
    r2 = run_bass_kernel_spmd(progs["s2"], maps, core_ids=core_ids)

    maps = []
    for i in range(ncore):
        m = {"x2_tm": np.asarray(r2.results[i]["x2_tm"])}
        m.update(w_s3)
        m.update(consts)
        maps.append(m)
    r3 = run_bass_kernel_spmd(progs["s3"], maps, core_ids=core_ids)

    maps = []
    for i in range(ncore):
        m = {"x3_tm": np.asarray(r3.results[i]["x3_tm"])}
        m.update(w_s4)
        m.update(consts)
        maps.append(m)
    r4 = run_bass_kernel_spmd(progs["s4"], maps, core_ids=core_ids)

    out = np.empty((16, 4096, 1024), np.float32)
    for i in range(ncore):
        out[2 * i:2 * i + 2] = np.asarray(r4.results[i]["out"]).reshape(2, 4096, 1024)
    return out


# version 09
# speedup vs baseline: 1.0130x; 1.0130x over previous
import numpy as np
import ml_dtypes
from contextlib import ExitStack

import concourse.bass as bass
import concourse.mybir as mybir
from concourse.bass_utils import run_bass_kernel_spmd

F32 = mybir.dt.float32
BF16 = mybir.dt.bfloat16
I8 = mybir.dt.int8
AF = mybir.ActivationFunctionType
ALU = mybir.AluOpType
AX = mybir.AxisListType

D = 1024
HID = 4096
DCH = D // 128
HCH = HID // 128
LN_EPS = 1e-5
L1_ATTN_EPS = 1e-6
GN_EPS = 1e-5


class Sem:
    def __init__(self, ctx, nc, name):
        self.h = ctx.enter_context(nc.semaphore(name))
        self.n = 0

    def inc(self, instr, k=1):
        instr.then_inc(self.h, k)
        self.n += k
        return self.n

    def wait(self, eng, n=None):
        eng.wait_ge(self.h, self.n if n is None else n)


class Env:
    def __init__(self, nc, ctx):
        self.nc = nc
        self.ctx = ctx
        self.pools = [ctx]
        self._semid = 0

    def sem(self, name=None):
        if not hasattr(self, "_sempool"):
            self._sempool = []
            self._semcur = 0
        if self._semcur < len(self._sempool):
            s = self._sempool[self._semcur]
        else:
            self._semid += 1
            s = Sem(self.ctx, self.nc, f"sem{self._semid}")
            self._sempool.append(s)
        self._semcur += 1
        return s

    def sem_scope_reset(self):
        self._semcur = 0

    def sb(self, name, shape, dt):
        return self.pools[-1].enter_context(self.nc.sbuf_tensor(name, shape, dt))

    def ps(self, name, shape, dt=F32):
        return self.pools[-1].enter_context(self.nc.psum_tensor(name, shape, dt))

    def push_pool(self):
        self.pools.append(ExitStack())

    def pop_pool(self):
        self.pools.pop().close()


def emit_cast_weights(env, pairs, tag):
    nc = env.nc
    maxc = max(p[0].shape[1] for p in pairs)
    buf = [env.sb(f"cw_{tag}_{i}", [128, maxc], BF16) for i in range(2)]
    ld = [env.sem(f"cw_ld{i}_{tag}") for i in range(2)]
    st = [env.sem(f"cw_st{i}_{tag}") for i in range(2)]
    chunks = []
    for src, dst in pairs:
        r, c = src.shape
        assert r % 128 == 0 and dst.shape[0] == r and dst.shape[1] == c
        for i in range(r // 128):
            chunks.append((src, dst, i, c))
    with nc.Block() as blk:
        ld00 = [ld[0].n, ld[1].n]
        st00 = [st[0].n, st[1].n]

        @blk.gpsimd
        def _(g):
            for k, (src, dst, i, c) in enumerate(chunks):
                p = k % 2
                if k >= 2:
                    g.wait_ge(st[p].h, st00[p] + 16 * (k // 2))
                ld[p].inc(g.dma_start(out=buf[p][:, 0:c],
                                      in_=src[i * 128:(i + 1) * 128, :]), 16)

        @blk.sync
        def _(s):
            for k, (src, dst, i, c) in enumerate(chunks):
                p = k % 2
                s.wait_ge(ld[p].h, ld00[p] + 16 * (k // 2 + 1))
                st[p].inc(s.dma_start(out=dst[i * 128:(i + 1) * 128, :],
                                      in_=buf[p][:, 0:c]), 16)
            s.wait_ge(st[0].h, st[0].n)
            s.wait_ge(st[1].h, st[1].n)


def emit_s4(env, dd, T, consts):
    nc = env.nc
    NT = 512
    ntiles = T // NT
    identbf = consts["identbf"]

    w1b = nc.dram_tensor("s4_w1b", [D, HID], BF16).ap()
    w2b = nc.dram_tensor("s4_w2b", [D, HID], BF16).ap()
    h1b = nc.dram_tensor("s4_h1b", [HID, HID], BF16).ap()
    h2b = nc.dram_tensor("s4_h2b", [HID, HID], BF16).ap()
    w3b = nc.dram_tensor("s4_w3b", [HID, D], BF16).ap()
    emit_cast_weights(env, [(dd["w1T"], w1b), (dd["w2T"], w2b),
                            (dd["h1T"], h1b), (dd["h2T"], h2b),
                            (dd["w3T"], w3b)], "s4")

    s1_sb = env.sb("s4_s1", [128, HCH], F32)
    s2_sb = env.sb("s4_s2", [128, HCH], F32)
    h1s_sb = env.sb("s4_h1s", [128, HCH], F32)
    h2s_sb = env.sb("s4_h2s", [128, HCH], F32)
    w3s_sb = env.sb("s4_w3s", [128, DCH], F32)
    ld0 = env.sem("s4_ld0")
    with nc.Block() as blk:
        @blk.sync
        def _(s):
            for sb, key in ((s1_sb, "s1"), (s2_sb, "s2"), (h1s_sb, "h1s"),
                            (h2s_sb, "h2s"), (w3s_sb, "w3s")):
                ld0.inc(s.dma_start(out=sb[:, :], in_=dd[key][:, :]), 16)
            s.wait_ge(ld0.h, ld0.n)

    x3tm = env.sb("s4_x3tm", [128, 2 * 4 * 1024], BF16)
    x3fm = env.sb("s4_x3fm", [128, DCH * NT], BF16)
    ht = env.sb("s4_ht", [128, HCH * NT], BF16)
    h2sb = env.sb("s4_h2", [128, HCH * NT], BF16)
    t1 = env.sb("s4_t1", [128, NT], BF16)
    t2 = env.sb("s4_t2", [128, NT], BF16)
    wblkA = [env.sb(f"s4_wA{i}", [128, DCH * 128], BF16) for i in range(2)]
    wblkB = [env.sb(f"s4_wB{i}", [128, DCH * 128], BF16) for i in range(2)]
    wblkH = [env.sb(f"s4_wH{i}", [128, HCH * 128], BF16) for i in range(2)]
    y4fm = env.sb("s4_y4fm", [128, DCH * NT], BF16)
    v4 = env.sb("s4_v4", [128, 4096], F32)
    sq4 = env.sb("s4_sq4", [128, 4096], F32)
    o_sb = env.sb("s4_o", [128, 4 * 1024], F32)
    ssq4 = env.sb("s4_ssq4", [128, 4], F32)
    sm1_4 = env.sb("s4_sm14", [128, 4], F32)
    sm2_4 = env.sb("s4_sm24", [128, 4], F32)
    ssq2_4 = env.sb("s4_ssq24", [128, 4], F32)
    t4 = env.sb("s4_t4", [128, 4], F32)
    rstd4 = env.sb("s4_rstd4", [128, 4], F32)

    psZ = [env.ps(f"s4_psZ{i}", [128, 512], F32) for i in range(4)]
    ps_tp = [env.ps(f"s4_pstp{i}", [128, 512], BF16) for i in range(2)]
    ps_tm = [env.ps(f"s4_pstm{i}", [128, 1024], BF16) for i in range(2)]

    ldx = [env.sem("s4_ldx0"), env.sem("s4_ldx1")]
    tpx = env.sem("s4_tpx")
    evx = env.sem("s4_evx")
    ldw = [env.sem("s4_ldw0"), env.sem("s4_ldw1")]
    mmA = env.sem("s4_mmA")
    mmB = env.sem("s4_mmB")
    acA = env.sem("s4_acA")
    evB = env.sem("s4_evB")
    mulS = env.sem("s4_mul")
    tpy = env.sem("s4_tpy")
    vst = env.sem("s4_vst")
    sselfS = env.sem("s4_sselfS")
    fin = env.sem("s4_fin")
    sto = env.sem("s4_sto")

    sto_base = sto.n
    with nc.Block() as blk:
        @blk.sync
        def _(s):
            for j in range(4):
                ldx[0].inc(s.dma_start(
                    out=x3tm[:, j * 1024:(j + 1) * 1024],
                    in_=dd["x3_tm"][j * 128:(j + 1) * 128, :]), 16)

    for t in range(ntiles):
        slot = t % 2
        xoff = slot * 4096
        nslot = (t + 1) % 2

        with nc.Block() as blk:
            ldx_t0 = ldx[slot].n
            evx0, tpx0 = evx.n, tpx.n

            @blk.sync
            def _(s):
                if t + 1 < ntiles:
                    for j in range(4):
                        ldx[nslot].inc(s.dma_start(
                            out=x3tm[:, nslot * 4096 + j * 1024: nslot * 4096 + (j + 1) * 1024],
                            in_=dd["x3_tm"][(t + 1) * NT + j * 128:(t + 1) * NT + (j + 1) * 128, :]), 16)

            @blk.tensor
            def _(te):
                te.wait_ge(ldx[slot].h, ldx_t0)
                for i in range(DCH):
                    if i >= 2:
                        te.wait_ge(evx.h, evx0 + i - 1)
                    for j in range(4):
                        ins = te.transpose(
                            ps_tp[i % 2][:, j * 128:(j + 1) * 128],
                            x3tm[:, xoff + j * 1024 + i * 128: xoff + j * 1024 + (i + 1) * 128],
                            identbf[:, :])
                    tpx.inc(ins)

            @blk.vector
            def _(v):
                for i in range(DCH):
                    v.wait_ge(tpx.h, tpx0 + i + 1)
                    evx.inc(v.tensor_copy(x3fm[:, i * NT:(i + 1) * NT], ps_tp[i % 2][:, :]))

        with nc.Block() as blk:
            mmA0, mmB0, acA0, evB0, mulS0 = mmA.n, mmB.n, acA.n, evB.n, mulS.n
            ldw0 = [ldw[0].n, ldw[1].n]

            @blk.gpsimd
            def _(g):
                for o in range(HCH):
                    p = o % 2
                    if o >= 2:
                        g.wait_ge(mmB.h, mmB0 + o - 1)
                    ldw[p].inc(g.dma_start(
                        out=wblkA[p][:, :],
                        in_=w1b.rearrange("(ic p) o -> p ic o", p=128)[:, :, o * 128:(o + 1) * 128]), 16)
                    ldw[p].inc(g.dma_start(
                        out=wblkB[p][:, :],
                        in_=w2b.rearrange("(ic p) o -> p ic o", p=128)[:, :, o * 128:(o + 1) * 128]), 16)

            @blk.tensor
            def _(te):
                for o in range(HCH):
                    te.wait_ge(ldw[o % 2].h, ldw0[o % 2] + 32 * (o // 2 + 1))
                    if o >= 2:
                        te.wait_ge(acA.h, acA0 + o - 1)
                        te.wait_ge(evB.h, evB0 + o - 1)
                    for i in range(DCH):
                        ins = te.matmul(psZ[2 * (o % 2)][:, :],
                                        wblkA[o % 2][:, i * 128:(i + 1) * 128],
                                        x3fm[:, i * NT:(i + 1) * NT],
                                        start=(i == 0), stop=(i == DCH - 1))
                    mmA.inc(ins)
                    for i in range(DCH):
                        ins = te.matmul(psZ[2 * (o % 2) + 1][:, :],
                                        wblkB[o % 2][:, i * 128:(i + 1) * 128],
                                        x3fm[:, i * NT:(i + 1) * NT],
                                        start=(i == 0), stop=(i == DCH - 1))
                    mmB.inc(ins)

            @blk.scalar
            def _(sc):
                for o in range(HCH):
                    sc.wait_ge(mmA.h, mmA0 + o + 1)
                    if o >= 1:
                        sc.wait_ge(mulS.h, mulS0 + o)
                    acA.inc(sc.activation(t1[:, :], psZ[2 * (o % 2)][:, :], AF.Silu,
                                          bias=0.0, scale=s1_sb[:, o:o + 1]))

            @blk.vector
            def _(v):
                for o in range(HCH):
                    v.wait_ge(mmB.h, mmB0 + o + 1)
                    evB.inc(v.tensor_scalar_mul(t2[:, :], psZ[2 * (o % 2) + 1][:, :],
                                                s2_sb[:, o:o + 1]))
                    v.wait_ge(acA.h, acA0 + o + 1)
                    mulS.inc(v.tensor_mul(ht[:, o * NT:(o + 1) * NT], t1[:, :], t2[:, :]))

        for wsrc, ssb, src, dst in ((h1b, h1s_sb, ht, h2sb), (h2b, h2s_sb, h2sb, ht)):
            with nc.Block() as blk:
                mm0, ac0 = mmA.n, acA.n
                ldw0 = [ldw[0].n, ldw[1].n]

                @blk.gpsimd
                def _(g):
                    for o in range(HCH):
                        if o >= 2:
                            g.wait_ge(mmA.h, mm0 + o - 1)
                        ldw[o % 2].inc(g.dma_start(
                            out=wblkH[o % 2][:, :],
                            in_=wsrc.rearrange("(ic p) o -> p ic o", p=128)[:, :, o * 128:(o + 1) * 128]), 16)

                @blk.tensor
                def _(te):
                    for o in range(HCH):
                        te.wait_ge(ldw[o % 2].h, ldw0[o % 2] + 16 * (o // 2 + 1))
                        if o >= 2:
                            te.wait_ge(acA.h, ac0 + o - 1)
                        ps = psZ[o % 2]
                        for i in range(HCH):
                            ins = te.matmul(ps[:, :], wblkH[o % 2][:, i * 128:(i + 1) * 128],
                                            src[:, i * NT:(i + 1) * NT],
                                            start=(i == 0), stop=(i == HCH - 1))
                        mmA.inc(ins)

                @blk.scalar
                def _(sc):
                    for o in range(HCH):
                        sc.wait_ge(mmA.h, mm0 + o + 1)
                        acA.inc(sc.activation(dst[:, o * NT:(o + 1) * NT], psZ[o % 2][:, :],
                                              AF.Gelu, bias=0.0, scale=ssb[:, o:o + 1]))

        with nc.Block() as blk:
            mm0, ev0 = mmA.n, evB.n
            ldw0 = [ldw[0].n, ldw[1].n]

            @blk.gpsimd
            def _(g):
                for o in range(DCH):
                    if o >= 2:
                        g.wait_ge(mmA.h, mm0 + o - 1)
                    ldw[o % 2].inc(g.dma_start(
                        out=wblkH[o % 2][:, :],
                        in_=w3b.rearrange("(ic p) o -> p ic o", p=128)[:, :, o * 128:(o + 1) * 128]), 16)

            @blk.tensor
            def _(te):
                for o in range(DCH):
                    te.wait_ge(ldw[o % 2].h, ldw0[o % 2] + 16 * (o // 2 + 1))
                    if o >= 2:
                        te.wait_ge(evB.h, ev0 + o - 1)
                    ps = psZ[o % 2]
                    for i in range(HCH):
                        ins = te.matmul(ps[:, :], wblkH[o % 2][:, i * 128:(i + 1) * 128],
                                        ht[:, i * NT:(i + 1) * NT],
                                        start=(i == 0), stop=(i == HCH - 1))
                    mmA.inc(ins)

            @blk.vector
            def _(v):
                for o in range(DCH):
                    v.wait_ge(mmA.h, mm0 + o + 1)
                    evB.inc(v.tensor_scalar_mul(y4fm[:, o * NT:(o + 1) * NT],
                                                psZ[o % 2][:, :], w3s_sb[:, o:o + 1]))

        with nc.Block() as blk:
            tp0, vst0 = tpy.n, vst.n

            @blk.tensor
            def _(te):
                for j in range(4):
                    if j >= 2:
                        te.wait_ge(vst.h, vst0 + j - 1)
                    for i in range(DCH):
                        ins = te.transpose(
                            ps_tm[j % 2][:, i * 128:(i + 1) * 128],
                            y4fm[:, i * NT + j * 128: i * NT + (j + 1) * 128],
                            identbf[:, :])
                    tpy.inc(ins)

            @blk.vector
            def _(v):
                for j in range(4):
                    v.wait_ge(tpy.h, tp0 + j + 1)
                    vst.inc(v.tensor_add(v4[:, j * 1024:(j + 1) * 1024],
                                         x3tm[:, xoff + j * 1024: xoff + (j + 1) * 1024],
                                         ps_tm[j % 2][:, :]))
                for j in range(4):
                    v.reduce_sum(ssq4[:, j:j + 1], v4[:, j * 1024:(j + 1) * 1024], axis=AX.X)

        with nc.Block() as blk:
            @blk.vector
            def _(v):
                v.tensor_scalar_mul(sm1_4[:, :], ssq4[:, :], -1.0 / D)
                v.tensor_scalar_mul(sm2_4[:, :], ssq4[:, :], 1.0 / D)

        with nc.Block() as blk:
            @blk.scalar
            def _(sc):
                for j in range(4):
                    sc.activation(sq4[:, j * 1024:(j + 1) * 1024],
                                  v4[:, j * 1024:(j + 1) * 1024], AF.Square,
                                  bias=sm1_4[:, j:j + 1], scale=1.0)

        with nc.Block() as blk:
            @blk.vector
            def _(v):
                for j in range(4):
                    v.reduce_sum(ssq2_4[:, j:j + 1], sq4[:, j * 1024:(j + 1) * 1024], axis=AX.X)

        with nc.Block() as blk:
            @blk.vector
            def _(v):
                v.tensor_scalar(t4[:, :], ssq2_4[:, :], 1.0 / D, LN_EPS, ALU.mult, ALU.add)

        with nc.Block() as blk:
            @blk.scalar
            def _(sc):
                sc.activation(rstd4[:, :], t4[:, :], AF.Ln).then_inc(sselfS.h, 1)
                sselfS.n += 1
                sc.wait_ge(sselfS.h, sselfS.n)
                sc.activation(rstd4[:, :], rstd4[:, :], AF.Exp, bias=0.0, scale=-0.5)

        with nc.Block() as blk:
            fin0 = fin.n

            @blk.vector
            def _(v):
                for j in range(4):
                    if t >= 1 and j == 0:
                        v.wait_ge(sto.h, sto_base + 16 * 4 * t)
                    fin.inc(v.tensor_scalar(o_sb[:, j * 1024:(j + 1) * 1024],
                                            v4[:, j * 1024:(j + 1) * 1024],
                                            sm2_4[:, j:j + 1], rstd4[:, j:j + 1],
                                            ALU.subtract, ALU.mult))

            @blk.sync
            def _(s):
                for j in range(4):
                    s.wait_ge(fin.h, fin0 + j + 1)
                    sto.inc(s.dma_start(
                        out=dd["out"][t * NT + j * 128: t * NT + (j + 1) * 128, :],
                        in_=o_sb[:, j * 1024:(j + 1) * 1024]), 16)

    with nc.Block() as blk:
        @blk.sync
        def _(s):
            s.wait_ge(sto.h, sto.n)
    return {"x3fm": x3fm, "ht": ht, "h2sb": h2sb, "y4fm": y4fm, "o_sb": o_sb,
            "x3tm": x3tm}


def emit_ln_tail(env, consts, v4, o_sb, nsub, stats, cast_out_dt=BF16):
    nc = env.nc
    sq4 = stats["sq4"]
    with nc.Block() as blk:
        @blk.vector
        def _(v):
            for j in range(nsub):
                v.reduce_sum(stats["ssq4"][:, j:j + 1], v4[:, j * 1024:(j + 1) * 1024], axis=AX.X)
    with nc.Block() as blk:
        @blk.vector
        def _(v):
            v.tensor_scalar_mul(stats["sm1_4"][:, 0:nsub], stats["ssq4"][:, 0:nsub], -1.0 / D)
            v.tensor_scalar_mul(stats["sm2_4"][:, 0:nsub], stats["ssq4"][:, 0:nsub], 1.0 / D)
    with nc.Block() as blk:
        @blk.scalar
        def _(sc):
            for j in range(nsub):
                sc.activation(sq4[:, j * 1024:(j + 1) * 1024],
                              v4[:, j * 1024:(j + 1) * 1024], AF.Square,
                              bias=stats["sm1_4"][:, j:j + 1], scale=1.0)
    with nc.Block() as blk:
        @blk.vector
        def _(v):
            for j in range(nsub):
                v.reduce_sum(stats["ssq2_4"][:, j:j + 1], sq4[:, j * 1024:(j + 1) * 1024], axis=AX.X)
    with nc.Block() as blk:
        @blk.vector
        def _(v):
            v.tensor_scalar(stats["t4"][:, 0:nsub], stats["ssq2_4"][:, 0:nsub],
                            1.0 / D, LN_EPS, ALU.mult, ALU.add)
    with nc.Block() as blk:
        @blk.scalar
        def _(sc):
            ss = stats["selfsem"]
            sc.activation(stats["rstd4"][:, 0:nsub], stats["t4"][:, 0:nsub], AF.Ln).then_inc(ss.h, 1)
            ss.n += 1
            sc.wait_ge(ss.h, ss.n)
            sc.activation(stats["rstd4"][:, 0:nsub], stats["rstd4"][:, 0:nsub],
                          AF.Exp, bias=0.0, scale=-0.5)
    with nc.Block() as blk:
        @blk.vector
        def _(v):
            for j in range(nsub):
                v.tensor_scalar(o_sb[:, j * 1024:(j + 1) * 1024],
                                v4[:, j * 1024:(j + 1) * 1024],
                                stats["sm2_4"][:, j:j + 1], stats["rstd4"][:, j:j + 1],
                                ALU.subtract, ALU.mult)


def mk_ln_stats(env, tag, nsub=4):
    return {
        "sq4": env.sb(f"{tag}_sq4", [128, nsub * 1024], BF16),
        "ssq4": env.sb(f"{tag}_ssq4", [128, nsub], F32),
        "sm1_4": env.sb(f"{tag}_sm14", [128, nsub], F32),
        "sm2_4": env.sb(f"{tag}_sm24", [128, nsub], F32),
        "ssq2_4": env.sb(f"{tag}_ssq24", [128, nsub], F32),
        "t4": env.sb(f"{tag}_t4", [128, nsub], F32),
        "rstd4": env.sb(f"{tag}_rstd4", [128, nsub], F32),
        "selfsem": env.sem(f"{tag}_lnself"),
    }


def emit_global_mean_abs(env, tag, src_hbm, R, C, out_bc, consts):
    nc = env.nc
    nch = R // 128
    buf = [env.sb(f"{tag}_mb{i}", [128, C], F32) for i in range(2)]
    red = env.sb(f"{tag}_red", [128, max(nch, 2)], F32)
    acc1 = env.sb(f"{tag}_acc1", [128, 1], F32)
    accp = env.sb(f"{tag}_accp", [1, 128], F32)
    sc1 = env.sb(f"{tag}_sc1", [1, 1], F32)
    sc2 = env.sb(f"{tag}_sc2", [1, 1], F32)
    scr_hbm = nc.dram_tensor(f"{tag}_schbm", [1, 2], F32).ap()
    ps = env.ps(f"{tag}_ps", [1, 128], F32)
    ld = [env.sem(f"{tag}_ld{i}") for i in range(2)]
    rd = env.sem(f"{tag}_rd")
    with nc.Block() as blk:
        ld00 = [ld[0].n, ld[1].n]
        rd0 = rd.n

        @blk.sync
        def _(s):
            for k in range(nch):
                p = k % 2
                if k >= 2:
                    s.wait_ge(rd.h, rd0 + k - 1)
                ld[p].inc(s.dma_start(out=buf[p][:, :],
                                      in_=src_hbm[k * 128:(k + 1) * 128, :]), 16)

        @blk.vector
        def _(v):
            for k in range(nch):
                p = k % 2
                v.wait_ge(ld[p].h, ld00[p] + 16 * (k // 2 + 1))
                rd.inc(v.reduce_sum(red[:, k:k + 1], buf[p][:, :], axis=AX.X,
                                    apply_absolute_value=True))
    with nc.Block() as blk:
        @blk.vector
        def _(v):
            v.reduce_sum(acc1[:, :], red[:, 0:nch], axis=AX.X)
    with nc.Block() as blk:
        @blk.tensor
        def _(te):
            te.transpose(ps[:, :], acc1[:, :], consts["identf"][:, :])
    with nc.Block() as blk:
        @blk.vector
        def _(v):
            v.tensor_copy(accp[:, :], ps[:, :])
    with nc.Block() as blk:
        @blk.vector
        def _(v):
            v.reduce_sum(sc1[:, :], accp[:, :], axis=AX.X)
    with nc.Block() as blk:
        @blk.vector
        def _(v):
            v.tensor_scalar(sc1[:, :], sc1[:, :], 1.0 / (R * C), 1e-5, ALU.mult, ALU.max)
    with nc.Block() as blk:
        @blk.vector
        def _(v):
            v.reciprocal(sc2[:, :], sc1[:, :])
    st = env.sem(f"{tag}_st")
    with nc.Block() as blk:
        @blk.sync
        def _(s):
            st.inc(s.dma_start(out=scr_hbm[:, 0:1], in_=sc1[:, :]), 16)
            st.inc(s.dma_start(out=scr_hbm[:, 1:2], in_=sc2[:, :]), 16)
            s.wait_ge(st.h, st.n)
            st.inc(s.dma_start(out=out_bc["mean"][:, :],
                               in_=scr_hbm[0:1, 0:1].broadcast_to((128, 1))), 16)
            st.inc(s.dma_start(out=out_bc["inv"][:, :],
                               in_=scr_hbm[0:1, 1:2].broadcast_to((128, 1))), 16)
            s.wait_ge(st.h, st.n)


def emit_ternarize(env, tag, src_hbm, dst_hbm, R, C, inv_bc):
    nc = env.nc
    buf = [env.sb(f"{tag}_tb{i}", [128, C], F32) for i in range(2)]
    tf = env.sb(f"{tag}_tf", [128, C], F32)
    ti = env.sb(f"{tag}_ti", [128, C], I8)
    tb = [env.sb(f"{tag}_tbf{i}", [128, C], BF16) for i in range(2)]
    ld = [env.sem(f"{tag}_tld{i}") for i in range(2)]
    cv = env.sem(f"{tag}_tcv")
    st = [env.sem(f"{tag}_tst{i}") for i in range(2)]
    nch = R // 128
    with nc.Block() as blk:
        ld00 = [ld[0].n, ld[1].n]
        st00 = [st[0].n, st[1].n]
        cv0 = cv.n

        @blk.gpsimd
        def _(g):
            for k in range(nch):
                p = k % 2
                if k >= 2:
                    g.wait_ge(cv.h, cv0 + (k - 1) * 4)
                ld[p].inc(g.dma_start(out=buf[p][:, :],
                                      in_=src_hbm[k * 128:(k + 1) * 128, :]), 16)

        @blk.vector
        def _(v):
            for k in range(nch):
                p = k % 2
                v.wait_ge(ld[p].h, ld00[p] + 16 * (k // 2 + 1))
                if k >= 2:
                    v.wait_ge(st[p].h, st00[p] + 16 * (k // 2))
                v.tensor_scalar_mul(tf[:, :], buf[p][:, :], inv_bc[:, :])
                v.tensor_copy(ti[:, :], tf[:, :])
                v.tensor_copy(tb[p][:, :], ti[:, :])
                cv.inc(v.tensor_scalar(tb[p][:, :], tb[p][:, :], 1.0, -1.0,
                                       ALU.min, ALU.max), 4)

        @blk.sync
        def _(s):
            for k in range(nch):
                p = k % 2
                s.wait_ge(cv.h, cv0 + (k + 1) * 4)
                st[p].inc(s.dma_start(out=dst_hbm[k * 128:(k + 1) * 128, :],
                                      in_=tb[p][:, :]), 16)
            s.wait_ge(st[0].h, st[0].n)
            s.wait_ge(st[1].h, st[1].n)


def emit_bcast_rows(env, tag, src_row_hbm, nrows, C, dst_tiles, consts, scale=1.0):
    nc = env.nc
    row_f = env.sb(f"{tag}_rowf", [1, C], F32)
    row_b = env.sb(f"{tag}_rowb", [1, C], BF16)
    ps = [env.ps(f"{tag}_bps{i}", [128, 512], F32) for i in range(2)]
    ld = env.sem(f"{tag}_bld")
    mm = env.sem(f"{tag}_bmm")
    ev = env.sem(f"{tag}_bev")
    for r in range(nrows):
        with nc.Block() as blk:
            @blk.sync
            def _(s):
                ld.inc(s.dma_start(out=row_f[:, :], in_=src_row_hbm[r:r + 1, :]), 16)

            @blk.vector
            def _(v):
                v.wait_ge(ld.h, ld.n)
                if scale != 1.0:
                    v.tensor_scalar_mul(row_b[:, :], row_f[:, :], scale)
                else:
                    v.tensor_copy(row_b[:, :], row_f[:, :])
        with nc.Block() as blk:
            mm0, ev0 = mm.n, ev.n

            @blk.tensor
            def _(te):
                for c in range(C // 512):
                    if c >= 2:
                        te.wait_ge(ev.h, ev0 + c - 1)
                    mm.inc(te.matmul(ps[c % 2][:, :], consts["onesbf"][:, :],
                                     row_b[:, c * 512:(c + 1) * 512],
                                     start=True, stop=True))

            @blk.vector
            def _(v):
                for c in range(C // 512):
                    v.wait_ge(mm.h, mm0 + c + 1)
                    ev.inc(v.tensor_copy(dst_tiles[r][:, c * 512:(c + 1) * 512],
                                         ps[c % 2][:, :]))


def emit_s2(env, dd, T, consts, seqlen=4096):
    nc = env.nc
    NT = 512
    S = seqlen
    ntiles = T // NT
    identbf = consts["identbf"]

    w1m = {"mean": env.sb("s2_w1mean", [128, 1], F32), "inv": env.sb("s2_w1inv", [128, 1], F32)}
    w2m = {"mean": env.sb("s2_w2mean", [128, 1], F32), "inv": env.sb("s2_w2inv", [128, 1], F32)}
    env.push_pool()
    emit_global_mean_abs(env, "s2w1", dd["w1T"], D, HID, w1m, consts)
    emit_global_mean_abs(env, "s2w2", dd["w2T"], HID, D, w2m, consts)
    env.pop_pool()
    t1_hbm = nc.dram_tensor("s2_t1b", [D, HID], BF16).ap()
    t2_hbm = nc.dram_tensor("s2_t2b", [HID, D], BF16).ap()
    env.push_pool()
    emit_ternarize(env, "s2t1", dd["w1T"], t1_hbm, D, HID, w1m["inv"])
    emit_ternarize(env, "s2t2", dd["w2T"], t2_hbm, HID, D, w2m["inv"])
    env.pop_pool()
    wbc = [env.sb(f"s2_wbc{i}", [128, HID], BF16) for i in range(4)]
    env.push_pool()
    emit_bcast_rows(env, "s2cw", dd["convw"], 3, HID, wbc[0:3], consts)
    emit_bcast_rows(env, "s2cb", dd["convb"], 1, HID, wbc[3:4], consts)
    env.pop_pool()

    h_hbm = nc.dram_tensor("s2_h", [T, HID], BF16).ap()

    qs = env.sem("s2_qs")

    def selfsem(v, ins):
        qs.n += 1
        ins.then_inc(qs.h, 1)
        v.wait_ge(qs.h, qs.n)

    env.push_pool()
    t1_sb = env.sb("s2_t1sb", [128, DCH * HID], BF16)
    ldr = env.sem("s2_ldr")
    with nc.Block() as blk:
        @blk.sync
        def _(s):
            ldr.inc(s.dma_start(out=t1_sb[:, :],
                                in_=t1_hbm.rearrange("(ic p) o -> p ic o", p=128)), 16)
            s.wait_ge(ldr.h, ldr.n)

    x1tm = env.sb("s2_x1tm", [128, 2 * 4 * 1024], BF16)
    am4 = env.sb("s2_am4", [128, 2 * 4], F32)
    s4t = env.sb("s2_s4t", [128, 4], F32)
    u4t = env.sb("s2_u4t", [128, 4], F32)
    xqf = env.sb("s2_xqf", [128, 1024], F32)
    xqi = env.sb("s2_xqi", [128, 1024], I8)
    xqb = env.sb("s2_xqb", [128, 4 * 1024], BF16)
    xq_fm = env.sb("s2_xqfm", [128, DCH * NT], BF16)
    h_sb = env.sb("s2_hsb", [128, 2 * 4096], BF16)
    ps_tp = [env.ps(f"s2_pstp{i}", [128, 512], BF16) for i in range(2)]
    psM = [env.ps(f"s2_psM{i}", [128, 512], F32) for i in range(2)]

    ldx = [env.sem("s2_ldx0"), env.sem("s2_ldx1")]
    tpq = env.sem("s2_tpq")
    evq = env.sem("s2_evq")
    mmh = env.sem("s2_mmh")
    evh = env.sem("s2_evh")
    sth = env.sem("s2_sth")

    with nc.Block() as blk:
        @blk.sync
        def _(s):
            for j in range(4):
                ldx[0].inc(s.dma_start(out=x1tm[:, j * 1024:(j + 1) * 1024],
                                       in_=dd["x1_tm"][j * 128:(j + 1) * 128, :]), 16)
            ldx[0].inc(s.dma_start(out=am4[:, 0:4], in_=dd["amax1_c"][:, 0:4]), 16)

    for t in range(ntiles):
        slot = t % 2
        xoff = slot * 4096
        nslot = (t + 1) % 2

        with nc.Block() as blk:
            ld_t0 = ldx[slot].n

            @blk.sync
            def _(s):
                if t + 1 < ntiles:
                    for j in range(4):
                        ldx[nslot].inc(s.dma_start(
                            out=x1tm[:, nslot * 4096 + j * 1024: nslot * 4096 + (j + 1) * 1024],
                            in_=dd["x1_tm"][(t + 1) * NT + j * 128:(t + 1) * NT + (j + 1) * 128, :]), 16)
                    ldx[nslot].inc(s.dma_start(out=am4[:, nslot * 4:nslot * 4 + 4],
                                               in_=dd["amax1_c"][:, (t + 1) * 4:(t + 1) * 4 + 4]), 16)

            @blk.vector
            def _(v):
                v.wait_ge(ldx[slot].h, ld_t0)
                selfsem(v, v.tensor_scalar_max(s4t[:, :], am4[:, slot * 4:slot * 4 + 4], 1e-5))
                selfsem(v, v.tensor_scalar_mul(u4t[:, :], s4t[:, :], w1m["mean"][:, :]))
                v.tensor_scalar_mul(u4t[:, :], u4t[:, :], 1.0 / 127.0)
                selfsem(v, v.reciprocal(s4t[:, :], s4t[:, :]))
                selfsem(v, v.tensor_scalar_mul(s4t[:, :], s4t[:, :], 127.0))
                for j in range(4):
                    v.tensor_scalar_mul(xqf[:, :], x1tm[:, xoff + j * 1024: xoff + (j + 1) * 1024],
                                        s4t[:, j:j + 1])
                    v.tensor_copy(xqi[:, :], xqf[:, :])
                    v.tensor_copy(xqb[:, j * 1024:(j + 1) * 1024], xqi[:, :])

        with nc.Block() as blk:
            tp0, ev0 = tpq.n, evq.n

            @blk.tensor
            def _(te):
                for k in range(8):
                    j, g = k // 2, k % 2
                    if k >= 2:
                        te.wait_ge(evq.h, ev0 + k - 1)
                    for ii in range(4):
                        i = g * 4 + ii
                        ins = te.transpose(ps_tp[k % 2][:, ii * 128:(ii + 1) * 128],
                                           xqb[:, j * 1024 + i * 128: j * 1024 + (i + 1) * 128],
                                           identbf[:, :])
                    tpq.inc(ins)

            @blk.vector
            def _(v):
                for k in range(8):
                    j, g = k // 2, k % 2
                    v.wait_ge(tpq.h, tp0 + k + 1)
                    evq.inc(v.tensor_copy(
                        xq_fm.rearrange("p (i n) -> p i n", n=NT)[:, g * 4:(g + 1) * 4, j * 128:(j + 1) * 128],
                        ps_tp[k % 2].rearrange("p (i n) -> p i n", n=128)[:, :, :]))

        with nc.Block() as blk:
            mm0, eh0, st0 = mmh.n, evh.n, sth.n

            @blk.tensor
            def _(te):
                for j in range(4):
                    for o in range(8):
                        k = j * 8 + o
                        if k >= 2:
                            te.wait_ge(evh.h, eh0 + k - 1)
                        for i in range(DCH):
                            ins = te.matmul(psM[k % 2][:, :],
                                            xq_fm[:, i * NT + j * 128: i * NT + (j + 1) * 128],
                                            t1_sb[:, i * HID + o * 512: i * HID + (o + 1) * 512],
                                            start=(i == 0), stop=(i == DCH - 1))
                        mmh.inc(ins)

            @blk.vector
            def _(v):
                for j in range(4):
                    hoff = (j % 2) * 4096
                    if j >= 2:
                        v.wait_ge(sth.h, st0 + 16 * j)
                    for o in range(8):
                        k = j * 8 + o
                        v.wait_ge(mmh.h, mm0 + k + 1)
                        evh.inc(v.tensor_scalar_mul(h_sb[:, hoff + o * 512: hoff + (o + 1) * 512],
                                                    psM[k % 2][:, :], u4t[:, j:j + 1]))

            @blk.sync
            def _(s):
                for j in range(4):
                    hoff = (j % 2) * 4096
                    s.wait_ge(evh.h, eh0 + (j + 1) * 8)
                    sth.inc(s.dma_start(
                        out=h_hbm[t * NT + j * 128: t * NT + (j + 1) * 128, :],
                        in_=h_sb[:, hoff:hoff + 4096]), 16)

    with nc.Block() as blk:
        @blk.sync
        def _(s):
            s.wait_ge(sth.h, sth.n)
    env.pop_pool()

    env.push_pool()
    h0_sb = env.sb("s2_h0", [128, 4 * 4096], BF16)
    hm_sb = env.sb("s2_hm", [128, 4096], BF16)
    hp_sb = env.sb("s2_hp", [128, 4096], BF16)
    cv_sb = env.sb("s2_cv", [128, 4096], BF16)
    sil_sb = env.sb("s2_sil", [128, 4096], BF16)
    x1p = env.sb("s2_x1p", [128, 4 * 1024], BF16)
    a2t = env.sb("s2_a2t", [128, 4], F32)
    s2t = env.sb("s2_s2t", [128, 4], F32)
    u2t = env.sb("s2_u2t", [128, 4], F32)
    xq2f = env.sb("s2_xq2f", [128, 2048], F32)
    xq2i = env.sb("s2_xq2i", [128, 4096], I8)
    xq2b = env.sb("s2_xq2b", [128, 4096], BF16)
    xq2fm = env.sb("s2_xq2fm", [128, 4 * 4096], BF16)
    y2_sb = env.sb("s2_y2", [128, 4 * 1024], BF16)
    t2blk = [env.sb(f"s2_t2blk{i}", [128, 512], BF16) for i in range(2)]
    ps_tp2 = [env.ps(f"s2_pstp2{i}", [128, 512], BF16) for i in range(2)]
    ps4 = [env.ps(f"s2_ps4{i}", [128, 512], F32) for i in range(4)]
    v4 = env.sb("s2_v4", [128, 4 * 1024], F32)
    o2_sb = env.sb("s2_o2", [128, 4 * 1024], BF16)
    lnst = mk_ln_stats(env, "s2ln")

    ldh = env.sem("s2_ldh")
    shd = env.sem("s2_shd")
    cvs = env.sem("s2_cvs")
    cvg = env.sem("s2_cvg")
    sil = env.sem("s2_sil")
    tp2 = env.sem("s2_tp2")
    ev2 = env.sem("s2_ev2")
    ld2 = [env.sem("s2_ld2a"), env.sem("s2_ld2b")]
    mm2 = env.sem("s2_mm2")
    evy = env.sem("s2_evy")
    st2 = env.sem("s2_st2")

    for t in range(ntiles):
        with nc.Block() as blk:
            ld0 = ldh.n

            @blk.sync
            def _(s):
                for j in range(4):
                    r0 = t * NT + j * 128
                    ldh.inc(s.dma_start(out=h0_sb[:, j * 4096:(j + 1) * 4096],
                                        in_=h_hbm[r0:r0 + 128, :]), 16)
                    ldh.inc(s.dma_start(out=x1p[:, j * 1024:(j + 1) * 1024],
                                        in_=dd["x1_tm"][r0:r0 + 128, :]), 16)

        for j in range(4):
            r0 = t * NT + j * 128
            hj = j * 4096
            with nc.Block() as blk:
                sh0 = shd.n

                @blk.vector
                def _(v):
                    if r0 % S == 0:
                        v.memset(hm_sb[:, :], 0.0)
                    if (r0 + 128) % S == 0:
                        v.memset(hp_sb[:, :], 0.0)

            with nc.Block() as blk:
                @blk.sync
                def _(s):
                    if r0 % S == 0:
                        shd.inc(s.dma_start(out=hm_sb[1:128, :], in_=h_hbm[r0:r0 + 127, :]), 16)
                    else:
                        shd.inc(s.dma_start(out=hm_sb[:, :], in_=h_hbm[r0 - 1:r0 + 127, :]), 16)
                    if (r0 + 128) % S == 0:
                        shd.inc(s.dma_start(out=hp_sb[0:127, :], in_=h_hbm[r0 + 1:r0 + 128, :]), 16)
                    else:
                        shd.inc(s.dma_start(out=hp_sb[:, :], in_=h_hbm[r0 + 1:r0 + 129, :]), 16)
                    s.wait_ge(shd.h, shd.n)
                    s.wait_ge(ldh.h, ldh.n)

            with nc.Block() as blk:
                cv0, cg0 = cvs.n, cvg.n

                @blk.vector
                def _(v):
                    cvs.inc(v.tensor_mul(cv_sb[:, :], hm_sb[:, :], wbc[0][:, :]))
                    cvs.inc(v.tensor_add(cv_sb[:, :], cv_sb[:, :], wbc[3][:, :]))
                    v.wait_ge(cvg.h, cg0 + 1)
                    cvs.inc(v.tensor_add(cv_sb[:, :], cv_sb[:, :], hp_sb[:, :]))

                @blk.gpsimd
                def _(g):
                    g.wait_ge(cvs.h, cv0 + 1)
                    g.tensor_mul(hm_sb[:, :], hp_sb[:, :], wbc[2][:, :])
                    g.tensor_mul(hp_sb[:, :], h0_sb[:, hj:hj + 4096], wbc[1][:, :])
                    cvg.inc(g.tensor_add(hp_sb[:, :], hp_sb[:, :], hm_sb[:, :]))

            with nc.Block() as blk:
                @blk.scalar
                def _(sc):
                    sc.activation(sil_sb[:, :], cv_sb[:, :], AF.Silu)

            with nc.Block() as blk:
                @blk.vector
                def _(v):
                    v.reduce_max(a2t[:, j:j + 1], sil_sb[:, :], axis=AX.X,
                                 apply_absolute_value=True)

            with nc.Block() as blk:
                @blk.vector
                def _(v):
                    selfsem(v, v.tensor_scalar_max(s2t[:, j:j + 1], a2t[:, j:j + 1], 1e-5))
                    selfsem(v, v.tensor_scalar_mul(u2t[:, j:j + 1], s2t[:, j:j + 1], w2m["mean"][:, :]))
                    v.tensor_scalar_mul(u2t[:, j:j + 1], u2t[:, j:j + 1], 1.0 / 127.0)
                    selfsem(v, v.reciprocal(s2t[:, j:j + 1], s2t[:, j:j + 1]))
                    selfsem(v, v.tensor_scalar_mul(s2t[:, j:j + 1], s2t[:, j:j + 1], 127.0))
                    for hh in range(2):
                        v.tensor_scalar_mul(xq2f[:, :], sil_sb[:, hh * 2048:(hh + 1) * 2048],
                                            s2t[:, j:j + 1])
                        v.tensor_copy(xq2i[:, hh * 2048:(hh + 1) * 2048], xq2f[:, :])
                    v.tensor_copy(xq2b[:, :], xq2i[:, :])

            with nc.Block() as blk:
                tp0, ev0 = tp2.n, ev2.n

                @blk.tensor
                def _(te):
                    for g in range(8):
                        if g >= 2:
                            te.wait_ge(ev2.h, ev0 + g - 1)
                        for ii in range(4):
                            c = g * 4 + ii
                            ins = te.transpose(ps_tp2[g % 2][:, ii * 128:(ii + 1) * 128],
                                               xq2b[:, c * 128:(c + 1) * 128], identbf[:, :])
                        tp2.inc(ins)

                @blk.vector
                def _(v):
                    for g in range(8):
                        v.wait_ge(tp2.h, tp0 + g + 1)
                        ev2.inc(v.tensor_copy(xq2fm[:, hj + g * 512: hj + (g + 1) * 512],
                                              ps_tp2[g % 2][:, :]))

        for o in range(2):
            with nc.Block() as blk:
                mm0, ey0 = mm2.n, evy.n
                l20 = [ld2[0].n, ld2[1].n]

                @blk.gpsimd
                def _(g):
                    for c in range(HCH):
                        p = c % 2
                        if c >= 2:
                            g.wait_ge(mm2.h, mm0 + 4 * (c - 1))
                        ld2[p].inc(g.dma_start(out=t2blk[p][:, :],
                                               in_=t2_hbm[c * 128:(c + 1) * 128, o * 512:(o + 1) * 512]), 16)

                @blk.tensor
                def _(te):
                    for c in range(HCH):
                        p = c % 2
                        te.wait_ge(ld2[p].h, l20[p] + 16 * (c // 2 + 1))
                        for j in range(4):
                            mm2.inc(te.matmul(ps4[j][:, :],
                                              xq2fm[:, j * 4096 + c * 128: j * 4096 + (c + 1) * 128],
                                              t2blk[p][:, :],
                                              start=(c == 0), stop=(c == HCH - 1)))

                @blk.vector
                def _(v):
                    v.wait_ge(mm2.h, mm0 + 4 * HCH)
                    for j in range(4):
                        evy.inc(v.tensor_scalar_mul(
                            y2_sb[:, j * 1024 + o * 512: j * 1024 + (o + 1) * 512],
                            ps4[j][:, :], u2t[:, j:j + 1]))

        with nc.Block() as blk:
            @blk.vector
            def _(v):
                for j in range(4):
                    v.tensor_add(v4[:, j * 1024:(j + 1) * 1024],
                                 x1p[:, j * 1024:(j + 1) * 1024],
                                 y2_sb[:, j * 1024:(j + 1) * 1024])

        emit_ln_tail(env, consts, v4, o2_sb, 4, lnst)
        with nc.Block() as blk:
            @blk.sync
            def _(s):
                for j in range(4):
                    st2.inc(s.dma_start(
                        out=dd["x2_tm"][t * NT + j * 128: t * NT + (j + 1) * 128, :],
                        in_=o2_sb[:, j * 1024:(j + 1) * 1024]), 16)
                s.wait_ge(st2.h, st2.n)
    env.pop_pool()


def emit_s3(env, dd, T, consts, L=4096):
    nc = env.nc
    nb = T // L
    LC = L // 128
    LT = L // 512
    identbf = consts["identbf"]
    identf = consts["identf"]
    EPS = L1_ATTN_EPS
    NH = 16
    NP = NH // 2

    ctx_hbm = nc.dram_tensor("s3_ctx", [T, D], BF16).ap()
    scr_hbm = nc.dram_tensor("s3_scr", [1, 8], F32).ap()

    qs = env.sem("s3_qs")

    def selfsem(v, ins):
        qs.n += 1
        ins.then_inc(qs.h, 1)
        v.wait_ge(qs.h, qs.n)

    env.push_pool()
    wq_sb = env.sb("s3_wq", [128, DCH * D], BF16)
    wk_sb = env.sb("s3_wk", [128, DCH * D], BF16)
    wv_sb = env.sb("s3_wv", [128, DCH * D], BF16)
    w0_sb = env.sb("s3_w0", [128, 512], BF16)
    bq_sb = env.sb("s3_bq", [128, 8], F32)
    bk_sb = env.sb("s3_bk", [128, 8], F32)
    bv_sb = env.sb("s3_bv", [128, 8], F32)
    ldw0 = env.sem("s3_ldw0")
    with nc.Block() as blk:
        @blk.gpsimd
        def _(g):
            for sb, key in ((wq_sb, "wqT"), (wk_sb, "wkT"), (wv_sb, "wvT")):
                ldw0.inc(g.dma_start(out=sb[:, :],
                                     in_=dd[key].rearrange("(ic p) o -> p ic o", p=128)), 16)
            ldw0.inc(g.dma_start(out=w0_sb[:, :], in_=dd["w0l"][:, :]), 16)

        @blk.sync
        def _(s):
            for sb, key in ((bq_sb, "bq"), (bk_sb, "bk"), (bv_sb, "bv")):
                ldw0.inc(s.dma_start(out=sb[:, :], in_=dd[key][:, :]), 16)
            s.wait_ge(ldw0.h, ldw0.n)

    x2fm = env.sb("s3_x2fm", [128, DCH * L], BF16)
    x2t = env.sb("s3_x2t", [128, 1024], BF16)
    Qp = env.sb("s3_Qp", [128, L], BF16)
    Kp = env.sb("s3_Kp", [128, L], BF16)
    Vf = env.sb("s3_Vf", [128, L], BF16)
    Vaug = env.sb("s3_Vaug", [128, LC * 130], BF16)
    Ktm = env.sb("s3_Ktm", [128, LC * 128], BF16)
    ksq = env.sb("s3_ksq", [128, 128], F32)
    ssqp = env.sb("s3_ssqp", [128, 2 * LC], F32)
    ssqb = env.sb("s3_ssqb", [128, 2 * LC], F32)
    mxk = env.sb("s3_mxk", [128, 2 * LC], F32)
    mx2 = env.sb("s3_mx2", [128, 2], F32)
    mrow2 = env.sb("s3_mrow2", [2, 128], F32)
    mb21 = env.sb("s3_mb21", [2, 1], F32)
    mr12 = env.sb("s3_mr12", [1, 2], F32)
    mh2 = env.sb("s3_mh2", [128, 2], F32)
    e_pair = env.sb("s3_ep", [128, 2 * LC * 64], BF16)
    phih = env.sb("s3_phih", [64, L], BF16)
    nd_sb = env.sb("s3_nd", [65, 64], F32)
    ndT = env.sb("s3_ndT", [64, 66], BF16)
    qlogt = env.sb("s3_qlogt", [128, 2048], F32)
    mxq = env.sb("s3_mxq", [128, 32], F32)
    mxqn = env.sb("s3_mxqn", [128, 32], F32)
    phitm = env.sb("s3_phitm", [128, 2048], BF16)
    znd = env.sb("s3_znd", [65, 2], F32)
    ez_bc = env.sb("s3_ez", [128, 2], F32)
    ez1 = env.sb("s3_ez1", [128, 1], F32)
    num_sb = env.sb("s3_num", [128, LC * 66], F32)
    dent = env.sb("s3_dent", [128, LC], F32)
    slt = env.sb("s3_slt", [128, LC], F32)
    rlt = env.sb("s3_rlt", [128, LC], F32)
    strip = env.sb("s3_strip", [128, LC * 128], BF16)

    ps_mm = [env.ps(f"s3_psmm{i}", [128, 512], F32) for i in range(2)]
    ps_tp = [env.ps(f"s3_pstp{i}", [128, 512], BF16) for i in range(2)]
    ps_small = [env.ps(f"s3_pssm{i}", [128, 512], F32) for i in range(2)]
    ps_kl = [ps_small[0][:, 0:64], ps_small[1][:, 0:64]]
    ps_num = [ps_small[0][:, 64:130], ps_small[1][:, 64:130]]
    ps_nd = ps_small[0][0:65, 192:256]
    ps_ndT = ps_small[1][0:64, 192:257]
    ps_ph = [ps_mm[0][0:64, :], ps_mm[1][0:64, :]]

    ldx = env.sem("s3_ldx")
    tpx = env.sem("s3_tpx")
    evx = env.sem("s3_evx")
    mmp = env.sem("s3_mmp")
    evp = env.sem("s3_evp")
    tpv = env.sem("s3_tpv")
    evv = env.sem("s3_evv")
    tpk = env.sem("s3_tpk")
    evk = env.sem("s3_evk")
    ack = env.sem("s3_ack")
    rdk = env.sem("s3_rdk")
    mmk = env.sem("s3_mmk")
    ace = env.sem("s3_ace")
    mmn = env.sem("s3_mmn")
    evn = env.sem("s3_evn")
    tpn = env.sem("s3_tpn")
    mmf = env.sem("s3_mmf")
    acf = env.sem("s3_acf")
    mmu = env.sem("s3_mmu")
    evu = env.sem("s3_evu")
    stc = env.sem("s3_stc")
    zst = env.sem("s3_zst")

    for b in range(nb):
        for lt in range(LT):
            with nc.Block() as blk:
                lx0, tp0, ev0 = ldx.n, tpx.n, evx.n

                @blk.sync
                def _(s):
                    for j in range(4):
                        if j > 0:
                            s.wait_ge(tpx.h, tp0 + 8 * j)
                        ldx.inc(s.dma_start(out=x2t[:, :],
                                            in_=dd["x2_tm"][b * L + lt * 512 + j * 128:
                                                            b * L + lt * 512 + (j + 1) * 128, :]), 16)
                        s.wait_ge(ldx.h, ldx.n)

                @blk.tensor
                def _(te):
                    for j in range(4):
                        te.wait_ge(ldx.h, lx0 + 16 * (j + 1))
                        for i in range(DCH):
                            k = j * DCH + i
                            if k >= 2:
                                te.wait_ge(evx.h, ev0 + k - 1)
                            tpx.inc(te.transpose(ps_tp[k % 2][:, 0:128],
                                                 x2t[:, i * 128:(i + 1) * 128], identbf[:, :]))

                @blk.vector
                def _(v):
                    for j in range(4):
                        for i in range(DCH):
                            k = j * DCH + i
                            v.wait_ge(tpx.h, tp0 + k + 1)
                            evx.inc(v.tensor_copy(
                                x2fm[:, i * L + lt * 512 + j * 128: i * L + lt * 512 + (j + 1) * 128],
                                ps_tp[k % 2][:, 0:128]))

        for hp in range(NP):
            po = hp * 128

            with nc.Block() as blk:
                mm0, ev0 = mmp.n, evp.n

                @blk.tensor
                def _(te):
                    for dst, wsb, _bias in ((Qp, wq_sb, bq_sb), (Kp, wk_sb, bk_sb), (Vf, wv_sb, bv_sb)):
                        for lt in range(LT):
                            k = mmp.n - mm0
                            if k >= 2:
                                te.wait_ge(evp.h, ev0 + k - 1)
                            for i in range(DCH):
                                ins = te.matmul(ps_mm[k % 2][:, :],
                                                wsb[:, i * D + po: i * D + po + 128],
                                                x2fm[:, i * L + lt * 512: i * L + (lt + 1) * 512],
                                                start=(i == 0), stop=(i == DCH - 1))
                            mmp.inc(ins)

                @blk.vector
                def _(v):
                    for di, (dst, wsb, bias) in enumerate(
                            ((Qp, wq_sb, bq_sb), (Kp, wk_sb, bk_sb), (Vf, wv_sb, bv_sb))):
                        for lt in range(LT):
                            k = di * LT + lt
                            v.wait_ge(mmp.h, mm0 + k + 1)
                            evp.inc(v.tensor_scalar_add(dst[:, lt * 512:(lt + 1) * 512],
                                                        ps_mm[k % 2][:, :], bias[:, hp:hp + 1]))

            with nc.Block() as blk:
                tp0, ev0 = tpv.n, evv.n

                @blk.vector
                def _(v):
                    v.memset(Vaug[:, :], 1.0)

                @blk.tensor
                def _(te):
                    for lc in range(LC):
                        if lc >= 2:
                            te.wait_ge(evv.h, ev0 + lc - 1)
                        tpv.inc(te.transpose(ps_tp[lc % 2][:, 0:128],
                                             Vf[:, lc * 128:(lc + 1) * 128], identbf[:, :]))

                @blk.vector
                def _(v):
                    for lc in range(LC):
                        v.wait_ge(tpv.h, tp0 + lc + 1)
                        v.tensor_copy(Vaug[:, lc * 130: lc * 130 + 64],
                                      ps_tp[lc % 2][:, 0:64])
                        evv.inc(v.tensor_copy(Vaug[:, lc * 130 + 65: lc * 130 + 129],
                                              ps_tp[lc % 2][:, 64:128]))

            with nc.Block() as blk:
                tp0, ev0 = tpk.n, evk.n

                @blk.tensor
                def _(te):
                    for lc in range(LC):
                        if lc >= 2:
                            te.wait_ge(evk.h, ev0 + lc - 1)
                        tpk.inc(te.transpose(ps_tp[lc % 2][:, 0:128],
                                             Kp[:, lc * 128:(lc + 1) * 128], identbf[:, :]))

                @blk.vector
                def _(v):
                    for lc in range(LC):
                        v.wait_ge(tpk.h, tp0 + lc + 1)
                        evk.inc(v.tensor_copy(Ktm[:, lc * 128:(lc + 1) * 128],
                                              ps_tp[lc % 2][:, 0:128]))

            with nc.Block() as blk:
                ac0, rd0 = ack.n, rdk.n

                @blk.scalar
                def _(sc):
                    for lc in range(LC):
                        if lc >= 1:
                            sc.wait_ge(rdk.h, rd0 + lc)
                        ack.inc(sc.activation(ksq[:, :], Ktm[:, lc * 128:(lc + 1) * 128],
                                              AF.Square))

                @blk.vector
                def _(v):
                    for lc in range(LC):
                        v.wait_ge(ack.h, ac0 + lc + 1)
                        v.reduce_sum(ssqp[:, lc:lc + 1], ksq[:, 0:64], axis=AX.X)
                        rdk.inc(v.reduce_sum(ssqp[:, LC + lc:LC + lc + 1],
                                             ksq[:, 64:128], axis=AX.X))

            with nc.Block() as blk:
                @blk.vector
                def _(v):
                    v.tensor_scalar_mul(ssqp[:, :], ssqp[:, :], -0.5)

            with nc.Block() as blk:
                mm0, ae0 = mmk.n, ace.n

                @blk.tensor
                def _(te):
                    for h2 in range(2):
                        for lc in range(LC):
                            k = h2 * LC + lc
                            if k >= 2:
                                te.wait_ge(ace.h, ae0 + k - 1)
                            mmk.inc(te.matmul(ps_kl[k % 2],
                                              Kp[h2 * 64:(h2 + 1) * 64, lc * 128:(lc + 1) * 128],
                                              w0_sb[h2 * 64:(h2 + 1) * 64, hp * 64:(hp + 1) * 64],
                                              start=True, stop=True))

                @blk.vector
                def _(v):
                    for h2 in range(2):
                        for lc in range(LC):
                            k = h2 * LC + lc
                            v.wait_ge(mmk.h, mm0 + k + 1)
                            ace.inc(v.reduce_max(mxk[:, k:k + 1], ps_kl[k % 2], axis=AX.X))

            with nc.Block() as blk:
                @blk.vector
                def _(v):
                    v.tensor_add(mxk[:, :], mxk[:, :], ssqp[:, :])

            with nc.Block() as blk:
                @blk.vector
                def _(v):
                    v.reduce_max(mx2[:, 0:1], mxk[:, 0:LC], axis=AX.X)
                    v.reduce_max(mx2[:, 1:2], mxk[:, LC:2 * LC], axis=AX.X)

            with nc.Block() as blk:
                @blk.tensor
                def _(te):
                    te.transpose(ps_small[1][0:2, 256:384], mx2[:, :], identf[:, :])

            with nc.Block() as blk:
                @blk.vector
                def _(v):
                    v.tensor_copy(mrow2[0:2, :], ps_small[1][0:2, 256:384])

            with nc.Block() as blk:
                @blk.vector
                def _(v):
                    v.reduce_max(mb21[0:2, :], mrow2[0:2, :], axis=AX.X)

            with nc.Block() as blk:
                @blk.tensor
                def _(te):
                    te.transpose(ps_small[1][0:1, 384:386], mb21[0:2, :], identf[0:2, 0:2])

            with nc.Block() as blk:
                @blk.vector
                def _(v):
                    v.tensor_copy(mr12[:, :], ps_small[1][0:1, 384:386])

            with nc.Block() as blk:
                @blk.sync
                def _(s):
                    zst.inc(s.dma_start(out=scr_hbm[0:1, 2:4], in_=mr12[:, :]), 16)
                    s.wait_ge(zst.h, zst.n)
                    zst.inc(s.dma_start(out=mh2[:, 0:1],
                                        in_=scr_hbm[0:1, 2:3].broadcast_to((128, 1))), 16)
                    zst.inc(s.dma_start(out=mh2[:, 1:2],
                                        in_=scr_hbm[0:1, 3:4].broadcast_to((128, 1))), 16)
                    s.wait_ge(zst.h, zst.n)

            with nc.Block() as blk:
                @blk.vector
                def _(v):
                    for h2 in range(2):
                        v.tensor_single_scalar(ssqb[:, h2 * LC:(h2 + 1) * LC],
                                               ssqp[:, h2 * LC:(h2 + 1) * LC],
                                               mh2[:, h2:h2 + 1], ALU.subtract)

            with nc.Block() as blk:
                mm0, ae0 = mmk.n, ace.n

                @blk.tensor
                def _(te):
                    for h2 in range(2):
                        for lc in range(LC):
                            k = h2 * LC + lc
                            if k >= 2:
                                te.wait_ge(ace.h, ae0 + k - 1)
                            mmk.inc(te.matmul(ps_kl[k % 2],
                                              Kp[h2 * 64:(h2 + 1) * 64, lc * 128:(lc + 1) * 128],
                                              w0_sb[h2 * 64:(h2 + 1) * 64, hp * 64:(hp + 1) * 64],
                                              start=True, stop=True))

                @blk.scalar
                def _(sc):
                    for h2 in range(2):
                        for lc in range(LC):
                            k = h2 * LC + lc
                            sc.wait_ge(mmk.h, mm0 + k + 1)
                            ace.inc(sc.activation(
                                e_pair[:, h2 * LC * 64 + lc * 64: h2 * LC * 64 + (lc + 1) * 64],
                                ps_kl[k % 2], AF.Exp,
                                bias=ssqb[:, h2 * LC + lc: h2 * LC + lc + 1],
                                scale=1.0))

            for h2 in range(2):
                with nc.Block() as blk:
                    @blk.tensor
                    def _(te):
                        for lc in range(LC):
                            ins = te.matmul(ps_nd,
                                            Vaug[:, lc * 130 + h2 * 65: lc * 130 + (h2 + 1) * 65],
                                            e_pair[:, h2 * LC * 64 + lc * 64: h2 * LC * 64 + (lc + 1) * 64],
                                            start=(lc == 0), stop=(lc == LC - 1))
                        mmn.inc(ins)

                with nc.Block() as blk:
                    @blk.vector
                    def _(v):
                        v.tensor_copy(nd_sb[:, :], ps_nd)

                with nc.Block() as blk:
                    @blk.vector
                    def _(v):
                        v.reduce_sum(znd[64:65, h2:h2 + 1], nd_sb[64:65, :], axis=AX.X)

                with nc.Block() as blk:
                    @blk.tensor
                    def _(te):
                        te.transpose(ps_ndT, nd_sb[:, :], identf[0:65, 0:65])

                with nc.Block() as blk:
                    @blk.vector
                    def _(v):
                        v.memset(ndT[:, 65:66], 1.0)
                        v.tensor_copy(ndT[:, 0:65], ps_ndT)

                with nc.Block() as blk:
                    mm0, cq0 = mmf.n, acf.n

                    @blk.tensor
                    def _(te):
                        for lc in range(LC):
                            if lc >= 2:
                                te.wait_ge(acf.h, cq0 + lc - 1)
                            mmf.inc(te.matmul(ps_kl[lc % 2],
                                              Qp[h2 * 64:(h2 + 1) * 64, lc * 128:(lc + 1) * 128],
                                              w0_sb[h2 * 64:(h2 + 1) * 64, hp * 64:(hp + 1) * 64],
                                              start=True, stop=True))

                    @blk.vector
                    def _(v):
                        for lc in range(LC):
                            v.wait_ge(mmf.h, mm0 + lc + 1)
                            v.reduce_max(mxq[:, lc:lc + 1], ps_kl[lc % 2], axis=AX.X)
                            acf.inc(v.tensor_copy(qlogt[:, lc * 64:(lc + 1) * 64],
                                                  ps_kl[lc % 2]))

                with nc.Block() as blk:
                    @blk.vector
                    def _(v):
                        v.tensor_scalar_mul(mxqn[:, :], mxq[:, :], -1.0)

                with nc.Block() as blk:
                    @blk.scalar
                    def _(sc):
                        for lc in range(LC):
                            sc.activation(phitm[:, lc * 64:(lc + 1) * 64],
                                          qlogt[:, lc * 64:(lc + 1) * 64], AF.Exp,
                                          bias=mxqn[:, lc:lc + 1], scale=1.0)

                with nc.Block() as blk:
                    tp0, ev0 = tpn.n, mmn.n

                    @blk.tensor
                    def _(te):
                        for lc in range(LC):
                            if lc >= 2:
                                te.wait_ge(mmn.h, ev0 + lc - 1)
                            tpn.inc(te.transpose(ps_tp[lc % 2][0:64, 0:128],
                                                 phitm[:, lc * 64:(lc + 1) * 64],
                                                 identbf[:, :]))

                    @blk.vector
                    def _(v):
                        for lc in range(LC):
                            v.wait_ge(tpn.h, tp0 + lc + 1)
                            mmn.inc(v.tensor_copy(phih[:, lc * 128:(lc + 1) * 128],
                                                  ps_tp[lc % 2][0:64, 0:128]))

                with nc.Block() as blk:
                    mm0, ev0 = mmu.n, evu.n

                    @blk.tensor
                    def _(te):
                        for lc in range(LC):
                            if lc >= 2:
                                te.wait_ge(evu.h, ev0 + lc - 1)
                            mmu.inc(te.matmul(ps_num[lc % 2],
                                              phih[:, lc * 128:(lc + 1) * 128],
                                              ndT[:, :], start=True, stop=True))

                    @blk.vector
                    def _(v):
                        for lc in range(LC):
                            v.wait_ge(mmu.h, mm0 + lc + 1)
                            evu.inc(v.tensor_copy(num_sb[:, lc * 66:(lc + 1) * 66],
                                                  ps_num[lc % 2]))

                with nc.Block() as blk:
                    @blk.sync
                    def _(s):
                        zst.inc(s.dma_start(out=scr_hbm[0:1, h2:h2 + 1],
                                            in_=znd[64:65, h2:h2 + 1]), 16)
                        s.wait_ge(zst.h, zst.n)
                        zst.inc(s.dma_start(out=ez_bc[:, h2:h2 + 1],
                                            in_=scr_hbm[0:1, h2:h2 + 1].broadcast_to((128, 1))), 16)
                        s.wait_ge(zst.h, zst.n)

                with nc.Block() as blk:
                    @blk.vector
                    def _(v):
                        v.tensor_copy(dent.rearrange("p (c o) -> p c o", o=1)[:, :, :],
                                      num_sb.rearrange("p (c k) -> p c k", k=66)[:, :, 64:65])
                        v.tensor_copy(slt.rearrange("p (c o) -> p c o", o=1)[:, :, :],
                                      num_sb.rearrange("p (c k) -> p c k", k=66)[:, :, 65:66])

                with nc.Block() as blk:
                    @blk.vector
                    def _(v):
                        selfsem(v, v.tensor_scalar_mul(ez1[:, :], ez_bc[:, h2:h2 + 1], EPS))
                        selfsem(v, v.scalar_tensor_tensor(dent[:, :], slt[:, :], ez1[:, :],
                                                          dent[:, :], ALU.mult, ALU.add))
                        selfsem(v, v.reciprocal(rlt[:, :], dent[:, :]))
                        for lc in range(LC):
                            v.tensor_scalar_mul(
                                strip[:, lc * 128 + h2 * 64: lc * 128 + h2 * 64 + 64],
                                num_sb[:, lc * 66: lc * 66 + 64], rlt[:, lc:lc + 1])

            with nc.Block() as blk:
                @blk.sync
                def _(s):
                    stc.inc(s.dma_start(
                        out=ctx_hbm[b * L:(b + 1) * L, po:po + 128]
                            .rearrange("(c p) f -> p c f", p=128),
                        in_=strip.rearrange("p (c f) -> p c f", f=128)), 16)
                    s.wait_ge(stc.h, stc.n)
    env.pop_pool()

    env.push_pool()
    wo_sb = env.sb("s3_wo", [128, DCH * D], BF16)
    bo_bc = env.sb("s3_bobc", [128, D], BF16)
    ldo = env.sem("s3_ldo")
    with nc.Block() as blk:
        @blk.gpsimd
        def _(g):
            ldo.inc(g.dma_start(out=wo_sb[:, :],
                                in_=dd["woT"].rearrange("(ic p) o -> p ic o", p=128)), 16)

        @blk.sync
        def _(s):
            s.wait_ge(ldo.h, ldo.n)
    emit_bcast_rows(env, "s3bo", dd["bo_row"], 1, D, [bo_bc], consts)

    ctxt = env.sb("s3_ctxt", [128, 4 * 1024], BF16)
    x2r = env.sb("s3_x2r", [128, 4 * 1024], BF16)
    cfm = env.sb("s3_cfm", [128, DCH * 128], BF16)
    y3 = env.sb("s3_y3", [128, 1024], BF16)
    v4 = env.sb("s3_v4", [128, 4 * 1024], F32)
    o3 = env.sb("s3_o3", [128, 4 * 1024], BF16)
    lnst3 = mk_ln_stats(env, "s3ln")
    ps_tp3 = [env.ps(f"s3_pstp3{i}", [128, 512], BF16) for i in range(2)]
    ps_mm3 = [env.ps(f"s3_psmm3{i}", [128, 512], F32) for i in range(2)]

    ldc = env.sem("s3_ldc")
    tpc = env.sem("s3_tpc")
    evc = env.sem("s3_evc")
    mmo = env.sem("s3_mmo")
    evo = env.sem("s3_evo")
    st3 = env.sem("s3_st3")

    for t in range(T // 512):
        with nc.Block() as blk:
            ld0 = ldc.n

            @blk.sync
            def _(s):
                for j in range(4):
                    r0 = t * 512 + j * 128
                    ldc.inc(s.dma_start(out=ctxt[:, j * 1024:(j + 1) * 1024],
                                        in_=ctx_hbm[r0:r0 + 128, :]), 16)
                    ldc.inc(s.dma_start(out=x2r[:, j * 1024:(j + 1) * 1024],
                                        in_=dd["x2_tm"][r0:r0 + 128, :]), 16)

        for j in range(4):
            with nc.Block() as blk:
                tp0, ev0 = tpc.n, evc.n

                @blk.tensor
                def _(te):
                    te.wait_ge(ldc.h, ldc.n)
                    for g in range(2):
                        if g >= 2:
                            te.wait_ge(evc.h, ev0 + g - 1)
                        for ii in range(4):
                            i = g * 4 + ii
                            ins = te.transpose(ps_tp3[g % 2][:, ii * 128:(ii + 1) * 128],
                                               ctxt[:, j * 1024 + i * 128: j * 1024 + (i + 1) * 128],
                                               identbf[:, :])
                        tpc.inc(ins)

                @blk.vector
                def _(v):
                    for g in range(2):
                        v.wait_ge(tpc.h, tp0 + g + 1)
                        evc.inc(v.tensor_copy(
                            cfm.rearrange("p (i n) -> p i n", n=128)[:, g * 4:(g + 1) * 4, :],
                            ps_tp3[g % 2].rearrange("p (i n) -> p i n", n=128)[:, :, :]))

            with nc.Block() as blk:
                mm0, eo0 = mmo.n, evo.n

                @blk.tensor
                def _(te):
                    for o in range(2):
                        for c in range(DCH):
                            ins = te.matmul(ps_mm3[o % 2][:, :],
                                            cfm[:, c * 128:(c + 1) * 128],
                                            wo_sb[:, c * D + o * 512: c * D + (o + 1) * 512],
                                            start=(c == 0), stop=(c == DCH - 1))
                        mmo.inc(ins)

                @blk.vector
                def _(v):
                    for o in range(2):
                        v.wait_ge(mmo.h, mm0 + o + 1)
                        evo.inc(v.scalar_tensor_tensor(y3[:, o * 512:(o + 1) * 512],
                                                       ps_mm3[o % 2][:, :], 1.0,
                                                       bo_bc[:, o * 512:(o + 1) * 512],
                                                       ALU.mult, ALU.add))
                    v.tensor_add(v4[:, j * 1024:(j + 1) * 1024],
                                 x2r[:, j * 1024:(j + 1) * 1024], y3[:, :])

        emit_ln_tail(env, consts, v4, o3, 4, lnst3)
        with nc.Block() as blk:
            @blk.sync
            def _(s):
                for j in range(4):
                    st3.inc(s.dma_start(
                        out=dd["x3_tm"][t * 512 + j * 128: t * 512 + (j + 1) * 128, :],
                        in_=o3[:, j * 1024:(j + 1) * 1024]), 16)
                s.wait_ge(st3.h, st3.n)
    env.pop_pool()


def emit_s1(env, dd, T, consts, S=4096):
    nc = env.nc
    nb = T // S
    RB = S // 16
    NSB = RB // 128
    NKT = S // 128
    HALF = S // 2
    NQC = HALF // 128
    identbf = consts["identbf"]
    identf = consts["identf"]

    y_hbm = nc.dram_tensor("s1_y", [T, D], BF16).ap()

    qs = env.sem("s1_qs")

    def selfsem(v, ins):
        qs.n += 1
        ins.then_inc(qs.h, 1)
        v.wait_ge(qs.h, qs.n)

    env.push_pool()
    wq_sb = env.sb("s1_wq", [128, DCH * D], BF16)
    wk_sb = env.sb("s1_wk", [128, DCH * D], BF16)
    wv_sb = env.sb("s1_wv", [128, DCH * D], BF16)
    ldw = env.sem("s1_ldw")
    with nc.Block() as blk:
        @blk.gpsimd
        def _(g):
            for sb, key in ((wq_sb, "wqT"), (wk_sb, "wkT"), (wv_sb, "wvT")):
                ldw.inc(g.dma_start(out=sb[:, :],
                                    in_=dd[key].rearrange("(ic p) o -> p ic o", p=128)), 16)

        @blk.sync
        def _(s):
            s.wait_ge(ldw.h, ldw.n)

    xf_sb = env.sb("s1_xf", [128, DCH * RB], F32)
    xb_sb = env.sb("s1_xb", [128, DCH * RB], BF16)
    xtm_sb = env.sb("s1_xtm", [128, NSB * 1024], F32)
    Yq = env.sb("s1_Yq", [128, DCH * RB], BF16)
    Yk = env.sb("s1_Yk", [128, DCH * RB], BF16)
    Yqa = env.sb("s1_Yqa", [128, DCH * RB], BF16)
    Vaug = env.sb("s1_Vaug", [128, NKT * 65], BF16)
    tmsq = env.sb("s1_tmsq", [128, 128], F32)
    ssqall = env.sb("s1_ssqall", [128, 4 * DCH * NSB], F32)
    mx1 = env.sb("s1_mx1", [128, 2], F32)
    s2r = env.sb("s1_s2r", [1, 256], F32)
    m11 = env.sb("s1_m11", [1, 4], F32)
    cb1 = env.sb("s1_cb1", [1, 1], BF16)
    cbias = env.sb("s1_cbias", [128, 1], F32)
    e_sb = [env.sb(f"s1_e{i}", [128, 2048], BF16) for i in range(2)]
    e1_sb = env.sb("s1_ea", [128, 2048], F32)
    e2_sb = env.sb("s1_eb", [128, 2048], F32)
    pv_sb = env.sb("s1_pv", [65, 2048], F32)
    rl1 = env.sb("s1_rl1", [128, 1], F32)
    ytm = env.sb("s1_ytm", [128, NSB * 1024], BF16)

    ldx = env.sem("s1_ldx")
    mmp = env.sem("s1_mmp")
    evp = env.sem("s1_evp")
    alt = env.sem("s1_alt")
    tpb = env.sem("s1_tpb")
    acb = env.sem("s1_acb")
    rdb = env.sem("s1_rdb")
    tsc = env.sem("s1_tsc")
    aex = env.sem("s1_aex")
    psq = env.sem("s1_psq")
    vsq = env.sem("s1_vsq")
    tpv2 = env.sem("s1_tpv")
    tpo = env.sem("s1_tpo")
    evo = env.sem("s1_evo")
    sty = env.sem("s1_sty")

    for inst in range(nb * 16):
        b, g = inst // 16, inst % 16
        c0 = b * S + g * RB

        with nc.Block() as blk:
            lx0 = ldx.n

            @blk.sync
            def _(s):
                ldx.inc(s.dma_start(
                    out=xf_sb.rearrange("p (i r) -> p i r", r=RB),
                    in_=dd["x_fm"].rearrange("(i p) t -> p i t", p=128)[:, :, c0:c0 + RB]), 16)
                for jj in range(NSB):
                    ldx.inc(s.dma_start(out=xtm_sb[:, jj * 1024:(jj + 1) * 1024],
                                        in_=dd["x_tm"][c0 + jj * 128: c0 + (jj + 1) * 128, :]), 16)
                s.wait_ge(ldx.h, ldx.n)

            @blk.vector
            def _(v):
                v.wait_ge(ldx.h, lx0 + 16 * (NSB + 1))
                v.tensor_copy(xb_sb[:, :], xf_sb[:, :])

        env.push_pool()
        ps_a = [env.ps(f"s1_psa{i}_{inst}", [128, 512], F32) for i in range(2)]
        ps_tpb = [env.ps(f"s1_pstpb{i}_{inst}", [128, 128], BF16) for i in range(2)]
        with nc.Block() as blk:
            mm0, ev0 = mmp.n, evp.n

            @blk.tensor
            def _(te):
                k = 0
                for dst, wsb in ((Yq, wq_sb), (Yk, wk_sb)):
                    for o in range(DCH):
                        if k >= 2:
                            te.wait_ge(evp.h, ev0 + k - 1)
                        for i in range(DCH):
                            ins = te.matmul(ps_a[k % 2][:, 0:RB],
                                            wsb[:, i * D + o * 128: i * D + (o + 1) * 128],
                                            xb_sb[:, i * RB:(i + 1) * RB],
                                            start=(i == 0), stop=(i == DCH - 1))
                        mmp.inc(ins)
                        k += 1
                for jj in range(NSB):
                    for o in range(2):
                        if k >= 2:
                            te.wait_ge(evp.h, ev0 + k - 1)
                        for i in range(DCH):
                            ins = te.matmul(ps_a[k % 2][:, :],
                                            xb_sb[:, i * RB + jj * 128: i * RB + (jj + 1) * 128],
                                            wv_sb[:, i * D + o * 512: i * D + (o + 1) * 512],
                                            start=(i == 0), stop=(i == DCH - 1))
                        mmp.inc(ins)
                        k += 1

            @blk.vector
            def _(v):
                v.memset(Vaug[:, :], 1.0)
                k = 0
                for dst, wsb in ((Yq, wq_sb), (Yk, wk_sb)):
                    for o in range(DCH):
                        v.wait_ge(mmp.h, mm0 + k + 1)
                        if dst is Yq:
                            evp.inc(v.tensor_scalar_mul(dst[:, o * RB:(o + 1) * RB],
                                                        ps_a[k % 2][:, 0:RB], 0.25))
                        else:
                            evp.inc(v.tensor_copy(dst[:, o * RB:(o + 1) * RB], ps_a[k % 2][:, 0:RB]))
                        k += 1
                for jj in range(NSB):
                    for o in range(2):
                        v.wait_ge(mmp.h, mm0 + k + 1)
                        for uu in range(8):
                            ku = o * 8 + uu
                            kt = ku * NSB + jj
                            ins = v.tensor_copy(Vaug[:, kt * 65: kt * 65 + 64],
                                                ps_a[k % 2][:, uu * 64:(uu + 1) * 64])
                        evp.inc(ins)
                        k += 1

        with nc.Block() as blk:
            @blk.sync
            def _(s):
                alt.inc(s.dma_start(out=Yqa[0:64, :], in_=Yq[64:128, :]), 16)
                alt.inc(s.dma_start(out=Yqa[64:128, :], in_=Yq[0:64, :]), 16)
                s.wait_ge(alt.h, alt.n)

        ntp = DCH * NSB
        with nc.Block() as blk:
            tp0, ac0, rd0 = tpb.n, acb.n, rdb.n

            @blk.tensor
            def _(te):
                for src_i, src in enumerate((Yq, Yk)):
                    for jc in range(ntp):
                        k = src_i * ntp + jc
                        if k >= 2:
                            te.wait_ge(acb.h, ac0 + k - 1)
                        tpb.inc(te.transpose(ps_tpb[k % 2][:, :],
                                             src[:, jc * 128:(jc + 1) * 128], identbf[:, :]))

            @blk.scalar
            def _(sc):
                for k in range(2 * ntp):
                    sc.wait_ge(tpb.h, tp0 + k + 1)
                    if k >= 1:
                        sc.wait_ge(rdb.h, rd0 + k)
                    acb.inc(sc.activation(tmsq[:, :], ps_tpb[k % 2][:, :], AF.Square))

            @blk.vector
            def _(v):
                for k in range(2 * ntp):
                    v.wait_ge(acb.h, ac0 + k + 1)
                    rdb.inc(v.reduce_sum(
                        ssqall.rearrange("p (k two) -> p k two", two=2)[:, k:k + 1, :],
                        tmsq.rearrange("p (u f) -> p u f", f=64), axis=AX.X))

        with nc.Block() as blk:
            @blk.vector
            def _(v):
                v.reduce_max(mx1[:, 0:1], ssqall[:, 0:2 * ntp], axis=AX.X)
                v.reduce_max(mx1[:, 1:2], ssqall[:, 2 * ntp:4 * ntp], axis=AX.X)

        with nc.Block() as blk:
            @blk.tensor
            def _(te):
                te.transpose(ps_a[0][0:1, 0:128], mx1[:, 0:1], identf[:, :])
                te.transpose(ps_a[0][0:1, 128:256], mx1[:, 1:2], identf[:, :])

        with nc.Block() as blk:
            @blk.vector
            def _(v):
                v.tensor_copy(s2r[:, :], ps_a[0][0:1, 0:256])

        with nc.Block() as blk:
            @blk.vector
            def _(v):
                v.reduce_max(m11[:, 0:1], s2r[:, 0:128], axis=AX.X)
                v.reduce_max(m11[:, 1:2], s2r[:, 128:256], axis=AX.X)

        with nc.Block() as blk:
            @blk.vector
            def _(v):
                v.tensor_mul(m11[:, 2:3], m11[:, 0:1], m11[:, 1:2])

        with nc.Block() as blk:
            @blk.scalar
            def _(sc):
                sc.activation(m11[:, 3:4], m11[:, 2:3], AF.Sqrt)

        with nc.Block() as blk:
            @blk.vector
            def _(v):
                selfsem(v, v.tensor_scalar_mul(m11[:, 2:3], m11[:, 3:4], -1.0 / 16.0))
                v.tensor_copy(cb1[:, :], m11[:, 2:3])

        with nc.Block() as blk:
            @blk.tensor
            def _(te):
                te.matmul(ps_a[1][:, 0:1], consts["onesbf"][:, :], cb1[:, :],
                          start=True, stop=True)

        with nc.Block() as blk:
            @blk.vector
            def _(v):
                v.tensor_copy(cbias[:, :], ps_a[1][:, 0:1])
        env.pop_pool()

        env.push_pool()
        ps_sc = env.ps(f"s1_pssc_{inst}", [128, HALF], F32)
        ps_pv = env.ps(f"s1_pspv_{inst}", [65, HALF], F32)
        for qh in range(2):
            with nc.Block() as blk:
                ts0, ae0, tv0, ps0, vs0 = tsc.n, aex.n, tpv2.n, psq.n, vsq.n

                def sc_mms(te, kt):
                    ku = kt // NSB
                    r0 = (kt % NSB) * 128
                    kb = (ku % 2) * 64
                    lhsT = Yk[kb:kb + 64, (ku // 2) * RB + r0:(ku // 2) * RB + r0 + 128]
                    for qq in range(8):
                        qu = qh * 8 + qq
                        srcq = Yq if (qu % 2) == (ku % 2) else Yqa
                        rhs = srcq[kb:kb + 64, (qu // 2) * RB:(qu // 2 + 1) * RB]
                        ins = te.matmul(ps_sc[:, qq * RB:(qq + 1) * RB], lhsT, rhs,
                                        start=True, stop=True)
                    return ins

                def pv_mms(te, kt):
                    for qg in range(HALF // 512):
                        ins = te.matmul(ps_pv[:, qg * 512:(qg + 1) * 512],
                                        Vaug[:, kt * 65:(kt + 1) * 65],
                                        e_sb[kt % 2][:, qg * 512:(qg + 1) * 512],
                                        start=(kt == 0), stop=(kt == NKT - 1))
                    return ins

                @blk.tensor
                def _(te):
                    tsc.inc(sc_mms(te, 0))
                    for kt in range(1, NKT):
                        te.wait_ge(aex.h, ae0 + kt)
                        te.wait_ge(vsq.h, vs0 + kt)
                        tpv2.inc(pv_mms(te, kt - 1))
                        tsc.inc(sc_mms(te, kt))
                    te.wait_ge(vsq.h, vs0 + NKT)
                    tpv2.inc(pv_mms(te, NKT - 1))

                @blk.scalar
                def _(sc):
                    for kt in range(NKT):
                        sc.wait_ge(tsc.h, ts0 + kt + 1)
                        if kt >= 1:
                            sc.wait_ge(psq.h, ps0 + kt)
                        aex.inc(sc.activation(e1_sb[:, 0:HALF], ps_sc[:, :], AF.Exp,
                                              bias=cbias[:, :], scale=1.0))

                @blk.gpsimd
                def _(g):
                    for kt in range(NKT):
                        g.wait_ge(aex.h, ae0 + kt + 1)
                        if kt >= 1:
                            g.wait_ge(vsq.h, vs0 + kt)
                        psq.inc(g.tensor_mul(e2_sb[:, 0:HALF], e1_sb[:, 0:HALF], e1_sb[:, 0:HALF]))

                @blk.vector
                def _(v):
                    for kt in range(NKT):
                        v.wait_ge(psq.h, ps0 + kt + 1)
                        if kt >= 2:
                            v.wait_ge(tpv2.h, tv0 + kt - 1)
                        vsq.inc(v.tensor_mul(e_sb[kt % 2][:, 0:HALF], e2_sb[:, 0:HALF],
                                             e2_sb[:, 0:HALF]))

            with nc.Block() as blk:
                @blk.vector
                def _(v):
                    v.tensor_copy(pv_sb[:, 0:HALF], ps_pv[:, :])

            with nc.Block() as blk:
                to0, eo0 = tpo.n, evo.n

                @blk.tensor
                def _(te):
                    for qc in range(NQC):
                        if qc >= 2:
                            te.wait_ge(evo.h, eo0 + qc - 1)
                        tpo.inc(te.transpose(ps_sc[:, (qc % 2) * 512:(qc % 2) * 512 + 65],
                                             pv_sb[:, qc * 128:(qc + 1) * 128],
                                             identf[0:65, 0:65]))

                @blk.vector
                def _(v):
                    for qc in range(NQC):
                        qu = qh * 8 + qc // NSB
                        sb0 = qc % NSB
                        v.wait_ge(tpo.h, to0 + qc + 1)
                        selfsem(v, v.reciprocal(rl1[:, :],
                                                ps_sc[:, (qc % 2) * 512 + 64:(qc % 2) * 512 + 65]))
                        evo.inc(v.scalar_tensor_tensor(
                            ytm[:, sb0 * 1024 + qu * 64: sb0 * 1024 + (qu + 1) * 64],
                            ps_sc[:, (qc % 2) * 512:(qc % 2) * 512 + 64],
                            rl1[:, :],
                            xtm_sb[:, sb0 * 1024 + qu * 64: sb0 * 1024 + (qu + 1) * 64],
                            ALU.mult, ALU.add))
        env.pop_pool()

        with nc.Block() as blk:
            @blk.sync
            def _(s):
                for jj in range(NSB):
                    sty.inc(s.dma_start(out=y_hbm[c0 + jj * 128: c0 + (jj + 1) * 128, :],
                                        in_=ytm[:, jj * 1024:(jj + 1) * 1024]), 16)
                s.wait_ge(sty.h, sty.n)
    env.pop_pool()
    return y_hbm


def emit_s1b(env, dd, y_hbm, T, consts, S=4096):
    nc = env.nc
    nb = T // S
    NBT = S // 128
    ntiles = T // 128
    GN = S * 1024
    identf = consts["identf"]

    env.push_pool()
    ybuf = env.sb("s1b_y", [128, 2 * 1024], BF16)
    ysq = env.sb("s1b_ysq", [128, 1024], BF16)
    ys = env.sb("s1b_ys", [128, ntiles], F32)
    sqs = env.sb("s1b_sqs", [128, ntiles], F32)
    bsum = env.sb("s1b_bsum", [128, 2], F32)
    brow2 = env.sb("s1b_brow2", [2, 128], F32)
    b21 = env.sb("s1b_b21", [2, 1], F32)
    bsc = env.sb("s1b_bsc", [1, 8], F32)
    mub = env.sb("s1b_mub", [128, 2], F32)
    rsb = env.sb("s1b_rsb", [128, 2], F32)
    scr_hbm = nc.dram_tensor("s1b_scr", [1, 4], F32).ap()
    ps_b = env.ps("s1b_ps", [128, 512], F32)

    qs = env.sem("s1b_qs")

    def selfsem(v, ins):
        qs.n += 1
        ins.then_inc(qs.h, 1)
        v.wait_ge(qs.h, qs.n)

    ldy = [env.sem("s1b_ldy0"), env.sem("s1b_ldy1")]
    rds = env.sem("s1b_rds")
    acq = env.sem("s1b_acq")
    rdq = env.sem("s1b_rdq")
    zst = env.sem("s1b_zst")

    with nc.Block() as blk:
        r0, a0, q0 = rds.n, acq.n, rdq.n
        ldy0 = [ldy[0].n, ldy[1].n]

        @blk.sync
        def _(s):
            for t in range(ntiles):
                p = t % 2
                if t >= 2:
                    s.wait_ge(rdq.h, q0 + t - 1)
                ldy[p].inc(s.dma_start(out=ybuf[:, p * 1024:(p + 1) * 1024],
                                       in_=y_hbm[t * 128:(t + 1) * 128, :]), 16)

        @blk.vector
        def _(v):
            for t in range(ntiles):
                p = t % 2
                v.wait_ge(ldy[p].h, ldy0[p] + 16 * (t // 2 + 1))
                rds.inc(v.reduce_sum(ys[:, t:t + 1], ybuf[:, p * 1024:(p + 1) * 1024], axis=AX.X))
                v.wait_ge(acq.h, a0 + t + 1)
                rdq.inc(v.reduce_sum(sqs[:, t:t + 1], ysq[:, :], axis=AX.X))

        @blk.scalar
        def _(sc):
            for t in range(ntiles):
                p = t % 2
                sc.wait_ge(ldy[p].h, ldy0[p] + 16 * (t // 2 + 1))
                if t >= 1:
                    sc.wait_ge(rdq.h, q0 + t)
                acq.inc(sc.activation(ysq[:, :], ybuf[:, p * 1024:(p + 1) * 1024], AF.Square))

    for b in range(nb):
        with nc.Block() as blk:
            @blk.vector
            def _(v):
                v.reduce_sum(bsum[:, 0:1], ys[:, b * NBT:(b + 1) * NBT], axis=AX.X)
                v.reduce_sum(bsum[:, 1:2], sqs[:, b * NBT:(b + 1) * NBT], axis=AX.X)

        with nc.Block() as blk:
            @blk.tensor
            def _(te):
                te.transpose(ps_b[0:2, 0:128], bsum[:, :], identf[:, :])

        with nc.Block() as blk:
            @blk.vector
            def _(v):
                v.tensor_copy(brow2[0:2, :], ps_b[0:2, 0:128])

        with nc.Block() as blk:
            @blk.vector
            def _(v):
                v.reduce_sum(b21[0:2, :], brow2[0:2, :], axis=AX.X)

        with nc.Block() as blk:
            @blk.tensor
            def _(te):
                te.transpose(ps_b[0:1, 0:2], b21[0:2, :], identf[0:2, 0:2])

        with nc.Block() as blk:
            @blk.vector
            def _(v):
                v.tensor_copy(bsc[:, 0:2], ps_b[0:1, 0:2])

        with nc.Block() as blk:
            @blk.vector
            def _(v):
                selfsem(v, v.tensor_scalar_mul(bsc[:, 2:3], bsc[:, 0:1], 1.0 / GN))
                selfsem(v, v.tensor_scalar_mul(bsc[:, 3:4], bsc[:, 1:2], 1.0 / GN))
                selfsem(v, v.tensor_mul(bsc[:, 4:5], bsc[:, 2:3], bsc[:, 2:3]))
                selfsem(v, v.tensor_sub(bsc[:, 5:6], bsc[:, 3:4], bsc[:, 4:5]))

        with nc.Block() as blk:
            @blk.scalar
            def _(sc):
                sc.activation(bsc[:, 6:7], bsc[:, 5:6], AF.Ln,
                              bias=consts["eps_gn"][0:1, :]).then_inc(qs.h, 1)
                qs.n += 1
                sc.wait_ge(qs.h, qs.n)
                sc.activation(bsc[:, 7:8], bsc[:, 6:7], AF.Exp, bias=0.0, scale=-0.5)

        with nc.Block() as blk:
            @blk.sync
            def _(s):
                zst.inc(s.dma_start(out=scr_hbm[0:1, 0:1], in_=bsc[:, 2:3]), 16)
                zst.inc(s.dma_start(out=scr_hbm[0:1, 1:2], in_=bsc[:, 7:8]), 16)
                s.wait_ge(zst.h, zst.n)
                zst.inc(s.dma_start(out=mub[:, b:b + 1],
                                    in_=scr_hbm[0:1, 0:1].broadcast_to((128, 1))), 16)
                zst.inc(s.dma_start(out=rsb[:, b:b + 1],
                                    in_=scr_hbm[0:1, 1:2].broadcast_to((128, 1))), 16)
                s.wait_ge(zst.h, zst.n)

    x1v = env.sb("s1b_x1v", [128, 1024], F32)
    xttile = env.sb("s1b_xt", [128, 4 * 1024], F32)
    v4 = env.sb("s1b_v4", [128, 4 * 1024], F32)
    o1 = env.sb("s1b_o1", [128, 4 * 1024], BF16)
    amall = env.sb("s1b_am", [128, ntiles], F32)
    lnst = mk_ln_stats(env, "s1bln")
    lda = env.sem("s1b_lda")
    sta = env.sem("s1b_sta")

    vcons = env.sem("s1b_vcons")
    for tt in range(T // 512):
        b = (tt * 512) // S
        with nc.Block() as blk:
            la0, vc0 = lda.n, vcons.n

            @blk.sync
            def _(s):
                for j in range(4):
                    r0 = tt * 512 + j * 128
                    if j >= 2:
                        s.wait_ge(vcons.h, vc0 + j - 1)
                    lda.inc(s.dma_start(out=ybuf[:, (j % 2) * 1024:(j % 2) * 1024 + 1024],
                                        in_=y_hbm[r0:r0 + 128, :]), 16)
                    s.wait_ge(lda.h, lda.n)
                    lda.inc(s.dma_start(out=xttile[:, j * 1024:(j + 1) * 1024],
                                        in_=dd["x_tm"][r0:r0 + 128, :]), 16)
                    s.wait_ge(lda.h, lda.n)

            @blk.vector
            def _(v):
                for j in range(4):
                    v.wait_ge(lda.h, la0 + 32 * (j + 1))
                    vcons.inc(v.tensor_scalar(x1v[:, :],
                                              ybuf[:, (j % 2) * 1024:(j % 2) * 1024 + 1024],
                                              mub[:, b:b + 1], rsb[:, b:b + 1],
                                              ALU.subtract, ALU.mult))
                    v.tensor_add(v4[:, j * 1024:(j + 1) * 1024],
                                 xttile[:, j * 1024:(j + 1) * 1024], x1v[:, :])

        emit_ln_tail(env, consts, v4, o1, 4, lnst)
        with nc.Block() as blk:
            @blk.vector
            def _(v):
                for j in range(4):
                    v.reduce_max(amall[:, tt * 4 + j: tt * 4 + j + 1],
                                 o1[:, j * 1024:(j + 1) * 1024], axis=AX.X,
                                 apply_absolute_value=True)

            @blk.sync
            def _(s):
                for j in range(4):
                    sta.inc(s.dma_start(
                        out=dd["x1_tm"][tt * 512 + j * 128: tt * 512 + (j + 1) * 128, :],
                        in_=o1[:, j * 1024:(j + 1) * 1024]), 16)
                s.wait_ge(sta.h, sta.n)

    with nc.Block() as blk:
        @blk.sync
        def _(s):
            sta.inc(s.dma_start(out=dd["amax1_c"][:, :], in_=amall[:, :]), 16)
            s.wait_ge(sta.h, sta.n)
    env.pop_pool()


def load_consts(env, dd):
    nc = env.nc
    identbf = env.sb("c_identbf", [128, 128], BF16)
    identf = env.sb("c_identf", [128, 128], F32)
    ones_bf = env.sb("c_onesbf", [1, 128], BF16)
    eps_ln = env.sb("c_epsln", [128, 1], F32)
    eps_gn = env.sb("c_epsgn", [128, 1], F32)
    ld = env.sem("c_ld")
    with nc.Block() as blk:
        @blk.sync
        def _(s):
            ld.inc(s.dma_start(out=identbf[:, :], in_=dd["identbf"][:, :]), 16)
            ld.inc(s.dma_start(out=identf[:, :], in_=dd["identf"][:, :]), 16)
            ld.inc(s.dma_start(out=ones_bf[:, :], in_=dd["onesbf"][:, :]), 16)
            s.wait_ge(ld.h, ld.n)

        @blk.vector
        def _(v):
            v.memset(eps_ln[:, :], LN_EPS)
            v.memset(eps_gn[:, :], GN_EPS)
    return {"identbf": identbf, "identf": identf, "onesbf": ones_bf,
            "eps_ln": eps_ln, "eps_gn": eps_gn}


def host_consts():
    return {
        "identbf": np.eye(128, dtype=ml_dtypes.bfloat16),
        "identf": np.eye(128, dtype=np.float32),
        "onesbf": np.ones((1, 128), dtype=ml_dtypes.bfloat16),
    }


def vec_layout(v, p=128):
    return np.ascontiguousarray(v.reshape(-1, p).T)


def build_s4(T, debug=False):
    nc = bass.Bass("TRN2", target_bir_lowering=False)
    dd = {}
    dd["x3_tm"] = nc.dram_tensor("x3_tm", [T, D], BF16, kind="ExternalInput").ap()
    dd["w1T"] = nc.dram_tensor("w1T", [D, HID], F32, kind="ExternalInput").ap()
    dd["w2T"] = nc.dram_tensor("w2T", [D, HID], F32, kind="ExternalInput").ap()
    dd["h1T"] = nc.dram_tensor("h1T", [HID, HID], F32, kind="ExternalInput").ap()
    dd["h2T"] = nc.dram_tensor("h2T", [HID, HID], F32, kind="ExternalInput").ap()
    dd["w3T"] = nc.dram_tensor("w3T", [HID, D], F32, kind="ExternalInput").ap()
    for k, w in (("s1", HCH), ("s2", HCH), ("h1s", HCH), ("h2s", HCH), ("w3s", DCH)):
        dd[k] = nc.dram_tensor(k, [128, w], F32, kind="ExternalInput").ap()
    for k in ("identbf", "identf", "onesbf"):
        shape = [1, 128] if k == "onesbf" else [128, 128]
        dt = F32 if k == "identf" else BF16
        dd[k] = nc.dram_tensor(k, shape, dt, kind="ExternalInput").ap()
    dd["out"] = nc.dram_tensor("out", [T, D], F32, kind="ExternalOutput").ap()
    with ExitStack() as ctx:
        env = Env(nc, ctx)
        consts = load_consts(env, dd)
        tens = emit_s4(env, dd, T, consts)
        if debug:
            dsem = env.sem("dbg_st")
            douts = {}
            for k, sb in tens.items():
                p, f = sb.shape
                douts[k] = nc.dram_tensor("dbg_" + k, [p, f],
                                          mybir.dt(sb.dtype) if not isinstance(sb.dtype, mybir.dt) else sb.dtype,
                                          kind="ExternalOutput").ap()
            with nc.Block() as blk:
                @blk.sync
                def _(s):
                    for k, sb in tens.items():
                        dsem.inc(s.dma_start(out=douts[k][:, :], in_=sb[:, :]), 16)
                    s.wait_ge(dsem.h, dsem.n)
    return nc


def build_s2(T, seqlen=4096):
    nc = bass.Bass("TRN2", target_bir_lowering=False)
    dd = {}
    dd["x1_tm"] = nc.dram_tensor("x1_tm", [T, D], BF16, kind="ExternalInput").ap()
    dd["amax1_c"] = nc.dram_tensor("amax1_c", [128, T // 128], F32, kind="ExternalInput").ap()
    dd["w1T"] = nc.dram_tensor("w1T", [D, HID], F32, kind="ExternalInput").ap()
    dd["convw"] = nc.dram_tensor("convw", [3, HID], F32, kind="ExternalInput").ap()
    dd["convb"] = nc.dram_tensor("convb", [1, HID], F32, kind="ExternalInput").ap()
    dd["w2T"] = nc.dram_tensor("w2T", [HID, D], F32, kind="ExternalInput").ap()
    for k in ("identbf", "identf", "onesbf"):
        shape = [1, 128] if k == "onesbf" else [128, 128]
        dt = F32 if k == "identf" else BF16
        dd[k] = nc.dram_tensor(k, shape, dt, kind="ExternalInput").ap()
    dd["x2_tm"] = nc.dram_tensor("x2_tm", [T, D], BF16, kind="ExternalOutput").ap()
    with ExitStack() as ctx:
        env = Env(nc, ctx)
        consts = load_consts(env, dd)
        emit_s2(env, dd, T, consts, seqlen=seqlen)
    return nc


def build_s3(T, L=4096):
    nc = bass.Bass("TRN2", target_bir_lowering=False)
    dd = {}
    dd["x2_tm"] = nc.dram_tensor("x2_tm", [T, D], BF16, kind="ExternalInput").ap()
    for k in ("wqT", "wkT", "wvT", "woT"):
        dd[k] = nc.dram_tensor(k, [D, D], F32, kind="ExternalInput").ap()
    for k in ("bq", "bk", "bv"):
        dd[k] = nc.dram_tensor(k, [128, 8], F32, kind="ExternalInput").ap()
    dd["bo_row"] = nc.dram_tensor("bo_row", [1, D], F32, kind="ExternalInput").ap()
    dd["w0l"] = nc.dram_tensor("w0l", [128, 512], F32, kind="ExternalInput").ap()
    for k in ("identbf", "identf", "onesbf"):
        shape = [1, 128] if k == "onesbf" else [128, 128]
        dt = F32 if k == "identf" else BF16
        dd[k] = nc.dram_tensor(k, shape, dt, kind="ExternalInput").ap()
    dd["x3_tm"] = nc.dram_tensor("x3_tm", [T, D], BF16, kind="ExternalOutput").ap()
    with ExitStack() as ctx:
        env = Env(nc, ctx)
        consts = load_consts(env, dd)
        emit_s3(env, dd, T, consts, L=L)
    return nc


def build_s1(T, S=4096):
    nc = bass.Bass("TRN2", target_bir_lowering=False)
    dd = {}
    dd["x_fm"] = nc.dram_tensor("x_fm", [D, T], F32, kind="ExternalInput").ap()
    dd["x_tm"] = nc.dram_tensor("x_tm", [T, D], F32, kind="ExternalInput").ap()
    for k in ("wqT", "wkT", "wvT"):
        dd[k] = nc.dram_tensor(k, [D, D], F32, kind="ExternalInput").ap()
    for k in ("identbf", "identf", "onesbf"):
        shape = [1, 128] if k == "onesbf" else [128, 128]
        dt = F32 if k == "identf" else BF16
        dd[k] = nc.dram_tensor(k, shape, dt, kind="ExternalInput").ap()
    dd["x1_tm"] = nc.dram_tensor("x1_tm", [T, D], BF16, kind="ExternalOutput").ap()
    dd["amax1_c"] = nc.dram_tensor("amax1_c", [128, T // 128], F32, kind="ExternalOutput").ap()
    with ExitStack() as ctx:
        env = Env(nc, ctx)
        consts = load_consts(env, dd)
        y_hbm = emit_s1(env, dd, T, consts, S=S)
        emit_s1b(env, dd, y_hbm, T, consts, S=S)
    return nc


def build_fused(T=8192):
    nc = bass.Bass("TRN2", target_bir_lowering=False)
    dd = {}
    dd["x_fm"] = nc.dram_tensor("x_fm", [D, T], F32, kind="ExternalInput").ap()
    dd["x_tm"] = nc.dram_tensor("x_tm", [T, D], F32, kind="ExternalInput").ap()
    for k in ("s1_wqT", "s1_wkT", "s1_wvT", "s3_wqT", "s3_wkT", "s3_wvT", "s3_woT"):
        dd[k] = nc.dram_tensor("w_" + k, [D, D], F32, kind="ExternalInput").ap()
    dd["s2_w1T"] = nc.dram_tensor("w_s2_w1T", [D, HID], F32, kind="ExternalInput").ap()
    dd["s2_convw"] = nc.dram_tensor("w_s2_convw", [3, HID], F32, kind="ExternalInput").ap()
    dd["s2_convb"] = nc.dram_tensor("w_s2_convb", [1, HID], F32, kind="ExternalInput").ap()
    dd["s2_w2T"] = nc.dram_tensor("w_s2_w2T", [HID, D], F32, kind="ExternalInput").ap()
    for k in ("s3_bq", "s3_bk", "s3_bv"):
        dd[k] = nc.dram_tensor("w_" + k, [128, 8], F32, kind="ExternalInput").ap()
    dd["s3_bo_row"] = nc.dram_tensor("w_s3_bo_row", [1, D], F32, kind="ExternalInput").ap()
    dd["s3_w0l"] = nc.dram_tensor("w_s3_w0l", [128, 512], F32, kind="ExternalInput").ap()
    dd["s4_w1T"] = nc.dram_tensor("w_s4_w1T", [D, HID], F32, kind="ExternalInput").ap()
    dd["s4_w2T"] = nc.dram_tensor("w_s4_w2T", [D, HID], F32, kind="ExternalInput").ap()
    dd["s4_h1T"] = nc.dram_tensor("w_s4_h1T", [HID, HID], F32, kind="ExternalInput").ap()
    dd["s4_h2T"] = nc.dram_tensor("w_s4_h2T", [HID, HID], F32, kind="ExternalInput").ap()
    dd["s4_w3T"] = nc.dram_tensor("w_s4_w3T", [HID, D], F32, kind="ExternalInput").ap()
    for k, w in (("s4_s1", HCH), ("s4_s2", HCH), ("s4_h1s", HCH), ("s4_h2s", HCH), ("s4_w3s", DCH)):
        dd[k] = nc.dram_tensor("w_" + k, [128, w], F32, kind="ExternalInput").ap()
    for k in ("identbf", "identf", "onesbf"):
        shape = [1, 128] if k == "onesbf" else [128, 128]
        dt = F32 if k == "identf" else BF16
        dd[k] = nc.dram_tensor(k, shape, dt, kind="ExternalInput").ap()
    dd["out"] = nc.dram_tensor("out", [T, D], F32, kind="ExternalOutput").ap()
    dd["x1_tm"] = nc.dram_tensor("x1_tm", [T, D], BF16).ap()
    dd["amax1_c"] = nc.dram_tensor("amax1_c", [128, T // 128], F32).ap()
    dd["x2_tm"] = nc.dram_tensor("x2_tm", [T, D], BF16).ap()
    dd["x3_tm"] = nc.dram_tensor("x3_tm", [T, D], BF16).ap()

    def view(mapping):
        d2 = dict(dd)
        for k, v in mapping.items():
            d2[k] = dd[v]
        return d2

    with ExitStack() as ctx:
        env = Env(nc, ctx)
        consts = load_consts(env, dd)
        env.push_pool()
        env.sem_scope_reset()
        y_hbm = emit_s1(env, view({"wqT": "s1_wqT", "wkT": "s1_wkT", "wvT": "s1_wvT"}),
                        T, consts)
        emit_s1b(env, view({}), y_hbm, T, consts)
        env.pop_pool()
        env.push_pool()
        env.sem_scope_reset()
        emit_s2(env, view({"w1T": "s2_w1T", "convw": "s2_convw", "convb": "s2_convb",
                           "w2T": "s2_w2T"}), T, consts)
        env.pop_pool()
        env.push_pool()
        env.sem_scope_reset()
        emit_s3(env, view({"wqT": "s3_wqT", "wkT": "s3_wkT", "wvT": "s3_wvT",
                           "woT": "s3_woT", "bq": "s3_bq", "bk": "s3_bk", "bv": "s3_bv",
                           "bo_row": "s3_bo_row", "w0l": "s3_w0l"}), T, consts)
        env.pop_pool()
        env.push_pool()
        env.sem_scope_reset()
        emit_s4(env, view({"w1T": "s4_w1T", "w2T": "s4_w2T", "h1T": "s4_h1T",
                           "h2T": "s4_h2T", "w3T": "s4_w3T", "s1": "s4_s1",
                           "s2": "s4_s2", "h1s": "s4_h1s", "h2s": "s4_h2s",
                           "w3s": "s4_w3s"}), T, consts)
        env.pop_pool()
    return nc


T_CORE = 8192
_PROGS = {}


def _get_progs():
    if not _PROGS:
        _PROGS["s1"] = build_s1(T_CORE)
        _PROGS["s2"] = build_s2(T_CORE)
        _PROGS["s3"] = build_s3(T_CORE)
        _PROGS["s4"] = build_s4(T_CORE)
    return _PROGS


def kernel(**inputs):
    ins = {k: np.asarray(v) for k, v in inputs.items()}
    x = ins["L0_attn_x"]
    ncore = 8
    consts = host_consts()

    w0l = np.zeros((128, 512), np.float32)
    wrf = ins["L1_attn_Wrf"][0]
    for h in range(16):
        w0l[(h % 2) * 64:(h % 2) * 64 + 64, (h // 2) * 64:(h // 2) * 64 + 64] = wrf[h]
    shared = {
        "w_s1_wqT": np.ascontiguousarray(ins["L0_attn_Wq"].T),
        "w_s1_wkT": np.ascontiguousarray(ins["L0_attn_Wk"].T),
        "w_s1_wvT": np.ascontiguousarray(ins["L0_attn_Wv"].T),
        "w_s2_w1T": np.ascontiguousarray(ins["L0_ffn_w1"].T),
        "w_s2_convw": np.ascontiguousarray(ins["L0_ffn_conv_w"].reshape(4096, 3).T),
        "w_s2_convb": np.ascontiguousarray(ins["L0_ffn_conv_b"].reshape(1, 4096)),
        "w_s2_w2T": np.ascontiguousarray(ins["L0_ffn_w2"].T),
        "w_s3_wqT": np.ascontiguousarray(ins["L1_attn_Wq"].T),
        "w_s3_wkT": np.ascontiguousarray(ins["L1_attn_Wk"].T),
        "w_s3_wvT": np.ascontiguousarray(ins["L1_attn_Wv"].T),
        "w_s3_woT": np.ascontiguousarray(ins["L1_attn_Wo"].T),
        "w_s3_bq": vec_layout(ins["L1_attn_bq"]), "w_s3_bk": vec_layout(ins["L1_attn_bk"]),
        "w_s3_bv": vec_layout(ins["L1_attn_bv"]),
        "w_s3_bo_row": np.ascontiguousarray(ins["L1_attn_bo"].reshape(1, 1024)),
        "w_s3_w0l": w0l,
        "w_s4_w1T": np.ascontiguousarray(ins["L1_ffn_w1_w"].T),
        "w_s4_w2T": np.ascontiguousarray(ins["L1_ffn_w2_w"].T),
        "w_s4_h1T": np.ascontiguousarray(ins["L1_ffn_h1_w"].T),
        "w_s4_h2T": np.ascontiguousarray(ins["L1_ffn_h2_w"].T),
        "w_s4_w3T": np.ascontiguousarray(ins["L1_ffn_w3_w"].T),
        "w_s4_s1": vec_layout(ins["L1_ffn_w1_s"]), "w_s4_s2": vec_layout(ins["L1_ffn_w2_s"]),
        "w_s4_h1s": vec_layout(ins["L1_ffn_h1_s"]), "w_s4_h2s": vec_layout(ins["L1_ffn_h2_s"]),
        "w_s4_w3s": vec_layout(ins["L1_ffn_w3_s"]),
    }
    shared.update(consts)

    if "fused" not in _PROGS:
        _PROGS["fused"] = build_fused(T_CORE)
    maps = []
    for i in range(ncore):
        xc = np.ascontiguousarray(x[2 * i:2 * i + 2].reshape(T_CORE, 1024))
        m = {"x_fm": np.ascontiguousarray(xc.T), "x_tm": xc}
        m.update(shared)
        maps.append(m)
    res = run_bass_kernel_spmd(_PROGS["fused"], maps, core_ids=list(range(ncore)))
    out = np.empty((16, 4096, 1024), np.float32)
    for i in range(ncore):
        out[2 * i:2 * i + 2] = np.asarray(res.results[i]["out"]).reshape(2, 4096, 1024)
    return out


# version 10
# speedup vs baseline: 1.4131x; 1.3950x over previous
import numpy as np
import ml_dtypes
from contextlib import ExitStack

import concourse.bass as bass
import concourse.mybir as mybir
from concourse.bass_utils import run_bass_kernel_spmd

F32 = mybir.dt.float32
BF16 = mybir.dt.bfloat16
I8 = mybir.dt.int8
AF = mybir.ActivationFunctionType
ALU = mybir.AluOpType
AX = mybir.AxisListType

D = 1024
HID = 4096
DCH = D // 128
HCH = HID // 128
LN_EPS = 1e-5
L1_ATTN_EPS = 1e-6
GN_EPS = 1e-5


class Sem:
    def __init__(self, ctx, nc, name):
        self.h = ctx.enter_context(nc.semaphore(name))
        self.n = 0

    def inc(self, instr, k=1):
        instr.then_inc(self.h, k)
        self.n += k
        return self.n

    def wait(self, eng, n=None):
        eng.wait_ge(self.h, self.n if n is None else n)


class Env:
    def __init__(self, nc, ctx):
        self.nc = nc
        self.ctx = ctx
        self.pools = [ctx]
        self._semid = 0

    def sem(self, name=None):
        if not hasattr(self, "_sempool"):
            self._sempool = []
            self._semcur = 0
        if self._semcur < len(self._sempool):
            s = self._sempool[self._semcur]
        else:
            self._semid += 1
            s = Sem(self.ctx, self.nc, f"sem{self._semid}")
            self._sempool.append(s)
        self._semcur += 1
        return s

    def sem_scope_reset(self):
        self._semcur = 0

    def sb(self, name, shape, dt):
        return self.pools[-1].enter_context(self.nc.sbuf_tensor(name, shape, dt))

    def ps(self, name, shape, dt=F32):
        return self.pools[-1].enter_context(self.nc.psum_tensor(name, shape, dt))

    def push_pool(self):
        self.pools.append(ExitStack())

    def pop_pool(self):
        self.pools.pop().close()


def emit_cast_weights(env, pairs, tag):
    nc = env.nc
    maxc = max(p[0].shape[1] for p in pairs)
    buf = [env.sb(f"cw_{tag}_{i}", [128, maxc], BF16) for i in range(2)]
    ld = [env.sem(f"cw_ld{i}_{tag}") for i in range(2)]
    st = [env.sem(f"cw_st{i}_{tag}") for i in range(2)]
    chunks = []
    for src, dst in pairs:
        r, c = src.shape
        assert r % 128 == 0 and dst.shape[0] == r and dst.shape[1] == c
        for i in range(r // 128):
            chunks.append((src, dst, i, c))
    with nc.Block() as blk:
        ld00 = [ld[0].n, ld[1].n]
        st00 = [st[0].n, st[1].n]

        @blk.gpsimd
        def _(g):
            for k, (src, dst, i, c) in enumerate(chunks):
                p = k % 2
                if k >= 2:
                    g.wait_ge(st[p].h, st00[p] + 16 * (k // 2))
                ld[p].inc(g.dma_start(out=buf[p][:, 0:c],
                                      in_=src[i * 128:(i + 1) * 128, :]), 16)

        @blk.sync
        def _(s):
            for k, (src, dst, i, c) in enumerate(chunks):
                p = k % 2
                s.wait_ge(ld[p].h, ld00[p] + 16 * (k // 2 + 1))
                st[p].inc(s.dma_start(out=dst[i * 128:(i + 1) * 128, :],
                                      in_=buf[p][:, 0:c]), 16)
            s.wait_ge(st[0].h, st[0].n)
            s.wait_ge(st[1].h, st[1].n)


def emit_s4(env, dd, T, consts):
    nc = env.nc
    NT = 512
    ntiles = T // NT
    identbf = consts["identbf"]

    w1b = nc.dram_tensor("s4_w1b", [D, HID], BF16).ap()
    w2b = nc.dram_tensor("s4_w2b", [D, HID], BF16).ap()
    h1b = nc.dram_tensor("s4_h1b", [HID, HID], BF16).ap()
    h2b = nc.dram_tensor("s4_h2b", [HID, HID], BF16).ap()
    w3b = nc.dram_tensor("s4_w3b", [HID, D], BF16).ap()
    emit_cast_weights(env, [(dd["w1T"], w1b), (dd["w2T"], w2b),
                            (dd["h1T"], h1b), (dd["h2T"], h2b),
                            (dd["w3T"], w3b)], "s4")

    s1_sb = env.sb("s4_s1", [128, HCH], F32)
    s2_sb = env.sb("s4_s2", [128, HCH], F32)
    h1s_sb = env.sb("s4_h1s", [128, HCH], F32)
    h2s_sb = env.sb("s4_h2s", [128, HCH], F32)
    w3s_sb = env.sb("s4_w3s", [128, DCH], F32)
    ld0 = env.sem("s4_ld0")
    with nc.Block() as blk:
        @blk.sync
        def _(s):
            for sb, key in ((s1_sb, "s1"), (s2_sb, "s2"), (h1s_sb, "h1s"),
                            (h2s_sb, "h2s"), (w3s_sb, "w3s")):
                ld0.inc(s.dma_start(out=sb[:, :], in_=dd[key][:, :]), 16)
            s.wait_ge(ld0.h, ld0.n)

    x3tm = env.sb("s4_x3tm", [128, 2 * 4 * 1024], BF16)
    x3fm = env.sb("s4_x3fm", [128, DCH * NT], BF16)
    ht = env.sb("s4_ht", [128, HCH * NT], BF16)
    h2sb = env.sb("s4_h2", [128, HCH * NT], BF16)
    t1 = env.sb("s4_t1", [128, NT], BF16)
    t2 = env.sb("s4_t2", [128, NT], BF16)
    wblkA = [env.sb(f"s4_wA{i}", [128, DCH * 128], BF16) for i in range(2)]
    wblkB = [env.sb(f"s4_wB{i}", [128, DCH * 128], BF16) for i in range(2)]
    wblkH = [env.sb(f"s4_wH{i}", [128, HCH * 128], BF16) for i in range(2)]
    y4fm = env.sb("s4_y4fm", [128, DCH * NT], BF16)
    v4 = env.sb("s4_v4", [128, 4096], F32)
    sq4 = env.sb("s4_sq4", [128, 4096], F32)
    o_sb = env.sb("s4_o", [128, 4 * 1024], F32)
    ssq4 = env.sb("s4_ssq4", [128, 4], F32)
    sm1_4 = env.sb("s4_sm14", [128, 4], F32)
    sm2_4 = env.sb("s4_sm24", [128, 4], F32)
    ssq2_4 = env.sb("s4_ssq24", [128, 4], F32)
    t4 = env.sb("s4_t4", [128, 4], F32)
    rstd4 = env.sb("s4_rstd4", [128, 4], F32)

    psZ = [env.ps(f"s4_psZ{i}", [128, 512], F32) for i in range(4)]
    ps_tp = [env.ps(f"s4_pstp{i}", [128, 512], BF16) for i in range(2)]
    ps_tm = [env.ps(f"s4_pstm{i}", [128, 1024], BF16) for i in range(2)]

    ldx = [env.sem("s4_ldx0"), env.sem("s4_ldx1")]
    tpx = env.sem("s4_tpx")
    evx = env.sem("s4_evx")
    ldw = [env.sem("s4_ldw0"), env.sem("s4_ldw1")]
    mmA = env.sem("s4_mmA")
    mmB = env.sem("s4_mmB")
    acA = env.sem("s4_acA")
    evB = env.sem("s4_evB")
    mulS = env.sem("s4_mul")
    tpy = env.sem("s4_tpy")
    vst = env.sem("s4_vst")
    sselfS = env.sem("s4_sselfS")
    fin = env.sem("s4_fin")
    sto = env.sem("s4_sto")

    sto_base = sto.n
    with nc.Block() as blk:
        @blk.sync
        def _(s):
            for j in range(4):
                ldx[0].inc(s.dma_start(
                    out=x3tm[:, j * 1024:(j + 1) * 1024],
                    in_=dd["x3_tm"][j * 128:(j + 1) * 128, :]), 16)

    for t in range(ntiles):
        slot = t % 2
        xoff = slot * 4096
        nslot = (t + 1) % 2

        with nc.Block() as blk:
            ldx_t0 = ldx[slot].n
            evx0, tpx0 = evx.n, tpx.n

            @blk.sync
            def _(s):
                if t + 1 < ntiles:
                    for j in range(4):
                        ldx[nslot].inc(s.dma_start(
                            out=x3tm[:, nslot * 4096 + j * 1024: nslot * 4096 + (j + 1) * 1024],
                            in_=dd["x3_tm"][(t + 1) * NT + j * 128:(t + 1) * NT + (j + 1) * 128, :]), 16)

            @blk.tensor
            def _(te):
                te.wait_ge(ldx[slot].h, ldx_t0)
                for i in range(DCH):
                    if i >= 2:
                        te.wait_ge(evx.h, evx0 + i - 1)
                    for j in range(4):
                        ins = te.transpose(
                            ps_tp[i % 2][:, j * 128:(j + 1) * 128],
                            x3tm[:, xoff + j * 1024 + i * 128: xoff + j * 1024 + (i + 1) * 128],
                            identbf[:, :])
                    tpx.inc(ins)

            @blk.vector
            def _(v):
                for i in range(DCH):
                    v.wait_ge(tpx.h, tpx0 + i + 1)
                    evx.inc(v.tensor_copy(x3fm[:, i * NT:(i + 1) * NT], ps_tp[i % 2][:, :]))

        with nc.Block() as blk:
            mmA0, mmB0, acA0, evB0, mulS0 = mmA.n, mmB.n, acA.n, evB.n, mulS.n
            ldw0 = [ldw[0].n, ldw[1].n]

            @blk.gpsimd
            def _(g):
                for o in range(HCH):
                    p = o % 2
                    if o >= 2:
                        g.wait_ge(mmB.h, mmB0 + o - 1)
                    ldw[p].inc(g.dma_start(
                        out=wblkA[p][:, :],
                        in_=w1b.rearrange("(ic p) o -> p ic o", p=128)[:, :, o * 128:(o + 1) * 128]), 16)
                    ldw[p].inc(g.dma_start(
                        out=wblkB[p][:, :],
                        in_=w2b.rearrange("(ic p) o -> p ic o", p=128)[:, :, o * 128:(o + 1) * 128]), 16)

            @blk.tensor
            def _(te):
                for o in range(HCH):
                    te.wait_ge(ldw[o % 2].h, ldw0[o % 2] + 32 * (o // 2 + 1))
                    if o >= 2:
                        te.wait_ge(acA.h, acA0 + o - 1)
                        te.wait_ge(evB.h, evB0 + o - 1)
                    for i in range(DCH):
                        ins = te.matmul(psZ[2 * (o % 2)][:, :],
                                        wblkA[o % 2][:, i * 128:(i + 1) * 128],
                                        x3fm[:, i * NT:(i + 1) * NT],
                                        start=(i == 0), stop=(i == DCH - 1))
                    mmA.inc(ins)
                    for i in range(DCH):
                        ins = te.matmul(psZ[2 * (o % 2) + 1][:, :],
                                        wblkB[o % 2][:, i * 128:(i + 1) * 128],
                                        x3fm[:, i * NT:(i + 1) * NT],
                                        start=(i == 0), stop=(i == DCH - 1))
                    mmB.inc(ins)

            @blk.scalar
            def _(sc):
                for o in range(HCH):
                    sc.wait_ge(mmA.h, mmA0 + o + 1)
                    if o >= 1:
                        sc.wait_ge(mulS.h, mulS0 + o)
                    acA.inc(sc.activation(t1[:, :], psZ[2 * (o % 2)][:, :], AF.Silu,
                                          bias=0.0, scale=s1_sb[:, o:o + 1]))

            @blk.vector
            def _(v):
                for o in range(HCH):
                    v.wait_ge(mmB.h, mmB0 + o + 1)
                    evB.inc(v.tensor_scalar_mul(t2[:, :], psZ[2 * (o % 2) + 1][:, :],
                                                s2_sb[:, o:o + 1]))
                    v.wait_ge(acA.h, acA0 + o + 1)
                    mulS.inc(v.tensor_mul(ht[:, o * NT:(o + 1) * NT], t1[:, :], t2[:, :]))

        for wsrc, ssb, src, dst in ((h1b, h1s_sb, ht, h2sb), (h2b, h2s_sb, h2sb, ht)):
            with nc.Block() as blk:
                mm0, ac0 = mmA.n, acA.n
                ldw0 = [ldw[0].n, ldw[1].n]

                @blk.gpsimd
                def _(g):
                    for o in range(HCH):
                        if o >= 2:
                            g.wait_ge(mmA.h, mm0 + o - 1)
                        ldw[o % 2].inc(g.dma_start(
                            out=wblkH[o % 2][:, :],
                            in_=wsrc.rearrange("(ic p) o -> p ic o", p=128)[:, :, o * 128:(o + 1) * 128]), 16)

                @blk.tensor
                def _(te):
                    for o in range(HCH):
                        te.wait_ge(ldw[o % 2].h, ldw0[o % 2] + 16 * (o // 2 + 1))
                        if o >= 2:
                            te.wait_ge(acA.h, ac0 + o - 1)
                        ps = psZ[o % 2]
                        for i in range(HCH):
                            ins = te.matmul(ps[:, :], wblkH[o % 2][:, i * 128:(i + 1) * 128],
                                            src[:, i * NT:(i + 1) * NT],
                                            start=(i == 0), stop=(i == HCH - 1))
                        mmA.inc(ins)

                @blk.scalar
                def _(sc):
                    for o in range(HCH):
                        sc.wait_ge(mmA.h, mm0 + o + 1)
                        acA.inc(sc.activation(dst[:, o * NT:(o + 1) * NT], psZ[o % 2][:, :],
                                              AF.Gelu, bias=0.0, scale=ssb[:, o:o + 1]))

        with nc.Block() as blk:
            mm0, ev0 = mmA.n, evB.n
            ldw0 = [ldw[0].n, ldw[1].n]

            @blk.gpsimd
            def _(g):
                for o in range(DCH):
                    if o >= 2:
                        g.wait_ge(mmA.h, mm0 + o - 1)
                    ldw[o % 2].inc(g.dma_start(
                        out=wblkH[o % 2][:, :],
                        in_=w3b.rearrange("(ic p) o -> p ic o", p=128)[:, :, o * 128:(o + 1) * 128]), 16)

            @blk.tensor
            def _(te):
                for o in range(DCH):
                    te.wait_ge(ldw[o % 2].h, ldw0[o % 2] + 16 * (o // 2 + 1))
                    if o >= 2:
                        te.wait_ge(evB.h, ev0 + o - 1)
                    ps = psZ[o % 2]
                    for i in range(HCH):
                        ins = te.matmul(ps[:, :], wblkH[o % 2][:, i * 128:(i + 1) * 128],
                                        ht[:, i * NT:(i + 1) * NT],
                                        start=(i == 0), stop=(i == HCH - 1))
                    mmA.inc(ins)

            @blk.vector
            def _(v):
                for o in range(DCH):
                    v.wait_ge(mmA.h, mm0 + o + 1)
                    evB.inc(v.tensor_scalar_mul(y4fm[:, o * NT:(o + 1) * NT],
                                                psZ[o % 2][:, :], w3s_sb[:, o:o + 1]))

        with nc.Block() as blk:
            tp0, vst0 = tpy.n, vst.n

            @blk.tensor
            def _(te):
                for j in range(4):
                    if j >= 2:
                        te.wait_ge(vst.h, vst0 + j - 1)
                    for i in range(DCH):
                        ins = te.transpose(
                            ps_tm[j % 2][:, i * 128:(i + 1) * 128],
                            y4fm[:, i * NT + j * 128: i * NT + (j + 1) * 128],
                            identbf[:, :])
                    tpy.inc(ins)

            @blk.vector
            def _(v):
                for j in range(4):
                    v.wait_ge(tpy.h, tp0 + j + 1)
                    vst.inc(v.tensor_add(v4[:, j * 1024:(j + 1) * 1024],
                                         x3tm[:, xoff + j * 1024: xoff + (j + 1) * 1024],
                                         ps_tm[j % 2][:, :]))
                for j in range(4):
                    v.reduce_sum(ssq4[:, j:j + 1], v4[:, j * 1024:(j + 1) * 1024], axis=AX.X)

        with nc.Block() as blk:
            @blk.vector
            def _(v):
                v.tensor_scalar_mul(sm1_4[:, :], ssq4[:, :], -1.0 / D)
                v.tensor_scalar_mul(sm2_4[:, :], ssq4[:, :], 1.0 / D)

        with nc.Block() as blk:
            @blk.scalar
            def _(sc):
                for j in range(4):
                    sc.activation(sq4[:, j * 1024:(j + 1) * 1024],
                                  v4[:, j * 1024:(j + 1) * 1024], AF.Square,
                                  bias=sm1_4[:, j:j + 1], scale=1.0)

        with nc.Block() as blk:
            @blk.vector
            def _(v):
                for j in range(4):
                    v.reduce_sum(ssq2_4[:, j:j + 1], sq4[:, j * 1024:(j + 1) * 1024], axis=AX.X)

        with nc.Block() as blk:
            @blk.vector
            def _(v):
                v.tensor_scalar(t4[:, :], ssq2_4[:, :], 1.0 / D, LN_EPS, ALU.mult, ALU.add)

        with nc.Block() as blk:
            @blk.scalar
            def _(sc):
                sc.activation(rstd4[:, :], t4[:, :], AF.Ln).then_inc(sselfS.h, 1)
                sselfS.n += 1
                sc.wait_ge(sselfS.h, sselfS.n)
                sc.activation(rstd4[:, :], rstd4[:, :], AF.Exp, bias=0.0, scale=-0.5)

        with nc.Block() as blk:
            fin0 = fin.n

            @blk.vector
            def _(v):
                for j in range(4):
                    if t >= 1 and j == 0:
                        v.wait_ge(sto.h, sto_base + 16 * 4 * t)
                    fin.inc(v.tensor_scalar(o_sb[:, j * 1024:(j + 1) * 1024],
                                            v4[:, j * 1024:(j + 1) * 1024],
                                            sm2_4[:, j:j + 1], rstd4[:, j:j + 1],
                                            ALU.subtract, ALU.mult))

            @blk.sync
            def _(s):
                for j in range(4):
                    s.wait_ge(fin.h, fin0 + j + 1)
                    sto.inc(s.dma_start(
                        out=dd["out"][t * NT + j * 128: t * NT + (j + 1) * 128, :],
                        in_=o_sb[:, j * 1024:(j + 1) * 1024]), 16)

    with nc.Block() as blk:
        @blk.sync
        def _(s):
            s.wait_ge(sto.h, sto.n)
    return {"x3fm": x3fm, "ht": ht, "h2sb": h2sb, "y4fm": y4fm, "o_sb": o_sb,
            "x3tm": x3tm}


def emit_ln_tail(env, consts, v4, o_sb, nsub, stats, cast_out_dt=BF16):
    nc = env.nc
    sq4 = stats["sq4"]
    with nc.Block() as blk:
        @blk.vector
        def _(v):
            for j in range(nsub):
                v.reduce_sum(stats["ssq4"][:, j:j + 1], v4[:, j * 1024:(j + 1) * 1024], axis=AX.X)
    with nc.Block() as blk:
        @blk.vector
        def _(v):
            v.tensor_scalar_mul(stats["sm1_4"][:, 0:nsub], stats["ssq4"][:, 0:nsub], -1.0 / D)
            v.tensor_scalar_mul(stats["sm2_4"][:, 0:nsub], stats["ssq4"][:, 0:nsub], 1.0 / D)
    with nc.Block() as blk:
        @blk.scalar
        def _(sc):
            for j in range(nsub):
                sc.activation(sq4[:, j * 1024:(j + 1) * 1024],
                              v4[:, j * 1024:(j + 1) * 1024], AF.Square,
                              bias=stats["sm1_4"][:, j:j + 1], scale=1.0)
    with nc.Block() as blk:
        @blk.vector
        def _(v):
            for j in range(nsub):
                v.reduce_sum(stats["ssq2_4"][:, j:j + 1], sq4[:, j * 1024:(j + 1) * 1024], axis=AX.X)
    with nc.Block() as blk:
        @blk.vector
        def _(v):
            v.tensor_scalar(stats["t4"][:, 0:nsub], stats["ssq2_4"][:, 0:nsub],
                            1.0 / D, LN_EPS, ALU.mult, ALU.add)
    with nc.Block() as blk:
        @blk.scalar
        def _(sc):
            ss = stats["selfsem"]
            sc.activation(stats["rstd4"][:, 0:nsub], stats["t4"][:, 0:nsub], AF.Ln).then_inc(ss.h, 1)
            ss.n += 1
            sc.wait_ge(ss.h, ss.n)
            sc.activation(stats["rstd4"][:, 0:nsub], stats["rstd4"][:, 0:nsub],
                          AF.Exp, bias=0.0, scale=-0.5)
    with nc.Block() as blk:
        @blk.vector
        def _(v):
            for j in range(nsub):
                v.tensor_scalar(o_sb[:, j * 1024:(j + 1) * 1024],
                                v4[:, j * 1024:(j + 1) * 1024],
                                stats["sm2_4"][:, j:j + 1], stats["rstd4"][:, j:j + 1],
                                ALU.subtract, ALU.mult)


def mk_ln_stats(env, tag, nsub=4):
    return {
        "sq4": env.sb(f"{tag}_sq4", [128, nsub * 1024], BF16),
        "ssq4": env.sb(f"{tag}_ssq4", [128, nsub], F32),
        "sm1_4": env.sb(f"{tag}_sm14", [128, nsub], F32),
        "sm2_4": env.sb(f"{tag}_sm24", [128, nsub], F32),
        "ssq2_4": env.sb(f"{tag}_ssq24", [128, nsub], F32),
        "t4": env.sb(f"{tag}_t4", [128, nsub], F32),
        "rstd4": env.sb(f"{tag}_rstd4", [128, nsub], F32),
        "selfsem": env.sem(f"{tag}_lnself"),
    }


def emit_global_mean_abs(env, tag, src_hbm, R, C, out_bc, consts):
    nc = env.nc
    nch = R // 128
    buf = [env.sb(f"{tag}_mb{i}", [128, C], F32) for i in range(2)]
    red = env.sb(f"{tag}_red", [128, max(nch, 2)], F32)
    acc1 = env.sb(f"{tag}_acc1", [128, 1], F32)
    accp = env.sb(f"{tag}_accp", [1, 128], F32)
    sc1 = env.sb(f"{tag}_sc1", [1, 1], F32)
    sc2 = env.sb(f"{tag}_sc2", [1, 1], F32)
    scr_hbm = nc.dram_tensor(f"{tag}_schbm", [1, 2], F32).ap()
    ps = env.ps(f"{tag}_ps", [1, 128], F32)
    ld = [env.sem(f"{tag}_ld{i}") for i in range(2)]
    rd = env.sem(f"{tag}_rd")
    with nc.Block() as blk:
        ld00 = [ld[0].n, ld[1].n]
        rd0 = rd.n

        @blk.sync
        def _(s):
            for k in range(nch):
                p = k % 2
                if k >= 2:
                    s.wait_ge(rd.h, rd0 + k - 1)
                ld[p].inc(s.dma_start(out=buf[p][:, :],
                                      in_=src_hbm[k * 128:(k + 1) * 128, :]), 16)

        @blk.vector
        def _(v):
            for k in range(nch):
                p = k % 2
                v.wait_ge(ld[p].h, ld00[p] + 16 * (k // 2 + 1))
                rd.inc(v.reduce_sum(red[:, k:k + 1], buf[p][:, :], axis=AX.X,
                                    apply_absolute_value=True))
    with nc.Block() as blk:
        @blk.vector
        def _(v):
            v.reduce_sum(acc1[:, :], red[:, 0:nch], axis=AX.X)
    with nc.Block() as blk:
        @blk.tensor
        def _(te):
            te.transpose(ps[:, :], acc1[:, :], consts["identf"][:, :])
    with nc.Block() as blk:
        @blk.vector
        def _(v):
            v.tensor_copy(accp[:, :], ps[:, :])
    with nc.Block() as blk:
        @blk.vector
        def _(v):
            v.reduce_sum(sc1[:, :], accp[:, :], axis=AX.X)
    with nc.Block() as blk:
        @blk.vector
        def _(v):
            v.tensor_scalar(sc1[:, :], sc1[:, :], 1.0 / (R * C), 1e-5, ALU.mult, ALU.max)
    with nc.Block() as blk:
        @blk.vector
        def _(v):
            v.reciprocal(sc2[:, :], sc1[:, :])
    st = env.sem(f"{tag}_st")
    with nc.Block() as blk:
        @blk.sync
        def _(s):
            st.inc(s.dma_start(out=scr_hbm[:, 0:1], in_=sc1[:, :]), 16)
            st.inc(s.dma_start(out=scr_hbm[:, 1:2], in_=sc2[:, :]), 16)
            s.wait_ge(st.h, st.n)
            st.inc(s.dma_start(out=out_bc["mean"][:, :],
                               in_=scr_hbm[0:1, 0:1].broadcast_to((128, 1))), 16)
            st.inc(s.dma_start(out=out_bc["inv"][:, :],
                               in_=scr_hbm[0:1, 1:2].broadcast_to((128, 1))), 16)
            s.wait_ge(st.h, st.n)


def emit_ternarize(env, tag, src_hbm, dst_hbm, R, C, inv_bc):
    nc = env.nc
    buf = [env.sb(f"{tag}_tb{i}", [128, C], F32) for i in range(2)]
    tf = env.sb(f"{tag}_tf", [128, C], F32)
    ti = env.sb(f"{tag}_ti", [128, C], I8)
    tb = [env.sb(f"{tag}_tbf{i}", [128, C], BF16) for i in range(2)]
    ld = [env.sem(f"{tag}_tld{i}") for i in range(2)]
    cv = env.sem(f"{tag}_tcv")
    st = [env.sem(f"{tag}_tst{i}") for i in range(2)]
    nch = R // 128
    with nc.Block() as blk:
        ld00 = [ld[0].n, ld[1].n]
        st00 = [st[0].n, st[1].n]
        cv0 = cv.n

        @blk.gpsimd
        def _(g):
            for k in range(nch):
                p = k % 2
                if k >= 2:
                    g.wait_ge(cv.h, cv0 + (k - 1) * 4)
                ld[p].inc(g.dma_start(out=buf[p][:, :],
                                      in_=src_hbm[k * 128:(k + 1) * 128, :]), 16)

        @blk.vector
        def _(v):
            for k in range(nch):
                p = k % 2
                v.wait_ge(ld[p].h, ld00[p] + 16 * (k // 2 + 1))
                if k >= 2:
                    v.wait_ge(st[p].h, st00[p] + 16 * (k // 2))
                v.tensor_scalar_mul(tf[:, :], buf[p][:, :], inv_bc[:, :])
                v.tensor_copy(ti[:, :], tf[:, :])
                v.tensor_copy(tb[p][:, :], ti[:, :])
                cv.inc(v.tensor_scalar(tb[p][:, :], tb[p][:, :], 1.0, -1.0,
                                       ALU.min, ALU.max), 4)

        @blk.sync
        def _(s):
            for k in range(nch):
                p = k % 2
                s.wait_ge(cv.h, cv0 + (k + 1) * 4)
                st[p].inc(s.dma_start(out=dst_hbm[k * 128:(k + 1) * 128, :],
                                      in_=tb[p][:, :]), 16)
            s.wait_ge(st[0].h, st[0].n)
            s.wait_ge(st[1].h, st[1].n)


def emit_bcast_rows(env, tag, src_row_hbm, nrows, C, dst_tiles, consts, scale=1.0):
    nc = env.nc
    row_f = env.sb(f"{tag}_rowf", [1, C], F32)
    row_b = env.sb(f"{tag}_rowb", [1, C], BF16)
    ps = [env.ps(f"{tag}_bps{i}", [128, 512], F32) for i in range(2)]
    ld = env.sem(f"{tag}_bld")
    mm = env.sem(f"{tag}_bmm")
    ev = env.sem(f"{tag}_bev")
    for r in range(nrows):
        with nc.Block() as blk:
            @blk.sync
            def _(s):
                ld.inc(s.dma_start(out=row_f[:, :], in_=src_row_hbm[r:r + 1, :]), 16)

            @blk.vector
            def _(v):
                v.wait_ge(ld.h, ld.n)
                if scale != 1.0:
                    v.tensor_scalar_mul(row_b[:, :], row_f[:, :], scale)
                else:
                    v.tensor_copy(row_b[:, :], row_f[:, :])
        with nc.Block() as blk:
            mm0, ev0 = mm.n, ev.n

            @blk.tensor
            def _(te):
                for c in range(C // 512):
                    if c >= 2:
                        te.wait_ge(ev.h, ev0 + c - 1)
                    mm.inc(te.matmul(ps[c % 2][:, :], consts["onesbf"][:, :],
                                     row_b[:, c * 512:(c + 1) * 512],
                                     start=True, stop=True))

            @blk.vector
            def _(v):
                for c in range(C // 512):
                    v.wait_ge(mm.h, mm0 + c + 1)
                    ev.inc(v.tensor_copy(dst_tiles[r][:, c * 512:(c + 1) * 512],
                                         ps[c % 2][:, :]))


def emit_s2(env, dd, T, consts, seqlen=4096):
    nc = env.nc
    NT = 512
    S = seqlen
    ntiles = T // NT
    identbf = consts["identbf"]

    w1m = {"mean": env.sb("s2_w1mean", [128, 1], F32), "inv": env.sb("s2_w1inv", [128, 1], F32)}
    w2m = {"mean": env.sb("s2_w2mean", [128, 1], F32), "inv": env.sb("s2_w2inv", [128, 1], F32)}
    env.push_pool()
    emit_global_mean_abs(env, "s2w1", dd["w1T"], D, HID, w1m, consts)
    emit_global_mean_abs(env, "s2w2", dd["w2T"], HID, D, w2m, consts)
    env.pop_pool()
    t1_hbm = nc.dram_tensor("s2_t1b", [D, HID], BF16).ap()
    t2_hbm = nc.dram_tensor("s2_t2b", [HID, D], BF16).ap()
    env.push_pool()
    emit_ternarize(env, "s2t1", dd["w1T"], t1_hbm, D, HID, w1m["inv"])
    emit_ternarize(env, "s2t2", dd["w2T"], t2_hbm, HID, D, w2m["inv"])
    env.pop_pool()
    wbc = [env.sb(f"s2_wbc{i}", [128, HID], BF16) for i in range(4)]
    env.push_pool()
    emit_bcast_rows(env, "s2cw", dd["convw"], 3, HID, wbc[0:3], consts)
    emit_bcast_rows(env, "s2cb", dd["convb"], 1, HID, wbc[3:4], consts)
    env.pop_pool()

    h_hbm = nc.dram_tensor("s2_h", [T, HID], BF16).ap()

    qs = env.sem("s2_qs")

    def selfsem(v, ins):
        qs.n += 1
        ins.then_inc(qs.h, 1)
        v.wait_ge(qs.h, qs.n)

    env.push_pool()
    t1_sb = env.sb("s2_t1sb", [128, DCH * HID], BF16)
    ldr = env.sem("s2_ldr")
    with nc.Block() as blk:
        @blk.sync
        def _(s):
            ldr.inc(s.dma_start(out=t1_sb[:, :],
                                in_=t1_hbm.rearrange("(ic p) o -> p ic o", p=128)), 16)
            s.wait_ge(ldr.h, ldr.n)

    x1tm = env.sb("s2_x1tm", [128, 2 * 4 * 1024], BF16)
    am4 = env.sb("s2_am4", [128, 2 * 4], F32)
    s4t = env.sb("s2_s4t", [128, 4], F32)
    u4t = env.sb("s2_u4t", [128, 4], F32)
    xqf = env.sb("s2_xqf", [128, 1024], F32)
    xqi = env.sb("s2_xqi", [128, 1024], I8)
    xqb = env.sb("s2_xqb", [128, 4 * 1024], BF16)
    xq_fm = env.sb("s2_xqfm", [128, DCH * NT], BF16)
    h_sb = env.sb("s2_hsb", [128, 2 * 4096], BF16)
    ps_tp = [env.ps(f"s2_pstp{i}", [128, 512], BF16) for i in range(2)]
    psM = [env.ps(f"s2_psM{i}", [128, 512], F32) for i in range(2)]

    ldx = [env.sem("s2_ldx0"), env.sem("s2_ldx1")]
    tpq = env.sem("s2_tpq")
    evq = env.sem("s2_evq")
    mmh = env.sem("s2_mmh")
    evh = env.sem("s2_evh")
    sth = env.sem("s2_sth")

    with nc.Block() as blk:
        @blk.sync
        def _(s):
            for j in range(4):
                ldx[0].inc(s.dma_start(out=x1tm[:, j * 1024:(j + 1) * 1024],
                                       in_=dd["x1_tm"][j * 128:(j + 1) * 128, :]), 16)
            ldx[0].inc(s.dma_start(out=am4[:, 0:4], in_=dd["amax1_c"][:, 0:4]), 16)

    for t in range(ntiles):
        slot = t % 2
        xoff = slot * 4096
        nslot = (t + 1) % 2

        with nc.Block() as blk:
            ld_t0 = ldx[slot].n

            @blk.sync
            def _(s):
                if t + 1 < ntiles:
                    for j in range(4):
                        ldx[nslot].inc(s.dma_start(
                            out=x1tm[:, nslot * 4096 + j * 1024: nslot * 4096 + (j + 1) * 1024],
                            in_=dd["x1_tm"][(t + 1) * NT + j * 128:(t + 1) * NT + (j + 1) * 128, :]), 16)
                    ldx[nslot].inc(s.dma_start(out=am4[:, nslot * 4:nslot * 4 + 4],
                                               in_=dd["amax1_c"][:, (t + 1) * 4:(t + 1) * 4 + 4]), 16)

            @blk.vector
            def _(v):
                v.wait_ge(ldx[slot].h, ld_t0)
                selfsem(v, v.tensor_scalar_max(s4t[:, :], am4[:, slot * 4:slot * 4 + 4], 1e-5))
                selfsem(v, v.tensor_scalar_mul(u4t[:, :], s4t[:, :], w1m["mean"][:, :]))
                v.tensor_scalar_mul(u4t[:, :], u4t[:, :], 1.0 / 127.0)
                selfsem(v, v.reciprocal(s4t[:, :], s4t[:, :]))
                selfsem(v, v.tensor_scalar_mul(s4t[:, :], s4t[:, :], 127.0))
                for j in range(4):
                    v.tensor_scalar_mul(xqf[:, :], x1tm[:, xoff + j * 1024: xoff + (j + 1) * 1024],
                                        s4t[:, j:j + 1])
                    v.tensor_copy(xqi[:, :], xqf[:, :])
                    v.tensor_copy(xqb[:, j * 1024:(j + 1) * 1024], xqi[:, :])

        with nc.Block() as blk:
            tp0, ev0 = tpq.n, evq.n

            @blk.tensor
            def _(te):
                for k in range(8):
                    j, g = k // 2, k % 2
                    if k >= 2:
                        te.wait_ge(evq.h, ev0 + k - 1)
                    for ii in range(4):
                        i = g * 4 + ii
                        ins = te.transpose(ps_tp[k % 2][:, ii * 128:(ii + 1) * 128],
                                           xqb[:, j * 1024 + i * 128: j * 1024 + (i + 1) * 128],
                                           identbf[:, :])
                    tpq.inc(ins)

            @blk.vector
            def _(v):
                for k in range(8):
                    j, g = k // 2, k % 2
                    v.wait_ge(tpq.h, tp0 + k + 1)
                    evq.inc(v.tensor_copy(
                        xq_fm.rearrange("p (i n) -> p i n", n=NT)[:, g * 4:(g + 1) * 4, j * 128:(j + 1) * 128],
                        ps_tp[k % 2].rearrange("p (i n) -> p i n", n=128)[:, :, :]))

        with nc.Block() as blk:
            mm0, eh0, st0 = mmh.n, evh.n, sth.n

            @blk.tensor
            def _(te):
                for j in range(4):
                    for o in range(8):
                        k = j * 8 + o
                        if k >= 2:
                            te.wait_ge(evh.h, eh0 + k - 1)
                        for i in range(DCH):
                            ins = te.matmul(psM[k % 2][:, :],
                                            xq_fm[:, i * NT + j * 128: i * NT + (j + 1) * 128],
                                            t1_sb[:, i * HID + o * 512: i * HID + (o + 1) * 512],
                                            start=(i == 0), stop=(i == DCH - 1))
                        mmh.inc(ins)

            @blk.vector
            def _(v):
                for j in range(4):
                    hoff = (j % 2) * 4096
                    if j >= 2:
                        v.wait_ge(sth.h, st0 + 16 * j)
                    for o in range(8):
                        k = j * 8 + o
                        v.wait_ge(mmh.h, mm0 + k + 1)
                        evh.inc(v.tensor_scalar_mul(h_sb[:, hoff + o * 512: hoff + (o + 1) * 512],
                                                    psM[k % 2][:, :], u4t[:, j:j + 1]))

            @blk.sync
            def _(s):
                for j in range(4):
                    hoff = (j % 2) * 4096
                    s.wait_ge(evh.h, eh0 + (j + 1) * 8)
                    sth.inc(s.dma_start(
                        out=h_hbm[t * NT + j * 128: t * NT + (j + 1) * 128, :],
                        in_=h_sb[:, hoff:hoff + 4096]), 16)

    with nc.Block() as blk:
        @blk.sync
        def _(s):
            s.wait_ge(sth.h, sth.n)
    env.pop_pool()

    env.push_pool()
    h0_sb = env.sb("s2_h0", [128, 4 * 4096], BF16)
    hm_sb = env.sb("s2_hm", [128, 4096], BF16)
    hp_sb = env.sb("s2_hp", [128, 4096], BF16)
    cv_sb = env.sb("s2_cv", [128, 4096], BF16)
    sil_sb = env.sb("s2_sil", [128, 4096], BF16)
    x1p = env.sb("s2_x1p", [128, 4 * 1024], BF16)
    a2t = env.sb("s2_a2t", [128, 4], F32)
    s2t = env.sb("s2_s2t", [128, 4], F32)
    u2t = env.sb("s2_u2t", [128, 4], F32)
    xq2f = env.sb("s2_xq2f", [128, 2048], F32)
    xq2i = env.sb("s2_xq2i", [128, 4096], I8)
    xq2b = env.sb("s2_xq2b", [128, 4096], BF16)
    xq2fm = env.sb("s2_xq2fm", [128, 4 * 4096], BF16)
    y2_sb = env.sb("s2_y2", [128, 4 * 1024], BF16)
    t2blk = [env.sb(f"s2_t2blk{i}", [128, 512], BF16) for i in range(2)]
    ps_tp2 = [env.ps(f"s2_pstp2{i}", [128, 512], BF16) for i in range(2)]
    ps4 = [env.ps(f"s2_ps4{i}", [128, 512], F32) for i in range(4)]
    v4 = env.sb("s2_v4", [128, 4 * 1024], F32)
    o2_sb = env.sb("s2_o2", [128, 4 * 1024], BF16)
    lnst = mk_ln_stats(env, "s2ln")

    ldh = env.sem("s2_ldh")
    shd = env.sem("s2_shd")
    cvs = env.sem("s2_cvs")
    cvg = env.sem("s2_cvg")
    sil = env.sem("s2_sil")
    tp2 = env.sem("s2_tp2")
    ev2 = env.sem("s2_ev2")
    ld2 = [env.sem("s2_ld2a"), env.sem("s2_ld2b")]
    mm2 = env.sem("s2_mm2")
    evy = env.sem("s2_evy")
    st2 = env.sem("s2_st2")

    for t in range(ntiles):
        with nc.Block() as blk:
            ld0 = ldh.n

            @blk.sync
            def _(s):
                for j in range(4):
                    r0 = t * NT + j * 128
                    ldh.inc(s.dma_start(out=h0_sb[:, j * 4096:(j + 1) * 4096],
                                        in_=h_hbm[r0:r0 + 128, :]), 16)
                    ldh.inc(s.dma_start(out=x1p[:, j * 1024:(j + 1) * 1024],
                                        in_=dd["x1_tm"][r0:r0 + 128, :]), 16)

        for j in range(4):
            r0 = t * NT + j * 128
            hj = j * 4096
            with nc.Block() as blk:
                sh0 = shd.n

                @blk.vector
                def _(v):
                    if r0 % S == 0:
                        v.memset(hm_sb[:, :], 0.0)
                    if (r0 + 128) % S == 0:
                        v.memset(hp_sb[:, :], 0.0)

            with nc.Block() as blk:
                @blk.sync
                def _(s):
                    if r0 % S == 0:
                        shd.inc(s.dma_start(out=hm_sb[1:128, :], in_=h_hbm[r0:r0 + 127, :]), 16)
                    else:
                        shd.inc(s.dma_start(out=hm_sb[:, :], in_=h_hbm[r0 - 1:r0 + 127, :]), 16)
                    if (r0 + 128) % S == 0:
                        shd.inc(s.dma_start(out=hp_sb[0:127, :], in_=h_hbm[r0 + 1:r0 + 128, :]), 16)
                    else:
                        shd.inc(s.dma_start(out=hp_sb[:, :], in_=h_hbm[r0 + 1:r0 + 129, :]), 16)
                    s.wait_ge(shd.h, shd.n)
                    s.wait_ge(ldh.h, ldh.n)

            with nc.Block() as blk:
                cv0, cg0 = cvs.n, cvg.n

                @blk.vector
                def _(v):
                    cvs.inc(v.tensor_mul(cv_sb[:, :], hm_sb[:, :], wbc[0][:, :]))
                    cvs.inc(v.tensor_add(cv_sb[:, :], cv_sb[:, :], wbc[3][:, :]))
                    v.wait_ge(cvg.h, cg0 + 1)
                    cvs.inc(v.tensor_add(cv_sb[:, :], cv_sb[:, :], hp_sb[:, :]))

                @blk.gpsimd
                def _(g):
                    g.wait_ge(cvs.h, cv0 + 1)
                    g.tensor_mul(hm_sb[:, :], hp_sb[:, :], wbc[2][:, :])
                    g.tensor_mul(hp_sb[:, :], h0_sb[:, hj:hj + 4096], wbc[1][:, :])
                    cvg.inc(g.tensor_add(hp_sb[:, :], hp_sb[:, :], hm_sb[:, :]))

            with nc.Block() as blk:
                @blk.scalar
                def _(sc):
                    sc.activation(sil_sb[:, :], cv_sb[:, :], AF.Silu)

            with nc.Block() as blk:
                @blk.vector
                def _(v):
                    v.reduce_max(a2t[:, j:j + 1], sil_sb[:, :], axis=AX.X,
                                 apply_absolute_value=True)

            with nc.Block() as blk:
                @blk.vector
                def _(v):
                    selfsem(v, v.tensor_scalar_max(s2t[:, j:j + 1], a2t[:, j:j + 1], 1e-5))
                    selfsem(v, v.tensor_scalar_mul(u2t[:, j:j + 1], s2t[:, j:j + 1], w2m["mean"][:, :]))
                    v.tensor_scalar_mul(u2t[:, j:j + 1], u2t[:, j:j + 1], 1.0 / 127.0)
                    selfsem(v, v.reciprocal(s2t[:, j:j + 1], s2t[:, j:j + 1]))
                    selfsem(v, v.tensor_scalar_mul(s2t[:, j:j + 1], s2t[:, j:j + 1], 127.0))
                    for hh in range(2):
                        v.tensor_scalar_mul(xq2f[:, :], sil_sb[:, hh * 2048:(hh + 1) * 2048],
                                            s2t[:, j:j + 1])
                        v.tensor_copy(xq2i[:, hh * 2048:(hh + 1) * 2048], xq2f[:, :])
                    v.tensor_copy(xq2b[:, :], xq2i[:, :])

            with nc.Block() as blk:
                tp0, ev0 = tp2.n, ev2.n

                @blk.tensor
                def _(te):
                    for g in range(8):
                        if g >= 2:
                            te.wait_ge(ev2.h, ev0 + g - 1)
                        for ii in range(4):
                            c = g * 4 + ii
                            ins = te.transpose(ps_tp2[g % 2][:, ii * 128:(ii + 1) * 128],
                                               xq2b[:, c * 128:(c + 1) * 128], identbf[:, :])
                        tp2.inc(ins)

                @blk.vector
                def _(v):
                    for g in range(8):
                        v.wait_ge(tp2.h, tp0 + g + 1)
                        ev2.inc(v.tensor_copy(xq2fm[:, hj + g * 512: hj + (g + 1) * 512],
                                              ps_tp2[g % 2][:, :]))

        for o in range(2):
            with nc.Block() as blk:
                mm0, ey0 = mm2.n, evy.n
                l20 = [ld2[0].n, ld2[1].n]

                @blk.gpsimd
                def _(g):
                    for c in range(HCH):
                        p = c % 2
                        if c >= 2:
                            g.wait_ge(mm2.h, mm0 + 4 * (c - 1))
                        ld2[p].inc(g.dma_start(out=t2blk[p][:, :],
                                               in_=t2_hbm[c * 128:(c + 1) * 128, o * 512:(o + 1) * 512]), 16)

                @blk.tensor
                def _(te):
                    for c in range(HCH):
                        p = c % 2
                        te.wait_ge(ld2[p].h, l20[p] + 16 * (c // 2 + 1))
                        for j in range(4):
                            mm2.inc(te.matmul(ps4[j][:, :],
                                              xq2fm[:, j * 4096 + c * 128: j * 4096 + (c + 1) * 128],
                                              t2blk[p][:, :],
                                              start=(c == 0), stop=(c == HCH - 1)))

                @blk.vector
                def _(v):
                    v.wait_ge(mm2.h, mm0 + 4 * HCH)
                    for j in range(4):
                        evy.inc(v.tensor_scalar_mul(
                            y2_sb[:, j * 1024 + o * 512: j * 1024 + (o + 1) * 512],
                            ps4[j][:, :], u2t[:, j:j + 1]))

        with nc.Block() as blk:
            @blk.vector
            def _(v):
                for j in range(4):
                    v.tensor_add(v4[:, j * 1024:(j + 1) * 1024],
                                 x1p[:, j * 1024:(j + 1) * 1024],
                                 y2_sb[:, j * 1024:(j + 1) * 1024])

        emit_ln_tail(env, consts, v4, o2_sb, 4, lnst)
        with nc.Block() as blk:
            @blk.sync
            def _(s):
                for j in range(4):
                    st2.inc(s.dma_start(
                        out=dd["x2_tm"][t * NT + j * 128: t * NT + (j + 1) * 128, :],
                        in_=o2_sb[:, j * 1024:(j + 1) * 1024]), 16)
                s.wait_ge(st2.h, st2.n)
    env.pop_pool()


def emit_s3(env, dd, T, consts, L=4096):
    nc = env.nc
    nb = T // L
    LC = L // 128
    LT = L // 512
    identbf = consts["identbf"]
    identf = consts["identf"]
    EPS = L1_ATTN_EPS
    NH = 16
    NP = NH // 2

    ctx_hbm = nc.dram_tensor("s3_ctx", [T, D], BF16).ap()
    scr_hbm = nc.dram_tensor("s3_scr", [1, 8], F32).ap()

    qs = env.sem("s3_qs")

    def selfsem(v, ins):
        qs.n += 1
        ins.then_inc(qs.h, 1)
        v.wait_ge(qs.h, qs.n)

    env.push_pool()
    wq_sb = env.sb("s3_wq", [128, DCH * D], BF16)
    wk_sb = env.sb("s3_wk", [128, DCH * D], BF16)
    wv_sb = env.sb("s3_wv", [128, DCH * D], BF16)
    w0_sb = env.sb("s3_w0", [128, 512], BF16)
    bq_sb = env.sb("s3_bq", [128, 8], F32)
    bk_sb = env.sb("s3_bk", [128, 8], F32)
    bv_sb = env.sb("s3_bv", [128, 8], F32)
    ldw0 = env.sem("s3_ldw0")
    with nc.Block() as blk:
        @blk.gpsimd
        def _(g):
            for sb, key in ((wq_sb, "wqT"), (wk_sb, "wkT"), (wv_sb, "wvT")):
                ldw0.inc(g.dma_start(out=sb[:, :],
                                     in_=dd[key].rearrange("(ic p) o -> p ic o", p=128)), 16)
            ldw0.inc(g.dma_start(out=w0_sb[:, :], in_=dd["w0l"][:, :]), 16)

        @blk.sync
        def _(s):
            for sb, key in ((bq_sb, "bq"), (bk_sb, "bk"), (bv_sb, "bv")):
                ldw0.inc(s.dma_start(out=sb[:, :], in_=dd[key][:, :]), 16)
            s.wait_ge(ldw0.h, ldw0.n)

    x2fm = env.sb("s3_x2fm", [128, DCH * L], BF16)
    x2t = env.sb("s3_x2t", [128, 1024], BF16)
    Qp = env.sb("s3_Qp", [128, L], BF16)
    Kp = env.sb("s3_Kp", [128, L], BF16)
    Vf = env.sb("s3_Vf", [128, L], BF16)
    Vaug = env.sb("s3_Vaug", [128, LC * 130], BF16)
    Ktm = env.sb("s3_Ktm", [128, LC * 128], BF16)
    ksq = env.sb("s3_ksq", [128, 128], F32)
    ssqp = env.sb("s3_ssqp", [128, 2 * LC], F32)
    ssqb = env.sb("s3_ssqb", [128, 2 * LC], F32)
    mxk = env.sb("s3_mxk", [128, 2 * LC], F32)
    mx2 = env.sb("s3_mx2", [128, 2], F32)
    mrow2 = env.sb("s3_mrow2", [2, 128], F32)
    mb21 = env.sb("s3_mb21", [2, 1], F32)
    mr12 = env.sb("s3_mr12", [1, 2], F32)
    mh2 = env.sb("s3_mh2", [128, 2], F32)
    e_pair = env.sb("s3_ep", [128, 2 * LC * 64], BF16)
    phih = env.sb("s3_phih", [64, L], BF16)
    nd_sb = env.sb("s3_nd", [65, 64], F32)
    ndT = env.sb("s3_ndT", [64, 66], BF16)
    qlogt = env.sb("s3_qlogt", [128, 2048], F32)
    mxq = env.sb("s3_mxq", [128, 32], F32)
    mxqn = env.sb("s3_mxqn", [128, 32], F32)
    phitm = env.sb("s3_phitm", [128, 2048], BF16)
    znd = env.sb("s3_znd", [65, 2], F32)
    ez_bc = env.sb("s3_ez", [128, 2], F32)
    ez1 = env.sb("s3_ez1", [128, 1], F32)
    num_sb = env.sb("s3_num", [128, LC * 66], F32)
    dent = env.sb("s3_dent", [128, LC], F32)
    slt = env.sb("s3_slt", [128, LC], F32)
    rlt = env.sb("s3_rlt", [128, LC], F32)
    strip = env.sb("s3_strip", [128, LC * 128], BF16)

    ps_mm = [env.ps(f"s3_psmm{i}", [128, 512], F32) for i in range(2)]
    ps_tp = [env.ps(f"s3_pstp{i}", [128, 512], BF16) for i in range(2)]
    ps_small = [env.ps(f"s3_pssm{i}", [128, 512], F32) for i in range(2)]
    ps_kl = [ps_small[0][:, 0:64], ps_small[1][:, 0:64]]
    ps_num = [ps_small[0][:, 64:130], ps_small[1][:, 64:130]]
    ps_nd = ps_small[0][0:65, 192:256]
    ps_ndT = ps_small[1][0:64, 192:257]
    ps_ph = [ps_mm[0][0:64, :], ps_mm[1][0:64, :]]

    ldx = env.sem("s3_ldx")
    tpx = env.sem("s3_tpx")
    evx = env.sem("s3_evx")
    mmp = env.sem("s3_mmp")
    evp = env.sem("s3_evp")
    tpv = env.sem("s3_tpv")
    evv = env.sem("s3_evv")
    tpk = env.sem("s3_tpk")
    evk = env.sem("s3_evk")
    ack = env.sem("s3_ack")
    rdk = env.sem("s3_rdk")
    mmk = env.sem("s3_mmk")
    ace = env.sem("s3_ace")
    mmn = env.sem("s3_mmn")
    evn = env.sem("s3_evn")
    tpn = env.sem("s3_tpn")
    mmf = env.sem("s3_mmf")
    acf = env.sem("s3_acf")
    mmu = env.sem("s3_mmu")
    evu = env.sem("s3_evu")
    stc = env.sem("s3_stc")
    zst = env.sem("s3_zst")

    for b in range(nb):
        for lt in range(LT):
            with nc.Block() as blk:
                lx0, tp0, ev0 = ldx.n, tpx.n, evx.n

                @blk.sync
                def _(s):
                    for j in range(4):
                        if j > 0:
                            s.wait_ge(tpx.h, tp0 + 8 * j)
                        ldx.inc(s.dma_start(out=x2t[:, :],
                                            in_=dd["x2_tm"][b * L + lt * 512 + j * 128:
                                                            b * L + lt * 512 + (j + 1) * 128, :]), 16)
                        s.wait_ge(ldx.h, ldx.n)

                @blk.tensor
                def _(te):
                    for j in range(4):
                        te.wait_ge(ldx.h, lx0 + 16 * (j + 1))
                        for i in range(DCH):
                            k = j * DCH + i
                            if k >= 2:
                                te.wait_ge(evx.h, ev0 + k - 1)
                            tpx.inc(te.transpose(ps_tp[k % 2][:, 0:128],
                                                 x2t[:, i * 128:(i + 1) * 128], identbf[:, :]))

                @blk.vector
                def _(v):
                    for j in range(4):
                        for i in range(DCH):
                            k = j * DCH + i
                            v.wait_ge(tpx.h, tp0 + k + 1)
                            evx.inc(v.tensor_copy(
                                x2fm[:, i * L + lt * 512 + j * 128: i * L + lt * 512 + (j + 1) * 128],
                                ps_tp[k % 2][:, 0:128]))

        for hp in range(NP):
            po = hp * 128

            with nc.Block() as blk:
                mm0, ev0 = mmp.n, evp.n

                @blk.tensor
                def _(te):
                    for dst, wsb, _bias in ((Qp, wq_sb, bq_sb), (Kp, wk_sb, bk_sb), (Vf, wv_sb, bv_sb)):
                        for lt in range(LT):
                            k = mmp.n - mm0
                            if k >= 2:
                                te.wait_ge(evp.h, ev0 + k - 1)
                            for i in range(DCH):
                                ins = te.matmul(ps_mm[k % 2][:, :],
                                                wsb[:, i * D + po: i * D + po + 128],
                                                x2fm[:, i * L + lt * 512: i * L + (lt + 1) * 512],
                                                start=(i == 0), stop=(i == DCH - 1))
                            mmp.inc(ins)

                @blk.vector
                def _(v):
                    for di, (dst, wsb, bias) in enumerate(
                            ((Qp, wq_sb, bq_sb), (Kp, wk_sb, bk_sb), (Vf, wv_sb, bv_sb))):
                        for lt in range(LT):
                            k = di * LT + lt
                            v.wait_ge(mmp.h, mm0 + k + 1)
                            evp.inc(v.tensor_scalar_add(dst[:, lt * 512:(lt + 1) * 512],
                                                        ps_mm[k % 2][:, :], bias[:, hp:hp + 1]))

            with nc.Block() as blk:
                tp0, ev0 = tpv.n, evv.n

                @blk.vector
                def _(v):
                    v.memset(Vaug[:, :], 1.0)

                @blk.tensor
                def _(te):
                    for lc in range(LC):
                        if lc >= 2:
                            te.wait_ge(evv.h, ev0 + lc - 1)
                        tpv.inc(te.transpose(ps_tp[lc % 2][:, 0:128],
                                             Vf[:, lc * 128:(lc + 1) * 128], identbf[:, :]))

                @blk.vector
                def _(v):
                    for lc in range(LC):
                        v.wait_ge(tpv.h, tp0 + lc + 1)
                        v.tensor_copy(Vaug[:, lc * 130: lc * 130 + 64],
                                      ps_tp[lc % 2][:, 0:64])
                        evv.inc(v.tensor_copy(Vaug[:, lc * 130 + 65: lc * 130 + 129],
                                              ps_tp[lc % 2][:, 64:128]))

            with nc.Block() as blk:
                tp0, ev0 = tpk.n, evk.n

                @blk.tensor
                def _(te):
                    for lc in range(LC):
                        if lc >= 2:
                            te.wait_ge(evk.h, ev0 + lc - 1)
                        tpk.inc(te.transpose(ps_tp[lc % 2][:, 0:128],
                                             Kp[:, lc * 128:(lc + 1) * 128], identbf[:, :]))

                @blk.vector
                def _(v):
                    for lc in range(LC):
                        v.wait_ge(tpk.h, tp0 + lc + 1)
                        evk.inc(v.tensor_copy(Ktm[:, lc * 128:(lc + 1) * 128],
                                              ps_tp[lc % 2][:, 0:128]))

            with nc.Block() as blk:
                ac0, rd0 = ack.n, rdk.n

                @blk.scalar
                def _(sc):
                    for lc in range(LC):
                        if lc >= 1:
                            sc.wait_ge(rdk.h, rd0 + lc)
                        ack.inc(sc.activation(ksq[:, :], Ktm[:, lc * 128:(lc + 1) * 128],
                                              AF.Square))

                @blk.vector
                def _(v):
                    for lc in range(LC):
                        v.wait_ge(ack.h, ac0 + lc + 1)
                        v.reduce_sum(ssqp[:, lc:lc + 1], ksq[:, 0:64], axis=AX.X)
                        rdk.inc(v.reduce_sum(ssqp[:, LC + lc:LC + lc + 1],
                                             ksq[:, 64:128], axis=AX.X))

            with nc.Block() as blk:
                @blk.vector
                def _(v):
                    v.tensor_scalar_mul(ssqp[:, :], ssqp[:, :], -0.5)

            with nc.Block() as blk:
                mm0, ae0 = mmk.n, ace.n

                @blk.tensor
                def _(te):
                    for h2 in range(2):
                        for lc in range(LC):
                            k = h2 * LC + lc
                            if k >= 2:
                                te.wait_ge(ace.h, ae0 + k - 1)
                            mmk.inc(te.matmul(ps_kl[k % 2],
                                              Kp[h2 * 64:(h2 + 1) * 64, lc * 128:(lc + 1) * 128],
                                              w0_sb[h2 * 64:(h2 + 1) * 64, hp * 64:(hp + 1) * 64],
                                              start=True, stop=True))

                @blk.vector
                def _(v):
                    for h2 in range(2):
                        for lc in range(LC):
                            k = h2 * LC + lc
                            v.wait_ge(mmk.h, mm0 + k + 1)
                            ace.inc(v.reduce_max(mxk[:, k:k + 1], ps_kl[k % 2], axis=AX.X))

            with nc.Block() as blk:
                @blk.vector
                def _(v):
                    v.tensor_add(mxk[:, :], mxk[:, :], ssqp[:, :])

            with nc.Block() as blk:
                @blk.vector
                def _(v):
                    v.reduce_max(mx2[:, 0:1], mxk[:, 0:LC], axis=AX.X)
                    v.reduce_max(mx2[:, 1:2], mxk[:, LC:2 * LC], axis=AX.X)

            with nc.Block() as blk:
                @blk.tensor
                def _(te):
                    te.transpose(ps_small[1][0:2, 256:384], mx2[:, :], identf[:, :])

            with nc.Block() as blk:
                @blk.vector
                def _(v):
                    v.tensor_copy(mrow2[0:2, :], ps_small[1][0:2, 256:384])

            with nc.Block() as blk:
                @blk.vector
                def _(v):
                    v.reduce_max(mb21[0:2, :], mrow2[0:2, :], axis=AX.X)

            with nc.Block() as blk:
                @blk.tensor
                def _(te):
                    te.transpose(ps_small[1][0:1, 384:386], mb21[0:2, :], identf[0:2, 0:2])

            with nc.Block() as blk:
                @blk.vector
                def _(v):
                    v.tensor_copy(mr12[:, :], ps_small[1][0:1, 384:386])

            with nc.Block() as blk:
                @blk.sync
                def _(s):
                    zst.inc(s.dma_start(out=scr_hbm[0:1, 2:4], in_=mr12[:, :]), 16)
                    s.wait_ge(zst.h, zst.n)
                    zst.inc(s.dma_start(out=mh2[:, 0:1],
                                        in_=scr_hbm[0:1, 2:3].broadcast_to((128, 1))), 16)
                    zst.inc(s.dma_start(out=mh2[:, 1:2],
                                        in_=scr_hbm[0:1, 3:4].broadcast_to((128, 1))), 16)
                    s.wait_ge(zst.h, zst.n)

            with nc.Block() as blk:
                @blk.vector
                def _(v):
                    for h2 in range(2):
                        v.tensor_single_scalar(ssqb[:, h2 * LC:(h2 + 1) * LC],
                                               ssqp[:, h2 * LC:(h2 + 1) * LC],
                                               mh2[:, h2:h2 + 1], ALU.subtract)

            with nc.Block() as blk:
                mm0, ae0 = mmk.n, ace.n

                @blk.tensor
                def _(te):
                    for h2 in range(2):
                        for lc in range(LC):
                            k = h2 * LC + lc
                            if k >= 2:
                                te.wait_ge(ace.h, ae0 + k - 1)
                            mmk.inc(te.matmul(ps_kl[k % 2],
                                              Kp[h2 * 64:(h2 + 1) * 64, lc * 128:(lc + 1) * 128],
                                              w0_sb[h2 * 64:(h2 + 1) * 64, hp * 64:(hp + 1) * 64],
                                              start=True, stop=True))

                @blk.scalar
                def _(sc):
                    for h2 in range(2):
                        for lc in range(LC):
                            k = h2 * LC + lc
                            sc.wait_ge(mmk.h, mm0 + k + 1)
                            ace.inc(sc.activation(
                                e_pair[:, h2 * LC * 64 + lc * 64: h2 * LC * 64 + (lc + 1) * 64],
                                ps_kl[k % 2], AF.Exp,
                                bias=ssqb[:, h2 * LC + lc: h2 * LC + lc + 1],
                                scale=1.0))

            for h2 in range(2):
                with nc.Block() as blk:
                    @blk.tensor
                    def _(te):
                        for lc in range(LC):
                            ins = te.matmul(ps_nd,
                                            Vaug[:, lc * 130 + h2 * 65: lc * 130 + (h2 + 1) * 65],
                                            e_pair[:, h2 * LC * 64 + lc * 64: h2 * LC * 64 + (lc + 1) * 64],
                                            start=(lc == 0), stop=(lc == LC - 1))
                        mmn.inc(ins)

                with nc.Block() as blk:
                    @blk.vector
                    def _(v):
                        v.tensor_copy(nd_sb[:, :], ps_nd)

                with nc.Block() as blk:
                    @blk.vector
                    def _(v):
                        v.reduce_sum(znd[64:65, h2:h2 + 1], nd_sb[64:65, :], axis=AX.X)

                with nc.Block() as blk:
                    @blk.tensor
                    def _(te):
                        te.transpose(ps_ndT, nd_sb[:, :], identf[0:65, 0:65])

                with nc.Block() as blk:
                    @blk.vector
                    def _(v):
                        v.memset(ndT[:, 65:66], 1.0)
                        v.tensor_copy(ndT[:, 0:65], ps_ndT)

                with nc.Block() as blk:
                    mm0, cq0 = mmf.n, acf.n

                    @blk.tensor
                    def _(te):
                        for lc in range(LC):
                            if lc >= 2:
                                te.wait_ge(acf.h, cq0 + lc - 1)
                            mmf.inc(te.matmul(ps_kl[lc % 2],
                                              Qp[h2 * 64:(h2 + 1) * 64, lc * 128:(lc + 1) * 128],
                                              w0_sb[h2 * 64:(h2 + 1) * 64, hp * 64:(hp + 1) * 64],
                                              start=True, stop=True))

                    @blk.vector
                    def _(v):
                        for lc in range(LC):
                            v.wait_ge(mmf.h, mm0 + lc + 1)
                            v.reduce_max(mxq[:, lc:lc + 1], ps_kl[lc % 2], axis=AX.X)
                            acf.inc(v.tensor_copy(qlogt[:, lc * 64:(lc + 1) * 64],
                                                  ps_kl[lc % 2]))

                with nc.Block() as blk:
                    @blk.vector
                    def _(v):
                        v.tensor_scalar_mul(mxqn[:, :], mxq[:, :], -1.0)

                with nc.Block() as blk:
                    @blk.scalar
                    def _(sc):
                        for lc in range(LC):
                            sc.activation(phitm[:, lc * 64:(lc + 1) * 64],
                                          qlogt[:, lc * 64:(lc + 1) * 64], AF.Exp,
                                          bias=mxqn[:, lc:lc + 1], scale=1.0)

                with nc.Block() as blk:
                    tp0, ev0 = tpn.n, mmn.n

                    @blk.tensor
                    def _(te):
                        for lc in range(LC):
                            if lc >= 2:
                                te.wait_ge(mmn.h, ev0 + lc - 1)
                            tpn.inc(te.transpose(ps_tp[lc % 2][0:64, 0:128],
                                                 phitm[:, lc * 64:(lc + 1) * 64],
                                                 identbf[:, :]))

                    @blk.vector
                    def _(v):
                        for lc in range(LC):
                            v.wait_ge(tpn.h, tp0 + lc + 1)
                            mmn.inc(v.tensor_copy(phih[:, lc * 128:(lc + 1) * 128],
                                                  ps_tp[lc % 2][0:64, 0:128]))

                with nc.Block() as blk:
                    mm0, ev0 = mmu.n, evu.n

                    @blk.tensor
                    def _(te):
                        for lc in range(LC):
                            if lc >= 2:
                                te.wait_ge(evu.h, ev0 + lc - 1)
                            mmu.inc(te.matmul(ps_num[lc % 2],
                                              phih[:, lc * 128:(lc + 1) * 128],
                                              ndT[:, :], start=True, stop=True))

                    @blk.vector
                    def _(v):
                        for lc in range(LC):
                            v.wait_ge(mmu.h, mm0 + lc + 1)
                            evu.inc(v.tensor_copy(num_sb[:, lc * 66:(lc + 1) * 66],
                                                  ps_num[lc % 2]))

                with nc.Block() as blk:
                    @blk.sync
                    def _(s):
                        zst.inc(s.dma_start(out=scr_hbm[0:1, h2:h2 + 1],
                                            in_=znd[64:65, h2:h2 + 1]), 16)
                        s.wait_ge(zst.h, zst.n)
                        zst.inc(s.dma_start(out=ez_bc[:, h2:h2 + 1],
                                            in_=scr_hbm[0:1, h2:h2 + 1].broadcast_to((128, 1))), 16)
                        s.wait_ge(zst.h, zst.n)

                with nc.Block() as blk:
                    @blk.vector
                    def _(v):
                        v.tensor_copy(dent.rearrange("p (c o) -> p c o", o=1)[:, :, :],
                                      num_sb.rearrange("p (c k) -> p c k", k=66)[:, :, 64:65])
                        v.tensor_copy(slt.rearrange("p (c o) -> p c o", o=1)[:, :, :],
                                      num_sb.rearrange("p (c k) -> p c k", k=66)[:, :, 65:66])

                with nc.Block() as blk:
                    @blk.vector
                    def _(v):
                        selfsem(v, v.tensor_scalar_mul(ez1[:, :], ez_bc[:, h2:h2 + 1], EPS))
                        selfsem(v, v.scalar_tensor_tensor(dent[:, :], slt[:, :], ez1[:, :],
                                                          dent[:, :], ALU.mult, ALU.add))
                        selfsem(v, v.reciprocal(rlt[:, :], dent[:, :]))
                        for lc in range(LC):
                            v.tensor_scalar_mul(
                                strip[:, lc * 128 + h2 * 64: lc * 128 + h2 * 64 + 64],
                                num_sb[:, lc * 66: lc * 66 + 64], rlt[:, lc:lc + 1])

            with nc.Block() as blk:
                @blk.sync
                def _(s):
                    stc.inc(s.dma_start(
                        out=ctx_hbm[b * L:(b + 1) * L, po:po + 128]
                            .rearrange("(c p) f -> p c f", p=128),
                        in_=strip.rearrange("p (c f) -> p c f", f=128)), 16)
                    s.wait_ge(stc.h, stc.n)
    env.pop_pool()

    env.push_pool()
    wo_sb = env.sb("s3_wo", [128, DCH * D], BF16)
    bo_bc = env.sb("s3_bobc", [128, D], BF16)
    ldo = env.sem("s3_ldo")
    with nc.Block() as blk:
        @blk.gpsimd
        def _(g):
            ldo.inc(g.dma_start(out=wo_sb[:, :],
                                in_=dd["woT"].rearrange("(ic p) o -> p ic o", p=128)), 16)

        @blk.sync
        def _(s):
            s.wait_ge(ldo.h, ldo.n)
    emit_bcast_rows(env, "s3bo", dd["bo_row"], 1, D, [bo_bc], consts)

    ctxt = env.sb("s3_ctxt", [128, 4 * 1024], BF16)
    x2r = env.sb("s3_x2r", [128, 4 * 1024], BF16)
    cfm = env.sb("s3_cfm", [128, DCH * 128], BF16)
    y3 = env.sb("s3_y3", [128, 1024], BF16)
    v4 = env.sb("s3_v4", [128, 4 * 1024], F32)
    o3 = env.sb("s3_o3", [128, 4 * 1024], BF16)
    lnst3 = mk_ln_stats(env, "s3ln")
    ps_tp3 = [env.ps(f"s3_pstp3{i}", [128, 512], BF16) for i in range(2)]
    ps_mm3 = [env.ps(f"s3_psmm3{i}", [128, 512], F32) for i in range(2)]

    ldc = env.sem("s3_ldc")
    tpc = env.sem("s3_tpc")
    evc = env.sem("s3_evc")
    mmo = env.sem("s3_mmo")
    evo = env.sem("s3_evo")
    st3 = env.sem("s3_st3")

    for t in range(T // 512):
        with nc.Block() as blk:
            ld0 = ldc.n

            @blk.sync
            def _(s):
                for j in range(4):
                    r0 = t * 512 + j * 128
                    ldc.inc(s.dma_start(out=ctxt[:, j * 1024:(j + 1) * 1024],
                                        in_=ctx_hbm[r0:r0 + 128, :]), 16)
                    ldc.inc(s.dma_start(out=x2r[:, j * 1024:(j + 1) * 1024],
                                        in_=dd["x2_tm"][r0:r0 + 128, :]), 16)

        for j in range(4):
            with nc.Block() as blk:
                tp0, ev0 = tpc.n, evc.n

                @blk.tensor
                def _(te):
                    te.wait_ge(ldc.h, ldc.n)
                    for g in range(2):
                        if g >= 2:
                            te.wait_ge(evc.h, ev0 + g - 1)
                        for ii in range(4):
                            i = g * 4 + ii
                            ins = te.transpose(ps_tp3[g % 2][:, ii * 128:(ii + 1) * 128],
                                               ctxt[:, j * 1024 + i * 128: j * 1024 + (i + 1) * 128],
                                               identbf[:, :])
                        tpc.inc(ins)

                @blk.vector
                def _(v):
                    for g in range(2):
                        v.wait_ge(tpc.h, tp0 + g + 1)
                        evc.inc(v.tensor_copy(
                            cfm.rearrange("p (i n) -> p i n", n=128)[:, g * 4:(g + 1) * 4, :],
                            ps_tp3[g % 2].rearrange("p (i n) -> p i n", n=128)[:, :, :]))

            with nc.Block() as blk:
                mm0, eo0 = mmo.n, evo.n

                @blk.tensor
                def _(te):
                    for o in range(2):
                        for c in range(DCH):
                            ins = te.matmul(ps_mm3[o % 2][:, :],
                                            cfm[:, c * 128:(c + 1) * 128],
                                            wo_sb[:, c * D + o * 512: c * D + (o + 1) * 512],
                                            start=(c == 0), stop=(c == DCH - 1))
                        mmo.inc(ins)

                @blk.vector
                def _(v):
                    for o in range(2):
                        v.wait_ge(mmo.h, mm0 + o + 1)
                        evo.inc(v.scalar_tensor_tensor(y3[:, o * 512:(o + 1) * 512],
                                                       ps_mm3[o % 2][:, :], 1.0,
                                                       bo_bc[:, o * 512:(o + 1) * 512],
                                                       ALU.mult, ALU.add))
                    v.tensor_add(v4[:, j * 1024:(j + 1) * 1024],
                                 x2r[:, j * 1024:(j + 1) * 1024], y3[:, :])

        emit_ln_tail(env, consts, v4, o3, 4, lnst3)
        with nc.Block() as blk:
            @blk.sync
            def _(s):
                for j in range(4):
                    st3.inc(s.dma_start(
                        out=dd["x3_tm"][t * 512 + j * 128: t * 512 + (j + 1) * 128, :],
                        in_=o3[:, j * 1024:(j + 1) * 1024]), 16)
                s.wait_ge(st3.h, st3.n)
    env.pop_pool()


def emit_s1(env, dd, T, consts, S=4096):
    nc = env.nc
    nb = T // S
    RB = S // 16
    NSB = RB // 128
    NKT = S // 128
    HALF = S // 2
    NQC = HALF // 128
    identbf = consts["identbf"]
    identf = consts["identf"]

    y_hbm = nc.dram_tensor("s1_y", [T, D], BF16).ap()

    qs = env.sem("s1_qs")

    def selfsem(v, ins):
        qs.n += 1
        ins.then_inc(qs.h, 1)
        v.wait_ge(qs.h, qs.n)

    env.push_pool()
    wq_sb = env.sb("s1_wq", [128, DCH * D], BF16)
    wk_sb = env.sb("s1_wk", [128, DCH * D], BF16)
    wv_sb = env.sb("s1_wv", [128, DCH * D], BF16)
    ldw = env.sem("s1_ldw")
    with nc.Block() as blk:
        @blk.gpsimd
        def _(g):
            for sb, key in ((wq_sb, "wqT"), (wk_sb, "wkT"), (wv_sb, "wvT")):
                ldw.inc(g.dma_start(out=sb[:, :],
                                    in_=dd[key].rearrange("(ic p) o -> p ic o", p=128)), 16)

        @blk.sync
        def _(s):
            s.wait_ge(ldw.h, ldw.n)

    xf_sb = env.sb("s1_xf", [128, DCH * RB], F32)
    xb_sb = env.sb("s1_xb", [128, DCH * RB], BF16)
    xtm_sb = env.sb("s1_xtm", [128, NSB * 1024], F32)
    Yq = env.sb("s1_Yq", [128, DCH * RB], BF16)
    Yk = env.sb("s1_Yk", [128, DCH * RB], BF16)
    Yqa = env.sb("s1_Yqa", [128, DCH * RB], BF16)
    Vaug = env.sb("s1_Vaug", [128, NKT * 65], BF16)
    tmsq_all = env.sb("s1_tmsqa", [128, 2 * DCH * NSB * 128], BF16)
    ssqall = env.sb("s1_ssqall", [128, 4 * DCH * NSB], F32)
    mx1 = env.sb("s1_mx1", [128, 2], F32)
    s2r = env.sb("s1_s2r", [1, 256], F32)
    m11 = env.sb("s1_m11", [1, 4], F32)
    cb1 = env.sb("s1_cb1", [1, 1], BF16)
    cbias = env.sb("s1_cbias", [128, 1], F32)
    e_sb = [env.sb(f"s1_e{i}", [128, 2048], BF16) for i in range(2)]
    e1_sb = [env.sb(f"s1_ea{i}", [128, 2048], BF16) for i in range(2)]
    pv_sb = env.sb("s1_pv", [65, 2048], F32)
    rl1 = env.sb("s1_rl1", [128, 1], F32)
    rl4 = env.sb("s1_rl4", [128, 4], F32)
    ytm = env.sb("s1_ytm", [128, NSB * 1024], BF16)

    ldx = env.sem("s1_ldx")
    mmp = env.sem("s1_mmp")
    evp = env.sem("s1_evp")
    alt = env.sem("s1_alt")
    tpb = env.sem("s1_tpb")
    acb = env.sem("s1_acb")
    rdb = env.sem("s1_rdb")
    tsc = env.sem("s1_tsc")
    aex = env.sem("s1_aex")
    psq = env.sem("s1_psq")
    vsq = env.sem("s1_vsq")
    tpv2 = env.sem("s1_tpv")
    tpo = env.sem("s1_tpo")
    evo = env.sem("s1_evo")
    sty = env.sem("s1_sty")

    for inst in range(nb * 16):
        b, g = inst // 16, inst % 16
        c0 = b * S + g * RB

        with nc.Block() as blk:
            lx0 = ldx.n

            @blk.sync
            def _(s):
                ldx.inc(s.dma_start(
                    out=xf_sb.rearrange("p (i r) -> p i r", r=RB),
                    in_=dd["x_fm"].rearrange("(i p) t -> p i t", p=128)[:, :, c0:c0 + RB]), 16)
                for jj in range(NSB):
                    ldx.inc(s.dma_start(out=xtm_sb[:, jj * 1024:(jj + 1) * 1024],
                                        in_=dd["x_tm"][c0 + jj * 128: c0 + (jj + 1) * 128, :]), 16)
                s.wait_ge(ldx.h, ldx.n)

            @blk.vector
            def _(v):
                v.wait_ge(ldx.h, lx0 + 16 * (NSB + 1))
                v.tensor_copy(xb_sb[:, :], xf_sb[:, :])

        env.push_pool()
        ps_a = [env.ps(f"s1_psa{i}_{inst}", [128, 512], F32) for i in range(2)]
        ps_tpb4 = [env.ps(f"s1_pstpb{i}_{inst}", [128, 512], BF16) for i in range(2)]
        with nc.Block() as blk:
            mm0, ev0 = mmp.n, evp.n

            @blk.tensor
            def _(te):
                k = 0
                for dst, wsb in ((Yq, wq_sb), (Yk, wk_sb)):
                    for o in range(DCH):
                        if k >= 2:
                            te.wait_ge(evp.h, ev0 + k - 1)
                        for i in range(DCH):
                            ins = te.matmul(ps_a[k % 2][:, 0:RB],
                                            wsb[:, i * D + o * 128: i * D + (o + 1) * 128],
                                            xb_sb[:, i * RB:(i + 1) * RB],
                                            start=(i == 0), stop=(i == DCH - 1))
                        mmp.inc(ins)
                        k += 1
                for jj in range(NSB):
                    for o in range(2):
                        if k >= 2:
                            te.wait_ge(evp.h, ev0 + k - 1)
                        for i in range(DCH):
                            ins = te.matmul(ps_a[k % 2][:, :],
                                            xb_sb[:, i * RB + jj * 128: i * RB + (jj + 1) * 128],
                                            wv_sb[:, i * D + o * 512: i * D + (o + 1) * 512],
                                            start=(i == 0), stop=(i == DCH - 1))
                        mmp.inc(ins)
                        k += 1

            @blk.vector
            def _(v):
                v.memset(Vaug[:, :], 1.0)
                k = 0
                for dst, wsb in ((Yq, wq_sb), (Yk, wk_sb)):
                    for o in range(DCH):
                        v.wait_ge(mmp.h, mm0 + k + 1)
                        if dst is Yq:
                            evp.inc(v.tensor_scalar_mul(dst[:, o * RB:(o + 1) * RB],
                                                        ps_a[k % 2][:, 0:RB], 0.5))
                        else:
                            evp.inc(v.tensor_copy(dst[:, o * RB:(o + 1) * RB], ps_a[k % 2][:, 0:RB]))
                        k += 1
                for jj in range(NSB):
                    for o in range(2):
                        v.wait_ge(mmp.h, mm0 + k + 1)
                        for uu in range(8):
                            ku = o * 8 + uu
                            kt = ku * NSB + jj
                            ins = v.tensor_copy(Vaug[:, kt * 65: kt * 65 + 64],
                                                ps_a[k % 2][:, uu * 64:(uu + 1) * 64])
                        evp.inc(ins)
                        k += 1

        with nc.Block() as blk:
            @blk.sync
            def _(s):
                alt.inc(s.dma_start(out=Yqa[0:64, :], in_=Yq[64:128, :]), 16)
                alt.inc(s.dma_start(out=Yqa[64:128, :], in_=Yq[0:64, :]), 16)
                s.wait_ge(alt.h, alt.n)

        ntp = DCH * NSB
        with nc.Block() as blk:
            tp0, ac0 = tpb.n, acb.n

            @blk.tensor
            def _(te):
                for src_i, src in enumerate((Yq, Yk)):
                    for grp in range(ntp // 4):
                        g4 = src_i * (ntp // 4) + grp
                        if g4 >= 2:
                            te.wait_ge(acb.h, ac0 + g4 - 1)
                        for ii in range(4):
                            jc = grp * 4 + ii
                            ins = te.transpose(ps_tpb4[g4 % 2][:, ii * 128:(ii + 1) * 128],
                                               src[:, jc * 128:(jc + 1) * 128], identbf[:, :])
                        tpb.inc(ins)

            @blk.vector
            def _(v):
                for g4 in range(2 * (ntp // 4)):
                    v.wait_ge(tpb.h, tp0 + g4 + 1)
                    acb.inc(v.tensor_copy(tmsq_all[:, g4 * 512:(g4 + 1) * 512],
                                          ps_tpb4[g4 % 2][:, :]))

        with nc.Block() as blk:
            @blk.scalar
            def _(sc):
                sc.activation(tmsq_all[:, :], tmsq_all[:, :], AF.Square)

        with nc.Block() as blk:
            @blk.vector
            def _(v):
                v.reduce_sum(ssqall.rearrange("p (k o) -> p k o", o=1)[:, :, :],
                             tmsq_all.rearrange("p (k f) -> p k f", f=64)[:, :, :],
                             axis=AX.X)

        with nc.Block() as blk:
            @blk.vector
            def _(v):
                v.reduce_max(mx1[:, 0:1], ssqall[:, 0:2 * ntp], axis=AX.X)
                v.reduce_max(mx1[:, 1:2], ssqall[:, 2 * ntp:4 * ntp], axis=AX.X)

        with nc.Block() as blk:
            @blk.tensor
            def _(te):
                te.transpose(ps_a[0][0:1, 0:128], mx1[:, 0:1], identf[:, :])
                te.transpose(ps_a[0][0:1, 128:256], mx1[:, 1:2], identf[:, :])

        with nc.Block() as blk:
            @blk.vector
            def _(v):
                v.tensor_copy(s2r[:, :], ps_a[0][0:1, 0:256])

        with nc.Block() as blk:
            @blk.vector
            def _(v):
                v.reduce_max(m11[:, 0:1], s2r[:, 0:128], axis=AX.X)
                v.reduce_max(m11[:, 1:2], s2r[:, 128:256], axis=AX.X)

        with nc.Block() as blk:
            @blk.vector
            def _(v):
                v.tensor_mul(m11[:, 2:3], m11[:, 0:1], m11[:, 1:2])

        with nc.Block() as blk:
            @blk.scalar
            def _(sc):
                sc.activation(m11[:, 3:4], m11[:, 2:3], AF.Sqrt)

        with nc.Block() as blk:
            @blk.vector
            def _(v):
                selfsem(v, v.tensor_scalar_mul(m11[:, 2:3], m11[:, 3:4], -1.0 / 8.0))
                v.tensor_copy(cb1[:, :], m11[:, 2:3])

        with nc.Block() as blk:
            @blk.tensor
            def _(te):
                te.matmul(ps_a[1][:, 0:1], consts["onesbf"][:, :], cb1[:, :],
                          start=True, stop=True)

        with nc.Block() as blk:
            @blk.vector
            def _(v):
                v.tensor_copy(cbias[:, :], ps_a[1][:, 0:1])
        env.pop_pool()

        env.push_pool()
        ps_scq = [env.ps(f"s1_psscA_{inst}", [128, 4 * RB], F32),
                  env.ps(f"s1_psscB_{inst}", [128, 4 * RB], F32)]
        ps_pv = env.ps(f"s1_pspv_{inst}", [65, HALF], F32)
        QW = 4 * RB
        NMM = QW // 512 if QW >= 512 else 1
        for qh in range(2):
            with nc.Block() as blk:
                ts0, ae0, tv0, vs0 = tsc.n, aex.n, tpv2.n, vsq.n

                def sc_mms(te, kt, qq):
                    ku = kt // NSB
                    r0 = (kt % NSB) * 128
                    kb = (ku % 2) * 64
                    lhsT = Yk[kb:kb + 64, (ku // 2) * RB + r0:(ku // 2) * RB + r0 + 128]
                    srcq = Yq if qq == (ku % 2) else Yqa
                    w = QW // NMM
                    for jj in range(NMM):
                        rhs = srcq[kb:kb + 64,
                                   (qh * 4) * RB + jj * w:(qh * 4) * RB + (jj + 1) * w]
                        ins = te.matmul(ps_scq[qq][:, jj * w:(jj + 1) * w], lhsT, rhs,
                                        start=True, stop=True)
                    return ins

                def pv_mms(te, kt):
                    for qg in range(HALF // 512):
                        ins = te.matmul(ps_pv[:, qg * 512:(qg + 1) * 512],
                                        Vaug[:, kt * 65:(kt + 1) * 65],
                                        e_sb[kt % 2][:, qg * 512:(qg + 1) * 512],
                                        start=(kt == 0), stop=(kt == NKT - 1))
                    return ins

                @blk.tensor
                def _(te):
                    for kt in range(NKT):
                        for qq in range(2):
                            if kt >= 1:
                                te.wait_ge(aex.h, ae0 + 2 * (kt - 1) + qq + 1)
                            tsc.inc(sc_mms(te, kt, qq))
                        if kt >= 1:
                            te.wait_ge(vsq.h, vs0 + kt)
                            tpv2.inc(pv_mms(te, kt - 1))
                    te.wait_ge(vsq.h, vs0 + NKT)
                    tpv2.inc(pv_mms(te, NKT - 1))

                @blk.scalar
                def _(sc):
                    for kt in range(NKT):
                        for qq in range(2):
                            sc.wait_ge(tsc.h, ts0 + 2 * kt + qq + 1)
                            if kt >= 1 and qq == 0:
                                sc.wait_ge(vsq.h, vs0 + kt - 1)
                            aex.inc(sc.activation(
                                e1_sb[kt % 2][:, qq * QW:(qq + 1) * QW],
                                ps_scq[qq][:, 0:QW], AF.Exp, bias=cbias[:, :], scale=1.0))

                @blk.vector
                def _(v):
                    for kt in range(NKT):
                        v.wait_ge(aex.h, ae0 + 2 * kt + 2)
                        if kt >= 2:
                            v.wait_ge(tpv2.h, tv0 + kt - 1)
                        vsq.inc(v.tensor_mul(e_sb[kt % 2][:, 0:HALF],
                                             e1_sb[kt % 2][:, 0:HALF],
                                             e1_sb[kt % 2][:, 0:HALF]))

            with nc.Block() as blk:
                @blk.vector
                def _(v):
                    v.tensor_copy(pv_sb[:, 0:HALF], ps_pv[:, :])

            with nc.Block() as blk:
                to0, eo0 = tpo.n, evo.n
                NG = NQC // 4

                @blk.tensor
                def _(te):
                    for g in range(NG):
                        if g >= 2:
                            te.wait_ge(evo.h, eo0 + g - 1)
                        for ii in range(4):
                            qc = g * 4 + ii
                            ins = te.transpose(ps_scq[g % 2][:, ii * 65:(ii + 1) * 65],
                                               pv_sb[:, qc * 128:(qc + 1) * 128],
                                               identf[0:65, 0:65])
                        tpo.inc(ins)

                @blk.vector
                def _(v):
                    for g in range(NG):
                        v.wait_ge(tpo.h, to0 + g + 1)
                        for ii in range(4):
                            ins = v.tensor_copy(rl4[:, ii:ii + 1],
                                                ps_scq[g % 2][:, ii * 65 + 64: ii * 65 + 65])
                        selfsem(v, ins)
                        selfsem(v, v.reciprocal(rl4[:, :], rl4[:, :]))
                        for ii in range(4):
                            qc = g * 4 + ii
                            col = qc * 128
                            quarter = col // QW
                            c2 = col % QW
                            qu = qh * 8 + 2 * (c2 // RB) + quarter
                            sb0 = (c2 % RB) // 128
                            ins = v.scalar_tensor_tensor(
                                ytm[:, sb0 * 1024 + qu * 64: sb0 * 1024 + (qu + 1) * 64],
                                ps_scq[g % 2][:, ii * 65: ii * 65 + 64],
                                rl4[:, ii:ii + 1],
                                xtm_sb[:, sb0 * 1024 + qu * 64: sb0 * 1024 + (qu + 1) * 64],
                                ALU.mult, ALU.add)
                        evo.inc(ins)
        env.pop_pool()

        with nc.Block() as blk:
            @blk.sync
            def _(s):
                for jj in range(NSB):
                    sty.inc(s.dma_start(out=y_hbm[c0 + jj * 128: c0 + (jj + 1) * 128, :],
                                        in_=ytm[:, jj * 1024:(jj + 1) * 1024]), 16)
                s.wait_ge(sty.h, sty.n)
    env.pop_pool()
    return y_hbm


def emit_s1b(env, dd, y_hbm, T, consts, S=4096):
    nc = env.nc
    nb = T // S
    NBT = S // 128
    ntiles = T // 128
    GN = S * 1024
    identf = consts["identf"]

    env.push_pool()
    ybuf = env.sb("s1b_y", [128, 2 * 1024], BF16)
    ysq = env.sb("s1b_ysq", [128, 1024], BF16)
    ys = env.sb("s1b_ys", [128, ntiles], F32)
    sqs = env.sb("s1b_sqs", [128, ntiles], F32)
    bsum = env.sb("s1b_bsum", [128, 2], F32)
    brow2 = env.sb("s1b_brow2", [2, 128], F32)
    b21 = env.sb("s1b_b21", [2, 1], F32)
    bsc = env.sb("s1b_bsc", [1, 8], F32)
    mub = env.sb("s1b_mub", [128, 2], F32)
    rsb = env.sb("s1b_rsb", [128, 2], F32)
    scr_hbm = nc.dram_tensor("s1b_scr", [1, 4], F32).ap()
    ps_b = env.ps("s1b_ps", [128, 512], F32)

    qs = env.sem("s1b_qs")

    def selfsem(v, ins):
        qs.n += 1
        ins.then_inc(qs.h, 1)
        v.wait_ge(qs.h, qs.n)

    ldy = [env.sem("s1b_ldy0"), env.sem("s1b_ldy1")]
    rds = env.sem("s1b_rds")
    acq = env.sem("s1b_acq")
    rdq = env.sem("s1b_rdq")
    zst = env.sem("s1b_zst")

    with nc.Block() as blk:
        r0, a0, q0 = rds.n, acq.n, rdq.n
        ldy0 = [ldy[0].n, ldy[1].n]

        @blk.sync
        def _(s):
            for t in range(ntiles):
                p = t % 2
                if t >= 2:
                    s.wait_ge(rdq.h, q0 + t - 1)
                ldy[p].inc(s.dma_start(out=ybuf[:, p * 1024:(p + 1) * 1024],
                                       in_=y_hbm[t * 128:(t + 1) * 128, :]), 16)

        @blk.vector
        def _(v):
            for t in range(ntiles):
                p = t % 2
                v.wait_ge(ldy[p].h, ldy0[p] + 16 * (t // 2 + 1))
                rds.inc(v.reduce_sum(ys[:, t:t + 1], ybuf[:, p * 1024:(p + 1) * 1024], axis=AX.X))
                v.wait_ge(acq.h, a0 + t + 1)
                rdq.inc(v.reduce_sum(sqs[:, t:t + 1], ysq[:, :], axis=AX.X))

        @blk.scalar
        def _(sc):
            for t in range(ntiles):
                p = t % 2
                sc.wait_ge(ldy[p].h, ldy0[p] + 16 * (t // 2 + 1))
                if t >= 1:
                    sc.wait_ge(rdq.h, q0 + t)
                acq.inc(sc.activation(ysq[:, :], ybuf[:, p * 1024:(p + 1) * 1024], AF.Square))

    for b in range(nb):
        with nc.Block() as blk:
            @blk.vector
            def _(v):
                v.reduce_sum(bsum[:, 0:1], ys[:, b * NBT:(b + 1) * NBT], axis=AX.X)
                v.reduce_sum(bsum[:, 1:2], sqs[:, b * NBT:(b + 1) * NBT], axis=AX.X)

        with nc.Block() as blk:
            @blk.tensor
            def _(te):
                te.transpose(ps_b[0:2, 0:128], bsum[:, :], identf[:, :])

        with nc.Block() as blk:
            @blk.vector
            def _(v):
                v.tensor_copy(brow2[0:2, :], ps_b[0:2, 0:128])

        with nc.Block() as blk:
            @blk.vector
            def _(v):
                v.reduce_sum(b21[0:2, :], brow2[0:2, :], axis=AX.X)

        with nc.Block() as blk:
            @blk.tensor
            def _(te):
                te.transpose(ps_b[0:1, 0:2], b21[0:2, :], identf[0:2, 0:2])

        with nc.Block() as blk:
            @blk.vector
            def _(v):
                v.tensor_copy(bsc[:, 0:2], ps_b[0:1, 0:2])

        with nc.Block() as blk:
            @blk.vector
            def _(v):
                selfsem(v, v.tensor_scalar_mul(bsc[:, 2:3], bsc[:, 0:1], 1.0 / GN))
                selfsem(v, v.tensor_scalar_mul(bsc[:, 3:4], bsc[:, 1:2], 1.0 / GN))
                selfsem(v, v.tensor_mul(bsc[:, 4:5], bsc[:, 2:3], bsc[:, 2:3]))
                selfsem(v, v.tensor_sub(bsc[:, 5:6], bsc[:, 3:4], bsc[:, 4:5]))

        with nc.Block() as blk:
            @blk.scalar
            def _(sc):
                sc.activation(bsc[:, 6:7], bsc[:, 5:6], AF.Ln,
                              bias=consts["eps_gn"][0:1, :]).then_inc(qs.h, 1)
                qs.n += 1
                sc.wait_ge(qs.h, qs.n)
                sc.activation(bsc[:, 7:8], bsc[:, 6:7], AF.Exp, bias=0.0, scale=-0.5)

        with nc.Block() as blk:
            @blk.sync
            def _(s):
                zst.inc(s.dma_start(out=scr_hbm[0:1, 0:1], in_=bsc[:, 2:3]), 16)
                zst.inc(s.dma_start(out=scr_hbm[0:1, 1:2], in_=bsc[:, 7:8]), 16)
                s.wait_ge(zst.h, zst.n)
                zst.inc(s.dma_start(out=mub[:, b:b + 1],
                                    in_=scr_hbm[0:1, 0:1].broadcast_to((128, 1))), 16)
                zst.inc(s.dma_start(out=rsb[:, b:b + 1],
                                    in_=scr_hbm[0:1, 1:2].broadcast_to((128, 1))), 16)
                s.wait_ge(zst.h, zst.n)

    x1v = env.sb("s1b_x1v", [128, 1024], F32)
    xttile = env.sb("s1b_xt", [128, 4 * 1024], F32)
    v4 = env.sb("s1b_v4", [128, 4 * 1024], F32)
    o1 = env.sb("s1b_o1", [128, 4 * 1024], BF16)
    amall = env.sb("s1b_am", [128, ntiles], F32)
    lnst = mk_ln_stats(env, "s1bln")
    lda = env.sem("s1b_lda")
    sta = env.sem("s1b_sta")

    vcons = env.sem("s1b_vcons")
    for tt in range(T // 512):
        b = (tt * 512) // S
        with nc.Block() as blk:
            la0, vc0 = lda.n, vcons.n

            @blk.sync
            def _(s):
                for j in range(4):
                    r0 = tt * 512 + j * 128
                    if j >= 2:
                        s.wait_ge(vcons.h, vc0 + j - 1)
                    lda.inc(s.dma_start(out=ybuf[:, (j % 2) * 1024:(j % 2) * 1024 + 1024],
                                        in_=y_hbm[r0:r0 + 128, :]), 16)
                    s.wait_ge(lda.h, lda.n)
                    lda.inc(s.dma_start(out=xttile[:, j * 1024:(j + 1) * 1024],
                                        in_=dd["x_tm"][r0:r0 + 128, :]), 16)
                    s.wait_ge(lda.h, lda.n)

            @blk.vector
            def _(v):
                for j in range(4):
                    v.wait_ge(lda.h, la0 + 32 * (j + 1))
                    vcons.inc(v.tensor_scalar(x1v[:, :],
                                              ybuf[:, (j % 2) * 1024:(j % 2) * 1024 + 1024],
                                              mub[:, b:b + 1], rsb[:, b:b + 1],
                                              ALU.subtract, ALU.mult))
                    v.tensor_add(v4[:, j * 1024:(j + 1) * 1024],
                                 xttile[:, j * 1024:(j + 1) * 1024], x1v[:, :])

        emit_ln_tail(env, consts, v4, o1, 4, lnst)
        with nc.Block() as blk:
            @blk.vector
            def _(v):
                for j in range(4):
                    v.reduce_max(amall[:, tt * 4 + j: tt * 4 + j + 1],
                                 o1[:, j * 1024:(j + 1) * 1024], axis=AX.X,
                                 apply_absolute_value=True)

            @blk.sync
            def _(s):
                for j in range(4):
                    sta.inc(s.dma_start(
                        out=dd["x1_tm"][tt * 512 + j * 128: tt * 512 + (j + 1) * 128, :],
                        in_=o1[:, j * 1024:(j + 1) * 1024]), 16)
                s.wait_ge(sta.h, sta.n)

    with nc.Block() as blk:
        @blk.sync
        def _(s):
            sta.inc(s.dma_start(out=dd["amax1_c"][:, :], in_=amall[:, :]), 16)
            s.wait_ge(sta.h, sta.n)
    env.pop_pool()


def load_consts(env, dd):
    nc = env.nc
    identbf = env.sb("c_identbf", [128, 128], BF16)
    identf = env.sb("c_identf", [128, 128], F32)
    ones_bf = env.sb("c_onesbf", [1, 128], BF16)
    eps_ln = env.sb("c_epsln", [128, 1], F32)
    eps_gn = env.sb("c_epsgn", [128, 1], F32)
    ld = env.sem("c_ld")
    with nc.Block() as blk:
        @blk.sync
        def _(s):
            ld.inc(s.dma_start(out=identbf[:, :], in_=dd["identbf"][:, :]), 16)
            ld.inc(s.dma_start(out=identf[:, :], in_=dd["identf"][:, :]), 16)
            ld.inc(s.dma_start(out=ones_bf[:, :], in_=dd["onesbf"][:, :]), 16)
            s.wait_ge(ld.h, ld.n)

        @blk.vector
        def _(v):
            v.memset(eps_ln[:, :], LN_EPS)
            v.memset(eps_gn[:, :], GN_EPS)
    return {"identbf": identbf, "identf": identf, "onesbf": ones_bf,
            "eps_ln": eps_ln, "eps_gn": eps_gn}


def host_consts():
    return {
        "identbf": np.eye(128, dtype=ml_dtypes.bfloat16),
        "identf": np.eye(128, dtype=np.float32),
        "onesbf": np.ones((1, 128), dtype=ml_dtypes.bfloat16),
    }


def vec_layout(v, p=128):
    return np.ascontiguousarray(v.reshape(-1, p).T)


def build_s4(T, debug=False):
    nc = bass.Bass("TRN2", target_bir_lowering=False)
    dd = {}
    dd["x3_tm"] = nc.dram_tensor("x3_tm", [T, D], BF16, kind="ExternalInput").ap()
    dd["w1T"] = nc.dram_tensor("w1T", [D, HID], F32, kind="ExternalInput").ap()
    dd["w2T"] = nc.dram_tensor("w2T", [D, HID], F32, kind="ExternalInput").ap()
    dd["h1T"] = nc.dram_tensor("h1T", [HID, HID], F32, kind="ExternalInput").ap()
    dd["h2T"] = nc.dram_tensor("h2T", [HID, HID], F32, kind="ExternalInput").ap()
    dd["w3T"] = nc.dram_tensor("w3T", [HID, D], F32, kind="ExternalInput").ap()
    for k, w in (("s1", HCH), ("s2", HCH), ("h1s", HCH), ("h2s", HCH), ("w3s", DCH)):
        dd[k] = nc.dram_tensor(k, [128, w], F32, kind="ExternalInput").ap()
    for k in ("identbf", "identf", "onesbf"):
        shape = [1, 128] if k == "onesbf" else [128, 128]
        dt = F32 if k == "identf" else BF16
        dd[k] = nc.dram_tensor(k, shape, dt, kind="ExternalInput").ap()
    dd["out"] = nc.dram_tensor("out", [T, D], F32, kind="ExternalOutput").ap()
    with ExitStack() as ctx:
        env = Env(nc, ctx)
        consts = load_consts(env, dd)
        tens = emit_s4(env, dd, T, consts)
        if debug:
            dsem = env.sem("dbg_st")
            douts = {}
            for k, sb in tens.items():
                p, f = sb.shape
                douts[k] = nc.dram_tensor("dbg_" + k, [p, f],
                                          mybir.dt(sb.dtype) if not isinstance(sb.dtype, mybir.dt) else sb.dtype,
                                          kind="ExternalOutput").ap()
            with nc.Block() as blk:
                @blk.sync
                def _(s):
                    for k, sb in tens.items():
                        dsem.inc(s.dma_start(out=douts[k][:, :], in_=sb[:, :]), 16)
                    s.wait_ge(dsem.h, dsem.n)
    return nc


def build_s2(T, seqlen=4096):
    nc = bass.Bass("TRN2", target_bir_lowering=False)
    dd = {}
    dd["x1_tm"] = nc.dram_tensor("x1_tm", [T, D], BF16, kind="ExternalInput").ap()
    dd["amax1_c"] = nc.dram_tensor("amax1_c", [128, T // 128], F32, kind="ExternalInput").ap()
    dd["w1T"] = nc.dram_tensor("w1T", [D, HID], F32, kind="ExternalInput").ap()
    dd["convw"] = nc.dram_tensor("convw", [3, HID], F32, kind="ExternalInput").ap()
    dd["convb"] = nc.dram_tensor("convb", [1, HID], F32, kind="ExternalInput").ap()
    dd["w2T"] = nc.dram_tensor("w2T", [HID, D], F32, kind="ExternalInput").ap()
    for k in ("identbf", "identf", "onesbf"):
        shape = [1, 128] if k == "onesbf" else [128, 128]
        dt = F32 if k == "identf" else BF16
        dd[k] = nc.dram_tensor(k, shape, dt, kind="ExternalInput").ap()
    dd["x2_tm"] = nc.dram_tensor("x2_tm", [T, D], BF16, kind="ExternalOutput").ap()
    with ExitStack() as ctx:
        env = Env(nc, ctx)
        consts = load_consts(env, dd)
        emit_s2(env, dd, T, consts, seqlen=seqlen)
    return nc


def build_s3(T, L=4096):
    nc = bass.Bass("TRN2", target_bir_lowering=False)
    dd = {}
    dd["x2_tm"] = nc.dram_tensor("x2_tm", [T, D], BF16, kind="ExternalInput").ap()
    for k in ("wqT", "wkT", "wvT", "woT"):
        dd[k] = nc.dram_tensor(k, [D, D], F32, kind="ExternalInput").ap()
    for k in ("bq", "bk", "bv"):
        dd[k] = nc.dram_tensor(k, [128, 8], F32, kind="ExternalInput").ap()
    dd["bo_row"] = nc.dram_tensor("bo_row", [1, D], F32, kind="ExternalInput").ap()
    dd["w0l"] = nc.dram_tensor("w0l", [128, 512], F32, kind="ExternalInput").ap()
    for k in ("identbf", "identf", "onesbf"):
        shape = [1, 128] if k == "onesbf" else [128, 128]
        dt = F32 if k == "identf" else BF16
        dd[k] = nc.dram_tensor(k, shape, dt, kind="ExternalInput").ap()
    dd["x3_tm"] = nc.dram_tensor("x3_tm", [T, D], BF16, kind="ExternalOutput").ap()
    with ExitStack() as ctx:
        env = Env(nc, ctx)
        consts = load_consts(env, dd)
        emit_s3(env, dd, T, consts, L=L)
    return nc


def build_s1(T, S=4096):
    nc = bass.Bass("TRN2", target_bir_lowering=False)
    dd = {}
    dd["x_fm"] = nc.dram_tensor("x_fm", [D, T], F32, kind="ExternalInput").ap()
    dd["x_tm"] = nc.dram_tensor("x_tm", [T, D], F32, kind="ExternalInput").ap()
    for k in ("wqT", "wkT", "wvT"):
        dd[k] = nc.dram_tensor(k, [D, D], F32, kind="ExternalInput").ap()
    for k in ("identbf", "identf", "onesbf"):
        shape = [1, 128] if k == "onesbf" else [128, 128]
        dt = F32 if k == "identf" else BF16
        dd[k] = nc.dram_tensor(k, shape, dt, kind="ExternalInput").ap()
    dd["x1_tm"] = nc.dram_tensor("x1_tm", [T, D], BF16, kind="ExternalOutput").ap()
    dd["amax1_c"] = nc.dram_tensor("amax1_c", [128, T // 128], F32, kind="ExternalOutput").ap()
    with ExitStack() as ctx:
        env = Env(nc, ctx)
        consts = load_consts(env, dd)
        y_hbm = emit_s1(env, dd, T, consts, S=S)
        emit_s1b(env, dd, y_hbm, T, consts, S=S)
    return nc


def build_fused(T=8192):
    nc = bass.Bass("TRN2", target_bir_lowering=False)
    dd = {}
    dd["x_fm"] = nc.dram_tensor("x_fm", [D, T], F32, kind="ExternalInput").ap()
    dd["x_tm"] = nc.dram_tensor("x_tm", [T, D], F32, kind="ExternalInput").ap()
    for k in ("s1_wqT", "s1_wkT", "s1_wvT", "s3_wqT", "s3_wkT", "s3_wvT", "s3_woT"):
        dd[k] = nc.dram_tensor("w_" + k, [D, D], F32, kind="ExternalInput").ap()
    dd["s2_w1T"] = nc.dram_tensor("w_s2_w1T", [D, HID], F32, kind="ExternalInput").ap()
    dd["s2_convw"] = nc.dram_tensor("w_s2_convw", [3, HID], F32, kind="ExternalInput").ap()
    dd["s2_convb"] = nc.dram_tensor("w_s2_convb", [1, HID], F32, kind="ExternalInput").ap()
    dd["s2_w2T"] = nc.dram_tensor("w_s2_w2T", [HID, D], F32, kind="ExternalInput").ap()
    for k in ("s3_bq", "s3_bk", "s3_bv"):
        dd[k] = nc.dram_tensor("w_" + k, [128, 8], F32, kind="ExternalInput").ap()
    dd["s3_bo_row"] = nc.dram_tensor("w_s3_bo_row", [1, D], F32, kind="ExternalInput").ap()
    dd["s3_w0l"] = nc.dram_tensor("w_s3_w0l", [128, 512], F32, kind="ExternalInput").ap()
    dd["s4_w1T"] = nc.dram_tensor("w_s4_w1T", [D, HID], F32, kind="ExternalInput").ap()
    dd["s4_w2T"] = nc.dram_tensor("w_s4_w2T", [D, HID], F32, kind="ExternalInput").ap()
    dd["s4_h1T"] = nc.dram_tensor("w_s4_h1T", [HID, HID], F32, kind="ExternalInput").ap()
    dd["s4_h2T"] = nc.dram_tensor("w_s4_h2T", [HID, HID], F32, kind="ExternalInput").ap()
    dd["s4_w3T"] = nc.dram_tensor("w_s4_w3T", [HID, D], F32, kind="ExternalInput").ap()
    for k, w in (("s4_s1", HCH), ("s4_s2", HCH), ("s4_h1s", HCH), ("s4_h2s", HCH), ("s4_w3s", DCH)):
        dd[k] = nc.dram_tensor("w_" + k, [128, w], F32, kind="ExternalInput").ap()
    for k in ("identbf", "identf", "onesbf"):
        shape = [1, 128] if k == "onesbf" else [128, 128]
        dt = F32 if k == "identf" else BF16
        dd[k] = nc.dram_tensor(k, shape, dt, kind="ExternalInput").ap()
    dd["out"] = nc.dram_tensor("out", [T, D], F32, kind="ExternalOutput").ap()
    dd["x1_tm"] = nc.dram_tensor("x1_tm", [T, D], BF16).ap()
    dd["amax1_c"] = nc.dram_tensor("amax1_c", [128, T // 128], F32).ap()
    dd["x2_tm"] = nc.dram_tensor("x2_tm", [T, D], BF16).ap()
    dd["x3_tm"] = nc.dram_tensor("x3_tm", [T, D], BF16).ap()

    def view(mapping):
        d2 = dict(dd)
        for k, v in mapping.items():
            d2[k] = dd[v]
        return d2

    with ExitStack() as ctx:
        env = Env(nc, ctx)
        consts = load_consts(env, dd)
        env.push_pool()
        env.sem_scope_reset()
        y_hbm = emit_s1(env, view({"wqT": "s1_wqT", "wkT": "s1_wkT", "wvT": "s1_wvT"}),
                        T, consts)
        emit_s1b(env, view({}), y_hbm, T, consts)
        env.pop_pool()
        env.push_pool()
        env.sem_scope_reset()
        emit_s2(env, view({"w1T": "s2_w1T", "convw": "s2_convw", "convb": "s2_convb",
                           "w2T": "s2_w2T"}), T, consts)
        env.pop_pool()
        env.push_pool()
        env.sem_scope_reset()
        emit_s3(env, view({"wqT": "s3_wqT", "wkT": "s3_wkT", "wvT": "s3_wvT",
                           "woT": "s3_woT", "bq": "s3_bq", "bk": "s3_bk", "bv": "s3_bv",
                           "bo_row": "s3_bo_row", "w0l": "s3_w0l"}), T, consts)
        env.pop_pool()
        env.push_pool()
        env.sem_scope_reset()
        emit_s4(env, view({"w1T": "s4_w1T", "w2T": "s4_w2T", "h1T": "s4_h1T",
                           "h2T": "s4_h2T", "w3T": "s4_w3T", "s1": "s4_s1",
                           "s2": "s4_s2", "h1s": "s4_h1s", "h2s": "s4_h2s",
                           "w3s": "s4_w3s"}), T, consts)
        env.pop_pool()
    return nc


T_CORE = 8192
_PROGS = {}


def _get_progs():
    if not _PROGS:
        _PROGS["s1"] = build_s1(T_CORE)
        _PROGS["s2"] = build_s2(T_CORE)
        _PROGS["s3"] = build_s3(T_CORE)
        _PROGS["s4"] = build_s4(T_CORE)
    return _PROGS


def kernel(**inputs):
    ins = {k: np.asarray(v) for k, v in inputs.items()}
    x = ins["L0_attn_x"]
    ncore = 8
    consts = host_consts()

    w0l = np.zeros((128, 512), np.float32)
    wrf = ins["L1_attn_Wrf"][0]
    for h in range(16):
        w0l[(h % 2) * 64:(h % 2) * 64 + 64, (h // 2) * 64:(h // 2) * 64 + 64] = wrf[h]
    shared = {
        "w_s1_wqT": np.ascontiguousarray(ins["L0_attn_Wq"].T),
        "w_s1_wkT": np.ascontiguousarray(ins["L0_attn_Wk"].T),
        "w_s1_wvT": np.ascontiguousarray(ins["L0_attn_Wv"].T),
        "w_s2_w1T": np.ascontiguousarray(ins["L0_ffn_w1"].T),
        "w_s2_convw": np.ascontiguousarray(ins["L0_ffn_conv_w"].reshape(4096, 3).T),
        "w_s2_convb": np.ascontiguousarray(ins["L0_ffn_conv_b"].reshape(1, 4096)),
        "w_s2_w2T": np.ascontiguousarray(ins["L0_ffn_w2"].T),
        "w_s3_wqT": np.ascontiguousarray(ins["L1_attn_Wq"].T),
        "w_s3_wkT": np.ascontiguousarray(ins["L1_attn_Wk"].T),
        "w_s3_wvT": np.ascontiguousarray(ins["L1_attn_Wv"].T),
        "w_s3_woT": np.ascontiguousarray(ins["L1_attn_Wo"].T),
        "w_s3_bq": vec_layout(ins["L1_attn_bq"]), "w_s3_bk": vec_layout(ins["L1_attn_bk"]),
        "w_s3_bv": vec_layout(ins["L1_attn_bv"]),
        "w_s3_bo_row": np.ascontiguousarray(ins["L1_attn_bo"].reshape(1, 1024)),
        "w_s3_w0l": w0l,
        "w_s4_w1T": np.ascontiguousarray(ins["L1_ffn_w1_w"].T),
        "w_s4_w2T": np.ascontiguousarray(ins["L1_ffn_w2_w"].T),
        "w_s4_h1T": np.ascontiguousarray(ins["L1_ffn_h1_w"].T),
        "w_s4_h2T": np.ascontiguousarray(ins["L1_ffn_h2_w"].T),
        "w_s4_w3T": np.ascontiguousarray(ins["L1_ffn_w3_w"].T),
        "w_s4_s1": vec_layout(ins["L1_ffn_w1_s"]), "w_s4_s2": vec_layout(ins["L1_ffn_w2_s"]),
        "w_s4_h1s": vec_layout(ins["L1_ffn_h1_s"]), "w_s4_h2s": vec_layout(ins["L1_ffn_h2_s"]),
        "w_s4_w3s": vec_layout(ins["L1_ffn_w3_s"]),
    }
    shared.update(consts)

    if "fused" not in _PROGS:
        _PROGS["fused"] = build_fused(T_CORE)
    maps = []
    for i in range(ncore):
        xc = np.ascontiguousarray(x[2 * i:2 * i + 2].reshape(T_CORE, 1024))
        m = {"x_fm": np.ascontiguousarray(xc.T), "x_tm": xc}
        m.update(shared)
        maps.append(m)
    res = run_bass_kernel_spmd(_PROGS["fused"], maps, core_ids=list(range(ncore)))
    out = np.empty((16, 4096, 1024), np.float32)
    for i in range(ncore):
        out[2 * i:2 * i + 2] = np.asarray(res.results[i]["out"]).reshape(2, 4096, 1024)
    return out


# version 11
# speedup vs baseline: 1.5402x; 1.0899x over previous
import numpy as np
import ml_dtypes
from contextlib import ExitStack

import concourse.bass as bass
import concourse.mybir as mybir
from concourse.bass_utils import run_bass_kernel_spmd

F32 = mybir.dt.float32
BF16 = mybir.dt.bfloat16
I8 = mybir.dt.int8
AF = mybir.ActivationFunctionType
ALU = mybir.AluOpType
AX = mybir.AxisListType

D = 1024
HID = 4096
DCH = D // 128
HCH = HID // 128
LN_EPS = 1e-5
L1_ATTN_EPS = 1e-6
GN_EPS = 1e-5


class Sem:
    def __init__(self, ctx, nc, name):
        self.h = ctx.enter_context(nc.semaphore(name))
        self.n = 0

    def inc(self, instr, k=1):
        instr.then_inc(self.h, k)
        self.n += k
        return self.n

    def wait(self, eng, n=None):
        eng.wait_ge(self.h, self.n if n is None else n)


class Env:
    def __init__(self, nc, ctx):
        self.nc = nc
        self.ctx = ctx
        self.pools = [ctx]
        self._semid = 0

    def sem(self, name=None):
        if not hasattr(self, "_sempool"):
            self._sempool = []
            self._semcur = 0
        if self._semcur < len(self._sempool):
            s = self._sempool[self._semcur]
        else:
            self._semid += 1
            s = Sem(self.ctx, self.nc, f"sem{self._semid}")
            self._sempool.append(s)
        self._semcur += 1
        return s

    def sem_scope_reset(self):
        self._semcur = 0

    def sb(self, name, shape, dt):
        return self.pools[-1].enter_context(self.nc.sbuf_tensor(name, shape, dt))

    def ps(self, name, shape, dt=F32):
        return self.pools[-1].enter_context(self.nc.psum_tensor(name, shape, dt))

    def push_pool(self):
        self.pools.append(ExitStack())

    def pop_pool(self):
        self.pools.pop().close()


def emit_cast_weights(env, pairs, tag):
    nc = env.nc
    maxc = max(p[0].shape[1] for p in pairs)
    buf = [env.sb(f"cw_{tag}_{i}", [128, maxc], BF16) for i in range(2)]
    ld = [env.sem(f"cw_ld{i}_{tag}") for i in range(2)]
    st = [env.sem(f"cw_st{i}_{tag}") for i in range(2)]
    chunks = []
    for src, dst in pairs:
        r, c = src.shape
        assert r % 128 == 0 and dst.shape[0] == r and dst.shape[1] == c
        for i in range(r // 128):
            chunks.append((src, dst, i, c))
    with nc.Block() as blk:
        ld00 = [ld[0].n, ld[1].n]
        st00 = [st[0].n, st[1].n]

        @blk.gpsimd
        def _(g):
            for k, (src, dst, i, c) in enumerate(chunks):
                p = k % 2
                if k >= 2:
                    g.wait_ge(st[p].h, st00[p] + 16 * (k // 2))
                ld[p].inc(g.dma_start(out=buf[p][:, 0:c],
                                      in_=src[i * 128:(i + 1) * 128, :]), 16)

        @blk.sync
        def _(s):
            for k, (src, dst, i, c) in enumerate(chunks):
                p = k % 2
                s.wait_ge(ld[p].h, ld00[p] + 16 * (k // 2 + 1))
                st[p].inc(s.dma_start(out=dst[i * 128:(i + 1) * 128, :],
                                      in_=buf[p][:, 0:c]), 16)
            s.wait_ge(st[0].h, st[0].n)
            s.wait_ge(st[1].h, st[1].n)


def emit_s4(env, dd, T, consts):
    nc = env.nc
    NT = 512
    ntiles = T // NT
    identbf = consts["identbf"]

    w1b = nc.dram_tensor("s4_w1b", [D, HID], BF16).ap()
    w2b = nc.dram_tensor("s4_w2b", [D, HID], BF16).ap()
    h1b = nc.dram_tensor("s4_h1b", [HID, HID], BF16).ap()
    h2b = nc.dram_tensor("s4_h2b", [HID, HID], BF16).ap()
    w3b = nc.dram_tensor("s4_w3b", [HID, D], BF16).ap()
    emit_cast_weights(env, [(dd["w1T"], w1b), (dd["w2T"], w2b),
                            (dd["h1T"], h1b), (dd["h2T"], h2b),
                            (dd["w3T"], w3b)], "s4")

    s1_sb = env.sb("s4_s1", [128, HCH], F32)
    s2_sb = env.sb("s4_s2", [128, HCH], F32)
    h1s_sb = env.sb("s4_h1s", [128, HCH], F32)
    h2s_sb = env.sb("s4_h2s", [128, HCH], F32)
    w3s_sb = env.sb("s4_w3s", [128, DCH], F32)
    ld0 = env.sem("s4_ld0")
    with nc.Block() as blk:
        @blk.sync
        def _(s):
            for sb, key in ((s1_sb, "s1"), (s2_sb, "s2"), (h1s_sb, "h1s"),
                            (h2s_sb, "h2s"), (w3s_sb, "w3s")):
                ld0.inc(s.dma_start(out=sb[:, :], in_=dd[key][:, :]), 16)
            s.wait_ge(ld0.h, ld0.n)

    x3tm = env.sb("s4_x3tm", [128, 2 * 4 * 1024], BF16)
    x3fm = env.sb("s4_x3fm", [128, DCH * NT], BF16)
    ht = env.sb("s4_ht", [128, HCH * NT], BF16)
    h2sb = env.sb("s4_h2", [128, HCH * NT], BF16)
    t1 = env.sb("s4_t1", [128, NT], BF16)
    t2 = env.sb("s4_t2", [128, NT], BF16)
    wblkA = [env.sb(f"s4_wA{i}", [128, DCH * 128], BF16) for i in range(2)]
    wblkB = [env.sb(f"s4_wB{i}", [128, DCH * 128], BF16) for i in range(2)]
    wblkH = [env.sb(f"s4_wH{i}", [128, HCH * 128], BF16) for i in range(4)]
    y4fm = env.sb("s4_y4fm", [128, DCH * NT], BF16)
    v4 = env.sb("s4_v4", [128, 4096], F32)
    sq4 = env.sb("s4_sq4", [128, 4096], F32)
    o_sb = env.sb("s4_o", [128, 4 * 1024], F32)
    ssq4 = env.sb("s4_ssq4", [128, 4], F32)
    sm1_4 = env.sb("s4_sm14", [128, 4], F32)
    sm2_4 = env.sb("s4_sm24", [128, 4], F32)
    ssq2_4 = env.sb("s4_ssq24", [128, 4], F32)
    t4 = env.sb("s4_t4", [128, 4], F32)
    rstd4 = env.sb("s4_rstd4", [128, 4], F32)

    psZ = [env.ps(f"s4_psZ{i}", [128, 512], F32) for i in range(4)]
    ps_tp = [env.ps(f"s4_pstp{i}", [128, 512], BF16) for i in range(2)]
    ps_tm = [env.ps(f"s4_pstm{i}", [128, 1024], BF16) for i in range(2)]

    ldx = [env.sem("s4_ldx0"), env.sem("s4_ldx1")]
    tpx = env.sem("s4_tpx")
    evx = env.sem("s4_evx")
    ldw = [env.sem("s4_ldw0"), env.sem("s4_ldw1")]
    ldwB = [env.sem("s4_ldwB0"), env.sem("s4_ldwB1")]
    mmA = env.sem("s4_mmA")
    mmB = env.sem("s4_mmB")
    acA = env.sem("s4_acA")
    evB = env.sem("s4_evB")
    mulS = env.sem("s4_mul")
    tpy = env.sem("s4_tpy")
    vst = env.sem("s4_vst")
    sselfS = env.sem("s4_sselfS")
    fin = env.sem("s4_fin")
    sto = env.sem("s4_sto")

    sto_base = sto.n
    with nc.Block() as blk:
        @blk.sync
        def _(s):
            for j in range(4):
                ldx[0].inc(s.dma_start(
                    out=x3tm[:, j * 1024:(j + 1) * 1024],
                    in_=dd["x3_tm"][j * 128:(j + 1) * 128, :]), 16)

    for t in range(ntiles):
        slot = t % 2
        xoff = slot * 4096
        nslot = (t + 1) % 2

        with nc.Block() as blk:
            ldx_t0 = ldx[slot].n
            evx0, tpx0 = evx.n, tpx.n

            @blk.sync
            def _(s):
                if t + 1 < ntiles:
                    for j in range(4):
                        ldx[nslot].inc(s.dma_start(
                            out=x3tm[:, nslot * 4096 + j * 1024: nslot * 4096 + (j + 1) * 1024],
                            in_=dd["x3_tm"][(t + 1) * NT + j * 128:(t + 1) * NT + (j + 1) * 128, :]), 16)

            @blk.tensor
            def _(te):
                te.wait_ge(ldx[slot].h, ldx_t0)
                for i in range(DCH):
                    if i >= 2:
                        te.wait_ge(evx.h, evx0 + i - 1)
                    for j in range(4):
                        ins = te.transpose(
                            ps_tp[i % 2][:, j * 128:(j + 1) * 128],
                            x3tm[:, xoff + j * 1024 + i * 128: xoff + j * 1024 + (i + 1) * 128],
                            identbf[:, :])
                    tpx.inc(ins)

            @blk.vector
            def _(v):
                for i in range(DCH):
                    v.wait_ge(tpx.h, tpx0 + i + 1)
                    evx.inc(v.tensor_copy(x3fm[:, i * NT:(i + 1) * NT], ps_tp[i % 2][:, :]))

        with nc.Block() as blk:
            mmA0, mmB0, acA0, evB0, mulS0 = mmA.n, mmB.n, acA.n, evB.n, mulS.n
            ldw0 = [ldw[0].n, ldw[1].n]
            ldwB0 = [ldwB[0].n, ldwB[1].n]

            @blk.gpsimd
            def _(g):
                for o in range(HCH):
                    p = o % 2
                    if o >= 2:
                        g.wait_ge(mmB.h, mmB0 + o - 1)
                    ldw[p].inc(g.dma_start(
                        out=wblkA[p][:, :],
                        in_=w1b.rearrange("(ic p) o -> p ic o", p=128)[:, :, o * 128:(o + 1) * 128]), 16)

            @blk.sync
            def _(s):
                for o in range(HCH):
                    p = o % 2
                    if o >= 2:
                        s.wait_ge(mmB.h, mmB0 + o - 1)
                    ldwB[p].inc(s.dma_start(
                        out=wblkB[p][:, :],
                        in_=w2b.rearrange("(ic p) o -> p ic o", p=128)[:, :, o * 128:(o + 1) * 128]), 16)

            @blk.tensor
            def _(te):
                for o in range(HCH):
                    te.wait_ge(ldw[o % 2].h, ldw0[o % 2] + 16 * (o // 2 + 1))
                    te.wait_ge(ldwB[o % 2].h, ldwB0[o % 2] + 16 * (o // 2 + 1))
                    if o >= 2:
                        te.wait_ge(acA.h, acA0 + o - 1)
                        te.wait_ge(evB.h, evB0 + o - 1)
                    for i in range(DCH):
                        ins = te.matmul(psZ[2 * (o % 2)][:, :],
                                        wblkA[o % 2][:, i * 128:(i + 1) * 128],
                                        x3fm[:, i * NT:(i + 1) * NT],
                                        start=(i == 0), stop=(i == DCH - 1))
                    mmA.inc(ins)
                    for i in range(DCH):
                        ins = te.matmul(psZ[2 * (o % 2) + 1][:, :],
                                        wblkB[o % 2][:, i * 128:(i + 1) * 128],
                                        x3fm[:, i * NT:(i + 1) * NT],
                                        start=(i == 0), stop=(i == DCH - 1))
                    mmB.inc(ins)

            @blk.scalar
            def _(sc):
                for o in range(HCH):
                    sc.wait_ge(mmA.h, mmA0 + o + 1)
                    if o >= 1:
                        sc.wait_ge(mulS.h, mulS0 + o)
                    acA.inc(sc.activation(t1[:, :], psZ[2 * (o % 2)][:, :], AF.Silu,
                                          bias=0.0, scale=s1_sb[:, o:o + 1]))

            @blk.vector
            def _(v):
                for o in range(HCH):
                    v.wait_ge(mmB.h, mmB0 + o + 1)
                    evB.inc(v.tensor_scalar_mul(t2[:, :], psZ[2 * (o % 2) + 1][:, :],
                                                s2_sb[:, o:o + 1]))
                    v.wait_ge(acA.h, acA0 + o + 1)
                    mulS.inc(v.tensor_mul(ht[:, o * NT:(o + 1) * NT], t1[:, :], t2[:, :]))

        for wsrc, ssb, src, dst in ((h1b, h1s_sb, ht, h2sb), (h2b, h2s_sb, h2sb, ht)):
            with nc.Block() as blk:
                mm0, ac0 = mmA.n, acA.n
                ldw0 = [ldw[0].n, ldw[1].n]

                @blk.gpsimd
                def _(g):
                    for o in range(0, HCH, 2):
                        if o >= 4:
                            g.wait_ge(mmA.h, mm0 + o - 3)
                        ldw[0].inc(g.dma_start(
                            out=wblkH[o % 4][:, :],
                            in_=wsrc.rearrange("(ic p) o -> p ic o", p=128)[:, :, o * 128:(o + 1) * 128]), 16)

                @blk.sync
                def _(s):
                    for o in range(1, HCH, 2):
                        if o >= 4:
                            s.wait_ge(mmA.h, mm0 + o - 3)
                        ldw[1].inc(s.dma_start(
                            out=wblkH[o % 4][:, :],
                            in_=wsrc.rearrange("(ic p) o -> p ic o", p=128)[:, :, o * 128:(o + 1) * 128]), 16)

                @blk.tensor
                def _(te):
                    for o in range(HCH):
                        te.wait_ge(ldw[o % 2].h, ldw0[o % 2] + 16 * (o // 2 + 1))
                        if o >= 2:
                            te.wait_ge(acA.h, ac0 + o - 1)
                        ps = psZ[o % 2]
                        for i in range(HCH):
                            ins = te.matmul(ps[:, :], wblkH[o % 4][:, i * 128:(i + 1) * 128],
                                            src[:, i * NT:(i + 1) * NT],
                                            start=(i == 0), stop=(i == HCH - 1))
                        mmA.inc(ins)

                @blk.scalar
                def _(sc):
                    for o in range(HCH):
                        sc.wait_ge(mmA.h, mm0 + o + 1)
                        acA.inc(sc.activation(dst[:, o * NT:(o + 1) * NT], psZ[o % 2][:, :],
                                              AF.Gelu, bias=0.0, scale=ssb[:, o:o + 1]))

        with nc.Block() as blk:
            mm0, ev0 = mmA.n, evB.n
            ldw0 = [ldw[0].n, ldw[1].n]

            @blk.gpsimd
            def _(g):
                for o in range(0, DCH, 2):
                    if o >= 4:
                        g.wait_ge(mmA.h, mm0 + o - 3)
                    ldw[0].inc(g.dma_start(
                        out=wblkH[o % 4][:, :],
                        in_=w3b.rearrange("(ic p) o -> p ic o", p=128)[:, :, o * 128:(o + 1) * 128]), 16)

            @blk.sync
            def _(s):
                for o in range(1, DCH, 2):
                    if o >= 4:
                        s.wait_ge(mmA.h, mm0 + o - 3)
                    ldw[1].inc(s.dma_start(
                        out=wblkH[o % 4][:, :],
                        in_=w3b.rearrange("(ic p) o -> p ic o", p=128)[:, :, o * 128:(o + 1) * 128]), 16)

            @blk.tensor
            def _(te):
                for o in range(DCH):
                    te.wait_ge(ldw[o % 2].h, ldw0[o % 2] + 16 * (o // 2 + 1))
                    if o >= 2:
                        te.wait_ge(evB.h, ev0 + o - 1)
                    ps = psZ[o % 2]
                    for i in range(HCH):
                        ins = te.matmul(ps[:, :], wblkH[o % 4][:, i * 128:(i + 1) * 128],
                                        ht[:, i * NT:(i + 1) * NT],
                                        start=(i == 0), stop=(i == HCH - 1))
                    mmA.inc(ins)

            @blk.vector
            def _(v):
                for o in range(DCH):
                    v.wait_ge(mmA.h, mm0 + o + 1)
                    evB.inc(v.tensor_scalar_mul(y4fm[:, o * NT:(o + 1) * NT],
                                                psZ[o % 2][:, :], w3s_sb[:, o:o + 1]))

        with nc.Block() as blk:
            tp0, vst0 = tpy.n, vst.n

            @blk.tensor
            def _(te):
                for j in range(4):
                    if j >= 2:
                        te.wait_ge(vst.h, vst0 + j - 1)
                    for i in range(DCH):
                        ins = te.transpose(
                            ps_tm[j % 2][:, i * 128:(i + 1) * 128],
                            y4fm[:, i * NT + j * 128: i * NT + (j + 1) * 128],
                            identbf[:, :])
                    tpy.inc(ins)

            @blk.vector
            def _(v):
                for j in range(4):
                    v.wait_ge(tpy.h, tp0 + j + 1)
                    vst.inc(v.tensor_add(v4[:, j * 1024:(j + 1) * 1024],
                                         x3tm[:, xoff + j * 1024: xoff + (j + 1) * 1024],
                                         ps_tm[j % 2][:, :]))
                for j in range(4):
                    v.reduce_sum(ssq4[:, j:j + 1], v4[:, j * 1024:(j + 1) * 1024], axis=AX.X)

        with nc.Block() as blk:
            @blk.vector
            def _(v):
                v.tensor_scalar_mul(sm1_4[:, :], ssq4[:, :], -1.0 / D)
                v.tensor_scalar_mul(sm2_4[:, :], ssq4[:, :], 1.0 / D)

        with nc.Block() as blk:
            @blk.scalar
            def _(sc):
                for j in range(4):
                    sc.activation(sq4[:, j * 1024:(j + 1) * 1024],
                                  v4[:, j * 1024:(j + 1) * 1024], AF.Square,
                                  bias=sm1_4[:, j:j + 1], scale=1.0)

        with nc.Block() as blk:
            @blk.vector
            def _(v):
                for j in range(4):
                    v.reduce_sum(ssq2_4[:, j:j + 1], sq4[:, j * 1024:(j + 1) * 1024], axis=AX.X)

        with nc.Block() as blk:
            @blk.vector
            def _(v):
                v.tensor_scalar(t4[:, :], ssq2_4[:, :], 1.0 / D, LN_EPS, ALU.mult, ALU.add)

        with nc.Block() as blk:
            @blk.scalar
            def _(sc):
                sc.activation(rstd4[:, :], t4[:, :], AF.Ln).then_inc(sselfS.h, 1)
                sselfS.n += 1
                sc.wait_ge(sselfS.h, sselfS.n)
                sc.activation(rstd4[:, :], rstd4[:, :], AF.Exp, bias=0.0, scale=-0.5)

        with nc.Block() as blk:
            fin0 = fin.n

            @blk.vector
            def _(v):
                for j in range(4):
                    if t >= 1 and j == 0:
                        v.wait_ge(sto.h, sto_base + 16 * 4 * t)
                    fin.inc(v.tensor_scalar(o_sb[:, j * 1024:(j + 1) * 1024],
                                            v4[:, j * 1024:(j + 1) * 1024],
                                            sm2_4[:, j:j + 1], rstd4[:, j:j + 1],
                                            ALU.subtract, ALU.mult))

            @blk.sync
            def _(s):
                for j in range(4):
                    s.wait_ge(fin.h, fin0 + j + 1)
                    sto.inc(s.dma_start(
                        out=dd["out"][t * NT + j * 128: t * NT + (j + 1) * 128, :],
                        in_=o_sb[:, j * 1024:(j + 1) * 1024]), 16)

    with nc.Block() as blk:
        @blk.sync
        def _(s):
            s.wait_ge(sto.h, sto.n)
    return {"x3fm": x3fm, "ht": ht, "h2sb": h2sb, "y4fm": y4fm, "o_sb": o_sb,
            "x3tm": x3tm}


def emit_ln_tail(env, consts, v4, o_sb, nsub, stats, cast_out_dt=BF16):
    nc = env.nc
    sq4 = stats["sq4"]
    with nc.Block() as blk:
        @blk.vector
        def _(v):
            ss = stats["selfsem"]
            for j in range(nsub):
                ins = v.reduce_sum(stats["ssq4"][:, j:j + 1],
                                   v4[:, j * 1024:(j + 1) * 1024], axis=AX.X)
            ss.n += 1
            ins.then_inc(ss.h, 1)
            v.wait_ge(ss.h, ss.n)
            v.tensor_scalar_mul(stats["sm1_4"][:, 0:nsub], stats["ssq4"][:, 0:nsub], -1.0 / D)
            v.tensor_scalar_mul(stats["sm2_4"][:, 0:nsub], stats["ssq4"][:, 0:nsub], 1.0 / D)

    with nc.Block() as blk:
        @blk.scalar
        def _(sc):
            for j in range(nsub):
                sc.activation(sq4[:, j * 1024:(j + 1) * 1024],
                              v4[:, j * 1024:(j + 1) * 1024], AF.Square,
                              bias=stats["sm1_4"][:, j:j + 1], scale=1.0)

    with nc.Block() as blk:
        @blk.vector
        def _(v):
            ss = stats["selfsem"]
            for j in range(nsub):
                ins = v.reduce_sum(stats["ssq2_4"][:, j:j + 1],
                                   sq4[:, j * 1024:(j + 1) * 1024], axis=AX.X)
            ss.n += 1
            ins.then_inc(ss.h, 1)
            v.wait_ge(ss.h, ss.n)
            v.tensor_scalar(stats["t4"][:, 0:nsub], stats["ssq2_4"][:, 0:nsub],
                            1.0 / D, LN_EPS, ALU.mult, ALU.add)

    with nc.Block() as blk:
        @blk.scalar
        def _(sc):
            ss = stats["selfsem"]
            ss.n += 1
            sc.activation(stats["rstd4"][:, 0:nsub], stats["t4"][:, 0:nsub],
                          AF.Ln).then_inc(ss.h, 1)
            sc.wait_ge(ss.h, ss.n)
            sc.activation(stats["rstd4"][:, 0:nsub], stats["rstd4"][:, 0:nsub],
                          AF.Exp, bias=0.0, scale=-0.5)

    with nc.Block() as blk:
        @blk.vector
        def _(v):
            for j in range(nsub):
                v.tensor_scalar(o_sb[:, j * 1024:(j + 1) * 1024],
                                v4[:, j * 1024:(j + 1) * 1024],
                                stats["sm2_4"][:, j:j + 1], stats["rstd4"][:, j:j + 1],
                                ALU.subtract, ALU.mult)


def mk_ln_stats(env, tag, nsub=4):
    return {
        "sq4": env.sb(f"{tag}_sq4", [128, nsub * 1024], BF16),
        "ssq4": env.sb(f"{tag}_ssq4", [128, nsub], F32),
        "sm1_4": env.sb(f"{tag}_sm14", [128, nsub], F32),
        "sm2_4": env.sb(f"{tag}_sm24", [128, nsub], F32),
        "ssq2_4": env.sb(f"{tag}_ssq24", [128, nsub], F32),
        "t4": env.sb(f"{tag}_t4", [128, nsub], F32),
        "rstd4": env.sb(f"{tag}_rstd4", [128, nsub], F32),
        "selfsem": env.sem(f"{tag}_lnself"),
        "cross": env.sem(f"{tag}_lncx"),
    }


def emit_global_mean_abs(env, tag, src_hbm, R, C, out_bc, consts):
    nc = env.nc
    nch = R // 128
    buf = [env.sb(f"{tag}_mb{i}", [128, C], F32) for i in range(2)]
    red = env.sb(f"{tag}_red", [128, max(nch, 2)], F32)
    acc1 = env.sb(f"{tag}_acc1", [128, 1], F32)
    accp = env.sb(f"{tag}_accp", [1, 128], F32)
    sc1 = env.sb(f"{tag}_sc1", [1, 1], F32)
    sc2 = env.sb(f"{tag}_sc2", [1, 1], F32)
    scr_hbm = nc.dram_tensor(f"{tag}_schbm", [1, 2], F32).ap()
    ps = env.ps(f"{tag}_ps", [1, 128], F32)
    ld = [env.sem(f"{tag}_ld{i}") for i in range(2)]
    rd = env.sem(f"{tag}_rd")
    with nc.Block() as blk:
        ld00 = [ld[0].n, ld[1].n]
        rd0 = rd.n

        @blk.sync
        def _(s):
            for k in range(nch):
                p = k % 2
                if k >= 2:
                    s.wait_ge(rd.h, rd0 + k - 1)
                ld[p].inc(s.dma_start(out=buf[p][:, :],
                                      in_=src_hbm[k * 128:(k + 1) * 128, :]), 16)

        @blk.vector
        def _(v):
            for k in range(nch):
                p = k % 2
                v.wait_ge(ld[p].h, ld00[p] + 16 * (k // 2 + 1))
                rd.inc(v.reduce_sum(red[:, k:k + 1], buf[p][:, :], axis=AX.X,
                                    apply_absolute_value=True))
    with nc.Block() as blk:
        @blk.vector
        def _(v):
            v.reduce_sum(acc1[:, :], red[:, 0:nch], axis=AX.X)
    with nc.Block() as blk:
        @blk.tensor
        def _(te):
            te.transpose(ps[:, :], acc1[:, :], consts["identf"][:, :])
    with nc.Block() as blk:
        @blk.vector
        def _(v):
            v.tensor_copy(accp[:, :], ps[:, :])
    with nc.Block() as blk:
        @blk.vector
        def _(v):
            v.reduce_sum(sc1[:, :], accp[:, :], axis=AX.X)
    with nc.Block() as blk:
        @blk.vector
        def _(v):
            v.tensor_scalar(sc1[:, :], sc1[:, :], 1.0 / (R * C), 1e-5, ALU.mult, ALU.max)
    with nc.Block() as blk:
        @blk.vector
        def _(v):
            v.reciprocal(sc2[:, :], sc1[:, :])
    st = env.sem(f"{tag}_st")
    with nc.Block() as blk:
        @blk.sync
        def _(s):
            st.inc(s.dma_start(out=scr_hbm[:, 0:1], in_=sc1[:, :]), 16)
            st.inc(s.dma_start(out=scr_hbm[:, 1:2], in_=sc2[:, :]), 16)
            s.wait_ge(st.h, st.n)
            st.inc(s.dma_start(out=out_bc["mean"][:, :],
                               in_=scr_hbm[0:1, 0:1].broadcast_to((128, 1))), 16)
            st.inc(s.dma_start(out=out_bc["inv"][:, :],
                               in_=scr_hbm[0:1, 1:2].broadcast_to((128, 1))), 16)
            s.wait_ge(st.h, st.n)


def emit_ternarize(env, tag, src_hbm, dst_hbm, R, C, inv_bc):
    nc = env.nc
    buf = [env.sb(f"{tag}_tb{i}", [128, C], F32) for i in range(2)]
    tf = env.sb(f"{tag}_tf", [128, C], F32)
    ti = env.sb(f"{tag}_ti", [128, C], I8)
    tb = [env.sb(f"{tag}_tbf{i}", [128, C], BF16) for i in range(2)]
    ld = [env.sem(f"{tag}_tld{i}") for i in range(2)]
    cv = env.sem(f"{tag}_tcv")
    st = [env.sem(f"{tag}_tst{i}") for i in range(2)]
    nch = R // 128
    with nc.Block() as blk:
        ld00 = [ld[0].n, ld[1].n]
        st00 = [st[0].n, st[1].n]
        cv0 = cv.n

        @blk.gpsimd
        def _(g):
            for k in range(nch):
                p = k % 2
                if k >= 2:
                    g.wait_ge(cv.h, cv0 + (k - 1) * 4)
                ld[p].inc(g.dma_start(out=buf[p][:, :],
                                      in_=src_hbm[k * 128:(k + 1) * 128, :]), 16)

        @blk.vector
        def _(v):
            for k in range(nch):
                p = k % 2
                v.wait_ge(ld[p].h, ld00[p] + 16 * (k // 2 + 1))
                if k >= 2:
                    v.wait_ge(st[p].h, st00[p] + 16 * (k // 2))
                v.tensor_scalar_mul(tf[:, :], buf[p][:, :], inv_bc[:, :])
                v.tensor_copy(ti[:, :], tf[:, :])
                v.tensor_copy(tb[p][:, :], ti[:, :])
                cv.inc(v.tensor_scalar(tb[p][:, :], tb[p][:, :], 1.0, -1.0,
                                       ALU.min, ALU.max), 4)

        @blk.sync
        def _(s):
            for k in range(nch):
                p = k % 2
                s.wait_ge(cv.h, cv0 + (k + 1) * 4)
                st[p].inc(s.dma_start(out=dst_hbm[k * 128:(k + 1) * 128, :],
                                      in_=tb[p][:, :]), 16)
            s.wait_ge(st[0].h, st[0].n)
            s.wait_ge(st[1].h, st[1].n)


def emit_bcast_rows(env, tag, src_row_hbm, nrows, C, dst_tiles, consts, scale=1.0):
    nc = env.nc
    row_f = env.sb(f"{tag}_rowf", [1, C], F32)
    row_b = env.sb(f"{tag}_rowb", [1, C], BF16)
    ps = [env.ps(f"{tag}_bps{i}", [128, 512], F32) for i in range(2)]
    ld = env.sem(f"{tag}_bld")
    mm = env.sem(f"{tag}_bmm")
    ev = env.sem(f"{tag}_bev")
    for r in range(nrows):
        with nc.Block() as blk:
            @blk.sync
            def _(s):
                ld.inc(s.dma_start(out=row_f[:, :], in_=src_row_hbm[r:r + 1, :]), 16)

            @blk.vector
            def _(v):
                v.wait_ge(ld.h, ld.n)
                if scale != 1.0:
                    v.tensor_scalar_mul(row_b[:, :], row_f[:, :], scale)
                else:
                    v.tensor_copy(row_b[:, :], row_f[:, :])
        with nc.Block() as blk:
            mm0, ev0 = mm.n, ev.n

            @blk.tensor
            def _(te):
                for c in range(C // 512):
                    if c >= 2:
                        te.wait_ge(ev.h, ev0 + c - 1)
                    mm.inc(te.matmul(ps[c % 2][:, :], consts["onesbf"][:, :],
                                     row_b[:, c * 512:(c + 1) * 512],
                                     start=True, stop=True))

            @blk.vector
            def _(v):
                for c in range(C // 512):
                    v.wait_ge(mm.h, mm0 + c + 1)
                    ev.inc(v.tensor_copy(dst_tiles[r][:, c * 512:(c + 1) * 512],
                                         ps[c % 2][:, :]))


def emit_s2(env, dd, T, consts, seqlen=4096):
    nc = env.nc
    NT = 512
    S = seqlen
    ntiles = T // NT
    identbf = consts["identbf"]

    w1m = {"mean": env.sb("s2_w1mean", [128, 1], F32), "inv": env.sb("s2_w1inv", [128, 1], F32)}
    w2m = {"mean": env.sb("s2_w2mean", [128, 1], F32), "inv": env.sb("s2_w2inv", [128, 1], F32)}
    env.push_pool()
    emit_global_mean_abs(env, "s2w1", dd["w1T"], D, HID, w1m, consts)
    emit_global_mean_abs(env, "s2w2", dd["w2T"], HID, D, w2m, consts)
    env.pop_pool()
    t1_hbm = nc.dram_tensor("s2_t1b", [D, HID], BF16).ap()
    t2_hbm = nc.dram_tensor("s2_t2b", [HID, D], BF16).ap()
    env.push_pool()
    emit_ternarize(env, "s2t1", dd["w1T"], t1_hbm, D, HID, w1m["inv"])
    emit_ternarize(env, "s2t2", dd["w2T"], t2_hbm, HID, D, w2m["inv"])
    env.pop_pool()
    wbc = [env.sb(f"s2_wbc{i}", [128, HID], BF16) for i in range(4)]
    env.push_pool()
    emit_bcast_rows(env, "s2cw", dd["convw"], 3, HID, wbc[0:3], consts)
    emit_bcast_rows(env, "s2cb", dd["convb"], 1, HID, wbc[3:4], consts)
    env.pop_pool()

    h_hbm = nc.dram_tensor("s2_h", [T, HID], BF16).ap()

    qs = env.sem("s2_qs")

    def selfsem(v, ins):
        qs.n += 1
        ins.then_inc(qs.h, 1)
        v.wait_ge(qs.h, qs.n)

    env.push_pool()
    t1_sb = env.sb("s2_t1sb", [128, DCH * HID], BF16)
    ldr = env.sem("s2_ldr")
    with nc.Block() as blk:
        @blk.sync
        def _(s):
            ldr.inc(s.dma_start(out=t1_sb[:, :],
                                in_=t1_hbm.rearrange("(ic p) o -> p ic o", p=128)), 16)
            s.wait_ge(ldr.h, ldr.n)

    x1tm = env.sb("s2_x1tm", [128, 2 * 4 * 1024], BF16)
    am4 = env.sb("s2_am4", [128, 2 * 4], F32)
    s4t = env.sb("s2_s4t", [128, 4], F32)
    u4t = env.sb("s2_u4t", [128, 4], F32)
    xqf = env.sb("s2_xqf", [128, 1024], F32)
    xqi = env.sb("s2_xqi", [128, 1024], I8)
    xqb = env.sb("s2_xqb", [128, 4 * 1024], BF16)
    xq_fm = env.sb("s2_xqfm", [128, DCH * NT], BF16)
    h_sb = env.sb("s2_hsb", [128, 2 * 4096], BF16)
    ps_tp = [env.ps(f"s2_pstp{i}", [128, 512], BF16) for i in range(2)]
    psM = [env.ps(f"s2_psM{i}", [128, 512], F32) for i in range(2)]

    ldx = [env.sem("s2_ldx0"), env.sem("s2_ldx1")]
    tpq = env.sem("s2_tpq")
    evq = env.sem("s2_evq")
    mmh = env.sem("s2_mmh")
    evh = env.sem("s2_evh")
    sth = env.sem("s2_sth")

    with nc.Block() as blk:
        @blk.sync
        def _(s):
            for j in range(4):
                ldx[0].inc(s.dma_start(out=x1tm[:, j * 1024:(j + 1) * 1024],
                                       in_=dd["x1_tm"][j * 128:(j + 1) * 128, :]), 16)
            ldx[0].inc(s.dma_start(out=am4[:, 0:4], in_=dd["amax1_c"][:, 0:4]), 16)

    for t in range(ntiles):
        slot = t % 2
        xoff = slot * 4096
        nslot = (t + 1) % 2

        with nc.Block() as blk:
            ld_t0 = ldx[slot].n

            @blk.sync
            def _(s):
                if t + 1 < ntiles:
                    for j in range(4):
                        ldx[nslot].inc(s.dma_start(
                            out=x1tm[:, nslot * 4096 + j * 1024: nslot * 4096 + (j + 1) * 1024],
                            in_=dd["x1_tm"][(t + 1) * NT + j * 128:(t + 1) * NT + (j + 1) * 128, :]), 16)
                    ldx[nslot].inc(s.dma_start(out=am4[:, nslot * 4:nslot * 4 + 4],
                                               in_=dd["amax1_c"][:, (t + 1) * 4:(t + 1) * 4 + 4]), 16)

            @blk.vector
            def _(v):
                v.wait_ge(ldx[slot].h, ld_t0)
                selfsem(v, v.tensor_scalar_max(s4t[:, :], am4[:, slot * 4:slot * 4 + 4], 1e-5))
                selfsem(v, v.tensor_scalar_mul(u4t[:, :], s4t[:, :], w1m["mean"][:, :]))
                v.tensor_scalar_mul(u4t[:, :], u4t[:, :], 1.0 / 127.0)
                selfsem(v, v.reciprocal(s4t[:, :], s4t[:, :]))
                selfsem(v, v.tensor_scalar_mul(s4t[:, :], s4t[:, :], 127.0))
                for j in range(4):
                    v.tensor_scalar_mul(xqf[:, :], x1tm[:, xoff + j * 1024: xoff + (j + 1) * 1024],
                                        s4t[:, j:j + 1])
                    v.tensor_copy(xqi[:, :], xqf[:, :])
                    v.tensor_copy(xqb[:, j * 1024:(j + 1) * 1024], xqi[:, :])

        with nc.Block() as blk:
            tp0, ev0 = tpq.n, evq.n

            @blk.tensor
            def _(te):
                for k in range(8):
                    j, g = k // 2, k % 2
                    if k >= 2:
                        te.wait_ge(evq.h, ev0 + k - 1)
                    for ii in range(4):
                        i = g * 4 + ii
                        ins = te.transpose(ps_tp[k % 2][:, ii * 128:(ii + 1) * 128],
                                           xqb[:, j * 1024 + i * 128: j * 1024 + (i + 1) * 128],
                                           identbf[:, :])
                    tpq.inc(ins)

            @blk.vector
            def _(v):
                for k in range(8):
                    j, g = k // 2, k % 2
                    v.wait_ge(tpq.h, tp0 + k + 1)
                    evq.inc(v.tensor_copy(
                        xq_fm.rearrange("p (i n) -> p i n", n=NT)[:, g * 4:(g + 1) * 4, j * 128:(j + 1) * 128],
                        ps_tp[k % 2].rearrange("p (i n) -> p i n", n=128)[:, :, :]))

        with nc.Block() as blk:
            mm0, eh0, st0 = mmh.n, evh.n, sth.n

            @blk.tensor
            def _(te):
                for j in range(4):
                    for o in range(8):
                        k = j * 8 + o
                        if k >= 2:
                            te.wait_ge(evh.h, eh0 + k - 1)
                        for i in range(DCH):
                            ins = te.matmul(psM[k % 2][:, :],
                                            xq_fm[:, i * NT + j * 128: i * NT + (j + 1) * 128],
                                            t1_sb[:, i * HID + o * 512: i * HID + (o + 1) * 512],
                                            start=(i == 0), stop=(i == DCH - 1))
                        mmh.inc(ins)

            @blk.vector
            def _(v):
                for j in range(4):
                    hoff = (j % 2) * 4096
                    if j >= 2:
                        v.wait_ge(sth.h, st0 + 16 * j)
                    for o in range(8):
                        k = j * 8 + o
                        v.wait_ge(mmh.h, mm0 + k + 1)
                        evh.inc(v.tensor_scalar_mul(h_sb[:, hoff + o * 512: hoff + (o + 1) * 512],
                                                    psM[k % 2][:, :], u4t[:, j:j + 1]))

            @blk.sync
            def _(s):
                for j in range(4):
                    hoff = (j % 2) * 4096
                    s.wait_ge(evh.h, eh0 + (j + 1) * 8)
                    sth.inc(s.dma_start(
                        out=h_hbm[t * NT + j * 128: t * NT + (j + 1) * 128, :],
                        in_=h_sb[:, hoff:hoff + 4096]), 16)

    with nc.Block() as blk:
        @blk.sync
        def _(s):
            s.wait_ge(sth.h, sth.n)
    env.pop_pool()

    env.push_pool()
    h0_sb = env.sb("s2_h0", [128, 4 * 4096], BF16)
    hm_sb = env.sb("s2_hm", [128, 4096], BF16)
    hp_sb = env.sb("s2_hp", [128, 4096], BF16)
    cv_sb = env.sb("s2_cv", [128, 4096], BF16)
    sil_sb = env.sb("s2_sil", [128, 4096], BF16)
    x1p = env.sb("s2_x1p", [128, 4 * 1024], BF16)
    a2t = env.sb("s2_a2t", [128, 4], F32)
    s2t = env.sb("s2_s2t", [128, 4], F32)
    u2t = env.sb("s2_u2t", [128, 4], F32)
    xq2f = env.sb("s2_xq2f", [128, 2048], F32)
    xq2i = env.sb("s2_xq2i", [128, 4096], I8)
    xq2b = [env.sb(f"s2_xq2b{i}", [128, 4096], BF16) for i in range(2)]
    xq2fm = env.sb("s2_xq2fm", [128, 4 * 4096], BF16)
    y2_sb = env.sb("s2_y2", [128, 4 * 1024], BF16)
    t2blk = [env.sb(f"s2_t2blk{i}", [128, 512], BF16) for i in range(2)]
    ps_tp2 = [env.ps(f"s2_pstp2{i}", [128, 512], BF16) for i in range(2)]
    ps4 = [env.ps(f"s2_ps4{i}", [128, 512], F32) for i in range(4)]
    v4 = env.sb("s2_v4", [128, 4 * 1024], F32)
    o2_sb = env.sb("s2_o2", [128, 4 * 1024], BF16)
    lnst = mk_ln_stats(env, "s2ln")

    ldh = env.sem("s2_ldh")
    shd = env.sem("s2_shd")
    mzb = env.sem("s2_mzb")
    vq2 = env.sem("s2_vq2")
    cvs = env.sem("s2_cvs")
    cvg = env.sem("s2_cvg")
    sil = env.sem("s2_sil")
    tp2 = env.sem("s2_tp2")
    ev2 = env.sem("s2_ev2")
    ld2 = [env.sem("s2_ld2a"), env.sem("s2_ld2b")]
    mm2 = env.sem("s2_mm2")
    evy = env.sem("s2_evy")
    st2 = env.sem("s2_st2")

    for t in range(ntiles):
        with nc.Block() as blk:
            ld0 = ldh.n

            @blk.sync
            def _(s):
                for j in range(4):
                    r0 = t * NT + j * 128
                    ldh.inc(s.dma_start(out=h0_sb[:, j * 4096:(j + 1) * 4096],
                                        in_=h_hbm[r0:r0 + 128, :]), 16)
                    ldh.inc(s.dma_start(out=x1p[:, j * 1024:(j + 1) * 1024],
                                        in_=dd["x1_tm"][r0:r0 + 128, :]), 16)

        with nc.Block() as blk:
            sh0, cv0, cg0, si0, q20, tp0, ev0, mz0 = (shd.n, cvs.n, cvg.n, sil.n,
                                                      vq2.n, tp2.n, ev2.n, mzb.n)
            bdry = [(t * NT + j * 128, (t * NT + j * 128) % S == 0,
                     (t * NT + j * 128 + 128) % S == 0) for j in range(4)]
            nmz_at = []
            nmz = 0
            for j in range(4):
                nmz += (1 if bdry[j][1] else 0) + (1 if bdry[j][2] else 0)
                nmz_at.append(nmz)

            @blk.sync
            def _(s):
                for j in range(4):
                    r0, bm, bp = bdry[j]
                    s.wait_ge(mzb.h, mz0 + nmz_at[j])
                    if j >= 1:
                        s.wait_ge(cvg.h, cg0 + j)
                        s.wait_ge(cvs.h, cv0 + 3 * j)
                    if bm:
                        shd.inc(s.dma_start(out=hm_sb[1:128, :], in_=h_hbm[r0:r0 + 127, :]), 16)
                    else:
                        shd.inc(s.dma_start(out=hm_sb[:, :], in_=h_hbm[r0 - 1:r0 + 127, :]), 16)
                    if bp:
                        shd.inc(s.dma_start(out=hp_sb[0:127, :], in_=h_hbm[r0 + 1:r0 + 128, :]), 16)
                    else:
                        shd.inc(s.dma_start(out=hp_sb[:, :], in_=h_hbm[r0 + 1:r0 + 129, :]), 16)
                    s.wait_ge(shd.h, shd.n)
                    s.wait_ge(ldh.h, ldh.n)

            @blk.gpsimd
            def _(g):
                for j in range(4):
                    g.wait_ge(shd.h, sh0 + 32 * (j + 1))
                    g.wait_ge(cvs.h, cv0 + 3 * j + 1)
                    g.tensor_mul(hm_sb[:, :], hp_sb[:, :], wbc[2][:, :])
                    g.tensor_mul(hp_sb[:, :], h0_sb[:, j * 4096:(j + 1) * 4096], wbc[1][:, :])
                    cvg.inc(g.tensor_add(hp_sb[:, :], hp_sb[:, :], hm_sb[:, :]))

            @blk.scalar
            def _(sc):
                for j in range(4):
                    sc.wait_ge(cvs.h, cv0 + 3 * j + 3)
                    if j >= 1:
                        sc.wait_ge(vq2.h, q20 + j)
                    sil.inc(sc.activation(sil_sb[:, :], cv_sb[:, :], AF.Silu))

            @blk.vector
            def _(v):
                for j in range(4):
                    r0, bm, bp = bdry[j]
                    if bm:
                        mzb.inc(v.memset(hm_sb[:, :], 0.0))
                    if bp:
                        mzb.inc(v.memset(hp_sb[:, :], 0.0))
                    v.wait_ge(shd.h, sh0 + 32 * (j + 1))
                    cvs.inc(v.tensor_mul(cv_sb[:, :], hm_sb[:, :], wbc[0][:, :]))
                    cvs.inc(v.tensor_add(cv_sb[:, :], cv_sb[:, :], wbc[3][:, :]))
                    v.wait_ge(cvg.h, cg0 + j + 1)
                    cvs.inc(v.tensor_add(cv_sb[:, :], cv_sb[:, :], hp_sb[:, :]))
                    v.wait_ge(sil.h, si0 + j + 1)
                    if j >= 2:
                        v.wait_ge(tp2.h, tp0 + 8 * (j - 1))
                    selfsem(v, v.reduce_max(a2t[:, j:j + 1], sil_sb[:, :], axis=AX.X,
                                            apply_absolute_value=True))
                    selfsem(v, v.tensor_scalar_max(s2t[:, j:j + 1], a2t[:, j:j + 1], 1e-5))
                    selfsem(v, v.tensor_scalar_mul(u2t[:, j:j + 1], s2t[:, j:j + 1], w2m["mean"][:, :]))
                    v.tensor_scalar_mul(u2t[:, j:j + 1], u2t[:, j:j + 1], 1.0 / 127.0)
                    selfsem(v, v.reciprocal(s2t[:, j:j + 1], s2t[:, j:j + 1]))
                    selfsem(v, v.tensor_scalar_mul(s2t[:, j:j + 1], s2t[:, j:j + 1], 127.0))
                    for hh in range(2):
                        v.tensor_scalar_mul(xq2f[:, :], sil_sb[:, hh * 2048:(hh + 1) * 2048],
                                            s2t[:, j:j + 1])
                        v.tensor_copy(xq2i[:, hh * 2048:(hh + 1) * 2048], xq2f[:, :])
                    vq2.inc(v.tensor_copy(xq2b[j % 2][:, :], xq2i[:, :]))
                    if j >= 1:
                        for g8 in range(8):
                            k = (j - 1) * 8 + g8
                            v.wait_ge(tp2.h, tp0 + k + 1)
                            ev2.inc(v.tensor_copy(
                                xq2fm[:, (j - 1) * 4096 + g8 * 512: (j - 1) * 4096 + (g8 + 1) * 512],
                                ps_tp2[k % 2][:, :]))
                for g8 in range(8):
                    k = 3 * 8 + g8
                    v.wait_ge(tp2.h, tp0 + k + 1)
                    ev2.inc(v.tensor_copy(xq2fm[:, 3 * 4096 + g8 * 512: 3 * 4096 + (g8 + 1) * 512],
                                          ps_tp2[k % 2][:, :]))

            @blk.tensor
            def _(te):
                for j in range(4):
                    te.wait_ge(vq2.h, q20 + j + 1)
                    for g8 in range(8):
                        k = j * 8 + g8
                        if k >= 2:
                            te.wait_ge(ev2.h, ev0 + k - 1)
                        for ii in range(4):
                            c = g8 * 4 + ii
                            ins = te.transpose(ps_tp2[k % 2][:, ii * 128:(ii + 1) * 128],
                                               xq2b[j % 2][:, c * 128:(c + 1) * 128], identbf[:, :])
                        tp2.inc(ins)

        for o in range(2):
            with nc.Block() as blk:
                mm0, ey0 = mm2.n, evy.n
                l20 = [ld2[0].n, ld2[1].n]

                @blk.gpsimd
                def _(g):
                    for c in range(HCH):
                        p = c % 2
                        if c >= 2:
                            g.wait_ge(mm2.h, mm0 + 4 * (c - 1))
                        ld2[p].inc(g.dma_start(out=t2blk[p][:, :],
                                               in_=t2_hbm[c * 128:(c + 1) * 128, o * 512:(o + 1) * 512]), 16)

                @blk.tensor
                def _(te):
                    for c in range(HCH):
                        p = c % 2
                        te.wait_ge(ld2[p].h, l20[p] + 16 * (c // 2 + 1))
                        for j in range(4):
                            mm2.inc(te.matmul(ps4[j][:, :],
                                              xq2fm[:, j * 4096 + c * 128: j * 4096 + (c + 1) * 128],
                                              t2blk[p][:, :],
                                              start=(c == 0), stop=(c == HCH - 1)))

                @blk.vector
                def _(v):
                    v.wait_ge(mm2.h, mm0 + 4 * HCH)
                    for j in range(4):
                        evy.inc(v.tensor_scalar_mul(
                            y2_sb[:, j * 1024 + o * 512: j * 1024 + (o + 1) * 512],
                            ps4[j][:, :], u2t[:, j:j + 1]))

        with nc.Block() as blk:
            @blk.vector
            def _(v):
                for j in range(4):
                    v.tensor_add(v4[:, j * 1024:(j + 1) * 1024],
                                 x1p[:, j * 1024:(j + 1) * 1024],
                                 y2_sb[:, j * 1024:(j + 1) * 1024])

        emit_ln_tail(env, consts, v4, o2_sb, 4, lnst)
        with nc.Block() as blk:
            @blk.sync
            def _(s):
                for j in range(4):
                    st2.inc(s.dma_start(
                        out=dd["x2_tm"][t * NT + j * 128: t * NT + (j + 1) * 128, :],
                        in_=o2_sb[:, j * 1024:(j + 1) * 1024]), 16)
                s.wait_ge(st2.h, st2.n)
    env.pop_pool()


def emit_s3(env, dd, T, consts, L=4096):
    nc = env.nc
    nb = T // L
    LC = L // 128
    LT = L // 512
    identbf = consts["identbf"]
    identf = consts["identf"]
    EPS = L1_ATTN_EPS
    NH = 16
    NP = NH // 2

    ctx_hbm = nc.dram_tensor("s3_ctx", [T, D], BF16).ap()
    scr_hbm = nc.dram_tensor("s3_scr", [1, 8], F32).ap()

    qs = env.sem("s3_qs")

    def selfsem(v, ins):
        qs.n += 1
        ins.then_inc(qs.h, 1)
        v.wait_ge(qs.h, qs.n)

    env.push_pool()
    wq_sb = env.sb("s3_wq", [128, DCH * D], BF16)
    wk_sb = env.sb("s3_wk", [128, DCH * D], BF16)
    wv_sb = env.sb("s3_wv", [128, DCH * D], BF16)
    w0_sb = env.sb("s3_w0", [128, 512], BF16)
    bq_sb = env.sb("s3_bq", [128, 8], F32)
    bk_sb = env.sb("s3_bk", [128, 8], F32)
    bv_sb = env.sb("s3_bv", [128, 8], F32)
    ldw0 = env.sem("s3_ldw0")
    with nc.Block() as blk:
        @blk.gpsimd
        def _(g):
            for sb, key in ((wq_sb, "wqT"), (wk_sb, "wkT"), (wv_sb, "wvT")):
                ldw0.inc(g.dma_start(out=sb[:, :],
                                     in_=dd[key].rearrange("(ic p) o -> p ic o", p=128)), 16)
            ldw0.inc(g.dma_start(out=w0_sb[:, :], in_=dd["w0l"][:, :]), 16)

        @blk.sync
        def _(s):
            for sb, key in ((bq_sb, "bq"), (bk_sb, "bk"), (bv_sb, "bv")):
                ldw0.inc(s.dma_start(out=sb[:, :], in_=dd[key][:, :]), 16)
            s.wait_ge(ldw0.h, ldw0.n)

    x2fm = env.sb("s3_x2fm", [128, DCH * L], BF16)
    x2t = env.sb("s3_x2t", [128, 1024], BF16)
    Qp = env.sb("s3_Qp", [128, L], BF16)
    Kp = env.sb("s3_Kp", [128, L], BF16)
    Vf = env.sb("s3_Vf", [128, L], BF16)
    Vaug = env.sb("s3_Vaug", [128, LC * 130], BF16)
    Ktm = env.sb("s3_Ktm", [128, LC * 128], BF16)
    ksq = env.sb("s3_ksq", [128, 128], F32)
    ssqp = env.sb("s3_ssqp", [128, 2 * LC], F32)
    ssqb = env.sb("s3_ssqb", [128, 2 * LC], F32)
    mxk = env.sb("s3_mxk", [128, 2 * LC], F32)
    mx2 = env.sb("s3_mx2", [128, 2], F32)
    mrow2 = env.sb("s3_mrow2", [2, 128], F32)
    mb21 = env.sb("s3_mb21", [2, 1], F32)
    mr12 = env.sb("s3_mr12", [1, 2], F32)
    mh2 = env.sb("s3_mh2", [128, 2], F32)
    e_pair = env.sb("s3_ep", [128, 2 * LC * 64], BF16)
    phih = env.sb("s3_phih", [64, L], BF16)
    nd_sb = env.sb("s3_nd", [65, 64], F32)
    ndT = env.sb("s3_ndT", [64, 66], BF16)
    qlogt = env.sb("s3_qlogt", [128, 2048], F32)
    mxq = env.sb("s3_mxq", [128, 32], F32)
    mxqn = env.sb("s3_mxqn", [128, 32], F32)
    phitm = env.sb("s3_phitm", [128, 2048], BF16)
    znd = env.sb("s3_znd", [65, 2], F32)
    ez_bc = env.sb("s3_ez", [128, 2], F32)
    ez1 = env.sb("s3_ez1", [128, 1], F32)
    num_sb = env.sb("s3_num", [128, LC * 66], F32)
    dent = env.sb("s3_dent", [128, LC], F32)
    slt = env.sb("s3_slt", [128, LC], F32)
    rlt = env.sb("s3_rlt", [128, LC], F32)
    strip = env.sb("s3_strip", [128, LC * 128], BF16)

    ps_mm = [env.ps(f"s3_psmm{i}", [128, 512], F32) for i in range(2)]
    ps_tp = [env.ps(f"s3_pstp{i}", [128, 512], BF16) for i in range(2)]
    ps_small = [env.ps(f"s3_pssm{i}", [128, 512], F32) for i in range(2)]
    ps_kl = [ps_small[0][:, 0:64], ps_small[1][:, 0:64]]
    ps_num = [ps_small[0][:, 64:130], ps_small[1][:, 64:130]]
    ps_nd = ps_small[0][0:65, 192:256]
    ps_ndT = ps_small[1][0:64, 192:257]
    ps_ph = [ps_mm[0][0:64, :], ps_mm[1][0:64, :]]

    ldx = env.sem("s3_ldx")
    tpx = env.sem("s3_tpx")
    evx = env.sem("s3_evx")
    mmp = env.sem("s3_mmp")
    evp = env.sem("s3_evp")
    tpv = env.sem("s3_tpv")
    evv = env.sem("s3_evv")
    tpk = env.sem("s3_tpk")
    evk = env.sem("s3_evk")
    ack = env.sem("s3_ack")
    rdk = env.sem("s3_rdk")
    mmk = env.sem("s3_mmk")
    ace = env.sem("s3_ace")
    mmn = env.sem("s3_mmn")
    evn = env.sem("s3_evn")
    tpn = env.sem("s3_tpn")
    mmf = env.sem("s3_mmf")
    acf = env.sem("s3_acf")
    mmu = env.sem("s3_mmu")
    evu = env.sem("s3_evu")
    stc = env.sem("s3_stc")
    zst = env.sem("s3_zst")

    for b in range(nb):
        for lt in range(LT):
            with nc.Block() as blk:
                lx0, tp0, ev0 = ldx.n, tpx.n, evx.n

                @blk.sync
                def _(s):
                    for j in range(4):
                        if j > 0:
                            s.wait_ge(tpx.h, tp0 + 8 * j)
                        ldx.inc(s.dma_start(out=x2t[:, :],
                                            in_=dd["x2_tm"][b * L + lt * 512 + j * 128:
                                                            b * L + lt * 512 + (j + 1) * 128, :]), 16)
                        s.wait_ge(ldx.h, ldx.n)

                @blk.tensor
                def _(te):
                    for j in range(4):
                        te.wait_ge(ldx.h, lx0 + 16 * (j + 1))
                        for i in range(DCH):
                            k = j * DCH + i
                            if k >= 2:
                                te.wait_ge(evx.h, ev0 + k - 1)
                            tpx.inc(te.transpose(ps_tp[k % 2][:, 0:128],
                                                 x2t[:, i * 128:(i + 1) * 128], identbf[:, :]))

                @blk.vector
                def _(v):
                    for j in range(4):
                        for i in range(DCH):
                            k = j * DCH + i
                            v.wait_ge(tpx.h, tp0 + k + 1)
                            evx.inc(v.tensor_copy(
                                x2fm[:, i * L + lt * 512 + j * 128: i * L + lt * 512 + (j + 1) * 128],
                                ps_tp[k % 2][:, 0:128]))

        for hp in range(NP):
            po = hp * 128

            with nc.Block() as blk:
                mm0, ev0 = mmp.n, evp.n

                @blk.tensor
                def _(te):
                    for dst, wsb, _bias in ((Qp, wq_sb, bq_sb), (Kp, wk_sb, bk_sb), (Vf, wv_sb, bv_sb)):
                        for lt in range(LT):
                            k = mmp.n - mm0
                            if k >= 2:
                                te.wait_ge(evp.h, ev0 + k - 1)
                            for i in range(DCH):
                                ins = te.matmul(ps_mm[k % 2][:, :],
                                                wsb[:, i * D + po: i * D + po + 128],
                                                x2fm[:, i * L + lt * 512: i * L + (lt + 1) * 512],
                                                start=(i == 0), stop=(i == DCH - 1))
                            mmp.inc(ins)

                @blk.vector
                def _(v):
                    for di, (dst, wsb, bias) in enumerate(
                            ((Qp, wq_sb, bq_sb), (Kp, wk_sb, bk_sb), (Vf, wv_sb, bv_sb))):
                        for lt in range(LT):
                            k = di * LT + lt
                            v.wait_ge(mmp.h, mm0 + k + 1)
                            evp.inc(v.tensor_scalar_add(dst[:, lt * 512:(lt + 1) * 512],
                                                        ps_mm[k % 2][:, :], bias[:, hp:hp + 1]))

            with nc.Block() as blk:
                tp0, ev0 = tpv.n, evv.n

                @blk.vector
                def _(v):
                    v.memset(Vaug[:, :], 1.0)

                @blk.tensor
                def _(te):
                    for lc in range(LC):
                        if lc >= 2:
                            te.wait_ge(evv.h, ev0 + lc - 1)
                        tpv.inc(te.transpose(ps_tp[lc % 2][:, 0:128],
                                             Vf[:, lc * 128:(lc + 1) * 128], identbf[:, :]))

                @blk.vector
                def _(v):
                    for lc in range(LC):
                        v.wait_ge(tpv.h, tp0 + lc + 1)
                        v.tensor_copy(Vaug[:, lc * 130: lc * 130 + 64],
                                      ps_tp[lc % 2][:, 0:64])
                        evv.inc(v.tensor_copy(Vaug[:, lc * 130 + 65: lc * 130 + 129],
                                              ps_tp[lc % 2][:, 64:128]))

            with nc.Block() as blk:
                tp0, ev0 = tpk.n, evk.n

                @blk.tensor
                def _(te):
                    for lc in range(LC):
                        if lc >= 2:
                            te.wait_ge(evk.h, ev0 + lc - 1)
                        tpk.inc(te.transpose(ps_tp[lc % 2][:, 0:128],
                                             Kp[:, lc * 128:(lc + 1) * 128], identbf[:, :]))

                @blk.vector
                def _(v):
                    for lc in range(LC):
                        v.wait_ge(tpk.h, tp0 + lc + 1)
                        evk.inc(v.tensor_copy(Ktm[:, lc * 128:(lc + 1) * 128],
                                              ps_tp[lc % 2][:, 0:128]))

            with nc.Block() as blk:
                ac0, rd0 = ack.n, rdk.n

                @blk.scalar
                def _(sc):
                    for lc in range(LC):
                        if lc >= 1:
                            sc.wait_ge(rdk.h, rd0 + lc)
                        ack.inc(sc.activation(ksq[:, :], Ktm[:, lc * 128:(lc + 1) * 128],
                                              AF.Square))

                @blk.vector
                def _(v):
                    for lc in range(LC):
                        v.wait_ge(ack.h, ac0 + lc + 1)
                        v.reduce_sum(ssqp[:, lc:lc + 1], ksq[:, 0:64], axis=AX.X)
                        rdk.inc(v.reduce_sum(ssqp[:, LC + lc:LC + lc + 1],
                                             ksq[:, 64:128], axis=AX.X))

            with nc.Block() as blk:
                @blk.vector
                def _(v):
                    v.tensor_scalar_mul(ssqp[:, :], ssqp[:, :], -0.5)

            with nc.Block() as blk:
                mm0, ae0 = mmk.n, ace.n

                @blk.tensor
                def _(te):
                    for h2 in range(2):
                        for lc in range(LC):
                            k = h2 * LC + lc
                            if k >= 2:
                                te.wait_ge(ace.h, ae0 + k - 1)
                            mmk.inc(te.matmul(ps_kl[k % 2],
                                              Kp[h2 * 64:(h2 + 1) * 64, lc * 128:(lc + 1) * 128],
                                              w0_sb[h2 * 64:(h2 + 1) * 64, hp * 64:(hp + 1) * 64],
                                              start=True, stop=True))

                @blk.vector
                def _(v):
                    for h2 in range(2):
                        for lc in range(LC):
                            k = h2 * LC + lc
                            v.wait_ge(mmk.h, mm0 + k + 1)
                            ace.inc(v.reduce_max(mxk[:, k:k + 1], ps_kl[k % 2], axis=AX.X))

            with nc.Block() as blk:
                @blk.vector
                def _(v):
                    v.tensor_add(mxk[:, :], mxk[:, :], ssqp[:, :])

            with nc.Block() as blk:
                @blk.vector
                def _(v):
                    v.reduce_max(mx2[:, 0:1], mxk[:, 0:LC], axis=AX.X)
                    v.reduce_max(mx2[:, 1:2], mxk[:, LC:2 * LC], axis=AX.X)

            with nc.Block() as blk:
                @blk.tensor
                def _(te):
                    te.transpose(ps_small[1][0:2, 256:384], mx2[:, :], identf[:, :])

            with nc.Block() as blk:
                @blk.vector
                def _(v):
                    v.tensor_copy(mrow2[0:2, :], ps_small[1][0:2, 256:384])

            with nc.Block() as blk:
                @blk.vector
                def _(v):
                    v.reduce_max(mb21[0:2, :], mrow2[0:2, :], axis=AX.X)

            with nc.Block() as blk:
                @blk.tensor
                def _(te):
                    te.transpose(ps_small[1][0:1, 384:386], mb21[0:2, :], identf[0:2, 0:2])

            with nc.Block() as blk:
                @blk.vector
                def _(v):
                    v.tensor_copy(mr12[:, :], ps_small[1][0:1, 384:386])

            with nc.Block() as blk:
                @blk.sync
                def _(s):
                    zst.inc(s.dma_start(out=scr_hbm[0:1, 2:4], in_=mr12[:, :]), 16)
                    s.wait_ge(zst.h, zst.n)
                    zst.inc(s.dma_start(out=mh2[:, 0:1],
                                        in_=scr_hbm[0:1, 2:3].broadcast_to((128, 1))), 16)
                    zst.inc(s.dma_start(out=mh2[:, 1:2],
                                        in_=scr_hbm[0:1, 3:4].broadcast_to((128, 1))), 16)
                    s.wait_ge(zst.h, zst.n)

            with nc.Block() as blk:
                @blk.vector
                def _(v):
                    for h2 in range(2):
                        v.tensor_single_scalar(ssqb[:, h2 * LC:(h2 + 1) * LC],
                                               ssqp[:, h2 * LC:(h2 + 1) * LC],
                                               mh2[:, h2:h2 + 1], ALU.subtract)

            with nc.Block() as blk:
                mm0, ae0 = mmk.n, ace.n

                @blk.tensor
                def _(te):
                    for h2 in range(2):
                        for lc in range(LC):
                            k = h2 * LC + lc
                            if k >= 2:
                                te.wait_ge(ace.h, ae0 + k - 1)
                            mmk.inc(te.matmul(ps_kl[k % 2],
                                              Kp[h2 * 64:(h2 + 1) * 64, lc * 128:(lc + 1) * 128],
                                              w0_sb[h2 * 64:(h2 + 1) * 64, hp * 64:(hp + 1) * 64],
                                              start=True, stop=True))

                @blk.scalar
                def _(sc):
                    for h2 in range(2):
                        for lc in range(LC):
                            k = h2 * LC + lc
                            sc.wait_ge(mmk.h, mm0 + k + 1)
                            ace.inc(sc.activation(
                                e_pair[:, h2 * LC * 64 + lc * 64: h2 * LC * 64 + (lc + 1) * 64],
                                ps_kl[k % 2], AF.Exp,
                                bias=ssqb[:, h2 * LC + lc: h2 * LC + lc + 1],
                                scale=1.0))

            for h2 in range(2):
                with nc.Block() as blk:
                    @blk.tensor
                    def _(te):
                        for lc in range(LC):
                            ins = te.matmul(ps_nd,
                                            Vaug[:, lc * 130 + h2 * 65: lc * 130 + (h2 + 1) * 65],
                                            e_pair[:, h2 * LC * 64 + lc * 64: h2 * LC * 64 + (lc + 1) * 64],
                                            start=(lc == 0), stop=(lc == LC - 1))
                        mmn.inc(ins)

                with nc.Block() as blk:
                    @blk.vector
                    def _(v):
                        v.tensor_copy(nd_sb[:, :], ps_nd)

                with nc.Block() as blk:
                    @blk.vector
                    def _(v):
                        v.reduce_sum(znd[64:65, h2:h2 + 1], nd_sb[64:65, :], axis=AX.X)

                with nc.Block() as blk:
                    @blk.tensor
                    def _(te):
                        te.transpose(ps_ndT, nd_sb[:, :], identf[0:65, 0:65])

                with nc.Block() as blk:
                    @blk.vector
                    def _(v):
                        v.memset(ndT[:, 65:66], 1.0)
                        v.tensor_copy(ndT[:, 0:65], ps_ndT)

                with nc.Block() as blk:
                    mm0, cq0 = mmf.n, acf.n

                    @blk.tensor
                    def _(te):
                        for lc in range(LC):
                            if lc >= 2:
                                te.wait_ge(acf.h, cq0 + lc - 1)
                            mmf.inc(te.matmul(ps_kl[lc % 2],
                                              Qp[h2 * 64:(h2 + 1) * 64, lc * 128:(lc + 1) * 128],
                                              w0_sb[h2 * 64:(h2 + 1) * 64, hp * 64:(hp + 1) * 64],
                                              start=True, stop=True))

                    @blk.vector
                    def _(v):
                        for lc in range(LC):
                            v.wait_ge(mmf.h, mm0 + lc + 1)
                            v.reduce_max(mxq[:, lc:lc + 1], ps_kl[lc % 2], axis=AX.X)
                            acf.inc(v.tensor_copy(qlogt[:, lc * 64:(lc + 1) * 64],
                                                  ps_kl[lc % 2]))

                with nc.Block() as blk:
                    @blk.vector
                    def _(v):
                        v.tensor_scalar_mul(mxqn[:, :], mxq[:, :], -1.0)

                with nc.Block() as blk:
                    @blk.scalar
                    def _(sc):
                        for lc in range(LC):
                            sc.activation(phitm[:, lc * 64:(lc + 1) * 64],
                                          qlogt[:, lc * 64:(lc + 1) * 64], AF.Exp,
                                          bias=mxqn[:, lc:lc + 1], scale=1.0)

                with nc.Block() as blk:
                    tp0, ev0 = tpn.n, mmn.n

                    @blk.tensor
                    def _(te):
                        for lc in range(LC):
                            if lc >= 2:
                                te.wait_ge(mmn.h, ev0 + lc - 1)
                            tpn.inc(te.transpose(ps_tp[lc % 2][0:64, 0:128],
                                                 phitm[:, lc * 64:(lc + 1) * 64],
                                                 identbf[:, :]))

                    @blk.vector
                    def _(v):
                        for lc in range(LC):
                            v.wait_ge(tpn.h, tp0 + lc + 1)
                            mmn.inc(v.tensor_copy(phih[:, lc * 128:(lc + 1) * 128],
                                                  ps_tp[lc % 2][0:64, 0:128]))

                with nc.Block() as blk:
                    mm0, ev0 = mmu.n, evu.n

                    @blk.tensor
                    def _(te):
                        for lc in range(LC):
                            if lc >= 2:
                                te.wait_ge(evu.h, ev0 + lc - 1)
                            mmu.inc(te.matmul(ps_num[lc % 2],
                                              phih[:, lc * 128:(lc + 1) * 128],
                                              ndT[:, :], start=True, stop=True))

                    @blk.vector
                    def _(v):
                        for lc in range(LC):
                            v.wait_ge(mmu.h, mm0 + lc + 1)
                            evu.inc(v.tensor_copy(num_sb[:, lc * 66:(lc + 1) * 66],
                                                  ps_num[lc % 2]))

                with nc.Block() as blk:
                    @blk.sync
                    def _(s):
                        zst.inc(s.dma_start(out=scr_hbm[0:1, h2:h2 + 1],
                                            in_=znd[64:65, h2:h2 + 1]), 16)
                        s.wait_ge(zst.h, zst.n)
                        zst.inc(s.dma_start(out=ez_bc[:, h2:h2 + 1],
                                            in_=scr_hbm[0:1, h2:h2 + 1].broadcast_to((128, 1))), 16)
                        s.wait_ge(zst.h, zst.n)

                with nc.Block() as blk:
                    @blk.vector
                    def _(v):
                        v.tensor_copy(dent.rearrange("p (c o) -> p c o", o=1)[:, :, :],
                                      num_sb.rearrange("p (c k) -> p c k", k=66)[:, :, 64:65])
                        v.tensor_copy(slt.rearrange("p (c o) -> p c o", o=1)[:, :, :],
                                      num_sb.rearrange("p (c k) -> p c k", k=66)[:, :, 65:66])

                with nc.Block() as blk:
                    @blk.vector
                    def _(v):
                        selfsem(v, v.tensor_scalar_mul(ez1[:, :], ez_bc[:, h2:h2 + 1], EPS))
                        selfsem(v, v.scalar_tensor_tensor(dent[:, :], slt[:, :], ez1[:, :],
                                                          dent[:, :], ALU.mult, ALU.add))
                        selfsem(v, v.reciprocal(rlt[:, :], dent[:, :]))
                        for lc in range(LC):
                            v.tensor_scalar_mul(
                                strip[:, lc * 128 + h2 * 64: lc * 128 + h2 * 64 + 64],
                                num_sb[:, lc * 66: lc * 66 + 64], rlt[:, lc:lc + 1])

            with nc.Block() as blk:
                @blk.sync
                def _(s):
                    stc.inc(s.dma_start(
                        out=ctx_hbm[b * L:(b + 1) * L, po:po + 128]
                            .rearrange("(c p) f -> p c f", p=128),
                        in_=strip.rearrange("p (c f) -> p c f", f=128)), 16)
                    s.wait_ge(stc.h, stc.n)
    env.pop_pool()

    env.push_pool()
    wo_sb = env.sb("s3_wo", [128, DCH * D], BF16)
    bo_bc = env.sb("s3_bobc", [128, D], BF16)
    ldo = env.sem("s3_ldo")
    with nc.Block() as blk:
        @blk.gpsimd
        def _(g):
            ldo.inc(g.dma_start(out=wo_sb[:, :],
                                in_=dd["woT"].rearrange("(ic p) o -> p ic o", p=128)), 16)

        @blk.sync
        def _(s):
            s.wait_ge(ldo.h, ldo.n)
    emit_bcast_rows(env, "s3bo", dd["bo_row"], 1, D, [bo_bc], consts)

    ctxt = env.sb("s3_ctxt", [128, 4 * 1024], BF16)
    x2r = env.sb("s3_x2r", [128, 4 * 1024], BF16)
    cfm = env.sb("s3_cfm", [128, DCH * 128], BF16)
    y3 = env.sb("s3_y3", [128, 1024], BF16)
    v4 = env.sb("s3_v4", [128, 4 * 1024], F32)
    o3 = env.sb("s3_o3", [128, 4 * 1024], BF16)
    lnst3 = mk_ln_stats(env, "s3ln")
    ps_tp3 = [env.ps(f"s3_pstp3{i}", [128, 512], BF16) for i in range(2)]
    ps_mm3 = [env.ps(f"s3_psmm3{i}", [128, 512], F32) for i in range(2)]

    ldc = env.sem("s3_ldc")
    tpc = env.sem("s3_tpc")
    evc = env.sem("s3_evc")
    mmo = env.sem("s3_mmo")
    evo = env.sem("s3_evo")
    st3 = env.sem("s3_st3")

    for t in range(T // 512):
        with nc.Block() as blk:
            ld0 = ldc.n

            @blk.sync
            def _(s):
                for j in range(4):
                    r0 = t * 512 + j * 128
                    ldc.inc(s.dma_start(out=ctxt[:, j * 1024:(j + 1) * 1024],
                                        in_=ctx_hbm[r0:r0 + 128, :]), 16)
                    ldc.inc(s.dma_start(out=x2r[:, j * 1024:(j + 1) * 1024],
                                        in_=dd["x2_tm"][r0:r0 + 128, :]), 16)

        for j in range(4):
            with nc.Block() as blk:
                tp0, ev0 = tpc.n, evc.n

                @blk.tensor
                def _(te):
                    te.wait_ge(ldc.h, ldc.n)
                    for g in range(2):
                        if g >= 2:
                            te.wait_ge(evc.h, ev0 + g - 1)
                        for ii in range(4):
                            i = g * 4 + ii
                            ins = te.transpose(ps_tp3[g % 2][:, ii * 128:(ii + 1) * 128],
                                               ctxt[:, j * 1024 + i * 128: j * 1024 + (i + 1) * 128],
                                               identbf[:, :])
                        tpc.inc(ins)

                @blk.vector
                def _(v):
                    for g in range(2):
                        v.wait_ge(tpc.h, tp0 + g + 1)
                        evc.inc(v.tensor_copy(
                            cfm.rearrange("p (i n) -> p i n", n=128)[:, g * 4:(g + 1) * 4, :],
                            ps_tp3[g % 2].rearrange("p (i n) -> p i n", n=128)[:, :, :]))

            with nc.Block() as blk:
                mm0, eo0 = mmo.n, evo.n

                @blk.tensor
                def _(te):
                    for o in range(2):
                        for c in range(DCH):
                            ins = te.matmul(ps_mm3[o % 2][:, :],
                                            cfm[:, c * 128:(c + 1) * 128],
                                            wo_sb[:, c * D + o * 512: c * D + (o + 1) * 512],
                                            start=(c == 0), stop=(c == DCH - 1))
                        mmo.inc(ins)

                @blk.vector
                def _(v):
                    for o in range(2):
                        v.wait_ge(mmo.h, mm0 + o + 1)
                        evo.inc(v.scalar_tensor_tensor(y3[:, o * 512:(o + 1) * 512],
                                                       ps_mm3[o % 2][:, :], 1.0,
                                                       bo_bc[:, o * 512:(o + 1) * 512],
                                                       ALU.mult, ALU.add))
                    v.tensor_add(v4[:, j * 1024:(j + 1) * 1024],
                                 x2r[:, j * 1024:(j + 1) * 1024], y3[:, :])

        emit_ln_tail(env, consts, v4, o3, 4, lnst3)
        with nc.Block() as blk:
            @blk.sync
            def _(s):
                for j in range(4):
                    st3.inc(s.dma_start(
                        out=dd["x3_tm"][t * 512 + j * 128: t * 512 + (j + 1) * 128, :],
                        in_=o3[:, j * 1024:(j + 1) * 1024]), 16)
                s.wait_ge(st3.h, st3.n)
    env.pop_pool()


def emit_s1(env, dd, T, consts, S=4096):
    nc = env.nc
    nb = T // S
    RB = S // 16
    NSB = RB // 128
    NKT = S // 128
    HALF = S // 2
    NQC = HALF // 128
    identbf = consts["identbf"]
    identf = consts["identf"]

    y_hbm = nc.dram_tensor("s1_y", [T, D], BF16).ap()

    qs = env.sem("s1_qs")

    def selfsem(v, ins):
        qs.n += 1
        ins.then_inc(qs.h, 1)
        v.wait_ge(qs.h, qs.n)

    env.push_pool()
    wq_sb = env.sb("s1_wq", [128, DCH * D], BF16)
    wk_sb = env.sb("s1_wk", [128, DCH * D], BF16)
    wv_sb = env.sb("s1_wv", [128, DCH * D], BF16)
    ldw = env.sem("s1_ldw")
    with nc.Block() as blk:
        @blk.gpsimd
        def _(g):
            for sb, key in ((wq_sb, "wqT"), (wk_sb, "wkT"), (wv_sb, "wvT")):
                ldw.inc(g.dma_start(out=sb[:, :],
                                    in_=dd[key].rearrange("(ic p) o -> p ic o", p=128)), 16)

        @blk.sync
        def _(s):
            s.wait_ge(ldw.h, ldw.n)

    xf_sb = env.sb("s1_xf", [128, DCH * RB], F32)
    xb_sb = env.sb("s1_xb", [128, DCH * RB], BF16)
    xtm_sb = env.sb("s1_xtm", [128, NSB * 1024], F32)
    Yq = env.sb("s1_Yq", [128, DCH * RB], BF16)
    Yk = env.sb("s1_Yk", [128, DCH * RB], BF16)
    Yqa = env.sb("s1_Yqa", [128, DCH * RB], BF16)
    Vaug = env.sb("s1_Vaug", [128, NKT * 65], BF16)
    tmsq_all = env.sb("s1_tmsqa", [128, 2 * DCH * NSB * 128], BF16)
    ssqall = env.sb("s1_ssqall", [128, 4 * DCH * NSB], F32)
    mx1 = env.sb("s1_mx1", [128, 2], F32)
    s2r = env.sb("s1_s2r", [1, 256], F32)
    m11 = env.sb("s1_m11", [1, 4], F32)
    cb1 = env.sb("s1_cb1", [1, 1], BF16)
    cbias = env.sb("s1_cbias", [128, 1], F32)
    e_sb = [env.sb(f"s1_e{i}", [128, 2048], BF16) for i in range(2)]
    e1_sb = [env.sb(f"s1_ea{i}", [128, 2048], BF16) for i in range(2)]
    pv_sb = env.sb("s1_pv", [65, 2048], F32)
    rl1 = env.sb("s1_rl1", [128, 1], F32)
    rl4 = env.sb("s1_rl4", [128, 4], F32)
    ytm = env.sb("s1_ytm", [128, NSB * 1024], BF16)

    ldx = env.sem("s1_ldx")
    mmp = env.sem("s1_mmp")
    evp = env.sem("s1_evp")
    alt = env.sem("s1_alt")
    tpb = env.sem("s1_tpb")
    acb = env.sem("s1_acb")
    rdb = env.sem("s1_rdb")
    tsc = env.sem("s1_tsc")
    aex = env.sem("s1_aex")
    psq = env.sem("s1_psq")
    vsq = env.sem("s1_vsq")
    tpv2 = env.sem("s1_tpv")
    tpo = env.sem("s1_tpo")
    evo = env.sem("s1_evo")
    sty = env.sem("s1_sty")

    for inst in range(nb * 16):
        b, g = inst // 16, inst % 16
        c0 = b * S + g * RB

        with nc.Block() as blk:
            lx0 = ldx.n

            @blk.sync
            def _(s):
                ldx.inc(s.dma_start(
                    out=xf_sb.rearrange("p (i r) -> p i r", r=RB),
                    in_=dd["x_fm"].rearrange("(i p) t -> p i t", p=128)[:, :, c0:c0 + RB]), 16)
                for jj in range(NSB):
                    ldx.inc(s.dma_start(out=xtm_sb[:, jj * 1024:(jj + 1) * 1024],
                                        in_=dd["x_tm"][c0 + jj * 128: c0 + (jj + 1) * 128, :]), 16)
                s.wait_ge(ldx.h, ldx.n)

            @blk.vector
            def _(v):
                v.wait_ge(ldx.h, lx0 + 16 * (NSB + 1))
                v.tensor_copy(xb_sb[:, :], xf_sb[:, :])

        env.push_pool()
        ps_a = [env.ps(f"s1_psa{i}_{inst}", [128, 512], F32) for i in range(2)]
        ps_tpb4 = [env.ps(f"s1_pstpb{i}_{inst}", [128, 512], BF16) for i in range(2)]
        with nc.Block() as blk:
            mm0, ev0 = mmp.n, evp.n

            @blk.tensor
            def _(te):
                k = 0
                for dst, wsb in ((Yq, wq_sb), (Yk, wk_sb)):
                    for o in range(DCH):
                        if k >= 2:
                            te.wait_ge(evp.h, ev0 + k - 1)
                        for i in range(DCH):
                            ins = te.matmul(ps_a[k % 2][:, 0:RB],
                                            wsb[:, i * D + o * 128: i * D + (o + 1) * 128],
                                            xb_sb[:, i * RB:(i + 1) * RB],
                                            start=(i == 0), stop=(i == DCH - 1))
                        mmp.inc(ins)
                        k += 1
                for jj in range(NSB):
                    for o in range(2):
                        if k >= 2:
                            te.wait_ge(evp.h, ev0 + k - 1)
                        for i in range(DCH):
                            ins = te.matmul(ps_a[k % 2][:, :],
                                            xb_sb[:, i * RB + jj * 128: i * RB + (jj + 1) * 128],
                                            wv_sb[:, i * D + o * 512: i * D + (o + 1) * 512],
                                            start=(i == 0), stop=(i == DCH - 1))
                        mmp.inc(ins)
                        k += 1

            @blk.vector
            def _(v):
                v.memset(Vaug[:, :], 1.0)
                k = 0
                for dst, wsb in ((Yq, wq_sb), (Yk, wk_sb)):
                    for o in range(DCH):
                        v.wait_ge(mmp.h, mm0 + k + 1)
                        if dst is Yq:
                            evp.inc(v.tensor_scalar_mul(dst[:, o * RB:(o + 1) * RB],
                                                        ps_a[k % 2][:, 0:RB], 0.5))
                        else:
                            evp.inc(v.tensor_copy(dst[:, o * RB:(o + 1) * RB], ps_a[k % 2][:, 0:RB]))
                        k += 1
                for jj in range(NSB):
                    for o in range(2):
                        v.wait_ge(mmp.h, mm0 + k + 1)
                        for uu in range(8):
                            ku = o * 8 + uu
                            kt = ku * NSB + jj
                            ins = v.tensor_copy(Vaug[:, kt * 65: kt * 65 + 64],
                                                ps_a[k % 2][:, uu * 64:(uu + 1) * 64])
                        evp.inc(ins)
                        k += 1

        with nc.Block() as blk:
            @blk.sync
            def _(s):
                alt.inc(s.dma_start(out=Yqa[0:64, :], in_=Yq[64:128, :]), 16)
                alt.inc(s.dma_start(out=Yqa[64:128, :], in_=Yq[0:64, :]), 16)
                s.wait_ge(alt.h, alt.n)

        ntp = DCH * NSB
        with nc.Block() as blk:
            tp0, ac0 = tpb.n, acb.n

            @blk.tensor
            def _(te):
                for src_i, src in enumerate((Yq, Yk)):
                    for grp in range(ntp // 4):
                        g4 = src_i * (ntp // 4) + grp
                        if g4 >= 2:
                            te.wait_ge(acb.h, ac0 + g4 - 1)
                        for ii in range(4):
                            jc = grp * 4 + ii
                            ins = te.transpose(ps_tpb4[g4 % 2][:, ii * 128:(ii + 1) * 128],
                                               src[:, jc * 128:(jc + 1) * 128], identbf[:, :])
                        tpb.inc(ins)

            @blk.vector
            def _(v):
                for g4 in range(2 * (ntp // 4)):
                    v.wait_ge(tpb.h, tp0 + g4 + 1)
                    acb.inc(v.tensor_copy(tmsq_all[:, g4 * 512:(g4 + 1) * 512],
                                          ps_tpb4[g4 % 2][:, :]))

        with nc.Block() as blk:
            @blk.scalar
            def _(sc):
                sc.activation(tmsq_all[:, :], tmsq_all[:, :], AF.Square)

        with nc.Block() as blk:
            @blk.vector
            def _(v):
                v.reduce_sum(ssqall.rearrange("p (k o) -> p k o", o=1)[:, :, :],
                             tmsq_all.rearrange("p (k f) -> p k f", f=64)[:, :, :],
                             axis=AX.X)

        with nc.Block() as blk:
            @blk.vector
            def _(v):
                v.reduce_max(mx1[:, 0:1], ssqall[:, 0:2 * ntp], axis=AX.X)
                v.reduce_max(mx1[:, 1:2], ssqall[:, 2 * ntp:4 * ntp], axis=AX.X)

        with nc.Block() as blk:
            @blk.tensor
            def _(te):
                te.transpose(ps_a[0][0:1, 0:128], mx1[:, 0:1], identf[:, :])
                te.transpose(ps_a[0][0:1, 128:256], mx1[:, 1:2], identf[:, :])

        with nc.Block() as blk:
            @blk.vector
            def _(v):
                v.tensor_copy(s2r[:, :], ps_a[0][0:1, 0:256])

        with nc.Block() as blk:
            @blk.vector
            def _(v):
                v.reduce_max(m11[:, 0:1], s2r[:, 0:128], axis=AX.X)
                v.reduce_max(m11[:, 1:2], s2r[:, 128:256], axis=AX.X)

        with nc.Block() as blk:
            @blk.vector
            def _(v):
                v.tensor_mul(m11[:, 2:3], m11[:, 0:1], m11[:, 1:2])

        with nc.Block() as blk:
            @blk.scalar
            def _(sc):
                sc.activation(m11[:, 3:4], m11[:, 2:3], AF.Sqrt)

        with nc.Block() as blk:
            @blk.vector
            def _(v):
                selfsem(v, v.tensor_scalar_mul(m11[:, 2:3], m11[:, 3:4], -1.0 / 8.0))
                v.tensor_copy(cb1[:, :], m11[:, 2:3])

        with nc.Block() as blk:
            @blk.tensor
            def _(te):
                te.matmul(ps_a[1][:, 0:1], consts["onesbf"][:, :], cb1[:, :],
                          start=True, stop=True)

        with nc.Block() as blk:
            @blk.vector
            def _(v):
                v.tensor_copy(cbias[:, :], ps_a[1][:, 0:1])
        env.pop_pool()

        env.push_pool()
        ps_scq = [env.ps(f"s1_psscA_{inst}", [128, 4 * RB], F32),
                  env.ps(f"s1_psscB_{inst}", [128, 4 * RB], F32)]
        ps_pv = env.ps(f"s1_pspv_{inst}", [65, HALF], F32)
        QW = 4 * RB
        NMM = QW // 512 if QW >= 512 else 1
        for qh in range(2):
            with nc.Block() as blk:
                ts0, ae0, tv0, vs0 = tsc.n, aex.n, tpv2.n, vsq.n

                def sc_mms(te, kt, qq):
                    ku = kt // NSB
                    r0 = (kt % NSB) * 128
                    kb = (ku % 2) * 64
                    lhsT = Yk[kb:kb + 64, (ku // 2) * RB + r0:(ku // 2) * RB + r0 + 128]
                    srcq = Yq if qq == (ku % 2) else Yqa
                    w = QW // NMM
                    for jj in range(NMM):
                        rhs = srcq[kb:kb + 64,
                                   (qh * 4) * RB + jj * w:(qh * 4) * RB + (jj + 1) * w]
                        ins = te.matmul(ps_scq[qq][:, jj * w:(jj + 1) * w], lhsT, rhs,
                                        start=True, stop=True)
                    return ins

                def pv_mms(te, kt):
                    for qg in range(HALF // 512):
                        ins = te.matmul(ps_pv[:, qg * 512:(qg + 1) * 512],
                                        Vaug[:, kt * 65:(kt + 1) * 65],
                                        e_sb[kt % 2][:, qg * 512:(qg + 1) * 512],
                                        start=(kt == 0), stop=(kt == NKT - 1))
                    return ins

                @blk.tensor
                def _(te):
                    for kt in range(NKT):
                        for qq in range(2):
                            if kt >= 1:
                                te.wait_ge(aex.h, ae0 + 2 * (kt - 1) + qq + 1)
                            tsc.inc(sc_mms(te, kt, qq))
                        if kt >= 1:
                            te.wait_ge(vsq.h, vs0 + kt)
                            tpv2.inc(pv_mms(te, kt - 1))
                    te.wait_ge(vsq.h, vs0 + NKT)
                    tpv2.inc(pv_mms(te, NKT - 1))

                @blk.scalar
                def _(sc):
                    for kt in range(NKT):
                        for qq in range(2):
                            sc.wait_ge(tsc.h, ts0 + 2 * kt + qq + 1)
                            if kt >= 1 and qq == 0:
                                sc.wait_ge(vsq.h, vs0 + kt - 1)
                            aex.inc(sc.activation(
                                e1_sb[kt % 2][:, qq * QW:(qq + 1) * QW],
                                ps_scq[qq][:, 0:QW], AF.Exp, bias=cbias[:, :], scale=1.0))

                @blk.vector
                def _(v):
                    for kt in range(NKT):
                        v.wait_ge(aex.h, ae0 + 2 * kt + 2)
                        if kt >= 2:
                            v.wait_ge(tpv2.h, tv0 + kt - 1)
                        vsq.inc(v.tensor_mul(e_sb[kt % 2][:, 0:HALF],
                                             e1_sb[kt % 2][:, 0:HALF],
                                             e1_sb[kt % 2][:, 0:HALF]))

            with nc.Block() as blk:
                @blk.vector
                def _(v):
                    v.tensor_copy(pv_sb[:, 0:HALF], ps_pv[:, :])

            with nc.Block() as blk:
                to0, eo0 = tpo.n, evo.n
                NG = NQC // 4

                @blk.tensor
                def _(te):
                    for g in range(NG):
                        if g >= 2:
                            te.wait_ge(evo.h, eo0 + g - 1)
                        for ii in range(4):
                            qc = g * 4 + ii
                            ins = te.transpose(ps_scq[g % 2][:, ii * 65:(ii + 1) * 65],
                                               pv_sb[:, qc * 128:(qc + 1) * 128],
                                               identf[0:65, 0:65])
                        tpo.inc(ins)

                @blk.vector
                def _(v):
                    for g in range(NG):
                        v.wait_ge(tpo.h, to0 + g + 1)
                        for ii in range(4):
                            ins = v.tensor_copy(rl4[:, ii:ii + 1],
                                                ps_scq[g % 2][:, ii * 65 + 64: ii * 65 + 65])
                        selfsem(v, ins)
                        selfsem(v, v.reciprocal(rl4[:, :], rl4[:, :]))
                        for ii in range(4):
                            qc = g * 4 + ii
                            col = qc * 128
                            quarter = col // QW
                            c2 = col % QW
                            qu = qh * 8 + 2 * (c2 // RB) + quarter
                            sb0 = (c2 % RB) // 128
                            ins = v.scalar_tensor_tensor(
                                ytm[:, sb0 * 1024 + qu * 64: sb0 * 1024 + (qu + 1) * 64],
                                ps_scq[g % 2][:, ii * 65: ii * 65 + 64],
                                rl4[:, ii:ii + 1],
                                xtm_sb[:, sb0 * 1024 + qu * 64: sb0 * 1024 + (qu + 1) * 64],
                                ALU.mult, ALU.add)
                        evo.inc(ins)
        env.pop_pool()

        with nc.Block() as blk:
            @blk.sync
            def _(s):
                for jj in range(NSB):
                    sty.inc(s.dma_start(out=y_hbm[c0 + jj * 128: c0 + (jj + 1) * 128, :],
                                        in_=ytm[:, jj * 1024:(jj + 1) * 1024]), 16)
                s.wait_ge(sty.h, sty.n)
    env.pop_pool()
    return y_hbm


def emit_s1b(env, dd, y_hbm, T, consts, S=4096):
    nc = env.nc
    nb = T // S
    NBT = S // 128
    ntiles = T // 128
    GN = S * 1024
    identf = consts["identf"]

    env.push_pool()
    ybuf = env.sb("s1b_y", [128, 2 * 1024], BF16)
    ysq = env.sb("s1b_ysq", [128, 1024], BF16)
    ys = env.sb("s1b_ys", [128, ntiles], F32)
    sqs = env.sb("s1b_sqs", [128, ntiles], F32)
    bsum = env.sb("s1b_bsum", [128, 2], F32)
    brow2 = env.sb("s1b_brow2", [2, 128], F32)
    b21 = env.sb("s1b_b21", [2, 1], F32)
    bsc = env.sb("s1b_bsc", [1, 8], F32)
    mub = env.sb("s1b_mub", [128, 2], F32)
    rsb = env.sb("s1b_rsb", [128, 2], F32)
    scr_hbm = nc.dram_tensor("s1b_scr", [1, 4], F32).ap()
    ps_b = env.ps("s1b_ps", [128, 512], F32)

    qs = env.sem("s1b_qs")

    def selfsem(v, ins):
        qs.n += 1
        ins.then_inc(qs.h, 1)
        v.wait_ge(qs.h, qs.n)

    ldy = [env.sem("s1b_ldy0"), env.sem("s1b_ldy1")]
    rds = env.sem("s1b_rds")
    acq = env.sem("s1b_acq")
    rdq = env.sem("s1b_rdq")
    zst = env.sem("s1b_zst")

    with nc.Block() as blk:
        r0, a0, q0 = rds.n, acq.n, rdq.n
        ldy0 = [ldy[0].n, ldy[1].n]

        @blk.sync
        def _(s):
            for t in range(ntiles):
                p = t % 2
                if t >= 2:
                    s.wait_ge(rdq.h, q0 + t - 1)
                ldy[p].inc(s.dma_start(out=ybuf[:, p * 1024:(p + 1) * 1024],
                                       in_=y_hbm[t * 128:(t + 1) * 128, :]), 16)

        @blk.vector
        def _(v):
            for t in range(ntiles):
                p = t % 2
                v.wait_ge(ldy[p].h, ldy0[p] + 16 * (t // 2 + 1))
                rds.inc(v.reduce_sum(ys[:, t:t + 1], ybuf[:, p * 1024:(p + 1) * 1024], axis=AX.X))
                v.wait_ge(acq.h, a0 + t + 1)
                rdq.inc(v.reduce_sum(sqs[:, t:t + 1], ysq[:, :], axis=AX.X))

        @blk.scalar
        def _(sc):
            for t in range(ntiles):
                p = t % 2
                sc.wait_ge(ldy[p].h, ldy0[p] + 16 * (t // 2 + 1))
                if t >= 1:
                    sc.wait_ge(rdq.h, q0 + t)
                acq.inc(sc.activation(ysq[:, :], ybuf[:, p * 1024:(p + 1) * 1024], AF.Square))

    for b in range(nb):
        with nc.Block() as blk:
            @blk.vector
            def _(v):
                v.reduce_sum(bsum[:, 0:1], ys[:, b * NBT:(b + 1) * NBT], axis=AX.X)
                v.reduce_sum(bsum[:, 1:2], sqs[:, b * NBT:(b + 1) * NBT], axis=AX.X)

        with nc.Block() as blk:
            @blk.tensor
            def _(te):
                te.transpose(ps_b[0:2, 0:128], bsum[:, :], identf[:, :])

        with nc.Block() as blk:
            @blk.vector
            def _(v):
                v.tensor_copy(brow2[0:2, :], ps_b[0:2, 0:128])

        with nc.Block() as blk:
            @blk.vector
            def _(v):
                v.reduce_sum(b21[0:2, :], brow2[0:2, :], axis=AX.X)

        with nc.Block() as blk:
            @blk.tensor
            def _(te):
                te.transpose(ps_b[0:1, 0:2], b21[0:2, :], identf[0:2, 0:2])

        with nc.Block() as blk:
            @blk.vector
            def _(v):
                v.tensor_copy(bsc[:, 0:2], ps_b[0:1, 0:2])

        with nc.Block() as blk:
            @blk.vector
            def _(v):
                selfsem(v, v.tensor_scalar_mul(bsc[:, 2:3], bsc[:, 0:1], 1.0 / GN))
                selfsem(v, v.tensor_scalar_mul(bsc[:, 3:4], bsc[:, 1:2], 1.0 / GN))
                selfsem(v, v.tensor_mul(bsc[:, 4:5], bsc[:, 2:3], bsc[:, 2:3]))
                selfsem(v, v.tensor_sub(bsc[:, 5:6], bsc[:, 3:4], bsc[:, 4:5]))

        with nc.Block() as blk:
            @blk.scalar
            def _(sc):
                sc.activation(bsc[:, 6:7], bsc[:, 5:6], AF.Ln,
                              bias=consts["eps_gn"][0:1, :]).then_inc(qs.h, 1)
                qs.n += 1
                sc.wait_ge(qs.h, qs.n)
                sc.activation(bsc[:, 7:8], bsc[:, 6:7], AF.Exp, bias=0.0, scale=-0.5)

        with nc.Block() as blk:
            @blk.sync
            def _(s):
                zst.inc(s.dma_start(out=scr_hbm[0:1, 0:1], in_=bsc[:, 2:3]), 16)
                zst.inc(s.dma_start(out=scr_hbm[0:1, 1:2], in_=bsc[:, 7:8]), 16)
                s.wait_ge(zst.h, zst.n)
                zst.inc(s.dma_start(out=mub[:, b:b + 1],
                                    in_=scr_hbm[0:1, 0:1].broadcast_to((128, 1))), 16)
                zst.inc(s.dma_start(out=rsb[:, b:b + 1],
                                    in_=scr_hbm[0:1, 1:2].broadcast_to((128, 1))), 16)
                s.wait_ge(zst.h, zst.n)

    x1v = env.sb("s1b_x1v", [128, 1024], F32)
    xttile = env.sb("s1b_xt", [128, 4 * 1024], F32)
    v4 = env.sb("s1b_v4", [128, 4 * 1024], F32)
    o1 = env.sb("s1b_o1", [128, 4 * 1024], BF16)
    amall = env.sb("s1b_am", [128, ntiles], F32)
    lnst = mk_ln_stats(env, "s1bln")
    lda = env.sem("s1b_lda")
    sta = env.sem("s1b_sta")

    vcons = env.sem("s1b_vcons")
    for tt in range(T // 512):
        b = (tt * 512) // S
        with nc.Block() as blk:
            la0, vc0 = lda.n, vcons.n

            @blk.sync
            def _(s):
                for j in range(4):
                    r0 = tt * 512 + j * 128
                    if j >= 2:
                        s.wait_ge(vcons.h, vc0 + j - 1)
                    lda.inc(s.dma_start(out=ybuf[:, (j % 2) * 1024:(j % 2) * 1024 + 1024],
                                        in_=y_hbm[r0:r0 + 128, :]), 16)
                    s.wait_ge(lda.h, lda.n)
                    lda.inc(s.dma_start(out=xttile[:, j * 1024:(j + 1) * 1024],
                                        in_=dd["x_tm"][r0:r0 + 128, :]), 16)
                    s.wait_ge(lda.h, lda.n)

            @blk.vector
            def _(v):
                for j in range(4):
                    v.wait_ge(lda.h, la0 + 32 * (j + 1))
                    vcons.inc(v.tensor_scalar(x1v[:, :],
                                              ybuf[:, (j % 2) * 1024:(j % 2) * 1024 + 1024],
                                              mub[:, b:b + 1], rsb[:, b:b + 1],
                                              ALU.subtract, ALU.mult))
                    v.tensor_add(v4[:, j * 1024:(j + 1) * 1024],
                                 xttile[:, j * 1024:(j + 1) * 1024], x1v[:, :])

        emit_ln_tail(env, consts, v4, o1, 4, lnst)
        with nc.Block() as blk:
            @blk.vector
            def _(v):
                for j in range(4):
                    v.reduce_max(amall[:, tt * 4 + j: tt * 4 + j + 1],
                                 o1[:, j * 1024:(j + 1) * 1024], axis=AX.X,
                                 apply_absolute_value=True)

            @blk.sync
            def _(s):
                for j in range(4):
                    sta.inc(s.dma_start(
                        out=dd["x1_tm"][tt * 512 + j * 128: tt * 512 + (j + 1) * 128, :],
                        in_=o1[:, j * 1024:(j + 1) * 1024]), 16)
                s.wait_ge(sta.h, sta.n)

    with nc.Block() as blk:
        @blk.sync
        def _(s):
            sta.inc(s.dma_start(out=dd["amax1_c"][:, :], in_=amall[:, :]), 16)
            s.wait_ge(sta.h, sta.n)
    env.pop_pool()


def load_consts(env, dd):
    nc = env.nc
    identbf = env.sb("c_identbf", [128, 128], BF16)
    identf = env.sb("c_identf", [128, 128], F32)
    ones_bf = env.sb("c_onesbf", [1, 128], BF16)
    eps_ln = env.sb("c_epsln", [128, 1], F32)
    eps_gn = env.sb("c_epsgn", [128, 1], F32)
    ld = env.sem("c_ld")
    with nc.Block() as blk:
        @blk.sync
        def _(s):
            ld.inc(s.dma_start(out=identbf[:, :], in_=dd["identbf"][:, :]), 16)
            ld.inc(s.dma_start(out=identf[:, :], in_=dd["identf"][:, :]), 16)
            ld.inc(s.dma_start(out=ones_bf[:, :], in_=dd["onesbf"][:, :]), 16)
            s.wait_ge(ld.h, ld.n)

        @blk.vector
        def _(v):
            v.memset(eps_ln[:, :], LN_EPS)
            v.memset(eps_gn[:, :], GN_EPS)
    return {"identbf": identbf, "identf": identf, "onesbf": ones_bf,
            "eps_ln": eps_ln, "eps_gn": eps_gn}


def host_consts():
    return {
        "identbf": np.eye(128, dtype=ml_dtypes.bfloat16),
        "identf": np.eye(128, dtype=np.float32),
        "onesbf": np.ones((1, 128), dtype=ml_dtypes.bfloat16),
    }


def vec_layout(v, p=128):
    return np.ascontiguousarray(v.reshape(-1, p).T)


def build_s4(T, debug=False):
    nc = bass.Bass("TRN2", target_bir_lowering=False)
    dd = {}
    dd["x3_tm"] = nc.dram_tensor("x3_tm", [T, D], BF16, kind="ExternalInput").ap()
    dd["w1T"] = nc.dram_tensor("w1T", [D, HID], F32, kind="ExternalInput").ap()
    dd["w2T"] = nc.dram_tensor("w2T", [D, HID], F32, kind="ExternalInput").ap()
    dd["h1T"] = nc.dram_tensor("h1T", [HID, HID], F32, kind="ExternalInput").ap()
    dd["h2T"] = nc.dram_tensor("h2T", [HID, HID], F32, kind="ExternalInput").ap()
    dd["w3T"] = nc.dram_tensor("w3T", [HID, D], F32, kind="ExternalInput").ap()
    for k, w in (("s1", HCH), ("s2", HCH), ("h1s", HCH), ("h2s", HCH), ("w3s", DCH)):
        dd[k] = nc.dram_tensor(k, [128, w], F32, kind="ExternalInput").ap()
    for k in ("identbf", "identf", "onesbf"):
        shape = [1, 128] if k == "onesbf" else [128, 128]
        dt = F32 if k == "identf" else BF16
        dd[k] = nc.dram_tensor(k, shape, dt, kind="ExternalInput").ap()
    dd["out"] = nc.dram_tensor("out", [T, D], F32, kind="ExternalOutput").ap()
    with ExitStack() as ctx:
        env = Env(nc, ctx)
        consts = load_consts(env, dd)
        tens = emit_s4(env, dd, T, consts)
        if debug:
            dsem = env.sem("dbg_st")
            douts = {}
            for k, sb in tens.items():
                p, f = sb.shape
                douts[k] = nc.dram_tensor("dbg_" + k, [p, f],
                                          mybir.dt(sb.dtype) if not isinstance(sb.dtype, mybir.dt) else sb.dtype,
                                          kind="ExternalOutput").ap()
            with nc.Block() as blk:
                @blk.sync
                def _(s):
                    for k, sb in tens.items():
                        dsem.inc(s.dma_start(out=douts[k][:, :], in_=sb[:, :]), 16)
                    s.wait_ge(dsem.h, dsem.n)
    return nc


def build_s2(T, seqlen=4096):
    nc = bass.Bass("TRN2", target_bir_lowering=False)
    dd = {}
    dd["x1_tm"] = nc.dram_tensor("x1_tm", [T, D], BF16, kind="ExternalInput").ap()
    dd["amax1_c"] = nc.dram_tensor("amax1_c", [128, T // 128], F32, kind="ExternalInput").ap()
    dd["w1T"] = nc.dram_tensor("w1T", [D, HID], F32, kind="ExternalInput").ap()
    dd["convw"] = nc.dram_tensor("convw", [3, HID], F32, kind="ExternalInput").ap()
    dd["convb"] = nc.dram_tensor("convb", [1, HID], F32, kind="ExternalInput").ap()
    dd["w2T"] = nc.dram_tensor("w2T", [HID, D], F32, kind="ExternalInput").ap()
    for k in ("identbf", "identf", "onesbf"):
        shape = [1, 128] if k == "onesbf" else [128, 128]
        dt = F32 if k == "identf" else BF16
        dd[k] = nc.dram_tensor(k, shape, dt, kind="ExternalInput").ap()
    dd["x2_tm"] = nc.dram_tensor("x2_tm", [T, D], BF16, kind="ExternalOutput").ap()
    with ExitStack() as ctx:
        env = Env(nc, ctx)
        consts = load_consts(env, dd)
        emit_s2(env, dd, T, consts, seqlen=seqlen)
    return nc


def build_s3(T, L=4096):
    nc = bass.Bass("TRN2", target_bir_lowering=False)
    dd = {}
    dd["x2_tm"] = nc.dram_tensor("x2_tm", [T, D], BF16, kind="ExternalInput").ap()
    for k in ("wqT", "wkT", "wvT", "woT"):
        dd[k] = nc.dram_tensor(k, [D, D], F32, kind="ExternalInput").ap()
    for k in ("bq", "bk", "bv"):
        dd[k] = nc.dram_tensor(k, [128, 8], F32, kind="ExternalInput").ap()
    dd["bo_row"] = nc.dram_tensor("bo_row", [1, D], F32, kind="ExternalInput").ap()
    dd["w0l"] = nc.dram_tensor("w0l", [128, 512], F32, kind="ExternalInput").ap()
    for k in ("identbf", "identf", "onesbf"):
        shape = [1, 128] if k == "onesbf" else [128, 128]
        dt = F32 if k == "identf" else BF16
        dd[k] = nc.dram_tensor(k, shape, dt, kind="ExternalInput").ap()
    dd["x3_tm"] = nc.dram_tensor("x3_tm", [T, D], BF16, kind="ExternalOutput").ap()
    with ExitStack() as ctx:
        env = Env(nc, ctx)
        consts = load_consts(env, dd)
        emit_s3(env, dd, T, consts, L=L)
    return nc


def build_s1(T, S=4096):
    nc = bass.Bass("TRN2", target_bir_lowering=False)
    dd = {}
    dd["x_fm"] = nc.dram_tensor("x_fm", [D, T], F32, kind="ExternalInput").ap()
    dd["x_tm"] = nc.dram_tensor("x_tm", [T, D], F32, kind="ExternalInput").ap()
    for k in ("wqT", "wkT", "wvT"):
        dd[k] = nc.dram_tensor(k, [D, D], F32, kind="ExternalInput").ap()
    for k in ("identbf", "identf", "onesbf"):
        shape = [1, 128] if k == "onesbf" else [128, 128]
        dt = F32 if k == "identf" else BF16
        dd[k] = nc.dram_tensor(k, shape, dt, kind="ExternalInput").ap()
    dd["x1_tm"] = nc.dram_tensor("x1_tm", [T, D], BF16, kind="ExternalOutput").ap()
    dd["amax1_c"] = nc.dram_tensor("amax1_c", [128, T // 128], F32, kind="ExternalOutput").ap()
    with ExitStack() as ctx:
        env = Env(nc, ctx)
        consts = load_consts(env, dd)
        y_hbm = emit_s1(env, dd, T, consts, S=S)
        emit_s1b(env, dd, y_hbm, T, consts, S=S)
    return nc


def build_fused(T=8192):
    nc = bass.Bass("TRN2", target_bir_lowering=False)
    dd = {}
    dd["x_fm"] = nc.dram_tensor("x_fm", [D, T], F32, kind="ExternalInput").ap()
    dd["x_tm"] = nc.dram_tensor("x_tm", [T, D], F32, kind="ExternalInput").ap()
    for k in ("s1_wqT", "s1_wkT", "s1_wvT", "s3_wqT", "s3_wkT", "s3_wvT", "s3_woT"):
        dd[k] = nc.dram_tensor("w_" + k, [D, D], F32, kind="ExternalInput").ap()
    dd["s2_w1T"] = nc.dram_tensor("w_s2_w1T", [D, HID], F32, kind="ExternalInput").ap()
    dd["s2_convw"] = nc.dram_tensor("w_s2_convw", [3, HID], F32, kind="ExternalInput").ap()
    dd["s2_convb"] = nc.dram_tensor("w_s2_convb", [1, HID], F32, kind="ExternalInput").ap()
    dd["s2_w2T"] = nc.dram_tensor("w_s2_w2T", [HID, D], F32, kind="ExternalInput").ap()
    for k in ("s3_bq", "s3_bk", "s3_bv"):
        dd[k] = nc.dram_tensor("w_" + k, [128, 8], F32, kind="ExternalInput").ap()
    dd["s3_bo_row"] = nc.dram_tensor("w_s3_bo_row", [1, D], F32, kind="ExternalInput").ap()
    dd["s3_w0l"] = nc.dram_tensor("w_s3_w0l", [128, 512], F32, kind="ExternalInput").ap()
    dd["s4_w1T"] = nc.dram_tensor("w_s4_w1T", [D, HID], F32, kind="ExternalInput").ap()
    dd["s4_w2T"] = nc.dram_tensor("w_s4_w2T", [D, HID], F32, kind="ExternalInput").ap()
    dd["s4_h1T"] = nc.dram_tensor("w_s4_h1T", [HID, HID], F32, kind="ExternalInput").ap()
    dd["s4_h2T"] = nc.dram_tensor("w_s4_h2T", [HID, HID], F32, kind="ExternalInput").ap()
    dd["s4_w3T"] = nc.dram_tensor("w_s4_w3T", [HID, D], F32, kind="ExternalInput").ap()
    for k, w in (("s4_s1", HCH), ("s4_s2", HCH), ("s4_h1s", HCH), ("s4_h2s", HCH), ("s4_w3s", DCH)):
        dd[k] = nc.dram_tensor("w_" + k, [128, w], F32, kind="ExternalInput").ap()
    for k in ("identbf", "identf", "onesbf"):
        shape = [1, 128] if k == "onesbf" else [128, 128]
        dt = F32 if k == "identf" else BF16
        dd[k] = nc.dram_tensor(k, shape, dt, kind="ExternalInput").ap()
    dd["out"] = nc.dram_tensor("out", [T, D], F32, kind="ExternalOutput").ap()
    dd["x1_tm"] = nc.dram_tensor("x1_tm", [T, D], BF16).ap()
    dd["amax1_c"] = nc.dram_tensor("amax1_c", [128, T // 128], F32).ap()
    dd["x2_tm"] = nc.dram_tensor("x2_tm", [T, D], BF16).ap()
    dd["x3_tm"] = nc.dram_tensor("x3_tm", [T, D], BF16).ap()

    def view(mapping):
        d2 = dict(dd)
        for k, v in mapping.items():
            d2[k] = dd[v]
        return d2

    with ExitStack() as ctx:
        env = Env(nc, ctx)
        consts = load_consts(env, dd)
        env.push_pool()
        env.sem_scope_reset()
        y_hbm = emit_s1(env, view({"wqT": "s1_wqT", "wkT": "s1_wkT", "wvT": "s1_wvT"}),
                        T, consts)
        emit_s1b(env, view({}), y_hbm, T, consts)
        env.pop_pool()
        env.push_pool()
        env.sem_scope_reset()
        emit_s2(env, view({"w1T": "s2_w1T", "convw": "s2_convw", "convb": "s2_convb",
                           "w2T": "s2_w2T"}), T, consts)
        env.pop_pool()
        env.push_pool()
        env.sem_scope_reset()
        emit_s3(env, view({"wqT": "s3_wqT", "wkT": "s3_wkT", "wvT": "s3_wvT",
                           "woT": "s3_woT", "bq": "s3_bq", "bk": "s3_bk", "bv": "s3_bv",
                           "bo_row": "s3_bo_row", "w0l": "s3_w0l"}), T, consts)
        env.pop_pool()
        env.push_pool()
        env.sem_scope_reset()
        emit_s4(env, view({"w1T": "s4_w1T", "w2T": "s4_w2T", "h1T": "s4_h1T",
                           "h2T": "s4_h2T", "w3T": "s4_w3T", "s1": "s4_s1",
                           "s2": "s4_s2", "h1s": "s4_h1s", "h2s": "s4_h2s",
                           "w3s": "s4_w3s"}), T, consts)
        env.pop_pool()
    return nc


T_CORE = 8192
_PROGS = {}


def _get_progs():
    if not _PROGS:
        _PROGS["s1"] = build_s1(T_CORE)
        _PROGS["s2"] = build_s2(T_CORE)
        _PROGS["s3"] = build_s3(T_CORE)
        _PROGS["s4"] = build_s4(T_CORE)
    return _PROGS


def kernel(**inputs):
    ins = {k: np.asarray(v) for k, v in inputs.items()}
    x = ins["L0_attn_x"]
    ncore = 8
    consts = host_consts()

    w0l = np.zeros((128, 512), np.float32)
    wrf = ins["L1_attn_Wrf"][0]
    for h in range(16):
        w0l[(h % 2) * 64:(h % 2) * 64 + 64, (h // 2) * 64:(h // 2) * 64 + 64] = wrf[h]
    shared = {
        "w_s1_wqT": np.ascontiguousarray(ins["L0_attn_Wq"].T),
        "w_s1_wkT": np.ascontiguousarray(ins["L0_attn_Wk"].T),
        "w_s1_wvT": np.ascontiguousarray(ins["L0_attn_Wv"].T),
        "w_s2_w1T": np.ascontiguousarray(ins["L0_ffn_w1"].T),
        "w_s2_convw": np.ascontiguousarray(ins["L0_ffn_conv_w"].reshape(4096, 3).T),
        "w_s2_convb": np.ascontiguousarray(ins["L0_ffn_conv_b"].reshape(1, 4096)),
        "w_s2_w2T": np.ascontiguousarray(ins["L0_ffn_w2"].T),
        "w_s3_wqT": np.ascontiguousarray(ins["L1_attn_Wq"].T),
        "w_s3_wkT": np.ascontiguousarray(ins["L1_attn_Wk"].T),
        "w_s3_wvT": np.ascontiguousarray(ins["L1_attn_Wv"].T),
        "w_s3_woT": np.ascontiguousarray(ins["L1_attn_Wo"].T),
        "w_s3_bq": vec_layout(ins["L1_attn_bq"]), "w_s3_bk": vec_layout(ins["L1_attn_bk"]),
        "w_s3_bv": vec_layout(ins["L1_attn_bv"]),
        "w_s3_bo_row": np.ascontiguousarray(ins["L1_attn_bo"].reshape(1, 1024)),
        "w_s3_w0l": w0l,
        "w_s4_w1T": np.ascontiguousarray(ins["L1_ffn_w1_w"].T),
        "w_s4_w2T": np.ascontiguousarray(ins["L1_ffn_w2_w"].T),
        "w_s4_h1T": np.ascontiguousarray(ins["L1_ffn_h1_w"].T),
        "w_s4_h2T": np.ascontiguousarray(ins["L1_ffn_h2_w"].T),
        "w_s4_w3T": np.ascontiguousarray(ins["L1_ffn_w3_w"].T),
        "w_s4_s1": vec_layout(ins["L1_ffn_w1_s"]), "w_s4_s2": vec_layout(ins["L1_ffn_w2_s"]),
        "w_s4_h1s": vec_layout(ins["L1_ffn_h1_s"]), "w_s4_h2s": vec_layout(ins["L1_ffn_h2_s"]),
        "w_s4_w3s": vec_layout(ins["L1_ffn_w3_s"]),
    }
    shared.update(consts)

    if "fused" not in _PROGS:
        _PROGS["fused"] = build_fused(T_CORE)
    maps = []
    for i in range(ncore):
        xc = np.ascontiguousarray(x[2 * i:2 * i + 2].reshape(T_CORE, 1024))
        m = {"x_fm": np.ascontiguousarray(xc.T), "x_tm": xc}
        m.update(shared)
        maps.append(m)
    res = run_bass_kernel_spmd(_PROGS["fused"], maps, core_ids=list(range(ncore)))
    out = np.empty((16, 4096, 1024), np.float32)
    for i in range(ncore):
        out[2 * i:2 * i + 2] = np.asarray(res.results[i]["out"]).reshape(2, 4096, 1024)
    return out
